# version 19
# baseline (speedup 1.0000x reference)
import os
import numpy as np
import ml_dtypes
from contextlib import ExitStack

import concourse.bass as bass
import concourse.mybir as mybir
from concourse.bass_utils import run_bass_kernel_spmd

F32 = mybir.dt.float32
BF16 = mybir.dt.bfloat16
AF = mybir.ActivationFunctionType
ALU = mybir.AluOpType
AX = mybir.AxisListType

D = 1024
NH = 16
NE = 8
HF = 512
T = 2048
HALO = 64
HT = T + 2 * HALO
NTI = T // 128
NHT = HT // 128
WIN = 192
FT = 256
NFT = T // FT
LN_EPS = 1e-5
ATT_EPS = 1e-6


class Sem:
    def __init__(self, handle):
        self.h = handle
        self.v = 0


class Ev:
    __slots__ = ("eng", "fn", "waits", "incs")

    def __init__(self, eng, fn, waits, incs):
        self.eng, self.fn, self.waits, self.incs = eng, fn, waits, incs


class Prog:
    def __init__(self):
        self.events = []

    def op(self, eng, fn, waits=(), inc=None, inc_n=1):
        incs = []
        if inc is not None:
            inc.v += inc_n
            incs.append((inc, inc_n))
        self.events.append(Ev(eng, fn, [w for w in waits if w is not None], incs))
        return (inc, inc.v) if inc is not None else None

    def emit(self, engine_name, eng):
        for ev in self.events:
            if ev.eng != engine_name:
                continue
            for (sem, val) in ev.waits:
                eng.wait_ge(sem.h, val)
            ins = ev.fn(eng)
            for (sem, n) in ev.incs:
                ins.then_inc(sem.h, n)


def build_nc():
    nc = bass.Bass("TRN2", target_bir_lowering=False)

    xh = nc.dram_tensor("xh", [HT, D], F32, kind="ExternalInput").ap()
    Wq = nc.dram_tensor("Wq", [D, D], F32, kind="ExternalInput").ap()
    Wk = nc.dram_tensor("Wk", [D, D], F32, kind="ExternalInput").ap()
    Wv = nc.dram_tensor("Wv", [D, D], F32, kind="ExternalInput").ap()
    Wo = nc.dram_tensor("Wo", [D, D], F32, kind="ExternalInput").ap()
    bqt = nc.dram_tensor("bqt", [128, 8], F32, kind="ExternalInput").ap()
    bkt = nc.dram_tensor("bkt", [128, 8], F32, kind="ExternalInput").ap()
    bvt = nc.dram_tensor("bvt", [128, 8], F32, kind="ExternalInput").ap()
    bo_row = nc.dram_tensor("bo_row", [1, D], F32, kind="ExternalInput").ap()
    gW = nc.dram_tensor("gW", [D, NE], F32, kind="ExternalInput").ap()
    W1 = nc.dram_tensor("W1", [NE, D, HF], F32, kind="ExternalInput").ap()
    b1t = nc.dram_tensor("b1t", [128, NE * 4], F32, kind="ExternalInput").ap()
    W2 = nc.dram_tensor("W2", [NE, HF, D], F32, kind="ExternalInput").ap()
    B2 = nc.dram_tensor("B2", [NE, D], F32, kind="ExternalInput").ap()
    Ws1 = nc.dram_tensor("Ws1", [D, HF], F32, kind="ExternalInput").ap()
    bs1t = nc.dram_tensor("bs1t", [128, 4], F32, kind="ExternalInput").ap()
    Ws2 = nc.dram_tensor("Ws2", [HF, D], F32, kind="ExternalInput").ap()
    bs2t = nc.dram_tensor("bs2t", [128, 8], F32, kind="ExternalInput").ap()
    rs = nc.dram_tensor("rs", [128, 1], F32, kind="ExternalInput").ap()
    mfirst = nc.dram_tensor("mfirst", [128, WIN], F32, kind="ExternalInput").ap()
    mint = nc.dram_tensor("mint", [128, WIN], F32, kind="ExternalInput").ap()
    mlast = nc.dram_tensor("mlast", [128, WIN], F32, kind="ExternalInput").ap()

    out = nc.dram_tensor("out", [T, D], F32, kind="ExternalOutput").ap()

    wq_bf = nc.dram_tensor("wq_bf", [D, D], BF16, kind="Internal").ap()
    wk_bf = nc.dram_tensor("wk_bf", [D, D], BF16, kind="Internal").ap()
    wv_bf = nc.dram_tensor("wv_bf", [D, D], BF16, kind="Internal").ap()
    wo_bf = nc.dram_tensor("wo_bf", [D, D], BF16, kind="Internal").ap()
    _dbgkind = "Internal" if os.environ.get("KDBG", "") != "1" else "ExternalOutput"
    xn_dram = nc.dram_tensor("xn_dram", [HT, D], BF16, kind=_dbgkind).ap()
    v_dram = nc.dram_tensor("v_dram", [D, HT + 128], BF16, kind=_dbgkind).ap()
    y2_dram = nc.dram_tensor("y2_dram", [D, T], BF16, kind=_dbgkind).ap()

    identb_d = nc.inline_tensor(np.eye(128, dtype=ml_dtypes.bfloat16), "c_identb").ap()
    identf_d = nc.inline_tensor(np.eye(128, dtype=np.float32), "c_identf").ap()
    _r0 = np.zeros((128, 128), np.float32); _r0[0, :] = 1.0
    oner0b_d = nc.inline_tensor(_r0.astype(ml_dtypes.bfloat16), "c_oner0b").ap()
    oner0f_d = nc.inline_tensor(_r0, "c_oner0f").ap()
    E_np = np.zeros((128, 8, 16), np.float32)
    for c in range(8):
        E_np[0:64, c, 2 * c] = 1.0
        E_np[64:128, c, 2 * c + 1] = 1.0
    Esel_d = nc.inline_tensor(
        E_np.reshape(128, 128).astype(ml_dtypes.bfloat16), "c_Esel").ap()
    S_np = np.zeros((16, 8, 128), np.float32)
    for c in range(8):
        S_np[2 * c, c, 0:64] = 1.0
        S_np[2 * c + 1, c, 64:128] = 1.0
    Ssel_d = nc.inline_tensor(S_np.reshape(16, 8 * 128), "c_Ssel").ap()
    Se_np = np.zeros((8, 8, 128), np.float32)
    for e in range(NE):
        Se_np[e, e, :] = 1.0
    Sesel_d = nc.inline_tensor(Se_np.reshape(8, 8 * 128).astype(ml_dtypes.bfloat16), "c_Sesel").ap()

    P = Prog()
    ctx = ExitStack()
    en = ctx.enter_context
    sb = nc.sbuf_tensor
    psn = nc.psum_tensor

    identf = en(sb("identf", [128, 128], F32))
    zer_sb = en(sb("zer_sb", [128, 128], BF16))
    oner0b = en(sb("oner0b", [128, 128], BF16))
    oner0f = en(sb("oner0f", [128, 128], F32))
    consts = en(sb("consts", [128, 4], F32))
    rs_s = en(sb("rs_s", [128, 1], F32))
    rsb = en(sb("rsb", [128, 1], F32))
    xnT_s = en(sb("xnT_s", [128, 8 * HT], BF16))
    stat = en(sb("stat", [128, 20], F32))
    g_scr = en(sb("g_scr", [128, 32], F32))

    sDg = Sem(en(nc.semaphore("sDg")))
    sDs = Sem(en(nc.semaphore("sDs")))
    sT = Sem(en(nc.semaphore("sT")))
    sS = Sem(en(nc.semaphore("sS")))
    sV = Sem(en(nc.semaphore("sV")))
    sOut = Sem(en(nc.semaphore("sOut")))

    def bar():
        return [(sT, sT.v), (sS, sS.v), (sV, sV.v), (sDg, sDg.v), (sDs, sDs.v)]

    def dmag(dst, src):
        return lambda g: g.dma_start(dst, src)

    _UPTO = int(os.environ.get("KUPTO", "9"))
    _SKIP = set(x for x in os.environ.get("KSKIP", "").split(",") if x)
    _DBG = os.environ.get("KDBG", "") == "1"

    def finish():
        with nc.Block() as block:
            @block.gpsimd
            def _(g):
                P.emit("gpsimd", g)

            @block.sync
            def _(sy):
                P.emit("sync", sy)

            @block.tensor
            def _(te):
                P.emit("tensor", te)

            @block.scalar
            def _(s):
                P.emit("scalar", s)

            @block.vector
            def _(v):
                P.emit("vector", v)
        return nc

    def stub_out(waits):
        P.op("gpsimd", lambda g: g.dma_start(out[:, :], xh[HALO:HALO + T, :]),
             waits=waits, inc=sOut, inc_n=16)
        P.op("gpsimd", lambda g: g.engine_nop(), waits=[(sOut, 16)])

    sbA = ExitStack()
    enA = sbA.enter_context
    Sesel = enA(sb("Sesel", [8, 8 * 128], BF16))
    b1_s = enA(sb("b1_s", [128, NE * 4], F32))
    bs1_s = enA(sb("bs1_s", [128, 4], F32))
    bs2_s = enA(sb("bs2_s", [128, 8], F32))
    B2_s = enA(sb("B2_s", [8, D], BF16))
    gateW_s = enA(sb("gateW_s", [128, 8 * NE], BF16))
    gT_s = enA(sb("gT_s", [8, T], BF16))
    x_sb = [enA(sb(f"x_sb{i}", [128, D], F32)) for i in range(2)]
    xn_sb = [enA(sb(f"xn_sb{i}", [128, D], BF16)) for i in range(2)]
    Ws1_s = enA(sb("Ws1_s", [128, 8 * HF], BF16))
    Ws2_s = enA(sb("Ws2_s", [128, 4 * D], BF16))
    W1a = enA(sb("W1a", [128, NE * 4096], BF16))
    W2a = enA(sb("W2a", [128, NE * 4096], BF16))
    hg_sb = [enA(sb(f"hg_sb{i}", [128, FT], BF16)) for i in range(2)]
    hsT_sb = enA(sb("hsT_sb", [128, 4 * FT], BF16))
    hgg_sb2 = [hsT_sb, enA(sb("hgg_b", [128, 4 * FT], BF16))]
    y2sb = enA(sb("y2sb", [128, 4 * FT], BF16))
    zsc_sb = enA(sb("zsc_sb", [8, 128], F32))
    psA = ExitStack()
    enAp = psA.enter_context
    sm_ps = enAp(psn("sm_ps", [128, 512], F32))
    pA = [enAp(psn(f"pA{i}", [128, 512], F32)) for i in range(2)]
    y2p = enAp(psn("y2p", [128, 2048], F32))
    rr_ps = enAp(psn("rr_ps", [128, 512], F32))

    for dst, srcap in [
        (identf[:, :], identf_d),
        (oner0b[:, :], oner0b_d), (oner0f[:, :], oner0f_d),
        (Sesel[:, :], Sesel_d),
        (b1_s[:, :], b1t), (bs1_s[:, :], bs1t), (bs2_s[:, :], bs2t),
        (B2_s[:, :], B2), (rs_s[:, :], rs),
        (gateW_s[:, :].rearrange("p (kc j) -> p kc j", kc=8),
         gW.rearrange("(kc p) j -> p kc j", p=128)),
        (Ws1_s[:, :].rearrange("p (kc f) -> p kc f", kc=8),
         Ws1.rearrange("(kc p) f -> p kc f", p=128)),
        (Ws2_s[:, :].rearrange("p (kc f) -> p kc f", kc=4),
         Ws2.rearrange("(kc p) f -> p kc f", p=128)),
    ]:
        P.op("gpsimd", dmag(dst, srcap), inc=sDg, inc_n=16)
    c_small = (sDg, sDg.v)

    def f_consts(v):
        v.memset(consts[:, 0:1], LN_EPS)
        v.memset(consts[:, 1:2], ATT_EPS)
        v.memset(zer_sb[:, :], 0.0)
        return v.memset(consts[:, 2:3], 0.0)
    P.op("vector", f_consts, inc=sV)
    c_consts = (sV, sV.v)

    P.op("tensor", lambda t: t.matmul(sm_ps[:, 0:1], oner0f[:, :], rs_s[:, :],
                                      start=True, stop=True),
         waits=[c_consts, c_small], inc=sT)
    c_rsmm = (sT, sT.v)
    P.op("vector", lambda v: v.tensor_copy(rsb[:, :], sm_ps[:, 0:1]),
         waits=[c_rsmm], inc=sV)
    c_rsb = (sV, sV.v)

    slot_store = {0: None, 1: None}
    slot_free = {0: None, 1: None}
    prev_var = None
    c_xn_tile = {}
    for t in range(NHT):
        sl = t % 2
        P.op("sync", lambda sy, sl=sl, t=t: sy.dma_start(
            x_sb[sl][:, :], xh[t * 128:(t + 1) * 128, :]),
             waits=[slot_free[sl]], inc=sDs, inc_n=16)
        c_xload = (sDs, sDs.v)
        def f_bn(v, sl=sl):
            v.bn_stats(stat[:, 6:12], x_sb[sl][:, 0:512])
            return v.bn_stats(stat[:, 12:18], x_sb[sl][:, 512:1024])
        P.op("vector", f_bn, waits=[c_xload, prev_var], inc=sV)
        c_bn = (sV, sV.v)
        P.op("vector", lambda v: v.bn_aggr(stat[:, 2:4], stat[:, 6:18]),
             waits=[c_bn], inc=sV)
        c_var = (sV, sV.v)
        prev_var = c_var
        P.op("scalar", lambda s: s.activation(stat[:, 5:6], stat[:, 3:4], AF.Ln,
                                              bias=consts[:, 0:1]),
             waits=[c_var, c_consts], inc=sS)
        c_ln = (sS, sS.v)
        P.op("scalar", lambda s: s.activation(stat[:, 5:6], stat[:, 5:6], AF.Exp,
                                              scale=-0.5), waits=[c_ln], inc=sS)
        c_rstd = (sS, sS.v)
        P.op("vector", lambda v, sl=sl: v.tensor_scalar(
            xn_sb[sl][:, :], x_sb[sl][:, :], stat[:, 2:3], stat[:, 5:6],
            ALU.subtract, ALU.mult),
             waits=[c_rstd, slot_store[sl]], inc=sV)
        c_xn = (sV, sV.v)
        slot_free[sl] = c_xn
        P.op("gpsimd", dmag(xn_dram[t * 128:(t + 1) * 128, :], xn_sb[sl][:, :]),
             waits=[c_xn], inc=sDg, inc_n=16)
        slot_store[sl] = (sDg, sDg.v)
        c_xn_tile[t] = (sDg, sDg.v)
    c_xn_all = (sDg, sDg.v)

    W1r = W1.rearrange("e (kc p) f -> p e kc f", p=128)
    W2r = W2.rearrange("e (kc p) f -> p e kc f", p=128)
    for e in range(NE):
        P.op("gpsimd", dmag(
            W1a[:, e * 4096:(e + 1) * 4096].rearrange("p (kc f) -> p kc f", kc=8),
            W1r[:, e:e + 1, :, :].rearrange("p one kc f -> p (one kc) f")),
             inc=sDg, inc_n=16)
        P.op("gpsimd", dmag(
            W2a[:, e * 4096:(e + 1) * 4096].rearrange("p (kc f) -> p kc f", kc=4),
            W2r[:, e:e + 1, :, :].rearrange("p one kc f -> p (one kc) f")),
             inc=sDg, inc_n=16)
    c_expw = (sDg, sDg.v)

    for wbf, w in [(wq_bf, Wq), (wk_bf, Wk), (wv_bf, Wv), (wo_bf, Wo)]:
        P.op("gpsimd", dmag(wbf, w), inc=sDg, inc_n=16)
    c_wcast = (sDg, sDg.v)


    bands = [576, 1152, HT]
    c_band = []
    b_prev = 0
    for bk in bands:
        mark = c_xn_tile[(bk + 127) // 128 - 1]
        for c in range(8):
            P.op("sync", lambda sy, c=c, b0=b_prev, b1=bk: sy.dma_start(
                xnT_s[:, c * HT + b0:c * HT + b1],
                xn_dram[b0:b1, c * 128:(c + 1) * 128],
                transpose=True), waits=[mark], inc=sDs, inc_n=16)
        c_band.append((sDs, sDs.v))
        b_prev = bk
    c_xnT = c_band[-1]

    def xnT_ready(last_col):
        for bk, cb in zip(bands, c_band):
            if last_col <= bk:
                return cb
        return c_xnT

    if _UPTO <= 0:
        stub_out(bar())
        return finish()

    gst = {"c_gdone": None, "c_ez": None}

    def emit_gate(t):
        c_gdone = gst["c_gdone"]
        c_ez = gst["c_ez"]
        hb = HALO + t * 128
    
        def mm_gate(te, hb=hb):
            for kc in range(8):
                ins = te.matmul(sm_ps[0:8, 0:128],
                                gateW_s[:, kc * 8:(kc + 1) * 8],
                                xnT_s[:, kc * HT + hb:kc * HT + hb + 128],
                                start=(kc == 0), stop=(kc == 7))
            return ins
        P.op("tensor", mm_gate,
             waits=[xnT_ready(hb + 128), c_small, c_gdone], inc=sT)
        c_zmm = (sT, sT.v)
        P.op("vector", lambda v: v.tensor_copy(zsc_sb[:, :], sm_ps[0:8, 0:128]),
             waits=[c_zmm], inc=sV)
        c_zev = (sV, sV.v)
        P.op("tensor", lambda te: te.transpose(sm_ps[:, 128:136], zsc_sb[:, :],
                                               identf[0:8, 0:8]),
             waits=[c_zev, c_ez], inc=sT)
        c_z = (sT, sT.v)
        P.op("scalar", lambda s: s.activation(g_scr[:, 8:16], sm_ps[:, 128:136],
                                              AF.Exp, scale=-1.0),
             waits=[c_z, c_gdone], inc=sS)
        c_ez = (sS, sS.v)
        prev = c_ez
        for fn in [
            lambda v: v.tensor_scalar(g_scr[:, 8:16], g_scr[:, 8:16], 1.0, None,
                                      ALU.add),
            lambda v: v.reciprocal(g_scr[:, 0:8], g_scr[:, 8:16]),
            lambda v: v.tensor_scalar(g_scr[:, 0:8], g_scr[:, 0:8], rsb[:, 0:1],
                                      None, ALU.mult),
            lambda v: v.max(out=g_scr[:, 16:24], in_=g_scr[:, 0:8]),
            lambda v: v.tensor_add(g_scr[:, 24:25], g_scr[:, 16:17], g_scr[:, 17:18]),
            lambda v: v.reciprocal(g_scr[:, 25:26], g_scr[:, 24:25]),
            lambda v: v.tensor_scalar(g_scr[:, 8:16], g_scr[:, 0:8], g_scr[:, 17:18],
                                      None, ALU.is_ge),
            lambda v: v.scalar_tensor_tensor(g_scr[:, 0:8], g_scr[:, 0:8],
                                             g_scr[:, 25:26], g_scr[:, 8:16],
                                             ALU.mult, ALU.mult),
        ]:
            P.op("vector", fn, waits=[prev, c_rsb], inc=sV)
            prev = (sV, sV.v)
        c_g = prev
        P.op("tensor", lambda te: te.transpose(
            sm_ps[0:8, 0:128], g_scr[:, 0:8], identf[:, :]),
             waits=[c_g, c_rsb, c_gdone], inc=sT)
        c_gT = (sT, sT.v)
        P.op("vector", lambda v, t=t: v.tensor_copy(
            gT_s[:, t * 128:(t + 1) * 128], sm_ps[0:8, 0:128]),
             waits=[c_gT], inc=sV)
        c_gdone = (sV, sV.v)
        gst["c_gdone"] = c_gdone
        gst["c_ez"] = c_ez

    c_y2_store = None
    c_prev_evac = None
    c_pA_free = [None, None]
    c_hsT_used = None
    c_hg_used = [None, None]
    c_hgg_used = None
    c_hgg_used2 = [None, None]
    c_G_used = [None, None]
    emit_gate(0)
    emit_gate(1)
    g_ready = {0: gst["c_gdone"]}
    for ft in range(NFT):
        fb = HALO + ft * FT
        if ft + 1 < NFT:
            emit_gate(2 * (ft + 1))
            emit_gate(2 * (ft + 1) + 1)
            g_ready[ft + 1] = gst["c_gdone"]
        c_gdone = g_ready[ft]
        c_hs_evac = [None] * 4
        for mc in range(4):
            def mm_hs(te, mc=mc, fb=fb):
                for kc in range(8):
                    ins = te.matmul(pA[mc % 2][:, 0:FT],
                                    Ws1_s[:, kc * HF + mc * 128:kc * HF + (mc + 1) * 128],
                                    xnT_s[:, kc * HT + fb:kc * HT + fb + FT],
                                    start=(kc == 0), stop=(kc == 7))
                return ins
            P.op("tensor", mm_hs,
                 waits=[xnT_ready(fb + FT), c_small, c_pA_free[mc % 2]], inc=sT)
            c_mm = (sT, sT.v)
            P.op("scalar", lambda s, mc=mc: s.activation(
                hsT_sb[:, mc * FT:(mc + 1) * FT], pA[mc % 2][:, 0:FT], AF.Gelu,
                bias=bs1_s[:, mc:mc + 1]),
                 waits=[c_mm, c_hsT_used, c_hgg_used2[0]], inc=sS)
            c_hs_evac[mc] = (sS, sS.v)
            c_pA_free[mc % 2] = c_hs_evac[mc]

        def mm_y2z(te):
            for b in range(4):
                ins = te.matmul(y2p[:, b * 512:(b + 1) * 512], zer_sb[:, :],
                                Ws1_s[:, 0:512], start=True, stop=False,
                                skip_group_check=True)
            return ins
        P.op("tensor", mm_y2z, waits=[c_prev_evac, c_consts, c_small], inc=sT)
        c_y2z = (sT, sT.v)

        def mm_ws2(te):
            ins = None
            for c in range(8):
                for mc in range(4):
                    ins = te.matmul(y2p[:, c * FT:(c + 1) * FT],
                                    Ws2_s[:, mc * D + c * 128:mc * D + (c + 1) * 128],
                                    hsT_sb[:, mc * FT:(mc + 1) * FT],
                                    start=False, stop=False,
                                    skip_group_check=True)
            return ins
        P.op("tensor", mm_ws2, waits=[c_hs_evac[3], c_y2z], inc=sT)
        c_hsT_used = (sT, sT.v)

        def mm_b2(te, ft=ft):
            ins = None
            for c in range(8):
                ins = te.matmul(y2p[:, c * FT:(c + 1) * FT],
                                B2_s[:, c * 128:(c + 1) * 128],
                                gT_s[:, ft * FT:(ft + 1) * FT],
                                start=False, stop=False, skip_group_check=True)
            return ins
        P.op("tensor", mm_b2, waits=[c_gdone], inc=sT)

        for e in range(NE):
            gsl = e % 2
            P.op("tensor", lambda te, e=e, ft=ft, gsl=gsl: te.matmul(
                rr_ps[:, gsl * FT:(gsl + 1) * FT],
                Sesel[:, e * 128:(e + 1) * 128],
                gT_s[:, ft * FT:(ft + 1) * FT], start=True, stop=True),
                 waits=[c_gdone, c_G_used[gsl]], inc=sT)
            c_G = (sT, sT.v)
            for mc in range(4):
                def mm_h(te, e=e, mc=mc, fb=fb):
                    for kc in range(8):
                        ins = te.matmul(
                            pA[mc % 2][:, 0:FT],
                            W1a[:, e * 4096 + kc * HF + mc * 128:
                                 e * 4096 + kc * HF + (mc + 1) * 128],
                            xnT_s[:, kc * HT + fb:kc * HT + fb + FT],
                            start=(kc == 0), stop=(kc == 7))
                    return ins
                P.op("tensor", mm_h,
                     waits=[xnT_ready(fb + FT), c_expw, c_pA_free[mc % 2]], inc=sT)
                c_hmm = (sT, sT.v)
                P.op("scalar", lambda s, e=e, mc=mc: s.activation(
                    hg_sb[mc % 2][:, :], pA[mc % 2][:, 0:FT], AF.Gelu,
                    bias=b1_s[:, e * 4 + mc:e * 4 + mc + 1]),
                     waits=[c_hmm, c_hg_used[mc % 2]], inc=sS)
                c_hg = (sS, sS.v)
                c_pA_free[mc % 2] = c_hg
                P.op("vector", lambda v, mc=mc, gsl=gsl, e=e: v.tensor_mul(
                    hgg_sb2[e % 2][:, mc * FT:(mc + 1) * FT], hg_sb[mc % 2][:, :],
                    rr_ps[:, gsl * FT:(gsl + 1) * FT]),
                     waits=[c_hg, c_G, c_hgg_used2[e % 2], c_hsT_used], inc=sV)
                c_hg_used[mc % 2] = (sV, sV.v)
            c_hgg = (sV, sV.v)
            c_G_used[gsl] = c_hgg

            def mm_w2(te, e=e):
                ins = None
                for c in range(8):
                    for mc in range(4):
                        ins = te.matmul(
                            y2p[:, c * FT:(c + 1) * FT],
                            W2a[:, e * 4096 + mc * D + c * 128:
                                 e * 4096 + mc * D + (c + 1) * 128],
                            hgg_sb2[e % 2][:, mc * FT:(mc + 1) * FT],
                            start=False, stop=(e == NE - 1 and c == 7 and mc == 3),
                            skip_group_check=True)
                return ins
            P.op("tensor", mm_w2, waits=[c_hgg], inc=sT)
            c_hgg_used2[e % 2] = (sT, sT.v)
            c_hgg_used = (sT, sT.v)
        c_y2mm = (sT, sT.v)

        for half in range(2):
            def ev_y2(s, half=half):
                for c in range(4 * half, 4 * half + 4):
                    ins = s.activation(y2sb[:, (c - 4 * half) * FT:(c - 4 * half + 1) * FT],
                                       y2p[:, c * FT:(c + 1) * FT], AF.Identity,
                                       bias=bs2_s[:, c:c + 1])
                return ins
            P.op("scalar", ev_y2, waits=[c_y2mm, c_y2_store], inc=sS)
            c_prev_evac = (sS, sS.v)

            def st_y2(g, ft=ft, half=half):
                dst = y2_dram.rearrange("(c p) t -> p c t", p=128)[
                    :, 4 * half:4 * half + 4, ft * FT:(ft + 1) * FT]
                srcap = y2sb[:, :].rearrange("p (c f) -> p c f", c=4)
                return g.dma_start(dst, srcap)
            P.op("gpsimd", st_y2, waits=[c_prev_evac], inc=sDg, inc_n=16)
            c_y2_store = (sDg, sDg.v)
    c_y2_all = c_y2_store
    if _DBG:
        d_gT = nc.dram_tensor("d_gT", [8, T], BF16, kind="ExternalOutput").ap()
        P.op("gpsimd", lambda g: g.dma_start(d_gT, gT_s[:, :]),
             waits=[gst["c_gdone"]], inc=sDg, inc_n=16)
    c_p2_bar = bar()
    if _UPTO <= 2:
        stub_out(c_p2_bar)
        return finish()

    sbA.close()
    psA.close()
    sbB = ExitStack()
    enB = sbB.enter_context
    WoT_s = enB(sb("WoT_s", [128, 8 * D], BF16))
    knT_s = enB(sb("knT_s", [128, 8 * HT], BF16))
    qT_s = enB(sb("qT_s", [128, 8 * T], BF16))
    rqT_s = enB(sb("rqT_s", [128, NTI * NH], F32))
    masks = enB(sb("masks", [128, 3 * WIN], BF16))
    identb = enB(sb("identb", [128, 128], BF16))
    bor = enB(sb("bor", [128, D], BF16))
    sbB1 = ExitStack()
    enB1 = sbB1.enter_context
    WqT_s = enB1(sb("WqT_s", [128, 8 * D], BF16))
    WkT_s = enB1(sb("WkT_s", [128, 8 * D], BF16))
    WvT_s = enB1(sb("WvT_s", [128, 8 * D], BF16))
    kraw_s = enB1(sb("kraw_s", [128, 8 * 512], BF16))
    sq_scr = [enB1(sb(f"sq_scr{i}", [128, D], BF16)) for i in range(2)]
    rq_f = enB1(sb("rq_f", [16, 512], F32))
    rk_f = enB1(sb("rk_f", [16, 512], F32))
    Ssel = enB1(sb("Ssel", [16, 8 * 128], F32))
    Esel = enB1(sb("Esel", [128, 128], BF16))
    bv_s = enB1(sb("bv_s", [128, 8], F32))
    bq_s = enB1(sb("bq_s", [128, 8], F32))
    bk_s = enB1(sb("bk_s", [128, 8], F32))
    P.op("vector", lambda v: v.memset(bor[:, :], 0.0), waits=c_p2_bar, inc=sV)
    c_bz = (sV, sV.v)
    for dst, srcap in [
        (masks[:, 0:WIN], mfirst), (masks[:, WIN:2 * WIN], mint),
        (masks[:, 2 * WIN:3 * WIN], mlast),
        (identb[:, :], identb_d), (bor[0:1, :], bo_row),
        (Ssel[:, :], Ssel_d), (Esel[:, :], Esel_d), (bv_s[:, :], bvt),
        (bq_s[:, :], bqt), (bk_s[:, :], bkt),
    ]:
        P.op("gpsimd", dmag(dst, srcap), waits=c_p2_bar + [c_bz], inc=sDg, inc_n=16)
    c_bsmall = (sDg, sDg.v)
    psB = ExitStack()
    enBp = psB.enter_context
    sm2_ps = enBp(psn("sm2_ps", [128, 128], F32))
    pB2 = [enBp(psn(f"pB2{i}", [128, 512], F32)) for i in range(4)]
    ss_ps = enBp(psn("ss_ps", [16, 512], F32))
    rr2_ps = [enBp(psn(f"rr2_ps{i}", [128, 512], F32)) for i in range(2)]

    wt_marks = {}
    for wbf, wt, nm in [(wk_bf, WkT_s, "k"), (wv_bf, WvT_s, "v"),
                        (wq_bf, WqT_s, "q"), (wo_bf, WoT_s, "o")]:
        for c in range(8):
            P.op("sync", lambda sy, wbf=wbf, wt=wt, c=c: sy.dma_start(
                wt[:, c * D:(c + 1) * D], wbf[:, c * 128:(c + 1) * 128],
                transpose=True), waits=[c_wcast] + c_p2_bar, inc=sDs, inc_n=16)
        wt_marks[nm] = (sDs, sDs.v)
    c_WkT, c_WvT, c_WqT, c_WoT = (wt_marks["k"], wt_marks["v"],
                                  wt_marks["q"], wt_marks["o"])

    c_pB2_free = [None, None, None, None]
    c_kraw_used = None
    c_rr2_used = [None, None]
    c_sq_used = [None, None]
    c_ss_free = None

    for (off, w) in [(0, 512), (512, 512), (1024, 512), (1536, 512), (2048, 128)]:
        c_kev = [None] * 8
        for cch in range(8):
            sl = cch % 4

            def mm_k(te, cch=cch, off=off, w=w, sl=sl):
                for kc in range(8):
                    ins = te.matmul(pB2[sl][:, 0:w],
                                    WkT_s[:, kc * D + cch * 128:kc * D + (cch + 1) * 128],
                                    xnT_s[:, kc * HT + off:kc * HT + off + w],
                                    start=(kc == 0), stop=(kc == 7))
                return ins
            P.op("tensor", mm_k, waits=[c_WkT, c_xnT, c_pB2_free[sl], c_kraw_used],
                 inc=sT)
            c_kmm = (sT, sT.v)
            P.op("scalar", lambda s, cch=cch, w=w, sl=sl: s.activation(
                kraw_s[:, cch * 512:cch * 512 + w], pB2[sl][:, 0:w], AF.Identity,
                bias=bk_s[:, cch:cch + 1]), waits=[c_kmm, c_bsmall], inc=sS)
            c_kev[cch] = (sS, sS.v)
            c_pB2_free[sl] = c_kev[cch]
            P.op("vector", lambda v, cch=cch, w=w, sq=cch % 2: v.tensor_mul(
                sq_scr[sq][:, 0:w], kraw_s[:, cch * 512:cch * 512 + w],
                kraw_s[:, cch * 512:cch * 512 + w]),
                 waits=[c_kev[cch], c_sq_used[cch % 2]], inc=sV)
            c_sq = (sV, sV.v)
            P.op("tensor", lambda te, cch=cch, w=w, sq=cch % 2: te.matmul(
                ss_ps[:, 0:w], Esel[:, cch * 16:(cch + 1) * 16], sq_scr[sq][:, 0:w],
                start=(cch == 0), stop=(cch == 7)),
                 waits=[c_sq, c_bsmall, c_ss_free if cch == 0 else None], inc=sT)
            c_sq_used[cch % 2] = (sT, sT.v)
        c_ssk = (sT, sT.v)

        prevk = c_ssk
        for fn in [
            lambda s, w=w: s.activation(rk_f[0:16, 0:w], ss_ps[0:16, 0:w], AF.Ln),
            lambda s, w=w: s.activation(rk_f[0:16, 0:w], rk_f[0:16, 0:w], AF.Exp,
                                        scale=0.5),
            lambda s, w=w: s.activation(rk_f[0:16, 0:w], rk_f[0:16, 0:w], AF.Ln,
                                        bias=consts[0:16, 1:2]),
            lambda s, w=w: s.activation(rk_f[0:16, 0:w], rk_f[0:16, 0:w], AF.Exp,
                                        scale=-1.0),
        ]:
            P.op("scalar", fn, waits=[prevk], inc=sS)
            prevk = (sS, sS.v)
        c_rk = prevk
        c_ss_free = c_rk
        for cch in range(8):
            sl = cch % 2
            P.op("tensor", lambda te, cch=cch, w=w, sl=sl: te.matmul(
                rr2_ps[sl][:, 0:w], Ssel[:, cch * 128:(cch + 1) * 128],
                rk_f[0:16, 0:w], start=True, stop=True),
                 waits=[c_rk, c_bsmall, c_rr2_used[sl]], inc=sT)
            c_R = (sT, sT.v)
            P.op("vector", lambda v, cch=cch, off=off, w=w, sl=sl: v.tensor_mul(
                knT_s[:, cch * HT + off:cch * HT + off + w],
                kraw_s[:, cch * 512:cch * 512 + w], rr2_ps[sl][:, 0:w]),
                 waits=[c_R], inc=sV)
            c_rr2_used[sl] = (sV, sV.v)
        c_kraw_used = (sV, sV.v)

    c_smT_used = None
    for m in range(4):
        off = HALO + m * 512
        for cch in range(8):
            sl = cch % 4

            def mm_q(te, cch=cch, off=off, sl=sl):
                for kc in range(8):
                    ins = te.matmul(pB2[sl][:, 0:512],
                                    WqT_s[:, kc * D + cch * 128:kc * D + (cch + 1) * 128],
                                    xnT_s[:, kc * HT + off:kc * HT + off + 512],
                                    start=(kc == 0), stop=(kc == 7))
                return ins
            P.op("tensor", mm_q, waits=[c_WqT, c_pB2_free[sl]], inc=sT)
            c_qmm = (sT, sT.v)
            P.op("scalar", lambda s, cch=cch, m=m, sl=sl: s.activation(
                qT_s[:, cch * T + m * 512:cch * T + (m + 1) * 512], pB2[sl][:, 0:512],
                AF.Identity, bias=bq_s[:, cch:cch + 1]), waits=[c_qmm, c_bsmall], inc=sS)
            c_qev = (sS, sS.v)
            c_pB2_free[sl] = c_qev
            P.op("vector", lambda v, cch=cch, m=m, sq=cch % 2: v.tensor_mul(
                sq_scr[sq][:, 0:512],
                qT_s[:, cch * T + m * 512:cch * T + (m + 1) * 512],
                qT_s[:, cch * T + m * 512:cch * T + (m + 1) * 512]),
                 waits=[c_qev, c_sq_used[cch % 2]], inc=sV)
            c_sq = (sV, sV.v)
            P.op("tensor", lambda te, cch=cch, sq=cch % 2: te.matmul(
                ss_ps[:, 0:512], Esel[:, cch * 16:(cch + 1) * 16],
                sq_scr[sq][:, 0:512], start=(cch == 0), stop=(cch == 7)),
                 waits=[c_sq, c_bsmall, c_ss_free if cch == 0 else None], inc=sT)
            c_sq_used[cch % 2] = (sT, sT.v)
        c_ssq_mm = (sT, sT.v)

        prevq = c_ssq_mm
        for fn in [
            lambda s: s.activation(rq_f[0:16, 0:512], ss_ps[0:16, 0:512], AF.Ln),
            lambda s: s.activation(rq_f[0:16, 0:512], rq_f[0:16, 0:512], AF.Exp,
                                   scale=0.5),
            lambda s: s.activation(rq_f[0:16, 0:512], rq_f[0:16, 0:512], AF.Ln,
                                   bias=consts[0:16, 1:2]),
            lambda s: s.activation(rq_f[0:16, 0:512], rq_f[0:16, 0:512], AF.Exp,
                                   scale=-1.0),
        ]:
            P.op("scalar", fn, waits=[prevq], inc=sS)
            prevq = (sS, sS.v)
        c_rq = prevq
        c_ss_free = c_rq
        for tt in range(4):
            P.op("tensor", lambda te, tt=tt: te.transpose(
                sm2_ps[0:128, 0:16], rq_f[0:16, tt * 128:(tt + 1) * 128],
                identf[0:16, 0:16]), waits=[c_rq, c_smT_used], inc=sT)
            c_trq = (sT, sT.v)
            P.op("vector", lambda v, m=m, tt=tt: v.tensor_copy(
                rqT_s[:, (m * 4 + tt) * 16:(m * 4 + tt) * 16 + 16],
                sm2_ps[0:128, 0:16]), waits=[c_trq], inc=sV)
            c_smT_used = (sV, sV.v)

    c_vst = None
    c_pB2v_free = list(c_pB2_free)
    for (off, w) in [(0, 512), (512, 512), (1024, 512), (1536, 512), (2048, 128)]:
        c_vev = [None] * 8
        for mc in range(8):
            sl = mc % 4

            def mm_v(te, mc=mc, off=off, w=w, sl=sl):
                for kc in range(8):
                    ins = te.matmul(
                        pB2[sl][:, 0:w],
                        WvT_s[:, kc * D + mc * 128:kc * D + (mc + 1) * 128],
                        xnT_s[:, kc * HT + off:kc * HT + off + w],
                        start=(kc == 0), stop=(kc == 7))
                return ins
            P.op("tensor", mm_v, waits=[c_WvT, c_pB2v_free[sl], c_kraw_used, c_vst],
                 inc=sT)
            c_vmm = (sT, sT.v)
            P.op("scalar", lambda s, mc=mc, w=w, sl=sl: s.activation(
                kraw_s[:, mc * 512:mc * 512 + w], pB2[sl][:, 0:w], AF.Identity,
                bias=bv_s[:, mc:mc + 1]),
                 waits=[c_vmm, c_bsmall], inc=sS)
            c_vev[mc] = (sS, sS.v)
            c_pB2v_free[sl] = c_vev[mc]

        def st_v(g, off=off, w=w):
            dst = v_dram.rearrange("(mc p) t -> p mc t", p=128)[:, :, off:off + w]
            srcap = kraw_s[:, :].rearrange("p (mc f) -> p mc f", mc=8)[:, :, 0:w]
            return g.dma_start(dst, srcap)
        P.op("gpsimd", st_v, waits=[c_vev[7]], inc=sDg, inc_n=16)
        c_vst = (sDg, sDg.v)
    c_p1a_bar = bar()

    if _DBG:
        d_knT = nc.dram_tensor("d_knT", [128, 8 * HT], BF16, kind="ExternalOutput").ap()
        d_qT = nc.dram_tensor("d_qT", [128, 8 * T], BF16, kind="ExternalOutput").ap()
        d_rqT = nc.dram_tensor("d_rqT", [128, NTI * NH], F32, kind="ExternalOutput").ap()
        for dst, srct in [(d_knT, knT_s), (d_qT, qT_s), (d_rqT, rqT_s)]:
            P.op("gpsimd", lambda g, dst=dst, srct=srct: g.dma_start(dst, srct[:, :]),
                 waits=c_p1a_bar, inc=sDg, inc_n=16)

    if _UPTO <= 3:
        stub_out(c_p1a_bar)
        return finish()

    psB.close()
    sbB1.close()
    xL_sb = [enB(sb(f"xL_sb{i}", [128, D], F32)) for i in range(2)]
    vw_a = [enB(sb(f"vw_a{i}", [128, D], BF16)) for i in range(2)]
    vw_b = [enB(sb(f"vw_b{i}", [128, D], BF16)) for i in range(2)]
    vw_c = [enB(sb(f"vw_c{i}", [128, D], BF16)) for i in range(2)]
    e_sb = [enB(sb(f"e_sb{i}", [128, 4 * WIN], BF16)) for i in range(2)]
    pn_sb = [enB(sb(f"pn_sb{i}", [128, 4 * WIN], BF16)) for i in range(4)]
    sums = enB(sb("sums", [128, NH], F32))
    rsum = enB(sb("rsum", [128, NH], F32))
    pT_sb = [enB(sb(f"pT_sb{i}", [128, 1024], BF16)) for i in range(2)]
    outT_sb = [enB(sb(f"outT_sb{i}", [128, 128], BF16)) for i in range(2)]
    out2_sb = [enB(sb(f"out2_sb{i}", [128, 128], BF16)) for i in range(2)]
    out_sb = [enB(sb(f"out_sb{i}", [128, D], F32)) for i in range(2)]
    y1_sb = [enB(sb(f"y1_sb{i}", [128, D], F32)) for i in range(2)]
    yb_sb = [enB(sb(f"yb_sb{i}", [128, D], BF16)) for i in range(2)]
    c_zer = c_consts
    psC = ExitStack()
    enCp = psC.enter_context
    y1_ps = enCp(psn("y1_ps", [128, 1024], F32))
    scg_ps = enCp(psn("scg_ps", [128, 4 * WIN], F32))
    pT_ps = [enCp(psn(f"pT_ps{i}", [128, 1024], BF16)) for i in range(2)]
    av_ps = [enCp(psn(f"av_ps{i}", [128, 512], F32)) for i in range(2)]

    c_vw_used = [None, None]
    c_x_free = [None, None]
    c_y1sb_free = [None, None]
    c_yb_free = [None, None]
    c_out_store = [None, None]
    c_y1_free = None
    c_scg_free = None
    c_exp_by = [None, None]
    c_stt_prev = None
    c_stt_by = [None, None]
    c_pn_by = [None, None]
    c_tr_by = [None, None]
    c_pTev_by = [None, None]
    c_av_by = [None, None]
    c_avev_by = [None, None]
    c_wo_by = [None, None]
    c_wo_last = None
    c_pn_key = {}

    def emit_A_hg(t, hg, first_bar):
        wb = t * 128 + 32
        moff = 0 if t == 0 else (2 * WIN if t == NTI - 1 else WIN)
        if True:
            par = hg % 2

            def mm_scg(te, t=t, hg=hg, wb=wb):
                for i in range(2):
                    c = 2 * hg + i
                    te.matmul(scg_ps[:, 2 * i * WIN:(2 * i + 1) * WIN],
                              qT_s[0:64, c * T + t * 128:c * T + (t + 1) * 128],
                              knT_s[0:64, c * HT + wb:c * HT + wb + WIN],
                              start=True, stop=True)
                te.matmul(y1_ps[:, 0:128], zer_sb[:, :], zer_sb[:, :],
                          start=False, stop=False, skip_group_check=True)
                for i in range(2):
                    c = 2 * hg + i
                    te.matmul(scg_ps[:, (2 * i + 1) * WIN:(2 * i + 2) * WIN],
                              qT_s[64:128, c * T + t * 128:c * T + (t + 1) * 128],
                              knT_s[64:128, c * HT + wb:c * HT + wb + WIN],
                              start=True, stop=True)
                return te.matmul(y1_ps[:, 0:128], zer_sb[:, :], zer_sb[:, :],
                                 start=False, stop=False, skip_group_check=True)
            P.op("tensor", mm_scg, waits=[st["scg_free"], c_zer] + first_bar,
                 inc=sT)
            c_sc = (sT, sT.v)

            def f_exp(s, t=t, hg=hg, par=par):
                for i in range(4):
                    h = 4 * hg + i
                    ins = s.activation(e_sb[par][:, i * WIN:(i + 1) * WIN],
                                       scg_ps[:, i * WIN:(i + 1) * WIN], AF.Exp,
                                       scale=rqT_s[:, t * NH + h:t * NH + h + 1])
                return ins
            P.op("scalar", f_exp,
                 waits=[c_sc, st["stt_by"][par]] + first_bar, inc=sS)
            c_exp = (sS, sS.v)
            st["scg_free"] = c_exp
            st["exp_by"][par] = c_exp

            def f_stt(v, t=t, hg=hg, par=par, moff=moff):
                for i in range(4):
                    h = 4 * hg + i
                    ins = v.scalar_tensor_tensor(
                        pn_sb[hg][:, i * WIN:(i + 1) * WIN],
                        e_sb[par][:, i * WIN:(i + 1) * WIN], 1.0,
                        masks[:, moff:moff + WIN], ALU.mult, ALU.mult,
                        accum_out=sums[:, 4 * hg + i:4 * hg + i + 1])
                return ins
            P.op("vector", f_stt,
                 waits=[c_exp, c_bsmall, st["tr_by"][hg]] + first_bar, inc=sV)
            st["stt_by"][par] = (sV, sV.v)
            P.op("vector", lambda v, hg=hg: v.reciprocal(
                rsum[:, 4 * hg:4 * hg + 4], sums[:, 4 * hg:4 * hg + 4]),
                 waits=[st["stt_by"][par]], inc=sV)
            c_rs2 = (sV, sV.v)

            def f_pn(v, hg=hg):
                for i in range(4):
                    h = 4 * hg + i
                    ins = v.tensor_scalar(pn_sb[hg][:, i * WIN:(i + 1) * WIN],
                                          pn_sb[hg][:, i * WIN:(i + 1) * WIN],
                                          rsum[:, h:h + 1], None, ALU.mult)
                return ins
            P.op("vector", f_pn, waits=[c_rs2], inc=sV)
            c_pn_key[(t, hg)] = (sV, sV.v)

    ld = {}

    def emit_loads(t):
        sl2 = t % 2
        hb = HALO + t * 128
        P.op("sync", lambda sy, t=t, sl2=sl2: sy.dma_start(
            vw_a[sl2][:, :], v_dram[:, t * 128 + 32:t * 128 + 160], transpose=True),
             waits=[c_vw_used[sl2], c_p1a_bar[3]], inc=sDs, inc_n=16)
        P.op("sync", lambda sy, t=t, sl2=sl2: sy.dma_start(
            vw_b[sl2][:, :], v_dram[:, t * 128 + 160:t * 128 + 288], transpose=True),
             inc=sDs, inc_n=16)
        P.op("sync", lambda sy, t=t, sl2=sl2: sy.dma_start(
            vw_c[sl2][:, :], v_dram[:, t * 128 + 96:t * 128 + 224], transpose=True),
             inc=sDs, inc_n=16)
        c_vw = (sDs, sDs.v)
        P.op("gpsimd", lambda g, hb=hb, sl2=sl2: g.dma_start(
            xL_sb[sl2][:, :], xh[hb:hb + 128, :]),
             waits=[c_x_free[sl2]], inc=sDg, inc_n=16)
        c_xl = (sDg, sDg.v)
        P.op("sync", lambda sy, t=t, sl2=sl2: sy.dma_start(
            yb_sb[sl2][:, :], y2_dram[:, t * 128:(t + 1) * 128], transpose=True),
             waits=[c_y2_all, c_yb_free[sl2]], inc=sDs, inc_n=16)
        c_yb = (sDs, sDs.v)
        ld[t] = (c_vw, c_xl, c_yb)

    def emit_B_pre(t, first_bar):
        sl2 = t % 2
        c_vw, c_xl, c_yb = ld[t]

        def mm_bo(te):
            te.matmul(y1_ps[:, 0:512], oner0b[:, :], bor[:, 0:512],
                      start=True, stop=False, skip_group_check=True)
            return te.matmul(y1_ps[:, 512:1024], oner0b[:, :], bor[:, 512:1024],
                             start=True, stop=False, skip_group_check=True)
        P.op("tensor", mm_bo, waits=[c_bsmall, st["y1_free"]] + first_bar, inc=sT)
        return c_vw, c_xl, c_yb

    def emit_B_sb(t, sb2, c_vw, first_bar):
        sl2 = t % 2
        if True:
            pairs = (2 * sb2, 2 * sb2 + 1)
            par = sb2 % 2
            slb = sb2 % 2

            def f_tr(te, t=t, sb2=sb2, slb=slb):
                for j in range(2):
                    i0 = 2 * j * WIN
                    o0 = 512 * j
                    te.transpose(pT_ps[slb][:, o0:o0 + 128],
                                 pn_sb[sb2][:, i0:i0 + 128], identb[:, :])
                    te.transpose(pT_ps[slb][0:64, o0 + 128:o0 + 256],
                                 pn_sb[sb2][:, i0 + 128:i0 + WIN], identb[:, :])
                    te.transpose(pT_ps[slb][0:64, o0 + 256:o0 + 384],
                                 pn_sb[sb2][:, i0 + WIN:i0 + WIN + 64], identb[:, :])
                    ins = te.transpose(pT_ps[slb][:, o0 + 384:o0 + 512],
                                       pn_sb[sb2][:, i0 + WIN + 64:i0 + 2 * WIN],
                                       identb[:, :])
                return ins
            P.op("tensor", f_tr,
                 waits=[c_pn_key[(t, sb2)], st["pTev_by"][slb]] + first_bar, inc=sT)
            st["tr_by"][sb2] = (sT, sT.v)
            c_tr = st["tr_by"][sb2]

            if sb2 % 2 == 0:
                P.op("scalar", lambda s, slb=slb: s.copy(pT_sb[slb][:, :],
                                                         pT_ps[slb][:, :]),
                     waits=[c_tr, st["av_by"][slb]], inc=sS)
                st["pTev_by"][slb] = (sS, sS.v)
            else:
                P.op("vector", lambda v, slb=slb: v.tensor_copy(pT_sb[slb][:, :],
                                                                pT_ps[slb][:, :]),
                     waits=[c_tr, st["av_by"][slb]], inc=sV)
                st["pTev_by"][slb] = (sV, sV.v)

            def f_av(te, pairs=pairs, sl2=sl2, slb=slb):
                for j, p in enumerate(pairs):
                    A, Bh = 2 * p, 2 * p + 1
                    co = j * 256
                    po = j * 512
                    te.matmul(av_ps[slb][0:64, co:co + 128],
                              vw_a[sl2][:, A * 64:(A + 1) * 64],
                              pT_sb[slb][:, po:po + 128], start=True, stop=False,
                              skip_group_check=True)
                    te.matmul(av_ps[slb][0:64, co:co + 128],
                              vw_b[sl2][0:64, A * 64:(A + 1) * 64],
                              pT_sb[slb][0:64, po + 128:po + 256], start=False,
                              stop=True, skip_group_check=True)
                    te.matmul(av_ps[slb][0:64, co + 128:co + 256],
                              vw_a[sl2][0:64, Bh * 64:(Bh + 1) * 64],
                              pT_sb[slb][0:64, po + 256:po + 384], start=True,
                              stop=False, skip_group_check=True)
                    ins = te.matmul(av_ps[slb][0:64, co + 128:co + 256],
                                    vw_c[sl2][:, Bh * 64:(Bh + 1) * 64],
                                    pT_sb[slb][:, po + 384:po + 512], start=False,
                                    stop=True, skip_group_check=True)
                return ins
            P.op("tensor", f_av,
                 waits=[st["pTev_by"][slb], c_vw, st["avev_by"][slb]] + first_bar,
                 inc=sT)
            st["av_by"][slb] = (sT, sT.v)

            if sb2 % 2 == 0:
                def f_ev(v, slb=slb):
                    v.tensor_copy(outT_sb[slb][0:64, :], av_ps[slb][0:64, 0:128])
                    v.tensor_copy(outT_sb[slb][64:128, :], av_ps[slb][0:64, 128:256])
                    v.tensor_copy(out2_sb[slb][0:64, :], av_ps[slb][0:64, 256:384])
                    return v.tensor_copy(out2_sb[slb][64:128, :],
                                         av_ps[slb][0:64, 384:512])
                P.op("vector", f_ev, waits=[st["av_by"][slb], st["wo_by"][slb]],
                     inc=sV)
                st["avev_by"][slb] = (sV, sV.v)
            else:
                def f_ev(s, slb=slb):
                    s.copy(outT_sb[slb][0:64, :], av_ps[slb][0:64, 0:128])
                    s.copy(outT_sb[slb][64:128, :], av_ps[slb][0:64, 128:256])
                    s.copy(out2_sb[slb][0:64, :], av_ps[slb][0:64, 256:384])
                    return s.copy(out2_sb[slb][64:128, :], av_ps[slb][0:64, 384:512])
                P.op("scalar", f_ev, waits=[st["av_by"][slb], st["wo_by"][slb]],
                     inc=sS)
                st["avev_by"][slb] = (sS, sS.v)

            def f_wo(te, pairs=pairs, slb=slb):
                for j, p in enumerate(pairs):
                    src_t = outT_sb[slb] if j == 0 else out2_sb[slb]
                    te.matmul(y1_ps[:, 0:512], src_t[:, :],
                              WoT_s[:, p * D:p * D + 512], start=False, stop=False,
                              skip_group_check=True)
                    ins = te.matmul(y1_ps[:, 512:1024], src_t[:, :],
                                    WoT_s[:, p * D + 512:(p + 1) * D], start=False,
                                    stop=(p == 7), skip_group_check=True)
                return ins
            P.op("tensor", f_wo, waits=[st["avev_by"][slb], c_WoT], inc=sT)
            st["wo_by"][slb] = (sT, sT.v)

    def emit_B_post(t, c_xl, c_yb):
        sl2 = t % 2
        c_wo_last = st["wo_by"][1]

        P.op("scalar", lambda s, sl2=sl2: s.copy(y1_sb[sl2][:, :], y1_ps[:, :]),
             waits=[c_wo_last, c_y1sb_free[sl2]], inc=sS)
        st["y1_free"] = (sS, sS.v)
        P.op("vector", lambda v, sl2=sl2: v.scalar_tensor_tensor(
            out_sb[sl2][:, :], y1_sb[sl2][:, :], 1.0, xL_sb[sl2][:, :],
            ALU.mult, ALU.add),
             waits=[st["y1_free"], c_xl, c_out_store[sl2]], inc=sV)
        P.op("vector", lambda v, sl2=sl2: v.tensor_add(
            out_sb[sl2][:, :], out_sb[sl2][:, :], yb_sb[sl2][:, :]),
             waits=[c_yb, (sV, sV.v)], inc=sV)
        c_fin = (sV, sV.v)
        c_y1sb_free[sl2] = c_fin
        c_yb_free[sl2] = c_fin
        c_x_free[sl2] = c_fin
        c_vw_used[sl2] = st["av_by"][1]
        P.op("gpsimd", lambda g, t=t, sl2=sl2: g.dma_start(
            out[t * 128:(t + 1) * 128, :], out_sb[sl2][:, :]),
             waits=[c_fin], inc=sOut, inc_n=16)
        c_out_store[sl2] = (sOut, sOut.v)

    st = {"scg_free": None, "exp_by": [None, None], "stt_by": [None, None],
          "tr_by": [None, None, None, None], "pTev_by": [None, None],
          "av_by": [None, None], "avev_by": [None, None], "wo_by": [None, None],
          "y1_free": None}

    for hg in range(4):
        emit_A_hg(0, hg, c_p1a_bar)
    for t in range(NTI):
        fb2 = c_p1a_bar if t == 0 else []
        emit_loads(t)
        c_vw, c_xl, c_yb = emit_B_pre(t, fb2)
        for i in range(4):
            emit_B_sb(t, i, c_vw, fb2)
            if t + 1 < NTI:
                emit_A_hg(t + 1, i, fb2)
        emit_B_post(t, c_xl, c_yb)

    P.op("gpsimd", lambda g: g.engine_nop(), waits=[(sOut, 16 * NTI)])

    return finish()


def _masks_for(q0):
    def m(t):
        qi = np.arange(128)[:, None]
        kk = np.arange(WIN)[None, :]
        qg = q0 + t * 128 + qi
        kg = q0 + t * 128 - 32 + kk
        allowed = ((np.abs(kg - qg) <= 32) & ((kg - qg) % 4 == 0)
                   & (kg >= 0) & (kg < 8192))
        return np.ascontiguousarray(allowed.astype(np.float32))
    return m(0), m(7), m(NTI - 1)


_NC_CACHE = None


def kernel(**inputs):
    global _NC_CACHE
    f = lambda k: np.ascontiguousarray(np.asarray(inputs[k], dtype=np.float32))
    x = f("L0_attn_x")
    Wq, bq = f("L0_attn_Wq"), f("L0_attn_bq")
    Wk, bk = f("L0_attn_Wk"), f("L0_attn_bk")
    Wv, bv = f("L0_attn_Wv"), f("L0_attn_bv")
    Wo, bo = f("L0_attn_Wo"), f("L0_attn_bo")
    gW = f("L0_ffn_gate_W")
    W1, b1 = f("L0_ffn_W1"), f("L0_ffn_b1")
    W2, b2 = f("L0_ffn_W2"), f("L0_ffn_b2")
    Ws1, bs1 = f("L0_ffn_Ws1"), f("L0_ffn_bs1")
    Ws2, bs2 = f("L0_ffn_Ws2"), f("L0_ffn_bs2")
    rsv = f("L0_ffn_route_scale")

    shared = {
        "Wq": Wq, "Wk": Wk, "Wv": Wv, "Wo": Wo,
        "bqt": np.ascontiguousarray(bq.reshape(8, 128).T),
        "bkt": np.ascontiguousarray(bk.reshape(8, 128).T),
        "bvt": np.ascontiguousarray(bv.reshape(8, 128).T),
        "bo_row": bo.reshape(1, D),
        "gW": gW, "W1": W1, "W2": W2,
        "b1t": np.ascontiguousarray(b1.reshape(8, 4, 128).transpose(2, 0, 1)
                                    .reshape(128, 32)),
        "B2": b2,
        "Ws1": Ws1, "Ws2": Ws2,
        "bs1t": np.ascontiguousarray(bs1.reshape(4, 128).T),
        "bs2t": np.ascontiguousarray(bs2.reshape(8, 128).T),
        "rs": np.concatenate([rsv.reshape(1, 1),
                              np.zeros((127, 1), np.float32)], axis=0),
    }

    in_maps = []
    for core in range(8):
        b, qq = core // 4, core % 4
        q0 = qq * T
        xh = np.zeros((HT, D), np.float32)
        lo = max(0, q0 - HALO)
        hi = min(8192, q0 + T + HALO)
        xh[lo - (q0 - HALO):hi - (q0 - HALO)] = x[b, lo:hi]
        mf, mi, ml = _masks_for(q0)
        m = dict(shared)
        m.update({"xh": xh, "mfirst": mf, "mint": mi, "mlast": ml})
        in_maps.append(m)

    if _NC_CACHE is None:
        _NC_CACHE = build_nc()
    nc = _NC_CACHE
    res = run_bass_kernel_spmd(nc, in_maps, core_ids=list(range(8)))

    outp = np.empty((2, 8192, D), np.float32)
    for core in range(8):
        b, qq = core // 4, core % 4
        outp[b, qq * T:(qq + 1) * T] = np.asarray(res.results[core]["out"],
                                                  dtype=np.float32)
    return outp


if __name__ == "__main__":
    import sys
    sys.path.insert(0, "/root/problem")
    from reference import setup_inputs, reference
    ins = setup_inputs(0)
    ref = np.asarray(reference(**ins))
    got = kernel(**{k: np.asarray(v) for k, v in ins.items()})
    err = np.mean((got - ref) ** 2) / np.mean(ref ** 2)
    print("residual variance ratio:", err)


# version 20
# speedup vs baseline: 1.0109x; 1.0109x over previous
import os
import numpy as np
import ml_dtypes
from contextlib import ExitStack

import concourse.bass as bass
import concourse.mybir as mybir
from concourse.bass_utils import run_bass_kernel_spmd

F32 = mybir.dt.float32
BF16 = mybir.dt.bfloat16
AF = mybir.ActivationFunctionType
ALU = mybir.AluOpType
AX = mybir.AxisListType

D = 1024
NH = 16
NE = 8
HF = 512
T = 2048
HALO = 64
HT = T + 2 * HALO
NTI = T // 128
NHT = HT // 128
WIN = 192
FT = 256
NFT = T // FT
LN_EPS = 1e-5
ATT_EPS = 1e-6


class Sem:
    def __init__(self, handle):
        self.h = handle
        self.v = 0


class Ev:
    __slots__ = ("eng", "fn", "waits", "incs")

    def __init__(self, eng, fn, waits, incs):
        self.eng, self.fn, self.waits, self.incs = eng, fn, waits, incs


class Prog:
    def __init__(self):
        self.events = []

    def op(self, eng, fn, waits=(), inc=None, inc_n=1):
        incs = []
        if inc is not None:
            inc.v += inc_n
            incs.append((inc, inc_n))
        self.events.append(Ev(eng, fn, [w for w in waits if w is not None], incs))
        return (inc, inc.v) if inc is not None else None

    def emit(self, engine_name, eng):
        for ev in self.events:
            if ev.eng != engine_name:
                continue
            for (sem, val) in ev.waits:
                eng.wait_ge(sem.h, val)
            ins = ev.fn(eng)
            for (sem, n) in ev.incs:
                ins.then_inc(sem.h, n)


def build_nc():
    nc = bass.Bass("TRN2", target_bir_lowering=False)

    xh = nc.dram_tensor("xh", [HT, D], F32, kind="ExternalInput").ap()
    Wq = nc.dram_tensor("Wq", [D, D], F32, kind="ExternalInput").ap()
    Wk = nc.dram_tensor("Wk", [D, D], F32, kind="ExternalInput").ap()
    Wv = nc.dram_tensor("Wv", [D, D], F32, kind="ExternalInput").ap()
    Wo = nc.dram_tensor("Wo", [D, D], F32, kind="ExternalInput").ap()
    bqt = nc.dram_tensor("bqt", [128, 8], F32, kind="ExternalInput").ap()
    bkt = nc.dram_tensor("bkt", [128, 8], F32, kind="ExternalInput").ap()
    bvt = nc.dram_tensor("bvt", [128, 8], F32, kind="ExternalInput").ap()
    bo_row = nc.dram_tensor("bo_row", [1, D], F32, kind="ExternalInput").ap()
    gW = nc.dram_tensor("gW", [D, NE], F32, kind="ExternalInput").ap()
    W1 = nc.dram_tensor("W1", [NE, D, HF], F32, kind="ExternalInput").ap()
    b1t = nc.dram_tensor("b1t", [128, NE * 4], F32, kind="ExternalInput").ap()
    W2 = nc.dram_tensor("W2", [NE, HF, D], F32, kind="ExternalInput").ap()
    B2 = nc.dram_tensor("B2", [NE, D], F32, kind="ExternalInput").ap()
    Ws1 = nc.dram_tensor("Ws1", [D, HF], F32, kind="ExternalInput").ap()
    bs1t = nc.dram_tensor("bs1t", [128, 4], F32, kind="ExternalInput").ap()
    Ws2 = nc.dram_tensor("Ws2", [HF, D], F32, kind="ExternalInput").ap()
    bs2t = nc.dram_tensor("bs2t", [128, 8], F32, kind="ExternalInput").ap()
    rs = nc.dram_tensor("rs", [128, 1], F32, kind="ExternalInput").ap()
    mfirst = nc.dram_tensor("mfirst", [128, WIN], F32, kind="ExternalInput").ap()
    mint = nc.dram_tensor("mint", [128, WIN], F32, kind="ExternalInput").ap()
    mlast = nc.dram_tensor("mlast", [128, WIN], F32, kind="ExternalInput").ap()

    out = nc.dram_tensor("out", [T, D], F32, kind="ExternalOutput").ap()

    wq_bf = nc.dram_tensor("wq_bf", [D, D], BF16, kind="Internal").ap()
    wk_bf = nc.dram_tensor("wk_bf", [D, D], BF16, kind="Internal").ap()
    wv_bf = nc.dram_tensor("wv_bf", [D, D], BF16, kind="Internal").ap()
    wo_bf = nc.dram_tensor("wo_bf", [D, D], BF16, kind="Internal").ap()
    _dbgkind = "Internal" if os.environ.get("KDBG", "") != "1" else "ExternalOutput"
    xn_dram = nc.dram_tensor("xn_dram", [HT, D], BF16, kind=_dbgkind).ap()
    v_dram = nc.dram_tensor("v_dram", [D, HT + 128], BF16, kind=_dbgkind).ap()
    y2_dram = nc.dram_tensor("y2_dram", [D, T], BF16, kind=_dbgkind).ap()

    identb_d = nc.inline_tensor(np.eye(128, dtype=ml_dtypes.bfloat16), "c_identb").ap()
    identf_d = nc.inline_tensor(np.eye(128, dtype=np.float32), "c_identf").ap()
    _r0 = np.zeros((128, 128), np.float32); _r0[0, :] = 1.0
    oner0b_d = nc.inline_tensor(_r0.astype(ml_dtypes.bfloat16), "c_oner0b").ap()
    oner0f_d = nc.inline_tensor(_r0, "c_oner0f").ap()
    E_np = np.zeros((128, 8, 16), np.float32)
    for c in range(8):
        E_np[0:64, c, 2 * c] = 1.0
        E_np[64:128, c, 2 * c + 1] = 1.0
    Esel_d = nc.inline_tensor(
        E_np.reshape(128, 128).astype(ml_dtypes.bfloat16), "c_Esel").ap()
    S_np = np.zeros((16, 8, 128), np.float32)
    for c in range(8):
        S_np[2 * c, c, 0:64] = 1.0
        S_np[2 * c + 1, c, 64:128] = 1.0
    Ssel_d = nc.inline_tensor(S_np.reshape(16, 8 * 128), "c_Ssel").ap()
    Se_np = np.zeros((8, 8, 128), np.float32)
    for e in range(NE):
        Se_np[e, e, :] = 1.0
    Sesel_d = nc.inline_tensor(Se_np.reshape(8, 8 * 128).astype(ml_dtypes.bfloat16), "c_Sesel").ap()

    P = Prog()
    ctx = ExitStack()
    en = ctx.enter_context
    sb = nc.sbuf_tensor
    psn = nc.psum_tensor

    identf = en(sb("identf", [128, 128], F32))
    zer_sb = en(sb("zer_sb", [128, 128], BF16))
    oner0b = en(sb("oner0b", [128, 128], BF16))
    oner0f = en(sb("oner0f", [128, 128], F32))
    consts = en(sb("consts", [128, 4], F32))
    rs_s = en(sb("rs_s", [128, 1], F32))
    rsb = en(sb("rsb", [128, 1], F32))
    xnT_s = en(sb("xnT_s", [128, 8 * HT], BF16))
    stat = en(sb("stat", [128, 20], F32))
    g_scr = en(sb("g_scr", [128, 32], F32))

    sDg = Sem(en(nc.semaphore("sDg")))
    sDs = Sem(en(nc.semaphore("sDs")))
    sT = Sem(en(nc.semaphore("sT")))
    sS = Sem(en(nc.semaphore("sS")))
    sV = Sem(en(nc.semaphore("sV")))
    sOut = Sem(en(nc.semaphore("sOut")))

    def bar():
        return [(sT, sT.v), (sS, sS.v), (sV, sV.v), (sDg, sDg.v), (sDs, sDs.v)]

    def dmag(dst, src):
        return lambda g: g.dma_start(dst, src)

    _UPTO = int(os.environ.get("KUPTO", "9"))
    _SKIP = set(x for x in os.environ.get("KSKIP", "").split(",") if x)
    _DBG = os.environ.get("KDBG", "") == "1"

    def finish():
        with nc.Block() as block:
            @block.gpsimd
            def _(g):
                P.emit("gpsimd", g)

            @block.sync
            def _(sy):
                P.emit("sync", sy)

            @block.tensor
            def _(te):
                P.emit("tensor", te)

            @block.scalar
            def _(s):
                P.emit("scalar", s)

            @block.vector
            def _(v):
                P.emit("vector", v)
        return nc

    def stub_out(waits):
        P.op("gpsimd", lambda g: g.dma_start(out[:, :], xh[HALO:HALO + T, :]),
             waits=waits, inc=sOut, inc_n=16)
        P.op("gpsimd", lambda g: g.engine_nop(), waits=[(sOut, 16)])

    sbA = ExitStack()
    enA = sbA.enter_context
    Sesel = enA(sb("Sesel", [8, 8 * 128], BF16))
    b1_s = enA(sb("b1_s", [128, NE * 4], F32))
    bs1_s = enA(sb("bs1_s", [128, 4], F32))
    bs2_s = enA(sb("bs2_s", [128, 8], F32))
    B2_s = enA(sb("B2_s", [8, D], BF16))
    gateW_s = enA(sb("gateW_s", [128, 8 * NE], BF16))
    gT_s = enA(sb("gT_s", [8, T], BF16))
    x_sb = [enA(sb(f"x_sb{i}", [128, D], F32)) for i in range(2)]
    xn_sb = [enA(sb(f"xn_sb{i}", [128, D], BF16)) for i in range(2)]
    Ws1_s = enA(sb("Ws1_s", [128, 8 * HF], BF16))
    Ws2_s = enA(sb("Ws2_s", [128, 4 * D], BF16))
    W1a = enA(sb("W1a", [128, NE * 4096], BF16))
    W2a = enA(sb("W2a", [128, NE * 4096], BF16))
    hg_sb = [enA(sb(f"hg_sb{i}", [128, FT], BF16)) for i in range(2)]
    hsT_sb = enA(sb("hsT_sb", [128, 4 * FT], BF16))
    hgg_sb2 = [hsT_sb, enA(sb("hgg_b", [128, 4 * FT], BF16))]
    y2sb = enA(sb("y2sb", [128, 4 * FT], BF16))
    zsc_sb = enA(sb("zsc_sb", [8, 128], F32))
    psA = ExitStack()
    enAp = psA.enter_context
    sm_ps = enAp(psn("sm_ps", [128, 512], F32))
    pA = [enAp(psn(f"pA{i}", [128, 512], F32)) for i in range(3)]
    y2p = enAp(psn("y2p", [128, 2048], F32))

    for dst, srcap in [
        (identf[:, :], identf_d),
        (oner0b[:, :], oner0b_d), (oner0f[:, :], oner0f_d),
        (Sesel[:, :], Sesel_d),
        (b1_s[:, :], b1t), (bs1_s[:, :], bs1t), (bs2_s[:, :], bs2t),
        (B2_s[:, :], B2), (rs_s[:, :], rs),
        (gateW_s[:, :].rearrange("p (kc j) -> p kc j", kc=8),
         gW.rearrange("(kc p) j -> p kc j", p=128)),
        (Ws1_s[:, :].rearrange("p (kc f) -> p kc f", kc=8),
         Ws1.rearrange("(kc p) f -> p kc f", p=128)),
        (Ws2_s[:, :].rearrange("p (kc f) -> p kc f", kc=4),
         Ws2.rearrange("(kc p) f -> p kc f", p=128)),
    ]:
        P.op("gpsimd", dmag(dst, srcap), inc=sDg, inc_n=16)
    c_small = (sDg, sDg.v)

    def f_consts(v):
        v.memset(consts[:, 0:1], LN_EPS)
        v.memset(consts[:, 1:2], ATT_EPS)
        v.memset(zer_sb[:, :], 0.0)
        return v.memset(consts[:, 2:3], 0.0)
    P.op("vector", f_consts, inc=sV)
    c_consts = (sV, sV.v)

    P.op("tensor", lambda t: t.matmul(sm_ps[:, 0:1], oner0f[:, :], rs_s[:, :],
                                      start=True, stop=True),
         waits=[c_consts, c_small], inc=sT)
    c_rsmm = (sT, sT.v)
    P.op("vector", lambda v: v.tensor_copy(rsb[:, :], sm_ps[:, 0:1]),
         waits=[c_rsmm], inc=sV)
    c_rsb = (sV, sV.v)

    slot_store = {0: None, 1: None}
    slot_free = {0: None, 1: None}
    prev_var = None
    c_xn_tile = {}
    for t in range(NHT):
        sl = t % 2
        P.op("sync", lambda sy, sl=sl, t=t: sy.dma_start(
            x_sb[sl][:, :], xh[t * 128:(t + 1) * 128, :]),
             waits=[slot_free[sl]], inc=sDs, inc_n=16)
        c_xload = (sDs, sDs.v)
        def f_bn(v, sl=sl):
            v.bn_stats(stat[:, 6:12], x_sb[sl][:, 0:512])
            return v.bn_stats(stat[:, 12:18], x_sb[sl][:, 512:1024])
        P.op("vector", f_bn, waits=[c_xload, prev_var], inc=sV)
        c_bn = (sV, sV.v)
        P.op("vector", lambda v: v.bn_aggr(stat[:, 2:4], stat[:, 6:18]),
             waits=[c_bn], inc=sV)
        c_var = (sV, sV.v)
        prev_var = c_var
        P.op("scalar", lambda s: s.activation(stat[:, 5:6], stat[:, 3:4], AF.Ln,
                                              bias=consts[:, 0:1]),
             waits=[c_var, c_consts], inc=sS)
        c_ln = (sS, sS.v)
        P.op("scalar", lambda s: s.activation(stat[:, 5:6], stat[:, 5:6], AF.Exp,
                                              scale=-0.5), waits=[c_ln], inc=sS)
        c_rstd = (sS, sS.v)
        P.op("vector", lambda v, sl=sl: v.tensor_scalar(
            xn_sb[sl][:, :], x_sb[sl][:, :], stat[:, 2:3], stat[:, 5:6],
            ALU.subtract, ALU.mult),
             waits=[c_rstd, slot_store[sl]], inc=sV)
        c_xn = (sV, sV.v)
        slot_free[sl] = c_xn
        P.op("gpsimd", dmag(xn_dram[t * 128:(t + 1) * 128, :], xn_sb[sl][:, :]),
             waits=[c_xn], inc=sDg, inc_n=16)
        slot_store[sl] = (sDg, sDg.v)
        c_xn_tile[t] = (sDg, sDg.v)
    c_xn_all = (sDg, sDg.v)

    W1r = W1.rearrange("e (kc p) f -> p e kc f", p=128)
    W2r = W2.rearrange("e (kc p) f -> p e kc f", p=128)
    for e in range(NE):
        P.op("gpsimd", dmag(
            W1a[:, e * 4096:(e + 1) * 4096].rearrange("p (kc f) -> p kc f", kc=8),
            W1r[:, e:e + 1, :, :].rearrange("p one kc f -> p (one kc) f")),
             inc=sDg, inc_n=16)
        P.op("gpsimd", dmag(
            W2a[:, e * 4096:(e + 1) * 4096].rearrange("p (kc f) -> p kc f", kc=4),
            W2r[:, e:e + 1, :, :].rearrange("p one kc f -> p (one kc) f")),
             inc=sDg, inc_n=16)
    c_expw = (sDg, sDg.v)

    for wbf, w in [(wq_bf, Wq), (wk_bf, Wk), (wv_bf, Wv), (wo_bf, Wo)]:
        P.op("gpsimd", dmag(wbf, w), inc=sDg, inc_n=16)
    c_wcast = (sDg, sDg.v)


    bands = [576, 1152, HT]
    c_band = []
    b_prev = 0
    for bk in bands:
        mark = c_xn_tile[(bk + 127) // 128 - 1]
        for c in range(8):
            P.op("sync", lambda sy, c=c, b0=b_prev, b1=bk: sy.dma_start(
                xnT_s[:, c * HT + b0:c * HT + b1],
                xn_dram[b0:b1, c * 128:(c + 1) * 128],
                transpose=True), waits=[mark], inc=sDs, inc_n=16)
        c_band.append((sDs, sDs.v))
        b_prev = bk
    c_xnT = c_band[-1]

    def xnT_ready(last_col):
        for bk, cb in zip(bands, c_band):
            if last_col <= bk:
                return cb
        return c_xnT

    if _UPTO <= 0:
        stub_out(bar())
        return finish()

    gst = {"c_gdone": None, "c_ez": None}

    def emit_gate(t):
        c_gdone = gst["c_gdone"]
        c_ez = gst["c_ez"]
        hb = HALO + t * 128
    
        def mm_gate(te, hb=hb):
            for kc in range(8):
                ins = te.matmul(sm_ps[0:8, 0:128],
                                gateW_s[:, kc * 8:(kc + 1) * 8],
                                xnT_s[:, kc * HT + hb:kc * HT + hb + 128],
                                start=(kc == 0), stop=(kc == 7))
            return ins
        P.op("tensor", mm_gate,
             waits=[xnT_ready(hb + 128), c_small, c_gdone], inc=sT)
        c_zmm = (sT, sT.v)
        P.op("vector", lambda v: v.tensor_copy(zsc_sb[:, :], sm_ps[0:8, 0:128]),
             waits=[c_zmm], inc=sV)
        c_zev = (sV, sV.v)
        P.op("tensor", lambda te: te.transpose(sm_ps[:, 128:136], zsc_sb[:, :],
                                               identf[0:8, 0:8]),
             waits=[c_zev, c_ez], inc=sT)
        c_z = (sT, sT.v)
        P.op("scalar", lambda s: s.activation(g_scr[:, 8:16], sm_ps[:, 128:136],
                                              AF.Exp, scale=-1.0),
             waits=[c_z, c_gdone], inc=sS)
        c_ez = (sS, sS.v)
        prev = c_ez
        for fn in [
            lambda v: v.tensor_scalar(g_scr[:, 8:16], g_scr[:, 8:16], 1.0, None,
                                      ALU.add),
            lambda v: v.reciprocal(g_scr[:, 0:8], g_scr[:, 8:16]),
            lambda v: v.tensor_scalar(g_scr[:, 0:8], g_scr[:, 0:8], rsb[:, 0:1],
                                      None, ALU.mult),
            lambda v: v.max(out=g_scr[:, 16:24], in_=g_scr[:, 0:8]),
            lambda v: v.tensor_add(g_scr[:, 24:25], g_scr[:, 16:17], g_scr[:, 17:18]),
            lambda v: v.reciprocal(g_scr[:, 25:26], g_scr[:, 24:25]),
            lambda v: v.tensor_scalar(g_scr[:, 8:16], g_scr[:, 0:8], g_scr[:, 17:18],
                                      None, ALU.is_ge),
            lambda v: v.scalar_tensor_tensor(g_scr[:, 0:8], g_scr[:, 0:8],
                                             g_scr[:, 25:26], g_scr[:, 8:16],
                                             ALU.mult, ALU.mult),
        ]:
            P.op("vector", fn, waits=[prev, c_rsb], inc=sV)
            prev = (sV, sV.v)
        c_g = prev
        P.op("tensor", lambda te: te.transpose(
            sm_ps[0:8, 0:128], g_scr[:, 0:8], identf[:, :]),
             waits=[c_g, c_rsb, c_gdone], inc=sT)
        c_gT = (sT, sT.v)
        P.op("vector", lambda v, t=t: v.tensor_copy(
            gT_s[:, t * 128:(t + 1) * 128], sm_ps[0:8, 0:128]),
             waits=[c_gT], inc=sV)
        c_gdone = (sV, sV.v)
        gst["c_gdone"] = c_gdone
        gst["c_ez"] = c_ez

    c_y2_store = None
    c_prev_evac = None
    c_pA_free = [None, None, None]
    c_hsT_used = None
    c_hg_used = [None, None]
    c_hgg_used = None
    c_hgg_used2 = [None, None]
    c_G_used = [None, None]
    emit_gate(0)
    emit_gate(1)
    g_ready = {0: gst["c_gdone"]}
    for ft in range(NFT):
        fb = HALO + ft * FT
        if ft + 1 < NFT:
            emit_gate(2 * (ft + 1))
            emit_gate(2 * (ft + 1) + 1)
            g_ready[ft + 1] = gst["c_gdone"]
        c_gdone = g_ready[ft]
        c_hs_evac = [None] * 4
        for mc in range(4):
            def mm_hs(te, mc=mc, fb=fb):
                for kc in range(8):
                    ins = te.matmul(pA[mc % 3][:, 0:FT],
                                    Ws1_s[:, kc * HF + mc * 128:kc * HF + (mc + 1) * 128],
                                    xnT_s[:, kc * HT + fb:kc * HT + fb + FT],
                                    start=(kc == 0), stop=(kc == 7))
                return ins
            P.op("tensor", mm_hs,
                 waits=[xnT_ready(fb + FT), c_small, c_pA_free[mc % 3]], inc=sT)
            c_mm = (sT, sT.v)
            P.op("scalar", lambda s, mc=mc: s.activation(
                hsT_sb[:, mc * FT:(mc + 1) * FT], pA[mc % 3][:, 0:FT], AF.Gelu,
                bias=bs1_s[:, mc:mc + 1]),
                 waits=[c_mm, c_hsT_used, c_hgg_used2[0]], inc=sS)
            c_hs_evac[mc] = (sS, sS.v)
            c_pA_free[mc % 3] = c_hs_evac[mc]

        def mm_y2z(te):
            for b in range(4):
                ins = te.matmul(y2p[:, b * 512:(b + 1) * 512], zer_sb[:, :],
                                Ws1_s[:, 0:512], start=True, stop=False,
                                skip_group_check=True)
            return ins
        P.op("tensor", mm_y2z, waits=[c_prev_evac, c_consts, c_small], inc=sT)
        c_y2z = (sT, sT.v)

        def mm_ws2(te):
            ins = None
            for c in range(8):
                for mc in range(4):
                    ins = te.matmul(y2p[:, c * FT:(c + 1) * FT],
                                    Ws2_s[:, mc * D + c * 128:mc * D + (c + 1) * 128],
                                    hsT_sb[:, mc * FT:(mc + 1) * FT],
                                    start=False, stop=False,
                                    skip_group_check=True)
            return ins
        P.op("tensor", mm_ws2, waits=[c_hs_evac[3], c_y2z], inc=sT)
        c_hsT_used = (sT, sT.v)

        def mm_b2(te, ft=ft):
            ins = None
            for c in range(8):
                ins = te.matmul(y2p[:, c * FT:(c + 1) * FT],
                                B2_s[:, c * 128:(c + 1) * 128],
                                gT_s[:, ft * FT:(ft + 1) * FT],
                                start=False, stop=False, skip_group_check=True)
            return ins
        P.op("tensor", mm_b2, waits=[c_gdone], inc=sT)

        for e in range(NE):
            P.op("tensor", lambda te, e=e, ft=ft: te.matmul(
                sm_ps[:, 256:256 + FT],
                Sesel[:, e * 128:(e + 1) * 128],
                gT_s[:, ft * FT:(ft + 1) * FT], start=True, stop=True),
                 waits=[c_gdone, c_G_used[0]], inc=sT)
            c_G = (sT, sT.v)
            for mc in range(4):
                psl = (4 * e + mc) % 3
                def mm_h(te, e=e, mc=mc, fb=fb, psl=psl):
                    for kc in range(8):
                        ins = te.matmul(
                            pA[psl][:, 0:FT],
                            W1a[:, e * 4096 + kc * HF + mc * 128:
                                 e * 4096 + kc * HF + (mc + 1) * 128],
                            xnT_s[:, kc * HT + fb:kc * HT + fb + FT],
                            start=(kc == 0), stop=(kc == 7))
                    return ins
                P.op("tensor", mm_h,
                     waits=[xnT_ready(fb + FT), c_expw, c_pA_free[psl]], inc=sT)
                c_hmm = (sT, sT.v)
                P.op("scalar", lambda s, e=e, mc=mc, psl=psl: s.activation(
                    hg_sb[mc % 2][:, :], pA[psl][:, 0:FT], AF.Gelu,
                    bias=b1_s[:, e * 4 + mc:e * 4 + mc + 1]),
                     waits=[c_hmm, c_hg_used[mc % 2]], inc=sS)
                c_hg = (sS, sS.v)
                c_pA_free[psl] = c_hg
                P.op("vector", lambda v, mc=mc, e=e: v.tensor_mul(
                    hgg_sb2[e % 2][:, mc * FT:(mc + 1) * FT], hg_sb[mc % 2][:, :],
                    sm_ps[:, 256:256 + FT]),
                     waits=[c_hg, c_G, c_hgg_used2[e % 2], c_hsT_used], inc=sV)
                c_hg_used[mc % 2] = (sV, sV.v)
            c_hgg = (sV, sV.v)
            c_G_used[0] = c_hgg

            def mm_w2(te, e=e):
                ins = None
                for c in range(8):
                    for mc in range(4):
                        ins = te.matmul(
                            y2p[:, c * FT:(c + 1) * FT],
                            W2a[:, e * 4096 + mc * D + c * 128:
                                 e * 4096 + mc * D + (c + 1) * 128],
                            hgg_sb2[e % 2][:, mc * FT:(mc + 1) * FT],
                            start=False, stop=(e == NE - 1 and c == 7 and mc == 3),
                            skip_group_check=True)
                return ins
            P.op("tensor", mm_w2, waits=[c_hgg], inc=sT)
            c_hgg_used2[e % 2] = (sT, sT.v)
            c_hgg_used = (sT, sT.v)
        c_y2mm = (sT, sT.v)

        for half in range(2):
            def ev_y2(s, half=half):
                for c in range(4 * half, 4 * half + 4):
                    ins = s.activation(y2sb[:, (c - 4 * half) * FT:(c - 4 * half + 1) * FT],
                                       y2p[:, c * FT:(c + 1) * FT], AF.Identity,
                                       bias=bs2_s[:, c:c + 1])
                return ins
            P.op("scalar", ev_y2, waits=[c_y2mm, c_y2_store], inc=sS)
            c_prev_evac = (sS, sS.v)

            def st_y2(g, ft=ft, half=half):
                dst = y2_dram.rearrange("(c p) t -> p c t", p=128)[
                    :, 4 * half:4 * half + 4, ft * FT:(ft + 1) * FT]
                srcap = y2sb[:, :].rearrange("p (c f) -> p c f", c=4)
                return g.dma_start(dst, srcap)
            P.op("gpsimd", st_y2, waits=[c_prev_evac], inc=sDg, inc_n=16)
            c_y2_store = (sDg, sDg.v)
    c_y2_all = c_y2_store
    if _DBG:
        d_gT = nc.dram_tensor("d_gT", [8, T], BF16, kind="ExternalOutput").ap()
        P.op("gpsimd", lambda g: g.dma_start(d_gT, gT_s[:, :]),
             waits=[gst["c_gdone"]], inc=sDg, inc_n=16)
    c_p2_bar = bar()
    if _UPTO <= 2:
        stub_out(c_p2_bar)
        return finish()

    sbA.close()
    psA.close()
    sbB = ExitStack()
    enB = sbB.enter_context
    WoT_s = enB(sb("WoT_s", [128, 8 * D], BF16))
    knT_s = enB(sb("knT_s", [128, 8 * HT], BF16))
    qT_s = enB(sb("qT_s", [128, 8 * T], BF16))
    rqT_s = enB(sb("rqT_s", [128, NTI * NH], F32))
    masks = enB(sb("masks", [128, 3 * WIN], BF16))
    identb = enB(sb("identb", [128, 128], BF16))
    bor = enB(sb("bor", [128, D], BF16))
    sbB1 = ExitStack()
    enB1 = sbB1.enter_context
    WqT_s = enB1(sb("WqT_s", [128, 8 * D], BF16))
    WkT_s = enB1(sb("WkT_s", [128, 8 * D], BF16))
    WvT_s = enB1(sb("WvT_s", [128, 8 * D], BF16))
    kraw_s = enB1(sb("kraw_s", [128, 8 * 512], BF16))
    sq_scr = [enB1(sb(f"sq_scr{i}", [128, D], BF16)) for i in range(2)]
    rq_f = enB1(sb("rq_f", [16, 512], F32))
    rk_f = enB1(sb("rk_f", [16, 512], F32))
    Ssel = enB1(sb("Ssel", [16, 8 * 128], F32))
    Esel = enB1(sb("Esel", [128, 128], BF16))
    bv_s = enB1(sb("bv_s", [128, 8], F32))
    bq_s = enB1(sb("bq_s", [128, 8], F32))
    bk_s = enB1(sb("bk_s", [128, 8], F32))
    P.op("vector", lambda v: v.memset(bor[:, :], 0.0), waits=c_p2_bar, inc=sV)
    c_bz = (sV, sV.v)
    for dst, srcap in [
        (masks[:, 0:WIN], mfirst), (masks[:, WIN:2 * WIN], mint),
        (masks[:, 2 * WIN:3 * WIN], mlast),
        (identb[:, :], identb_d), (bor[0:1, :], bo_row),
        (Ssel[:, :], Ssel_d), (Esel[:, :], Esel_d), (bv_s[:, :], bvt),
        (bq_s[:, :], bqt), (bk_s[:, :], bkt),
    ]:
        P.op("gpsimd", dmag(dst, srcap), waits=c_p2_bar + [c_bz], inc=sDg, inc_n=16)
    c_bsmall = (sDg, sDg.v)
    psB = ExitStack()
    enBp = psB.enter_context
    sm2_ps = enBp(psn("sm2_ps", [128, 128], F32))
    pB2 = [enBp(psn(f"pB2{i}", [128, 512], F32)) for i in range(4)]
    ss_ps = enBp(psn("ss_ps", [16, 512], F32))
    rr2_ps = [enBp(psn(f"rr2_ps{i}", [128, 512], F32)) for i in range(2)]

    wt_marks = {}
    for wbf, wt, nm in [(wk_bf, WkT_s, "k"), (wv_bf, WvT_s, "v"),
                        (wq_bf, WqT_s, "q"), (wo_bf, WoT_s, "o")]:
        for c in range(8):
            P.op("sync", lambda sy, wbf=wbf, wt=wt, c=c: sy.dma_start(
                wt[:, c * D:(c + 1) * D], wbf[:, c * 128:(c + 1) * 128],
                transpose=True), waits=[c_wcast] + c_p2_bar, inc=sDs, inc_n=16)
        wt_marks[nm] = (sDs, sDs.v)
    c_WkT, c_WvT, c_WqT, c_WoT = (wt_marks["k"], wt_marks["v"],
                                  wt_marks["q"], wt_marks["o"])

    c_pB2_free = [None, None, None, None]
    c_kraw_used = None
    c_rr2_used = [None, None]
    c_sq_used = [None, None]
    c_ss_free = None

    for (off, w) in [(0, 512), (512, 512), (1024, 512), (1536, 512), (2048, 128)]:
        c_kev = [None] * 8
        for cch in range(8):
            sl = cch % 4

            def mm_k(te, cch=cch, off=off, w=w, sl=sl):
                for kc in range(8):
                    ins = te.matmul(pB2[sl][:, 0:w],
                                    WkT_s[:, kc * D + cch * 128:kc * D + (cch + 1) * 128],
                                    xnT_s[:, kc * HT + off:kc * HT + off + w],
                                    start=(kc == 0), stop=(kc == 7))
                return ins
            P.op("tensor", mm_k, waits=[c_WkT, c_xnT, c_pB2_free[sl], c_kraw_used],
                 inc=sT)
            c_kmm = (sT, sT.v)
            P.op("scalar", lambda s, cch=cch, w=w, sl=sl: s.activation(
                kraw_s[:, cch * 512:cch * 512 + w], pB2[sl][:, 0:w], AF.Identity,
                bias=bk_s[:, cch:cch + 1]), waits=[c_kmm, c_bsmall], inc=sS)
            c_kev[cch] = (sS, sS.v)
            c_pB2_free[sl] = c_kev[cch]
            P.op("vector", lambda v, cch=cch, w=w, sq=cch % 2: v.tensor_mul(
                sq_scr[sq][:, 0:w], kraw_s[:, cch * 512:cch * 512 + w],
                kraw_s[:, cch * 512:cch * 512 + w]),
                 waits=[c_kev[cch], c_sq_used[cch % 2]], inc=sV)
            c_sq = (sV, sV.v)
            P.op("tensor", lambda te, cch=cch, w=w, sq=cch % 2: te.matmul(
                ss_ps[:, 0:w], Esel[:, cch * 16:(cch + 1) * 16], sq_scr[sq][:, 0:w],
                start=(cch == 0), stop=(cch == 7)),
                 waits=[c_sq, c_bsmall, c_ss_free if cch == 0 else None], inc=sT)
            c_sq_used[cch % 2] = (sT, sT.v)
        c_ssk = (sT, sT.v)

        prevk = c_ssk
        for fn in [
            lambda s, w=w: s.activation(rk_f[0:16, 0:w], ss_ps[0:16, 0:w], AF.Ln),
            lambda s, w=w: s.activation(rk_f[0:16, 0:w], rk_f[0:16, 0:w], AF.Exp,
                                        scale=0.5),
            lambda s, w=w: s.activation(rk_f[0:16, 0:w], rk_f[0:16, 0:w], AF.Ln,
                                        bias=consts[0:16, 1:2]),
            lambda s, w=w: s.activation(rk_f[0:16, 0:w], rk_f[0:16, 0:w], AF.Exp,
                                        scale=-1.0),
        ]:
            P.op("scalar", fn, waits=[prevk], inc=sS)
            prevk = (sS, sS.v)
        c_rk = prevk
        c_ss_free = c_rk
        for cch in range(8):
            sl = cch % 2
            P.op("tensor", lambda te, cch=cch, w=w, sl=sl: te.matmul(
                rr2_ps[sl][:, 0:w], Ssel[:, cch * 128:(cch + 1) * 128],
                rk_f[0:16, 0:w], start=True, stop=True),
                 waits=[c_rk, c_bsmall, c_rr2_used[sl]], inc=sT)
            c_R = (sT, sT.v)
            P.op("vector", lambda v, cch=cch, off=off, w=w, sl=sl: v.tensor_mul(
                knT_s[:, cch * HT + off:cch * HT + off + w],
                kraw_s[:, cch * 512:cch * 512 + w], rr2_ps[sl][:, 0:w]),
                 waits=[c_R], inc=sV)
            c_rr2_used[sl] = (sV, sV.v)
        c_kraw_used = (sV, sV.v)

    c_smT_used = None
    for m in range(4):
        off = HALO + m * 512
        for cch in range(8):
            sl = cch % 4

            def mm_q(te, cch=cch, off=off, sl=sl):
                for kc in range(8):
                    ins = te.matmul(pB2[sl][:, 0:512],
                                    WqT_s[:, kc * D + cch * 128:kc * D + (cch + 1) * 128],
                                    xnT_s[:, kc * HT + off:kc * HT + off + 512],
                                    start=(kc == 0), stop=(kc == 7))
                return ins
            P.op("tensor", mm_q, waits=[c_WqT, c_pB2_free[sl]], inc=sT)
            c_qmm = (sT, sT.v)
            P.op("scalar", lambda s, cch=cch, m=m, sl=sl: s.activation(
                qT_s[:, cch * T + m * 512:cch * T + (m + 1) * 512], pB2[sl][:, 0:512],
                AF.Identity, bias=bq_s[:, cch:cch + 1]), waits=[c_qmm, c_bsmall], inc=sS)
            c_qev = (sS, sS.v)
            c_pB2_free[sl] = c_qev
            P.op("vector", lambda v, cch=cch, m=m, sq=cch % 2: v.tensor_mul(
                sq_scr[sq][:, 0:512],
                qT_s[:, cch * T + m * 512:cch * T + (m + 1) * 512],
                qT_s[:, cch * T + m * 512:cch * T + (m + 1) * 512]),
                 waits=[c_qev, c_sq_used[cch % 2]], inc=sV)
            c_sq = (sV, sV.v)
            P.op("tensor", lambda te, cch=cch, sq=cch % 2: te.matmul(
                ss_ps[:, 0:512], Esel[:, cch * 16:(cch + 1) * 16],
                sq_scr[sq][:, 0:512], start=(cch == 0), stop=(cch == 7)),
                 waits=[c_sq, c_bsmall, c_ss_free if cch == 0 else None], inc=sT)
            c_sq_used[cch % 2] = (sT, sT.v)
        c_ssq_mm = (sT, sT.v)

        prevq = c_ssq_mm
        for fn in [
            lambda s: s.activation(rq_f[0:16, 0:512], ss_ps[0:16, 0:512], AF.Ln),
            lambda s: s.activation(rq_f[0:16, 0:512], rq_f[0:16, 0:512], AF.Exp,
                                   scale=0.5),
            lambda s: s.activation(rq_f[0:16, 0:512], rq_f[0:16, 0:512], AF.Ln,
                                   bias=consts[0:16, 1:2]),
            lambda s: s.activation(rq_f[0:16, 0:512], rq_f[0:16, 0:512], AF.Exp,
                                   scale=-1.0),
        ]:
            P.op("scalar", fn, waits=[prevq], inc=sS)
            prevq = (sS, sS.v)
        c_rq = prevq
        c_ss_free = c_rq
        for tt in range(4):
            P.op("tensor", lambda te, tt=tt: te.transpose(
                sm2_ps[0:128, 0:16], rq_f[0:16, tt * 128:(tt + 1) * 128],
                identf[0:16, 0:16]), waits=[c_rq, c_smT_used], inc=sT)
            c_trq = (sT, sT.v)
            P.op("vector", lambda v, m=m, tt=tt: v.tensor_copy(
                rqT_s[:, (m * 4 + tt) * 16:(m * 4 + tt) * 16 + 16],
                sm2_ps[0:128, 0:16]), waits=[c_trq], inc=sV)
            c_smT_used = (sV, sV.v)

    c_vst = None
    c_pB2v_free = list(c_pB2_free)
    for (off, w) in [(0, 512), (512, 512), (1024, 512), (1536, 512), (2048, 128)]:
        c_vev = [None] * 8
        for mc in range(8):
            sl = mc % 4

            def mm_v(te, mc=mc, off=off, w=w, sl=sl):
                for kc in range(8):
                    ins = te.matmul(
                        pB2[sl][:, 0:w],
                        WvT_s[:, kc * D + mc * 128:kc * D + (mc + 1) * 128],
                        xnT_s[:, kc * HT + off:kc * HT + off + w],
                        start=(kc == 0), stop=(kc == 7))
                return ins
            P.op("tensor", mm_v, waits=[c_WvT, c_pB2v_free[sl], c_kraw_used, c_vst],
                 inc=sT)
            c_vmm = (sT, sT.v)
            P.op("scalar", lambda s, mc=mc, w=w, sl=sl: s.activation(
                kraw_s[:, mc * 512:mc * 512 + w], pB2[sl][:, 0:w], AF.Identity,
                bias=bv_s[:, mc:mc + 1]),
                 waits=[c_vmm, c_bsmall], inc=sS)
            c_vev[mc] = (sS, sS.v)
            c_pB2v_free[sl] = c_vev[mc]

        def st_v(g, off=off, w=w):
            dst = v_dram.rearrange("(mc p) t -> p mc t", p=128)[:, :, off:off + w]
            srcap = kraw_s[:, :].rearrange("p (mc f) -> p mc f", mc=8)[:, :, 0:w]
            return g.dma_start(dst, srcap)
        P.op("gpsimd", st_v, waits=[c_vev[7]], inc=sDg, inc_n=16)
        c_vst = (sDg, sDg.v)
    c_p1a_bar = bar()

    if _DBG:
        d_knT = nc.dram_tensor("d_knT", [128, 8 * HT], BF16, kind="ExternalOutput").ap()
        d_qT = nc.dram_tensor("d_qT", [128, 8 * T], BF16, kind="ExternalOutput").ap()
        d_rqT = nc.dram_tensor("d_rqT", [128, NTI * NH], F32, kind="ExternalOutput").ap()
        for dst, srct in [(d_knT, knT_s), (d_qT, qT_s), (d_rqT, rqT_s)]:
            P.op("gpsimd", lambda g, dst=dst, srct=srct: g.dma_start(dst, srct[:, :]),
                 waits=c_p1a_bar, inc=sDg, inc_n=16)

    if _UPTO <= 3:
        stub_out(c_p1a_bar)
        return finish()

    psB.close()
    sbB1.close()
    xL_sb = [enB(sb(f"xL_sb{i}", [128, D], F32)) for i in range(2)]
    vw_a = [enB(sb(f"vw_a{i}", [128, D], BF16)) for i in range(2)]
    vw_b = [enB(sb(f"vw_b{i}", [128, D], BF16)) for i in range(2)]
    vw_c = [enB(sb(f"vw_c{i}", [128, D], BF16)) for i in range(2)]
    e_sb = [enB(sb(f"e_sb{i}", [128, 4 * WIN], BF16)) for i in range(2)]
    pn_sb = [enB(sb(f"pn_sb{i}", [128, 4 * WIN], BF16)) for i in range(4)]
    sums = enB(sb("sums", [128, NH], F32))
    rsum = enB(sb("rsum", [128, NH], F32))
    pT_sb = [enB(sb(f"pT_sb{i}", [128, 1024], BF16)) for i in range(2)]
    outT_sb = [enB(sb(f"outT_sb{i}", [128, 128], BF16)) for i in range(2)]
    out2_sb = [enB(sb(f"out2_sb{i}", [128, 128], BF16)) for i in range(2)]
    out_sb = [enB(sb(f"out_sb{i}", [128, D], F32)) for i in range(2)]
    y1_sb = [enB(sb(f"y1_sb{i}", [128, D], F32)) for i in range(2)]
    yb_sb = [enB(sb(f"yb_sb{i}", [128, D], BF16)) for i in range(2)]
    c_zer = c_consts
    psC = ExitStack()
    enCp = psC.enter_context
    y1_ps = enCp(psn("y1_ps", [128, 1024], F32))
    scg_ps = enCp(psn("scg_ps", [128, 4 * WIN], F32))
    pT_ps = [enCp(psn(f"pT_ps{i}", [128, 1024], BF16)) for i in range(2)]
    av_ps = [enCp(psn(f"av_ps{i}", [128, 512], F32)) for i in range(2)]

    c_vw_used = [None, None]
    c_x_free = [None, None]
    c_y1sb_free = [None, None]
    c_yb_free = [None, None]
    c_out_store = [None, None]
    c_y1_free = None
    c_scg_free = None
    c_exp_by = [None, None]
    c_stt_prev = None
    c_stt_by = [None, None]
    c_pn_by = [None, None]
    c_tr_by = [None, None]
    c_pTev_by = [None, None]
    c_av_by = [None, None]
    c_avev_by = [None, None]
    c_wo_by = [None, None]
    c_wo_last = None
    c_pn_key = {}

    def emit_A_hg(t, hg, first_bar):
        wb = t * 128 + 32
        moff = 0 if t == 0 else (2 * WIN if t == NTI - 1 else WIN)
        if True:
            par = hg % 2

            def mm_scg(te, t=t, hg=hg, wb=wb):
                for i in range(2):
                    c = 2 * hg + i
                    te.matmul(scg_ps[:, 2 * i * WIN:(2 * i + 1) * WIN],
                              qT_s[0:64, c * T + t * 128:c * T + (t + 1) * 128],
                              knT_s[0:64, c * HT + wb:c * HT + wb + WIN],
                              start=True, stop=True)
                te.matmul(y1_ps[:, 0:128], zer_sb[:, :], zer_sb[:, :],
                          start=False, stop=False, skip_group_check=True)
                for i in range(2):
                    c = 2 * hg + i
                    te.matmul(scg_ps[:, (2 * i + 1) * WIN:(2 * i + 2) * WIN],
                              qT_s[64:128, c * T + t * 128:c * T + (t + 1) * 128],
                              knT_s[64:128, c * HT + wb:c * HT + wb + WIN],
                              start=True, stop=True)
                return te.matmul(y1_ps[:, 0:128], zer_sb[:, :], zer_sb[:, :],
                                 start=False, stop=False, skip_group_check=True)
            P.op("tensor", mm_scg, waits=[st["scg_free"], c_zer] + first_bar,
                 inc=sT)
            c_sc = (sT, sT.v)

            def f_exp(s, t=t, hg=hg, par=par):
                for i in range(4):
                    h = 4 * hg + i
                    ins = s.activation(e_sb[par][:, i * WIN:(i + 1) * WIN],
                                       scg_ps[:, i * WIN:(i + 1) * WIN], AF.Exp,
                                       scale=rqT_s[:, t * NH + h:t * NH + h + 1])
                return ins
            P.op("scalar", f_exp,
                 waits=[c_sc, st["stt_by"][par]] + first_bar, inc=sS)
            c_exp = (sS, sS.v)
            st["scg_free"] = c_exp
            st["exp_by"][par] = c_exp

            def f_stt(v, t=t, hg=hg, par=par, moff=moff):
                for i in range(4):
                    h = 4 * hg + i
                    ins = v.scalar_tensor_tensor(
                        pn_sb[hg][:, i * WIN:(i + 1) * WIN],
                        e_sb[par][:, i * WIN:(i + 1) * WIN], 1.0,
                        masks[:, moff:moff + WIN], ALU.mult, ALU.mult,
                        accum_out=sums[:, 4 * hg + i:4 * hg + i + 1])
                return ins
            P.op("vector", f_stt,
                 waits=[c_exp, c_bsmall, st["tr_by"][hg]] + first_bar, inc=sV)
            st["stt_by"][par] = (sV, sV.v)
            P.op("vector", lambda v, hg=hg: v.reciprocal(
                rsum[:, 4 * hg:4 * hg + 4], sums[:, 4 * hg:4 * hg + 4]),
                 waits=[st["stt_by"][par]], inc=sV)
            c_rs2 = (sV, sV.v)

            def f_pn(v, hg=hg):
                for i in range(4):
                    h = 4 * hg + i
                    ins = v.tensor_scalar(pn_sb[hg][:, i * WIN:(i + 1) * WIN],
                                          pn_sb[hg][:, i * WIN:(i + 1) * WIN],
                                          rsum[:, h:h + 1], None, ALU.mult)
                return ins
            P.op("vector", f_pn, waits=[c_rs2], inc=sV)
            c_pn_key[(t, hg)] = (sV, sV.v)

    ld = {}

    def emit_loads(t):
        sl2 = t % 2
        hb = HALO + t * 128
        P.op("sync", lambda sy, t=t, sl2=sl2: sy.dma_start(
            vw_a[sl2][:, :], v_dram[:, t * 128 + 32:t * 128 + 160], transpose=True),
             waits=[c_vw_used[sl2], c_p1a_bar[3]], inc=sDs, inc_n=16)
        P.op("sync", lambda sy, t=t, sl2=sl2: sy.dma_start(
            vw_b[sl2][:, :], v_dram[:, t * 128 + 160:t * 128 + 288], transpose=True),
             inc=sDs, inc_n=16)
        P.op("sync", lambda sy, t=t, sl2=sl2: sy.dma_start(
            vw_c[sl2][:, :], v_dram[:, t * 128 + 96:t * 128 + 224], transpose=True),
             inc=sDs, inc_n=16)
        c_vw = (sDs, sDs.v)
        P.op("gpsimd", lambda g, hb=hb, sl2=sl2: g.dma_start(
            xL_sb[sl2][:, :], xh[hb:hb + 128, :]),
             waits=[c_x_free[sl2]], inc=sDg, inc_n=16)
        c_xl = (sDg, sDg.v)
        P.op("sync", lambda sy, t=t, sl2=sl2: sy.dma_start(
            yb_sb[sl2][:, :], y2_dram[:, t * 128:(t + 1) * 128], transpose=True),
             waits=[c_y2_all, c_yb_free[sl2]], inc=sDs, inc_n=16)
        c_yb = (sDs, sDs.v)
        ld[t] = (c_vw, c_xl, c_yb)

    def emit_B_pre(t, first_bar):
        sl2 = t % 2
        c_vw, c_xl, c_yb = ld[t]

        def mm_bo(te):
            te.matmul(y1_ps[:, 0:512], oner0b[:, :], bor[:, 0:512],
                      start=True, stop=False, skip_group_check=True)
            return te.matmul(y1_ps[:, 512:1024], oner0b[:, :], bor[:, 512:1024],
                             start=True, stop=False, skip_group_check=True)
        P.op("tensor", mm_bo, waits=[c_bsmall, st["y1_free"]] + first_bar, inc=sT)
        return c_vw, c_xl, c_yb

    def emit_B_sb(t, sb2, c_vw, first_bar):
        sl2 = t % 2
        if True:
            pairs = (2 * sb2, 2 * sb2 + 1)
            par = sb2 % 2
            slb = sb2 % 2

            def f_tr(te, t=t, sb2=sb2, slb=slb):
                for j in range(2):
                    i0 = 2 * j * WIN
                    o0 = 512 * j
                    te.transpose(pT_ps[slb][:, o0:o0 + 128],
                                 pn_sb[sb2][:, i0:i0 + 128], identb[:, :])
                    te.transpose(pT_ps[slb][0:64, o0 + 128:o0 + 256],
                                 pn_sb[sb2][:, i0 + 128:i0 + WIN], identb[:, :])
                    te.transpose(pT_ps[slb][0:64, o0 + 256:o0 + 384],
                                 pn_sb[sb2][:, i0 + WIN:i0 + WIN + 64], identb[:, :])
                    ins = te.transpose(pT_ps[slb][:, o0 + 384:o0 + 512],
                                       pn_sb[sb2][:, i0 + WIN + 64:i0 + 2 * WIN],
                                       identb[:, :])
                return ins
            P.op("tensor", f_tr,
                 waits=[c_pn_key[(t, sb2)], st["pTev_by"][slb]] + first_bar, inc=sT)
            st["tr_by"][sb2] = (sT, sT.v)
            c_tr = st["tr_by"][sb2]

            if sb2 % 2 == 0:
                P.op("scalar", lambda s, slb=slb: s.copy(pT_sb[slb][:, :],
                                                         pT_ps[slb][:, :]),
                     waits=[c_tr, st["av_by"][slb]], inc=sS)
                st["pTev_by"][slb] = (sS, sS.v)
            else:
                P.op("vector", lambda v, slb=slb: v.tensor_copy(pT_sb[slb][:, :],
                                                                pT_ps[slb][:, :]),
                     waits=[c_tr, st["av_by"][slb]], inc=sV)
                st["pTev_by"][slb] = (sV, sV.v)

            def f_av(te, pairs=pairs, sl2=sl2, slb=slb):
                for j, p in enumerate(pairs):
                    A, Bh = 2 * p, 2 * p + 1
                    co = j * 256
                    po = j * 512
                    te.matmul(av_ps[slb][0:64, co:co + 128],
                              vw_a[sl2][:, A * 64:(A + 1) * 64],
                              pT_sb[slb][:, po:po + 128], start=True, stop=False,
                              skip_group_check=True)
                    te.matmul(av_ps[slb][0:64, co:co + 128],
                              vw_b[sl2][0:64, A * 64:(A + 1) * 64],
                              pT_sb[slb][0:64, po + 128:po + 256], start=False,
                              stop=True, skip_group_check=True)
                    te.matmul(av_ps[slb][0:64, co + 128:co + 256],
                              vw_a[sl2][0:64, Bh * 64:(Bh + 1) * 64],
                              pT_sb[slb][0:64, po + 256:po + 384], start=True,
                              stop=False, skip_group_check=True)
                    ins = te.matmul(av_ps[slb][0:64, co + 128:co + 256],
                                    vw_c[sl2][:, Bh * 64:(Bh + 1) * 64],
                                    pT_sb[slb][:, po + 384:po + 512], start=False,
                                    stop=True, skip_group_check=True)
                return ins
            P.op("tensor", f_av,
                 waits=[st["pTev_by"][slb], c_vw, st["avev_by"][slb]] + first_bar,
                 inc=sT)
            st["av_by"][slb] = (sT, sT.v)

            if sb2 % 2 == 0:
                def f_ev(v, slb=slb):
                    v.tensor_copy(outT_sb[slb][0:64, :], av_ps[slb][0:64, 0:128])
                    v.tensor_copy(outT_sb[slb][64:128, :], av_ps[slb][0:64, 128:256])
                    v.tensor_copy(out2_sb[slb][0:64, :], av_ps[slb][0:64, 256:384])
                    return v.tensor_copy(out2_sb[slb][64:128, :],
                                         av_ps[slb][0:64, 384:512])
                P.op("vector", f_ev, waits=[st["av_by"][slb], st["wo_by"][slb]],
                     inc=sV)
                st["avev_by"][slb] = (sV, sV.v)
            else:
                def f_ev(s, slb=slb):
                    s.copy(outT_sb[slb][0:64, :], av_ps[slb][0:64, 0:128])
                    s.copy(outT_sb[slb][64:128, :], av_ps[slb][0:64, 128:256])
                    s.copy(out2_sb[slb][0:64, :], av_ps[slb][0:64, 256:384])
                    return s.copy(out2_sb[slb][64:128, :], av_ps[slb][0:64, 384:512])
                P.op("scalar", f_ev, waits=[st["av_by"][slb], st["wo_by"][slb]],
                     inc=sS)
                st["avev_by"][slb] = (sS, sS.v)

            def f_wo(te, pairs=pairs, slb=slb):
                for j, p in enumerate(pairs):
                    src_t = outT_sb[slb] if j == 0 else out2_sb[slb]
                    te.matmul(y1_ps[:, 0:512], src_t[:, :],
                              WoT_s[:, p * D:p * D + 512], start=False, stop=False,
                              skip_group_check=True)
                    ins = te.matmul(y1_ps[:, 512:1024], src_t[:, :],
                                    WoT_s[:, p * D + 512:(p + 1) * D], start=False,
                                    stop=(p == 7), skip_group_check=True)
                return ins
            P.op("tensor", f_wo, waits=[st["avev_by"][slb], c_WoT], inc=sT)
            st["wo_by"][slb] = (sT, sT.v)

    def emit_B_post(t, c_xl, c_yb):
        sl2 = t % 2
        c_wo_last = st["wo_by"][1]

        P.op("scalar", lambda s, sl2=sl2: s.copy(y1_sb[sl2][:, :], y1_ps[:, :]),
             waits=[c_wo_last, c_y1sb_free[sl2]], inc=sS)
        st["y1_free"] = (sS, sS.v)
        P.op("vector", lambda v, sl2=sl2: v.scalar_tensor_tensor(
            out_sb[sl2][:, :], y1_sb[sl2][:, :], 1.0, xL_sb[sl2][:, :],
            ALU.mult, ALU.add),
             waits=[st["y1_free"], c_xl, c_out_store[sl2]], inc=sV)
        P.op("vector", lambda v, sl2=sl2: v.tensor_add(
            out_sb[sl2][:, :], out_sb[sl2][:, :], yb_sb[sl2][:, :]),
             waits=[c_yb, (sV, sV.v)], inc=sV)
        c_fin = (sV, sV.v)
        c_y1sb_free[sl2] = c_fin
        c_yb_free[sl2] = c_fin
        c_x_free[sl2] = c_fin
        c_vw_used[sl2] = st["av_by"][1]
        P.op("gpsimd", lambda g, t=t, sl2=sl2: g.dma_start(
            out[t * 128:(t + 1) * 128, :], out_sb[sl2][:, :]),
             waits=[c_fin], inc=sOut, inc_n=16)
        c_out_store[sl2] = (sOut, sOut.v)

    st = {"scg_free": None, "exp_by": [None, None], "stt_by": [None, None],
          "tr_by": [None, None, None, None], "pTev_by": [None, None],
          "av_by": [None, None], "avev_by": [None, None], "wo_by": [None, None],
          "y1_free": None}

    for hg in range(4):
        emit_A_hg(0, hg, c_p1a_bar)
    for t in range(NTI):
        fb2 = c_p1a_bar if t == 0 else []
        emit_loads(t)
        c_vw, c_xl, c_yb = emit_B_pre(t, fb2)
        for i in range(4):
            emit_B_sb(t, i, c_vw, fb2)
            if t + 1 < NTI:
                emit_A_hg(t + 1, i, fb2)
        emit_B_post(t, c_xl, c_yb)

    P.op("gpsimd", lambda g: g.engine_nop(), waits=[(sOut, 16 * NTI)])

    return finish()


def _masks_for(q0):
    def m(t):
        qi = np.arange(128)[:, None]
        kk = np.arange(WIN)[None, :]
        qg = q0 + t * 128 + qi
        kg = q0 + t * 128 - 32 + kk
        allowed = ((np.abs(kg - qg) <= 32) & ((kg - qg) % 4 == 0)
                   & (kg >= 0) & (kg < 8192))
        return np.ascontiguousarray(allowed.astype(np.float32))
    return m(0), m(7), m(NTI - 1)


_NC_CACHE = None


def kernel(**inputs):
    global _NC_CACHE
    f = lambda k: np.ascontiguousarray(np.asarray(inputs[k], dtype=np.float32))
    x = f("L0_attn_x")
    Wq, bq = f("L0_attn_Wq"), f("L0_attn_bq")
    Wk, bk = f("L0_attn_Wk"), f("L0_attn_bk")
    Wv, bv = f("L0_attn_Wv"), f("L0_attn_bv")
    Wo, bo = f("L0_attn_Wo"), f("L0_attn_bo")
    gW = f("L0_ffn_gate_W")
    W1, b1 = f("L0_ffn_W1"), f("L0_ffn_b1")
    W2, b2 = f("L0_ffn_W2"), f("L0_ffn_b2")
    Ws1, bs1 = f("L0_ffn_Ws1"), f("L0_ffn_bs1")
    Ws2, bs2 = f("L0_ffn_Ws2"), f("L0_ffn_bs2")
    rsv = f("L0_ffn_route_scale")

    shared = {
        "Wq": Wq, "Wk": Wk, "Wv": Wv, "Wo": Wo,
        "bqt": np.ascontiguousarray(bq.reshape(8, 128).T),
        "bkt": np.ascontiguousarray(bk.reshape(8, 128).T),
        "bvt": np.ascontiguousarray(bv.reshape(8, 128).T),
        "bo_row": bo.reshape(1, D),
        "gW": gW, "W1": W1, "W2": W2,
        "b1t": np.ascontiguousarray(b1.reshape(8, 4, 128).transpose(2, 0, 1)
                                    .reshape(128, 32)),
        "B2": b2,
        "Ws1": Ws1, "Ws2": Ws2,
        "bs1t": np.ascontiguousarray(bs1.reshape(4, 128).T),
        "bs2t": np.ascontiguousarray(bs2.reshape(8, 128).T),
        "rs": np.concatenate([rsv.reshape(1, 1),
                              np.zeros((127, 1), np.float32)], axis=0),
    }

    in_maps = []
    for core in range(8):
        b, qq = core // 4, core % 4
        q0 = qq * T
        xh = np.zeros((HT, D), np.float32)
        lo = max(0, q0 - HALO)
        hi = min(8192, q0 + T + HALO)
        xh[lo - (q0 - HALO):hi - (q0 - HALO)] = x[b, lo:hi]
        mf, mi, ml = _masks_for(q0)
        m = dict(shared)
        m.update({"xh": xh, "mfirst": mf, "mint": mi, "mlast": ml})
        in_maps.append(m)

    if _NC_CACHE is None:
        _NC_CACHE = build_nc()
    nc = _NC_CACHE
    res = run_bass_kernel_spmd(nc, in_maps, core_ids=list(range(8)))

    outp = np.empty((2, 8192, D), np.float32)
    for core in range(8):
        b, qq = core // 4, core % 4
        outp[b, qq * T:(qq + 1) * T] = np.asarray(res.results[core]["out"],
                                                  dtype=np.float32)
    return outp


if __name__ == "__main__":
    import sys
    sys.path.insert(0, "/root/problem")
    from reference import setup_inputs, reference
    ins = setup_inputs(0)
    ref = np.asarray(reference(**ins))
    got = kernel(**{k: np.asarray(v) for k, v in ins.items()})
    err = np.mean((got - ref) ** 2) / np.mean(ref ** 2)
    print("residual variance ratio:", err)


# version 21
# speedup vs baseline: 1.0291x; 1.0181x over previous
import os
import numpy as np
import ml_dtypes
from contextlib import ExitStack

import concourse.bass as bass
import concourse.mybir as mybir
from concourse.bass_utils import run_bass_kernel_spmd

F32 = mybir.dt.float32
BF16 = mybir.dt.bfloat16
AF = mybir.ActivationFunctionType
ALU = mybir.AluOpType
AX = mybir.AxisListType

D = 1024
NH = 16
NE = 8
HF = 512
T = 2048
HALO = 64
HT = T + 2 * HALO
NTI = T // 128
NHT = HT // 128
WIN = 192
FT = 256
NFT = T // FT
LN_EPS = 1e-5
ATT_EPS = 1e-6


class Sem:
    def __init__(self, handle):
        self.h = handle
        self.v = 0


class Ev:
    __slots__ = ("eng", "fn", "waits", "incs")

    def __init__(self, eng, fn, waits, incs):
        self.eng, self.fn, self.waits, self.incs = eng, fn, waits, incs


class Prog:
    def __init__(self):
        self.events = []

    def op(self, eng, fn, waits=(), inc=None, inc_n=1):
        incs = []
        if inc is not None:
            inc.v += inc_n
            incs.append((inc, inc_n))
        self.events.append(Ev(eng, fn, [w for w in waits if w is not None], incs))
        return (inc, inc.v) if inc is not None else None

    def emit(self, engine_name, eng):
        for ev in self.events:
            if ev.eng != engine_name:
                continue
            for (sem, val) in ev.waits:
                eng.wait_ge(sem.h, val)
            ins = ev.fn(eng)
            for (sem, n) in ev.incs:
                ins.then_inc(sem.h, n)


def build_nc():
    nc = bass.Bass("TRN2", target_bir_lowering=False)

    xh = nc.dram_tensor("xh", [HT, D], F32, kind="ExternalInput").ap()
    Wq = nc.dram_tensor("Wq", [D, D], F32, kind="ExternalInput").ap()
    Wk = nc.dram_tensor("Wk", [D, D], F32, kind="ExternalInput").ap()
    Wv = nc.dram_tensor("Wv", [D, D], F32, kind="ExternalInput").ap()
    Wo = nc.dram_tensor("Wo", [D, D], F32, kind="ExternalInput").ap()
    bqt = nc.dram_tensor("bqt", [128, 8], F32, kind="ExternalInput").ap()
    bkt = nc.dram_tensor("bkt", [128, 8], F32, kind="ExternalInput").ap()
    bvt = nc.dram_tensor("bvt", [128, 8], F32, kind="ExternalInput").ap()
    bo_row = nc.dram_tensor("bo_row", [1, D], F32, kind="ExternalInput").ap()
    gW = nc.dram_tensor("gW", [D, NE], F32, kind="ExternalInput").ap()
    W1 = nc.dram_tensor("W1", [NE, D, HF], F32, kind="ExternalInput").ap()
    b1t = nc.dram_tensor("b1t", [128, NE * 4], F32, kind="ExternalInput").ap()
    W2 = nc.dram_tensor("W2", [NE, HF, D], F32, kind="ExternalInput").ap()
    B2 = nc.dram_tensor("B2", [NE, D], F32, kind="ExternalInput").ap()
    Ws1 = nc.dram_tensor("Ws1", [D, HF], F32, kind="ExternalInput").ap()
    bs1t = nc.dram_tensor("bs1t", [128, 4], F32, kind="ExternalInput").ap()
    Ws2 = nc.dram_tensor("Ws2", [HF, D], F32, kind="ExternalInput").ap()
    bs2t = nc.dram_tensor("bs2t", [128, 8], F32, kind="ExternalInput").ap()
    rs = nc.dram_tensor("rs", [128, 1], F32, kind="ExternalInput").ap()
    mfirst = nc.dram_tensor("mfirst", [128, WIN], F32, kind="ExternalInput").ap()
    mint = nc.dram_tensor("mint", [128, WIN], F32, kind="ExternalInput").ap()
    mlast = nc.dram_tensor("mlast", [128, WIN], F32, kind="ExternalInput").ap()

    out = nc.dram_tensor("out", [T, D], F32, kind="ExternalOutput").ap()

    wq_bf = nc.dram_tensor("wq_bf", [D, D], BF16, kind="Internal").ap()
    wk_bf = nc.dram_tensor("wk_bf", [D, D], BF16, kind="Internal").ap()
    wv_bf = nc.dram_tensor("wv_bf", [D, D], BF16, kind="Internal").ap()
    wo_bf = nc.dram_tensor("wo_bf", [D, D], BF16, kind="Internal").ap()
    _dbgkind = "Internal" if os.environ.get("KDBG", "") != "1" else "ExternalOutput"
    xn_dram = nc.dram_tensor("xn_dram", [HT, D], BF16, kind=_dbgkind).ap()
    v_dram = nc.dram_tensor("v_dram", [D, HT + 128], BF16, kind=_dbgkind).ap()
    y2_dram = nc.dram_tensor("y2_dram", [D, T], BF16, kind=_dbgkind).ap()

    identb_d = nc.inline_tensor(np.eye(128, dtype=ml_dtypes.bfloat16), "c_identb").ap()
    identf_d = nc.inline_tensor(np.eye(128, dtype=np.float32), "c_identf").ap()
    _r0 = np.zeros((128, 128), np.float32); _r0[0, :] = 1.0
    oner0b_d = nc.inline_tensor(_r0.astype(ml_dtypes.bfloat16), "c_oner0b").ap()
    oner0f_d = nc.inline_tensor(_r0, "c_oner0f").ap()
    E_np = np.zeros((128, 8, 16), np.float32)
    for c in range(8):
        E_np[0:64, c, 2 * c] = 1.0
        E_np[64:128, c, 2 * c + 1] = 1.0
    Esel_d = nc.inline_tensor(
        E_np.reshape(128, 128).astype(ml_dtypes.bfloat16), "c_Esel").ap()
    S_np = np.zeros((16, 8, 128), np.float32)
    for c in range(8):
        S_np[2 * c, c, 0:64] = 1.0
        S_np[2 * c + 1, c, 64:128] = 1.0
    Ssel_d = nc.inline_tensor(S_np.reshape(16, 8 * 128), "c_Ssel").ap()
    Se_np = np.zeros((8, 8, 128), np.float32)
    for e in range(NE):
        Se_np[e, e, :] = 1.0
    Sesel_d = nc.inline_tensor(Se_np.reshape(8, 8 * 128).astype(ml_dtypes.bfloat16), "c_Sesel").ap()

    P = Prog()
    ctx = ExitStack()
    en = ctx.enter_context
    sb = nc.sbuf_tensor
    psn = nc.psum_tensor

    identf = en(sb("identf", [128, 128], F32))
    zer_sb = en(sb("zer_sb", [128, 128], BF16))
    oner0b = en(sb("oner0b", [128, 128], BF16))
    oner0f = en(sb("oner0f", [128, 128], F32))
    consts = en(sb("consts", [128, 4], F32))
    rs_s = en(sb("rs_s", [128, 1], F32))
    rsb = en(sb("rsb", [128, 1], F32))
    xnT_s = en(sb("xnT_s", [128, 8 * HT], BF16))
    stat = en(sb("stat", [128, 20], F32))
    g_scr = en(sb("g_scr", [128, 32], F32))

    sDg = Sem(en(nc.semaphore("sDg")))
    sDs = Sem(en(nc.semaphore("sDs")))
    sT = Sem(en(nc.semaphore("sT")))
    sS = Sem(en(nc.semaphore("sS")))
    sV = Sem(en(nc.semaphore("sV")))
    sOut = Sem(en(nc.semaphore("sOut")))

    def bar():
        return [(sT, sT.v), (sS, sS.v), (sV, sV.v), (sDg, sDg.v), (sDs, sDs.v)]

    def dmag(dst, src):
        return lambda g: g.dma_start(dst, src)

    _UPTO = int(os.environ.get("KUPTO", "9"))
    _SKIP = set(x for x in os.environ.get("KSKIP", "").split(",") if x)
    _DBG = os.environ.get("KDBG", "") == "1"

    def finish():
        with nc.Block() as block:
            @block.gpsimd
            def _(g):
                P.emit("gpsimd", g)

            @block.sync
            def _(sy):
                P.emit("sync", sy)

            @block.tensor
            def _(te):
                P.emit("tensor", te)

            @block.scalar
            def _(s):
                P.emit("scalar", s)

            @block.vector
            def _(v):
                P.emit("vector", v)
        return nc

    def stub_out(waits):
        P.op("gpsimd", lambda g: g.dma_start(out[:, :], xh[HALO:HALO + T, :]),
             waits=waits, inc=sOut, inc_n=16)
        P.op("gpsimd", lambda g: g.engine_nop(), waits=[(sOut, 16)])

    sbA = ExitStack()
    enA = sbA.enter_context
    Sesel = enA(sb("Sesel", [8, 8 * 128], BF16))
    b1_s = enA(sb("b1_s", [128, NE * 4], F32))
    bs1_s = enA(sb("bs1_s", [128, 4], F32))
    bs2_s = enA(sb("bs2_s", [128, 8], F32))
    B2_s = enA(sb("B2_s", [8, D], BF16))
    gateW_s = enA(sb("gateW_s", [128, 8 * NE], BF16))
    gT_s = enA(sb("gT_s", [8, T], BF16))
    x_sb = [enA(sb(f"x_sb{i}", [128, D], F32)) for i in range(2)]
    xn_sb = [enA(sb(f"xn_sb{i}", [128, D], BF16)) for i in range(2)]
    Ws1_s = enA(sb("Ws1_s", [128, 8 * HF], BF16))
    Ws2_s = enA(sb("Ws2_s", [128, 4 * D], BF16))
    W1a = enA(sb("W1a", [128, NE * 4096], BF16))
    W2a = enA(sb("W2a", [128, NE * 4096], BF16))
    hg_sb = [enA(sb(f"hg_sb{i}", [128, FT], BF16)) for i in range(2)]
    hsT_sb = enA(sb("hsT_sb", [128, 4 * FT], BF16))
    hgg_sb2 = [hsT_sb, enA(sb("hgg_b", [128, 4 * FT], BF16))]
    y2sb = enA(sb("y2sb", [128, 4 * FT], BF16))
    zsc_sb = enA(sb("zsc_sb", [8, 128], F32))
    psA = ExitStack()
    enAp = psA.enter_context
    sm_ps = enAp(psn("sm_ps", [128, 512], F32))
    pA = [enAp(psn(f"pA{i}", [128, 512], F32)) for i in range(3)]
    y2p = enAp(psn("y2p", [128, 2048], F32))

    for dst, srcap in [
        (identf[:, :], identf_d),
        (oner0b[:, :], oner0b_d), (oner0f[:, :], oner0f_d),
        (Sesel[:, :], Sesel_d),
        (b1_s[:, :], b1t), (bs1_s[:, :], bs1t), (bs2_s[:, :], bs2t),
        (B2_s[:, :], B2), (rs_s[:, :], rs),
        (gateW_s[:, :].rearrange("p (kc j) -> p kc j", kc=8),
         gW.rearrange("(kc p) j -> p kc j", p=128)),
        (Ws1_s[:, :].rearrange("p (kc f) -> p kc f", kc=8),
         Ws1.rearrange("(kc p) f -> p kc f", p=128)),
        (Ws2_s[:, :].rearrange("p (kc f) -> p kc f", kc=4),
         Ws2.rearrange("(kc p) f -> p kc f", p=128)),
    ]:
        P.op("gpsimd", dmag(dst, srcap), inc=sDg, inc_n=16)
    c_small = (sDg, sDg.v)

    def f_consts(v):
        v.memset(consts[:, 0:1], LN_EPS)
        v.memset(consts[:, 1:2], ATT_EPS)
        v.memset(zer_sb[:, :], 0.0)
        return v.memset(consts[:, 2:3], 0.0)
    P.op("vector", f_consts, inc=sV)
    c_consts = (sV, sV.v)

    P.op("tensor", lambda t: t.matmul(sm_ps[:, 0:1], oner0f[:, :], rs_s[:, :],
                                      start=True, stop=True),
         waits=[c_consts, c_small], inc=sT)
    c_rsmm = (sT, sT.v)
    P.op("vector", lambda v: v.tensor_copy(rsb[:, :], sm_ps[:, 0:1]),
         waits=[c_rsmm], inc=sV)
    c_rsb = (sV, sV.v)

    slot_store = {0: None, 1: None}
    slot_free = {0: None, 1: None}
    prev_var = None
    c_xn_tile = {}
    for t in range(NHT):
        sl = t % 2
        P.op("sync", lambda sy, sl=sl, t=t: sy.dma_start(
            x_sb[sl][:, :], xh[t * 128:(t + 1) * 128, :]),
             waits=[slot_free[sl]], inc=sDs, inc_n=16)
        c_xload = (sDs, sDs.v)
        def f_bn(v, sl=sl):
            v.bn_stats(stat[:, 6:12], x_sb[sl][:, 0:512])
            return v.bn_stats(stat[:, 12:18], x_sb[sl][:, 512:1024])
        P.op("vector", f_bn, waits=[c_xload, prev_var], inc=sV)
        c_bn = (sV, sV.v)
        P.op("vector", lambda v: v.bn_aggr(stat[:, 2:4], stat[:, 6:18]),
             waits=[c_bn], inc=sV)
        c_var = (sV, sV.v)
        prev_var = c_var
        P.op("scalar", lambda s: s.activation(stat[:, 5:6], stat[:, 3:4], AF.Ln,
                                              bias=consts[:, 0:1]),
             waits=[c_var, c_consts], inc=sS)
        c_ln = (sS, sS.v)
        P.op("scalar", lambda s: s.activation(stat[:, 5:6], stat[:, 5:6], AF.Exp,
                                              scale=-0.5), waits=[c_ln], inc=sS)
        c_rstd = (sS, sS.v)
        P.op("vector", lambda v, sl=sl: v.tensor_scalar(
            xn_sb[sl][:, :], x_sb[sl][:, :], stat[:, 2:3], stat[:, 5:6],
            ALU.subtract, ALU.mult),
             waits=[c_rstd, slot_store[sl]], inc=sV)
        c_xn = (sV, sV.v)
        slot_free[sl] = c_xn
        P.op("gpsimd", dmag(xn_dram[t * 128:(t + 1) * 128, :], xn_sb[sl][:, :]),
             waits=[c_xn], inc=sDg, inc_n=16)
        slot_store[sl] = (sDg, sDg.v)
        c_xn_tile[t] = (sDg, sDg.v)
    c_xn_all = (sDg, sDg.v)

    W1r = W1.rearrange("e (kc p) f -> p e kc f", p=128)
    W2r = W2.rearrange("e (kc p) f -> p e kc f", p=128)
    for e in range(NE):
        P.op("gpsimd", dmag(
            W1a[:, e * 4096:(e + 1) * 4096].rearrange("p (kc f) -> p kc f", kc=8),
            W1r[:, e:e + 1, :, :].rearrange("p one kc f -> p (one kc) f")),
             inc=sDg, inc_n=16)
        P.op("gpsimd", dmag(
            W2a[:, e * 4096:(e + 1) * 4096].rearrange("p (kc f) -> p kc f", kc=4),
            W2r[:, e:e + 1, :, :].rearrange("p one kc f -> p (one kc) f")),
             inc=sDg, inc_n=16)
    c_expw = (sDg, sDg.v)

    for wbf, w in [(wq_bf, Wq), (wk_bf, Wk), (wv_bf, Wv), (wo_bf, Wo)]:
        P.op("gpsimd", dmag(wbf, w), inc=sDg, inc_n=16)
    c_wcast = (sDg, sDg.v)


    bands = [576, 1152, HT]
    c_band = []
    b_prev = 0
    for bk in bands:
        mark = c_xn_tile[(bk + 127) // 128 - 1]
        for c in range(8):
            P.op("sync", lambda sy, c=c, b0=b_prev, b1=bk: sy.dma_start(
                xnT_s[:, c * HT + b0:c * HT + b1],
                xn_dram[b0:b1, c * 128:(c + 1) * 128],
                transpose=True), waits=[mark], inc=sDs, inc_n=16)
        c_band.append((sDs, sDs.v))
        b_prev = bk
    c_xnT = c_band[-1]

    def xnT_ready(last_col):
        for bk, cb in zip(bands, c_band):
            if last_col <= bk:
                return cb
        return c_xnT

    if _UPTO <= 0:
        stub_out(bar())
        return finish()

    gst = {"c_gdone": None, "c_ez": None}

    def emit_gate(t):
        c_gdone = gst["c_gdone"]
        c_ez = gst["c_ez"]
        hb = HALO + t * 128
    
        def mm_gate(te, hb=hb):
            for kc in range(8):
                ins = te.matmul(sm_ps[0:8, 0:128],
                                gateW_s[:, kc * 8:(kc + 1) * 8],
                                xnT_s[:, kc * HT + hb:kc * HT + hb + 128],
                                start=(kc == 0), stop=(kc == 7))
            return ins
        P.op("tensor", mm_gate,
             waits=[xnT_ready(hb + 128), c_small, c_gdone], inc=sT)
        c_zmm = (sT, sT.v)
        P.op("vector", lambda v: v.tensor_copy(zsc_sb[:, :], sm_ps[0:8, 0:128]),
             waits=[c_zmm], inc=sV)
        c_zev = (sV, sV.v)
        P.op("tensor", lambda te: te.transpose(sm_ps[:, 128:136], zsc_sb[:, :],
                                               identf[0:8, 0:8]),
             waits=[c_zev, c_ez], inc=sT)
        c_z = (sT, sT.v)
        P.op("scalar", lambda s: s.activation(g_scr[:, 8:16], sm_ps[:, 128:136],
                                              AF.Exp, scale=-1.0),
             waits=[c_z, c_gdone], inc=sS)
        c_ez = (sS, sS.v)
        prev = c_ez
        for fn in [
            lambda v: v.tensor_scalar(g_scr[:, 8:16], g_scr[:, 8:16], 1.0, None,
                                      ALU.add),
            lambda v: v.reciprocal(g_scr[:, 0:8], g_scr[:, 8:16]),
            lambda v: v.tensor_scalar(g_scr[:, 0:8], g_scr[:, 0:8], rsb[:, 0:1],
                                      None, ALU.mult),
            lambda v: v.max(out=g_scr[:, 16:24], in_=g_scr[:, 0:8]),
            lambda v: v.tensor_add(g_scr[:, 24:25], g_scr[:, 16:17], g_scr[:, 17:18]),
            lambda v: v.reciprocal(g_scr[:, 25:26], g_scr[:, 24:25]),
            lambda v: v.tensor_scalar(g_scr[:, 8:16], g_scr[:, 0:8], g_scr[:, 17:18],
                                      None, ALU.is_ge),
            lambda v: v.scalar_tensor_tensor(g_scr[:, 0:8], g_scr[:, 0:8],
                                             g_scr[:, 25:26], g_scr[:, 8:16],
                                             ALU.mult, ALU.mult),
        ]:
            P.op("vector", fn, waits=[prev, c_rsb], inc=sV)
            prev = (sV, sV.v)
        c_g = prev
        P.op("tensor", lambda te: te.transpose(
            sm_ps[0:8, 0:128], g_scr[:, 0:8], identf[:, :]),
             waits=[c_g, c_rsb, c_gdone], inc=sT)
        c_gT = (sT, sT.v)
        P.op("vector", lambda v, t=t: v.tensor_copy(
            gT_s[:, t * 128:(t + 1) * 128], sm_ps[0:8, 0:128]),
             waits=[c_gT], inc=sV)
        c_gdone = (sV, sV.v)
        gst["c_gdone"] = c_gdone
        gst["c_ez"] = c_ez

    c_y2_store = None
    c_prev_evac = None
    c_pA_free = [None, None, None]
    c_hsT_used = None
    c_hg_used = [None, None]
    c_hgg_used = None
    c_hgg_used2 = [None, None]
    c_G_used = [None, None]
    emit_gate(0)
    emit_gate(1)
    g_ready = {0: gst["c_gdone"]}
    for ft in range(NFT):
        fb = HALO + ft * FT
        if ft + 1 < NFT:
            emit_gate(2 * (ft + 1))
            emit_gate(2 * (ft + 1) + 1)
            g_ready[ft + 1] = gst["c_gdone"]
        c_gdone = g_ready[ft]
        c_hs_evac = [None] * 4
        for mc in range(4):
            def mm_hs(te, mc=mc, fb=fb):
                for kc in range(8):
                    ins = te.matmul(pA[mc % 3][:, 0:FT],
                                    Ws1_s[:, kc * HF + mc * 128:kc * HF + (mc + 1) * 128],
                                    xnT_s[:, kc * HT + fb:kc * HT + fb + FT],
                                    start=(kc == 0), stop=(kc == 7))
                return ins
            P.op("tensor", mm_hs,
                 waits=[xnT_ready(fb + FT), c_small, c_pA_free[mc % 3]], inc=sT)
            c_mm = (sT, sT.v)
            P.op("scalar", lambda s, mc=mc: s.activation(
                hsT_sb[:, mc * FT:(mc + 1) * FT], pA[mc % 3][:, 0:FT], AF.Gelu,
                bias=bs1_s[:, mc:mc + 1]),
                 waits=[c_mm, c_hsT_used, c_hgg_used2[0]], inc=sS)
            c_hs_evac[mc] = (sS, sS.v)
            c_pA_free[mc % 3] = c_hs_evac[mc]

        def mm_y2z(te):
            for b in range(4):
                ins = te.matmul(y2p[:, b * 512:(b + 1) * 512], zer_sb[:, :],
                                Ws1_s[:, 0:512], start=True, stop=False,
                                skip_group_check=True)
            return ins
        P.op("tensor", mm_y2z, waits=[c_prev_evac, c_consts, c_small], inc=sT)
        c_y2z = (sT, sT.v)

        def mm_ws2(te):
            ins = None
            for c in range(8):
                for mc in range(4):
                    ins = te.matmul(y2p[:, c * FT:(c + 1) * FT],
                                    Ws2_s[:, mc * D + c * 128:mc * D + (c + 1) * 128],
                                    hsT_sb[:, mc * FT:(mc + 1) * FT],
                                    start=False, stop=False,
                                    skip_group_check=True)
            return ins
        P.op("tensor", mm_ws2, waits=[c_hs_evac[3], c_y2z], inc=sT)
        c_hsT_used = (sT, sT.v)

        def mm_b2(te, ft=ft):
            ins = None
            for c in range(8):
                ins = te.matmul(y2p[:, c * FT:(c + 1) * FT],
                                B2_s[:, c * 128:(c + 1) * 128],
                                gT_s[:, ft * FT:(ft + 1) * FT],
                                start=False, stop=False, skip_group_check=True)
            return ins
        P.op("tensor", mm_b2, waits=[c_gdone], inc=sT)

        for e in range(NE):
            P.op("tensor", lambda te, e=e, ft=ft: te.matmul(
                sm_ps[:, 256:256 + FT],
                Sesel[:, e * 128:(e + 1) * 128],
                gT_s[:, ft * FT:(ft + 1) * FT], start=True, stop=True),
                 waits=[c_gdone, c_G_used[0]], inc=sT)
            c_G = (sT, sT.v)
            for mc in range(4):
                psl = (4 * e + mc) % 3
                def mm_h(te, e=e, mc=mc, fb=fb, psl=psl):
                    for kc in range(8):
                        ins = te.matmul(
                            pA[psl][:, 0:FT],
                            W1a[:, e * 4096 + kc * HF + mc * 128:
                                 e * 4096 + kc * HF + (mc + 1) * 128],
                            xnT_s[:, kc * HT + fb:kc * HT + fb + FT],
                            start=(kc == 0), stop=(kc == 7))
                    return ins
                P.op("tensor", mm_h,
                     waits=[xnT_ready(fb + FT), c_expw, c_pA_free[psl]], inc=sT)
                c_hmm = (sT, sT.v)
                P.op("scalar", lambda s, e=e, mc=mc, psl=psl: s.activation(
                    hg_sb[mc % 2][:, :], pA[psl][:, 0:FT], AF.Gelu,
                    bias=b1_s[:, e * 4 + mc:e * 4 + mc + 1]),
                     waits=[c_hmm, c_hg_used[mc % 2]], inc=sS)
                c_hg = (sS, sS.v)
                c_pA_free[psl] = c_hg
                P.op("vector", lambda v, mc=mc, e=e: v.tensor_mul(
                    hgg_sb2[e % 2][:, mc * FT:(mc + 1) * FT], hg_sb[mc % 2][:, :],
                    sm_ps[:, 256:256 + FT]),
                     waits=[c_hg, c_G, c_hgg_used2[e % 2], c_hsT_used], inc=sV)
                c_hg_used[mc % 2] = (sV, sV.v)
            c_hgg = (sV, sV.v)
            c_G_used[0] = c_hgg

            def mm_w2(te, e=e):
                ins = None
                for c in range(8):
                    for mc in range(4):
                        ins = te.matmul(
                            y2p[:, c * FT:(c + 1) * FT],
                            W2a[:, e * 4096 + mc * D + c * 128:
                                 e * 4096 + mc * D + (c + 1) * 128],
                            hgg_sb2[e % 2][:, mc * FT:(mc + 1) * FT],
                            start=False, stop=(e == NE - 1 and c == 7 and mc == 3),
                            skip_group_check=True)
                return ins
            P.op("tensor", mm_w2, waits=[c_hgg], inc=sT)
            c_hgg_used2[e % 2] = (sT, sT.v)
            c_hgg_used = (sT, sT.v)
        c_y2mm = (sT, sT.v)

        for half in range(2):
            def ev_y2(s, half=half):
                for c in range(4 * half, 4 * half + 4):
                    ins = s.activation(y2sb[:, (c - 4 * half) * FT:(c - 4 * half + 1) * FT],
                                       y2p[:, c * FT:(c + 1) * FT], AF.Identity,
                                       bias=bs2_s[:, c:c + 1])
                return ins
            P.op("scalar", ev_y2, waits=[c_y2mm, c_y2_store], inc=sS)
            c_prev_evac = (sS, sS.v)

            def st_y2(g, ft=ft, half=half):
                dst = y2_dram.rearrange("(c p) t -> p c t", p=128)[
                    :, 4 * half:4 * half + 4, ft * FT:(ft + 1) * FT]
                srcap = y2sb[:, :].rearrange("p (c f) -> p c f", c=4)
                return g.dma_start(dst, srcap)
            P.op("gpsimd", st_y2, waits=[c_prev_evac], inc=sDg, inc_n=16)
            c_y2_store = (sDg, sDg.v)
    c_y2_all = c_y2_store
    if _DBG:
        d_gT = nc.dram_tensor("d_gT", [8, T], BF16, kind="ExternalOutput").ap()
        P.op("gpsimd", lambda g: g.dma_start(d_gT, gT_s[:, :]),
             waits=[gst["c_gdone"]], inc=sDg, inc_n=16)
    c_p2_bar = bar()
    if _UPTO <= 2:
        stub_out(c_p2_bar)
        return finish()

    sbA.close()
    psA.close()
    sbB = ExitStack()
    enB = sbB.enter_context
    WoT_s = enB(sb("WoT_s", [128, 8 * D], BF16))
    knT_s = enB(sb("knT_s", [128, 8 * HT], BF16))
    qT_s = enB(sb("qT_s", [128, 8 * T], BF16))
    rqT_s = enB(sb("rqT_s", [128, NTI * NH], F32))
    masks = enB(sb("masks", [128, 3 * WIN], BF16))
    identb = enB(sb("identb", [128, 128], BF16))
    bor = enB(sb("bor", [128, D], BF16))
    sbB1 = ExitStack()
    enB1 = sbB1.enter_context
    WqT_s = enB1(sb("WqT_s", [128, 8 * D], BF16))
    WkT_s = enB1(sb("WkT_s", [128, 8 * D], BF16))
    WvT_s = enB1(sb("WvT_s", [128, 8 * D], BF16))
    kraw_s = enB1(sb("kraw_s", [128, 8 * 512], BF16))
    sq_scr = [enB1(sb(f"sq_scr{i}", [128, D], BF16)) for i in range(2)]
    rq_f = enB1(sb("rq_f", [16, 512], F32))
    rk_f = enB1(sb("rk_f", [16, 512], F32))
    Ssel = enB1(sb("Ssel", [16, 8 * 128], F32))
    Esel = enB1(sb("Esel", [128, 128], BF16))
    bv_s = enB1(sb("bv_s", [128, 8], F32))
    bq_s = enB1(sb("bq_s", [128, 8], F32))
    bk_s = enB1(sb("bk_s", [128, 8], F32))
    P.op("vector", lambda v: v.memset(bor[:, :], 0.0), waits=c_p2_bar, inc=sV)
    c_bz = (sV, sV.v)
    for dst, srcap in [
        (masks[:, 0:WIN], mfirst), (masks[:, WIN:2 * WIN], mint),
        (masks[:, 2 * WIN:3 * WIN], mlast),
        (identb[:, :], identb_d), (bor[0:1, :], bo_row),
        (Ssel[:, :], Ssel_d), (Esel[:, :], Esel_d), (bv_s[:, :], bvt),
        (bq_s[:, :], bqt), (bk_s[:, :], bkt),
    ]:
        P.op("gpsimd", dmag(dst, srcap), waits=c_p2_bar + [c_bz], inc=sDg, inc_n=16)
    c_bsmall = (sDg, sDg.v)
    psB = ExitStack()
    enBp = psB.enter_context
    sm2_ps = enBp(psn("sm2_ps", [128, 128], F32))
    pB2 = [enBp(psn(f"pB2{i}", [128, 512], F32)) for i in range(4)]
    ss_ps = enBp(psn("ss_ps", [16, 512], F32))
    rr2_ps = [enBp(psn(f"rr2_ps{i}", [128, 512], F32)) for i in range(2)]

    wt_marks = {}
    for wbf, wt, nm in [(wk_bf, WkT_s, "k"), (wv_bf, WvT_s, "v"),
                        (wq_bf, WqT_s, "q"), (wo_bf, WoT_s, "o")]:
        for c in range(8):
            P.op("sync", lambda sy, wbf=wbf, wt=wt, c=c: sy.dma_start(
                wt[:, c * D:(c + 1) * D], wbf[:, c * 128:(c + 1) * 128],
                transpose=True), waits=[c_wcast] + c_p2_bar, inc=sDs, inc_n=16)
        wt_marks[nm] = (sDs, sDs.v)
    c_WkT, c_WvT, c_WqT, c_WoT = (wt_marks["k"], wt_marks["v"],
                                  wt_marks["q"], wt_marks["o"])

    c_pB2_free = [None, None, None, None]
    c_kraw_used = None
    c_rr2_used = [None, None]
    c_sq_used = [None, None]
    c_ss_free = None

    k_macs = [(0, 512), (512, 512), (1024, 512), (1536, 512), (2048, 128)]
    st1 = {"ss_free": None, "kraw_used": None, "smT_used": None}

    def emit_proj_k(i):
        off, w = k_macs[i]
        prev_tt = st1.get(("tt", i - 1))
        for cch in range(8):
            sl = cch % 4

            def mm_k(te, cch=cch, off=off, w=w, sl=sl):
                for kc in range(8):
                    ins = te.matmul(pB2[sl][:, 0:w],
                                    WkT_s[:, kc * D + cch * 128:kc * D + (cch + 1) * 128],
                                    xnT_s[:, kc * HT + off:kc * HT + off + w],
                                    start=(kc == 0), stop=(kc == 7))
                return ins
            P.op("tensor", mm_k, waits=[c_WkT, c_xnT, c_pB2_free[sl]], inc=sT)
            c_kmm = (sT, sT.v)
            P.op("scalar", lambda s, cch=cch, w=w, sl=sl: s.activation(
                kraw_s[:, cch * 512:cch * 512 + w], pB2[sl][:, 0:w], AF.Identity,
                bias=bk_s[:, cch:cch + 1]),
                 waits=[c_kmm, c_bsmall, prev_tt[cch] if prev_tt else None], inc=sS)
            c_kev = (sS, sS.v)
            c_pB2_free[sl] = c_kev
            P.op("vector", lambda v, cch=cch, w=w, sq=cch % 2: v.tensor_mul(
                sq_scr[sq][:, 0:w], kraw_s[:, cch * 512:cch * 512 + w],
                kraw_s[:, cch * 512:cch * 512 + w]),
                 waits=[c_kev, c_sq_used[cch % 2]], inc=sV)
            c_sq = (sV, sV.v)
            P.op("tensor", lambda te, cch=cch, w=w, sq=cch % 2: te.matmul(
                ss_ps[:, 0:w], Esel[:, cch * 16:(cch + 1) * 16], sq_scr[sq][:, 0:w],
                start=(cch == 0), stop=(cch == 7)),
                 waits=[c_sq, c_bsmall, st1["ss_free"] if cch == 0 else None], inc=sT)
            c_sq_used[cch % 2] = (sT, sT.v)
        st1[("ssmm", "k", i)] = (sT, sT.v)

    def emit_chain_k(i):
        off, w = k_macs[i]
        prevk = st1[("ssmm", "k", i)]
        for j, fn in enumerate([
            lambda s, w=w: s.activation(rk_f[0:16, 0:w], ss_ps[0:16, 0:w], AF.Ln),
            lambda s, w=w: s.activation(rk_f[0:16, 0:w], rk_f[0:16, 0:w], AF.Exp,
                                        scale=0.5),
            lambda s, w=w: s.activation(rk_f[0:16, 0:w], rk_f[0:16, 0:w], AF.Ln,
                                        bias=consts[0:16, 1:2]),
            lambda s, w=w: s.activation(rk_f[0:16, 0:w], rk_f[0:16, 0:w], AF.Exp,
                                        scale=-1.0),
        ]):
            P.op("scalar", fn, waits=[prevk], inc=sS)
            prevk = (sS, sS.v)
            if j == 0:
                st1["ss_free"] = prevk
        st1[("rk", i)] = prevk

    def emit_Rtt_k(i):
        off, w = k_macs[i]
        c_rk = st1[("rk", i)]
        tts = [None] * 8
        for cch in range(8):
            sl = cch % 2
            P.op("tensor", lambda te, cch=cch, w=w, sl=sl: te.matmul(
                rr2_ps[sl][:, 0:w], Ssel[:, cch * 128:(cch + 1) * 128],
                rk_f[0:16, 0:w], start=True, stop=True),
                 waits=[c_rk, c_bsmall, c_rr2_used[sl]], inc=sT)
            c_R = (sT, sT.v)
            P.op("vector", lambda v, cch=cch, off=off, w=w, sl=sl: v.tensor_mul(
                knT_s[:, cch * HT + off:cch * HT + off + w],
                kraw_s[:, cch * 512:cch * 512 + w], rr2_ps[sl][:, 0:w]),
                 waits=[c_R], inc=sV)
            c_rr2_used[sl] = (sV, sV.v)
            tts[cch] = (sV, sV.v)
        st1[("tt", i)] = tts
        st1["kraw_used"] = (sV, sV.v)

    def emit_proj_q(m):
        off = HALO + m * 512
        for cch in range(8):
            sl = cch % 4

            def mm_q(te, cch=cch, off=off, sl=sl):
                for kc in range(8):
                    ins = te.matmul(pB2[sl][:, 0:512],
                                    WqT_s[:, kc * D + cch * 128:kc * D + (cch + 1) * 128],
                                    xnT_s[:, kc * HT + off:kc * HT + off + 512],
                                    start=(kc == 0), stop=(kc == 7))
                return ins
            P.op("tensor", mm_q, waits=[c_WqT, c_pB2_free[sl]], inc=sT)
            c_qmm = (sT, sT.v)
            P.op("scalar", lambda s, cch=cch, m=m, sl=sl: s.activation(
                qT_s[:, cch * T + m * 512:cch * T + (m + 1) * 512], pB2[sl][:, 0:512],
                AF.Identity, bias=bq_s[:, cch:cch + 1]), waits=[c_qmm, c_bsmall], inc=sS)
            c_qev = (sS, sS.v)
            c_pB2_free[sl] = c_qev
            P.op("vector", lambda v, cch=cch, m=m, sq=cch % 2: v.tensor_mul(
                sq_scr[sq][:, 0:512],
                qT_s[:, cch * T + m * 512:cch * T + (m + 1) * 512],
                qT_s[:, cch * T + m * 512:cch * T + (m + 1) * 512]),
                 waits=[c_qev, c_sq_used[cch % 2]], inc=sV)
            c_sq = (sV, sV.v)
            P.op("tensor", lambda te, cch=cch, sq=cch % 2: te.matmul(
                ss_ps[:, 0:512], Esel[:, cch * 16:(cch + 1) * 16],
                sq_scr[sq][:, 0:512], start=(cch == 0), stop=(cch == 7)),
                 waits=[c_sq, c_bsmall, st1["ss_free"] if cch == 0 else None], inc=sT)
            c_sq_used[cch % 2] = (sT, sT.v)
        st1[("ssmm", "q", m)] = (sT, sT.v)

    def emit_chain_q(m):
        prevq = st1[("ssmm", "q", m)]
        for j, fn in enumerate([
            lambda s: s.activation(rq_f[0:16, 0:512], ss_ps[0:16, 0:512], AF.Ln),
            lambda s: s.activation(rq_f[0:16, 0:512], rq_f[0:16, 0:512], AF.Exp,
                                   scale=0.5),
            lambda s: s.activation(rq_f[0:16, 0:512], rq_f[0:16, 0:512], AF.Ln,
                                   bias=consts[0:16, 1:2]),
            lambda s: s.activation(rq_f[0:16, 0:512], rq_f[0:16, 0:512], AF.Exp,
                                   scale=-1.0),
        ]):
            P.op("scalar", fn, waits=[prevq], inc=sS)
            prevq = (sS, sS.v)
            if j == 0:
                st1["ss_free"] = prevq
        st1[("rq", m)] = prevq

    def emit_tr_q(m):
        c_rq = st1[("rq", m)]
        for tt in range(4):
            P.op("tensor", lambda te, tt=tt: te.transpose(
                sm2_ps[0:128, 0:16], rq_f[0:16, tt * 128:(tt + 1) * 128],
                identf[0:16, 0:16]), waits=[c_rq, st1["smT_used"]], inc=sT)
            c_trq = (sT, sT.v)
            P.op("vector", lambda v, m=m, tt=tt: v.tensor_copy(
                rqT_s[:, (m * 4 + tt) * 16:(m * 4 + tt) * 16 + 16],
                sm2_ps[0:128, 0:16]), waits=[c_trq], inc=sV)
            st1["smT_used"] = (sV, sV.v)

    pend_tr = None
    for i in range(5):
        emit_proj_k(i)
        if pend_tr is not None:
            emit_tr_q(pend_tr)
        emit_chain_k(i)
        if i < 4:
            emit_proj_q(i)
        emit_Rtt_k(i)
        if i < 4:
            emit_chain_q(i)
            pend_tr = i
    emit_tr_q(pend_tr)
    c_kraw_used = st1["kraw_used"]

    c_vst = None
    c_pB2v_free = list(c_pB2_free)
    for (off, w) in [(0, 512), (512, 512), (1024, 512), (1536, 512), (2048, 128)]:
        c_vev = [None] * 8
        for mc in range(8):
            sl = mc % 4

            def mm_v(te, mc=mc, off=off, w=w, sl=sl):
                for kc in range(8):
                    ins = te.matmul(
                        pB2[sl][:, 0:w],
                        WvT_s[:, kc * D + mc * 128:kc * D + (mc + 1) * 128],
                        xnT_s[:, kc * HT + off:kc * HT + off + w],
                        start=(kc == 0), stop=(kc == 7))
                return ins
            P.op("tensor", mm_v, waits=[c_WvT, c_pB2v_free[sl], c_kraw_used, c_vst],
                 inc=sT)
            c_vmm = (sT, sT.v)
            P.op("scalar", lambda s, mc=mc, w=w, sl=sl: s.activation(
                kraw_s[:, mc * 512:mc * 512 + w], pB2[sl][:, 0:w], AF.Identity,
                bias=bv_s[:, mc:mc + 1]),
                 waits=[c_vmm, c_bsmall], inc=sS)
            c_vev[mc] = (sS, sS.v)
            c_pB2v_free[sl] = c_vev[mc]

        def st_v(g, off=off, w=w):
            dst = v_dram.rearrange("(mc p) t -> p mc t", p=128)[:, :, off:off + w]
            srcap = kraw_s[:, :].rearrange("p (mc f) -> p mc f", mc=8)[:, :, 0:w]
            return g.dma_start(dst, srcap)
        P.op("gpsimd", st_v, waits=[c_vev[7]], inc=sDg, inc_n=16)
        c_vst = (sDg, sDg.v)
    c_p1a_bar = bar()

    if _DBG:
        d_knT = nc.dram_tensor("d_knT", [128, 8 * HT], BF16, kind="ExternalOutput").ap()
        d_qT = nc.dram_tensor("d_qT", [128, 8 * T], BF16, kind="ExternalOutput").ap()
        d_rqT = nc.dram_tensor("d_rqT", [128, NTI * NH], F32, kind="ExternalOutput").ap()
        for dst, srct in [(d_knT, knT_s), (d_qT, qT_s), (d_rqT, rqT_s)]:
            P.op("gpsimd", lambda g, dst=dst, srct=srct: g.dma_start(dst, srct[:, :]),
                 waits=c_p1a_bar, inc=sDg, inc_n=16)

    if _UPTO <= 3:
        stub_out(c_p1a_bar)
        return finish()

    psB.close()
    sbB1.close()
    xL_sb = [enB(sb(f"xL_sb{i}", [128, D], F32)) for i in range(2)]
    vw_a = [enB(sb(f"vw_a{i}", [128, D], BF16)) for i in range(2)]
    vw_b = [enB(sb(f"vw_b{i}", [128, D], BF16)) for i in range(2)]
    vw_c = [enB(sb(f"vw_c{i}", [128, D], BF16)) for i in range(2)]
    e_sb = [enB(sb(f"e_sb{i}", [128, 4 * WIN], BF16)) for i in range(2)]
    pn_sb = [enB(sb(f"pn_sb{i}", [128, 4 * WIN], BF16)) for i in range(4)]
    sums = enB(sb("sums", [128, NH], F32))
    rsum = enB(sb("rsum", [128, NH], F32))
    pT_sb = [enB(sb(f"pT_sb{i}", [128, 1024], BF16)) for i in range(2)]
    outT_sb = [enB(sb(f"outT_sb{i}", [128, 128], BF16)) for i in range(2)]
    out2_sb = [enB(sb(f"out2_sb{i}", [128, 128], BF16)) for i in range(2)]
    out_sb = [enB(sb(f"out_sb{i}", [128, D], F32)) for i in range(2)]
    y1_sb = [enB(sb(f"y1_sb{i}", [128, D], F32)) for i in range(2)]
    yb_sb = [enB(sb(f"yb_sb{i}", [128, D], BF16)) for i in range(2)]
    c_zer = c_consts
    psC = ExitStack()
    enCp = psC.enter_context
    y1_ps = enCp(psn("y1_ps", [128, 1024], F32))
    scg_ps = enCp(psn("scg_ps", [128, 4 * WIN], F32))
    pT_ps = [enCp(psn(f"pT_ps{i}", [128, 1024], BF16)) for i in range(2)]
    av_ps = [enCp(psn(f"av_ps{i}", [128, 512], F32)) for i in range(2)]

    c_vw_used = [None, None]
    c_x_free = [None, None]
    c_y1sb_free = [None, None]
    c_yb_free = [None, None]
    c_out_store = [None, None]
    c_y1_free = None
    c_scg_free = None
    c_exp_by = [None, None]
    c_stt_prev = None
    c_stt_by = [None, None]
    c_pn_by = [None, None]
    c_tr_by = [None, None]
    c_pTev_by = [None, None]
    c_av_by = [None, None]
    c_avev_by = [None, None]
    c_wo_by = [None, None]
    c_wo_last = None
    c_pn_key = {}

    def emit_A_hg(t, hg, first_bar):
        wb = t * 128 + 32
        moff = 0 if t == 0 else (2 * WIN if t == NTI - 1 else WIN)
        if True:
            par = hg % 2

            def mm_scg(te, t=t, hg=hg, wb=wb):
                for i in range(2):
                    c = 2 * hg + i
                    te.matmul(scg_ps[:, 2 * i * WIN:(2 * i + 1) * WIN],
                              qT_s[0:64, c * T + t * 128:c * T + (t + 1) * 128],
                              knT_s[0:64, c * HT + wb:c * HT + wb + WIN],
                              start=True, stop=True)
                te.matmul(y1_ps[:, 0:128], zer_sb[:, :], zer_sb[:, :],
                          start=False, stop=False, skip_group_check=True)
                for i in range(2):
                    c = 2 * hg + i
                    te.matmul(scg_ps[:, (2 * i + 1) * WIN:(2 * i + 2) * WIN],
                              qT_s[64:128, c * T + t * 128:c * T + (t + 1) * 128],
                              knT_s[64:128, c * HT + wb:c * HT + wb + WIN],
                              start=True, stop=True)
                return te.matmul(y1_ps[:, 0:128], zer_sb[:, :], zer_sb[:, :],
                                 start=False, stop=False, skip_group_check=True)
            P.op("tensor", mm_scg, waits=[st["scg_free"], c_zer] + first_bar,
                 inc=sT)
            c_sc = (sT, sT.v)

            def f_exp(s, t=t, hg=hg, par=par):
                for i in range(4):
                    h = 4 * hg + i
                    ins = s.activation(e_sb[par][:, i * WIN:(i + 1) * WIN],
                                       scg_ps[:, i * WIN:(i + 1) * WIN], AF.Exp,
                                       scale=rqT_s[:, t * NH + h:t * NH + h + 1])
                return ins
            P.op("scalar", f_exp,
                 waits=[c_sc, st["stt_by"][par]] + first_bar, inc=sS)
            c_exp = (sS, sS.v)
            st["scg_free"] = c_exp
            st["exp_by"][par] = c_exp

            def f_stt(v, t=t, hg=hg, par=par, moff=moff):
                for i in range(4):
                    h = 4 * hg + i
                    ins = v.scalar_tensor_tensor(
                        pn_sb[hg][:, i * WIN:(i + 1) * WIN],
                        e_sb[par][:, i * WIN:(i + 1) * WIN], 1.0,
                        masks[:, moff:moff + WIN], ALU.mult, ALU.mult,
                        accum_out=sums[:, 4 * hg + i:4 * hg + i + 1])
                return ins
            P.op("vector", f_stt,
                 waits=[c_exp, c_bsmall, st["tr_by"][hg]] + first_bar, inc=sV)
            st["stt_by"][par] = (sV, sV.v)
            P.op("vector", lambda v, hg=hg: v.reciprocal(
                rsum[:, 4 * hg:4 * hg + 4], sums[:, 4 * hg:4 * hg + 4]),
                 waits=[st["stt_by"][par]], inc=sV)
            c_rs2 = (sV, sV.v)

            def f_pn(v, hg=hg):
                for i in range(4):
                    h = 4 * hg + i
                    ins = v.tensor_scalar(pn_sb[hg][:, i * WIN:(i + 1) * WIN],
                                          pn_sb[hg][:, i * WIN:(i + 1) * WIN],
                                          rsum[:, h:h + 1], None, ALU.mult)
                return ins
            P.op("vector", f_pn, waits=[c_rs2], inc=sV)
            c_pn_key[(t, hg)] = (sV, sV.v)

    ld = {}

    def emit_loads(t):
        sl2 = t % 2
        hb = HALO + t * 128
        P.op("sync", lambda sy, t=t, sl2=sl2: sy.dma_start(
            vw_a[sl2][:, :], v_dram[:, t * 128 + 32:t * 128 + 160], transpose=True),
             waits=[c_vw_used[sl2], c_p1a_bar[3]], inc=sDs, inc_n=16)
        P.op("sync", lambda sy, t=t, sl2=sl2: sy.dma_start(
            vw_b[sl2][:, :], v_dram[:, t * 128 + 160:t * 128 + 288], transpose=True),
             inc=sDs, inc_n=16)
        P.op("sync", lambda sy, t=t, sl2=sl2: sy.dma_start(
            vw_c[sl2][:, :], v_dram[:, t * 128 + 96:t * 128 + 224], transpose=True),
             inc=sDs, inc_n=16)
        c_vw = (sDs, sDs.v)
        P.op("gpsimd", lambda g, hb=hb, sl2=sl2: g.dma_start(
            xL_sb[sl2][:, :], xh[hb:hb + 128, :]),
             waits=[c_x_free[sl2]], inc=sDg, inc_n=16)
        c_xl = (sDg, sDg.v)
        P.op("sync", lambda sy, t=t, sl2=sl2: sy.dma_start(
            yb_sb[sl2][:, :], y2_dram[:, t * 128:(t + 1) * 128], transpose=True),
             waits=[c_y2_all, c_yb_free[sl2]], inc=sDs, inc_n=16)
        c_yb = (sDs, sDs.v)
        ld[t] = (c_vw, c_xl, c_yb)

    def emit_B_pre(t, first_bar):
        sl2 = t % 2
        c_vw, c_xl, c_yb = ld[t]

        def mm_bo(te):
            te.matmul(y1_ps[:, 0:512], oner0b[:, :], bor[:, 0:512],
                      start=True, stop=False, skip_group_check=True)
            return te.matmul(y1_ps[:, 512:1024], oner0b[:, :], bor[:, 512:1024],
                             start=True, stop=False, skip_group_check=True)
        P.op("tensor", mm_bo, waits=[c_bsmall, st["y1_free"]] + first_bar, inc=sT)
        return c_vw, c_xl, c_yb

    def emit_B_sb(t, sb2, c_vw, first_bar):
        sl2 = t % 2
        if True:
            pairs = (2 * sb2, 2 * sb2 + 1)
            par = sb2 % 2
            slb = sb2 % 2

            def f_tr(te, t=t, sb2=sb2, slb=slb):
                for j in range(2):
                    i0 = 2 * j * WIN
                    o0 = 512 * j
                    te.transpose(pT_ps[slb][:, o0:o0 + 128],
                                 pn_sb[sb2][:, i0:i0 + 128], identb[:, :])
                    te.transpose(pT_ps[slb][0:64, o0 + 128:o0 + 256],
                                 pn_sb[sb2][:, i0 + 128:i0 + WIN], identb[:, :])
                    te.transpose(pT_ps[slb][0:64, o0 + 256:o0 + 384],
                                 pn_sb[sb2][:, i0 + WIN:i0 + WIN + 64], identb[:, :])
                    ins = te.transpose(pT_ps[slb][:, o0 + 384:o0 + 512],
                                       pn_sb[sb2][:, i0 + WIN + 64:i0 + 2 * WIN],
                                       identb[:, :])
                return ins
            P.op("tensor", f_tr,
                 waits=[c_pn_key[(t, sb2)], st["pTev_by"][slb]] + first_bar, inc=sT)
            st["tr_by"][sb2] = (sT, sT.v)
            c_tr = st["tr_by"][sb2]

            if sb2 % 2 == 0:
                P.op("scalar", lambda s, slb=slb: s.copy(pT_sb[slb][:, :],
                                                         pT_ps[slb][:, :]),
                     waits=[c_tr, st["av_by"][slb]], inc=sS)
                st["pTev_by"][slb] = (sS, sS.v)
            else:
                P.op("vector", lambda v, slb=slb: v.tensor_copy(pT_sb[slb][:, :],
                                                                pT_ps[slb][:, :]),
                     waits=[c_tr, st["av_by"][slb]], inc=sV)
                st["pTev_by"][slb] = (sV, sV.v)

            def f_av(te, pairs=pairs, sl2=sl2, slb=slb):
                for j, p in enumerate(pairs):
                    A, Bh = 2 * p, 2 * p + 1
                    co = j * 256
                    po = j * 512
                    te.matmul(av_ps[slb][0:64, co:co + 128],
                              vw_a[sl2][:, A * 64:(A + 1) * 64],
                              pT_sb[slb][:, po:po + 128], start=True, stop=False,
                              skip_group_check=True)
                    te.matmul(av_ps[slb][0:64, co:co + 128],
                              vw_b[sl2][0:64, A * 64:(A + 1) * 64],
                              pT_sb[slb][0:64, po + 128:po + 256], start=False,
                              stop=True, skip_group_check=True)
                    te.matmul(av_ps[slb][0:64, co + 128:co + 256],
                              vw_a[sl2][0:64, Bh * 64:(Bh + 1) * 64],
                              pT_sb[slb][0:64, po + 256:po + 384], start=True,
                              stop=False, skip_group_check=True)
                    ins = te.matmul(av_ps[slb][0:64, co + 128:co + 256],
                                    vw_c[sl2][:, Bh * 64:(Bh + 1) * 64],
                                    pT_sb[slb][:, po + 384:po + 512], start=False,
                                    stop=True, skip_group_check=True)
                return ins
            P.op("tensor", f_av,
                 waits=[st["pTev_by"][slb], c_vw, st["avev_by"][slb]] + first_bar,
                 inc=sT)
            st["av_by"][slb] = (sT, sT.v)

            if sb2 % 2 == 0:
                def f_ev(v, slb=slb):
                    v.tensor_copy(outT_sb[slb][0:64, :], av_ps[slb][0:64, 0:128])
                    v.tensor_copy(outT_sb[slb][64:128, :], av_ps[slb][0:64, 128:256])
                    v.tensor_copy(out2_sb[slb][0:64, :], av_ps[slb][0:64, 256:384])
                    return v.tensor_copy(out2_sb[slb][64:128, :],
                                         av_ps[slb][0:64, 384:512])
                P.op("vector", f_ev, waits=[st["av_by"][slb], st["wo_by"][slb]],
                     inc=sV)
                st["avev_by"][slb] = (sV, sV.v)
            else:
                def f_ev(s, slb=slb):
                    s.copy(outT_sb[slb][0:64, :], av_ps[slb][0:64, 0:128])
                    s.copy(outT_sb[slb][64:128, :], av_ps[slb][0:64, 128:256])
                    s.copy(out2_sb[slb][0:64, :], av_ps[slb][0:64, 256:384])
                    return s.copy(out2_sb[slb][64:128, :], av_ps[slb][0:64, 384:512])
                P.op("scalar", f_ev, waits=[st["av_by"][slb], st["wo_by"][slb]],
                     inc=sS)
                st["avev_by"][slb] = (sS, sS.v)

            def f_wo(te, pairs=pairs, slb=slb):
                for j, p in enumerate(pairs):
                    src_t = outT_sb[slb] if j == 0 else out2_sb[slb]
                    te.matmul(y1_ps[:, 0:512], src_t[:, :],
                              WoT_s[:, p * D:p * D + 512], start=False, stop=False,
                              skip_group_check=True)
                    ins = te.matmul(y1_ps[:, 512:1024], src_t[:, :],
                                    WoT_s[:, p * D + 512:(p + 1) * D], start=False,
                                    stop=(p == 7), skip_group_check=True)
                return ins
            P.op("tensor", f_wo, waits=[st["avev_by"][slb], c_WoT], inc=sT)
            st["wo_by"][slb] = (sT, sT.v)

    def emit_B_post(t, c_xl, c_yb):
        sl2 = t % 2
        c_wo_last = st["wo_by"][1]

        P.op("scalar", lambda s, sl2=sl2: s.copy(y1_sb[sl2][:, :], y1_ps[:, :]),
             waits=[c_wo_last, c_y1sb_free[sl2]], inc=sS)
        st["y1_free"] = (sS, sS.v)
        P.op("vector", lambda v, sl2=sl2: v.scalar_tensor_tensor(
            out_sb[sl2][:, :], y1_sb[sl2][:, :], 1.0, xL_sb[sl2][:, :],
            ALU.mult, ALU.add),
             waits=[st["y1_free"], c_xl, c_out_store[sl2]], inc=sV)
        P.op("vector", lambda v, sl2=sl2: v.tensor_add(
            out_sb[sl2][:, :], out_sb[sl2][:, :], yb_sb[sl2][:, :]),
             waits=[c_yb, (sV, sV.v)], inc=sV)
        c_fin = (sV, sV.v)
        c_y1sb_free[sl2] = c_fin
        c_yb_free[sl2] = c_fin
        c_x_free[sl2] = c_fin
        c_vw_used[sl2] = st["av_by"][1]
        P.op("gpsimd", lambda g, t=t, sl2=sl2: g.dma_start(
            out[t * 128:(t + 1) * 128, :], out_sb[sl2][:, :]),
             waits=[c_fin], inc=sOut, inc_n=16)
        c_out_store[sl2] = (sOut, sOut.v)

    st = {"scg_free": None, "exp_by": [None, None], "stt_by": [None, None],
          "tr_by": [None, None, None, None], "pTev_by": [None, None],
          "av_by": [None, None], "avev_by": [None, None], "wo_by": [None, None],
          "y1_free": None}

    for hg in range(4):
        emit_A_hg(0, hg, c_p1a_bar)
    for t in range(NTI):
        fb2 = c_p1a_bar if t == 0 else []
        emit_loads(t)
        c_vw, c_xl, c_yb = emit_B_pre(t, fb2)
        for i in range(4):
            emit_B_sb(t, i, c_vw, fb2)
            if t + 1 < NTI:
                emit_A_hg(t + 1, i, fb2)
        emit_B_post(t, c_xl, c_yb)

    P.op("gpsimd", lambda g: g.engine_nop(), waits=[(sOut, 16 * NTI)])

    return finish()


def _masks_for(q0):
    def m(t):
        qi = np.arange(128)[:, None]
        kk = np.arange(WIN)[None, :]
        qg = q0 + t * 128 + qi
        kg = q0 + t * 128 - 32 + kk
        allowed = ((np.abs(kg - qg) <= 32) & ((kg - qg) % 4 == 0)
                   & (kg >= 0) & (kg < 8192))
        return np.ascontiguousarray(allowed.astype(np.float32))
    return m(0), m(7), m(NTI - 1)


_NC_CACHE = None


def kernel(**inputs):
    global _NC_CACHE
    f = lambda k: np.ascontiguousarray(np.asarray(inputs[k], dtype=np.float32))
    x = f("L0_attn_x")
    Wq, bq = f("L0_attn_Wq"), f("L0_attn_bq")
    Wk, bk = f("L0_attn_Wk"), f("L0_attn_bk")
    Wv, bv = f("L0_attn_Wv"), f("L0_attn_bv")
    Wo, bo = f("L0_attn_Wo"), f("L0_attn_bo")
    gW = f("L0_ffn_gate_W")
    W1, b1 = f("L0_ffn_W1"), f("L0_ffn_b1")
    W2, b2 = f("L0_ffn_W2"), f("L0_ffn_b2")
    Ws1, bs1 = f("L0_ffn_Ws1"), f("L0_ffn_bs1")
    Ws2, bs2 = f("L0_ffn_Ws2"), f("L0_ffn_bs2")
    rsv = f("L0_ffn_route_scale")

    shared = {
        "Wq": Wq, "Wk": Wk, "Wv": Wv, "Wo": Wo,
        "bqt": np.ascontiguousarray(bq.reshape(8, 128).T),
        "bkt": np.ascontiguousarray(bk.reshape(8, 128).T),
        "bvt": np.ascontiguousarray(bv.reshape(8, 128).T),
        "bo_row": bo.reshape(1, D),
        "gW": gW, "W1": W1, "W2": W2,
        "b1t": np.ascontiguousarray(b1.reshape(8, 4, 128).transpose(2, 0, 1)
                                    .reshape(128, 32)),
        "B2": b2,
        "Ws1": Ws1, "Ws2": Ws2,
        "bs1t": np.ascontiguousarray(bs1.reshape(4, 128).T),
        "bs2t": np.ascontiguousarray(bs2.reshape(8, 128).T),
        "rs": np.concatenate([rsv.reshape(1, 1),
                              np.zeros((127, 1), np.float32)], axis=0),
    }

    in_maps = []
    for core in range(8):
        b, qq = core // 4, core % 4
        q0 = qq * T
        xh = np.zeros((HT, D), np.float32)
        lo = max(0, q0 - HALO)
        hi = min(8192, q0 + T + HALO)
        xh[lo - (q0 - HALO):hi - (q0 - HALO)] = x[b, lo:hi]
        mf, mi, ml = _masks_for(q0)
        m = dict(shared)
        m.update({"xh": xh, "mfirst": mf, "mint": mi, "mlast": ml})
        in_maps.append(m)

    if _NC_CACHE is None:
        _NC_CACHE = build_nc()
    nc = _NC_CACHE
    res = run_bass_kernel_spmd(nc, in_maps, core_ids=list(range(8)))

    outp = np.empty((2, 8192, D), np.float32)
    for core in range(8):
        b, qq = core // 4, core % 4
        outp[b, qq * T:(qq + 1) * T] = np.asarray(res.results[core]["out"],
                                                  dtype=np.float32)
    return outp


if __name__ == "__main__":
    import sys
    sys.path.insert(0, "/root/problem")
    from reference import setup_inputs, reference
    ins = setup_inputs(0)
    ref = np.asarray(reference(**ins))
    got = kernel(**{k: np.asarray(v) for k, v in ins.items()})
    err = np.mean((got - ref) ** 2) / np.mean(ref ** 2)
    print("residual variance ratio:", err)


# version 22
# speedup vs baseline: 1.0348x; 1.0055x over previous
import os
import numpy as np
import ml_dtypes
from contextlib import ExitStack

import concourse.bass as bass
import concourse.mybir as mybir
from concourse.bass_utils import run_bass_kernel_spmd

F32 = mybir.dt.float32
BF16 = mybir.dt.bfloat16
AF = mybir.ActivationFunctionType
ALU = mybir.AluOpType
AX = mybir.AxisListType

D = 1024
NH = 16
NE = 8
HF = 512
T = 2048
HALO = 64
HT = T + 2 * HALO
NTI = T // 128
NHT = HT // 128
WIN = 192
FT = 256
NFT = T // FT
LN_EPS = 1e-5
ATT_EPS = 1e-6


class Sem:
    def __init__(self, handle):
        self.h = handle
        self.v = 0


class Ev:
    __slots__ = ("eng", "fn", "waits", "incs")

    def __init__(self, eng, fn, waits, incs):
        self.eng, self.fn, self.waits, self.incs = eng, fn, waits, incs


class Prog:
    def __init__(self):
        self.events = []

    def op(self, eng, fn, waits=(), inc=None, inc_n=1):
        incs = []
        if inc is not None:
            inc.v += inc_n
            incs.append((inc, inc_n))
        self.events.append(Ev(eng, fn, [w for w in waits if w is not None], incs))
        return (inc, inc.v) if inc is not None else None

    def emit(self, engine_name, eng):
        for ev in self.events:
            if ev.eng != engine_name:
                continue
            for (sem, val) in ev.waits:
                eng.wait_ge(sem.h, val)
            ins = ev.fn(eng)
            for (sem, n) in ev.incs:
                ins.then_inc(sem.h, n)


def build_nc():
    nc = bass.Bass("TRN2", target_bir_lowering=False)

    xh = nc.dram_tensor("xh", [HT, D], F32, kind="ExternalInput").ap()
    Wq = nc.dram_tensor("Wq", [D, D], F32, kind="ExternalInput").ap()
    Wk = nc.dram_tensor("Wk", [D, D], F32, kind="ExternalInput").ap()
    Wv = nc.dram_tensor("Wv", [D, D], F32, kind="ExternalInput").ap()
    Wo = nc.dram_tensor("Wo", [D, D], F32, kind="ExternalInput").ap()
    bqt = nc.dram_tensor("bqt", [128, 8], F32, kind="ExternalInput").ap()
    bkt = nc.dram_tensor("bkt", [128, 8], F32, kind="ExternalInput").ap()
    bvt = nc.dram_tensor("bvt", [128, 8], F32, kind="ExternalInput").ap()
    bo_row = nc.dram_tensor("bo_row", [1, D], F32, kind="ExternalInput").ap()
    gW = nc.dram_tensor("gW", [D, NE], F32, kind="ExternalInput").ap()
    W1 = nc.dram_tensor("W1", [NE, D, HF], F32, kind="ExternalInput").ap()
    b1t = nc.dram_tensor("b1t", [128, NE * 4], F32, kind="ExternalInput").ap()
    W2 = nc.dram_tensor("W2", [NE, HF, D], F32, kind="ExternalInput").ap()
    B2 = nc.dram_tensor("B2", [NE, D], F32, kind="ExternalInput").ap()
    Ws1 = nc.dram_tensor("Ws1", [D, HF], F32, kind="ExternalInput").ap()
    bs1t = nc.dram_tensor("bs1t", [128, 4], F32, kind="ExternalInput").ap()
    Ws2 = nc.dram_tensor("Ws2", [HF, D], F32, kind="ExternalInput").ap()
    bs2t = nc.dram_tensor("bs2t", [128, 8], F32, kind="ExternalInput").ap()
    rs = nc.dram_tensor("rs", [128, 1], F32, kind="ExternalInput").ap()
    mfirst = nc.dram_tensor("mfirst", [128, WIN], F32, kind="ExternalInput").ap()
    mint = nc.dram_tensor("mint", [128, WIN], F32, kind="ExternalInput").ap()
    mlast = nc.dram_tensor("mlast", [128, WIN], F32, kind="ExternalInput").ap()

    out = nc.dram_tensor("out", [T, D], F32, kind="ExternalOutput").ap()

    wq_bf = nc.dram_tensor("wq_bf", [D, D], BF16, kind="Internal").ap()
    wk_bf = nc.dram_tensor("wk_bf", [D, D], BF16, kind="Internal").ap()
    wv_bf = nc.dram_tensor("wv_bf", [D, D], BF16, kind="Internal").ap()
    wo_bf = nc.dram_tensor("wo_bf", [D, D], BF16, kind="Internal").ap()
    _dbgkind = "Internal" if os.environ.get("KDBG", "") != "1" else "ExternalOutput"
    xn_dram = nc.dram_tensor("xn_dram", [HT, D], BF16, kind=_dbgkind).ap()
    v_dram = nc.dram_tensor("v_dram", [D, HT + 128], BF16, kind=_dbgkind).ap()
    y2_dram = nc.dram_tensor("y2_dram", [D, T], BF16, kind=_dbgkind).ap()

    identb_d = nc.inline_tensor(np.eye(128, dtype=ml_dtypes.bfloat16), "c_identb").ap()
    identf_d = nc.inline_tensor(np.eye(128, dtype=np.float32), "c_identf").ap()
    _r0 = np.zeros((128, 128), np.float32); _r0[0, :] = 1.0
    oner0b_d = nc.inline_tensor(_r0.astype(ml_dtypes.bfloat16), "c_oner0b").ap()
    oner0f_d = nc.inline_tensor(_r0, "c_oner0f").ap()
    E_np = np.zeros((128, 8, 16), np.float32)
    for c in range(8):
        E_np[0:64, c, 2 * c] = 1.0
        E_np[64:128, c, 2 * c + 1] = 1.0
    Esel_d = nc.inline_tensor(
        E_np.reshape(128, 128).astype(ml_dtypes.bfloat16), "c_Esel").ap()
    S_np = np.zeros((16, 8, 128), np.float32)
    for c in range(8):
        S_np[2 * c, c, 0:64] = 1.0
        S_np[2 * c + 1, c, 64:128] = 1.0
    Ssel_d = nc.inline_tensor(S_np.reshape(16, 8 * 128), "c_Ssel").ap()
    Se_np = np.zeros((8, 8, 128), np.float32)
    for e in range(NE):
        Se_np[e, e, :] = 1.0
    Sesel_d = nc.inline_tensor(Se_np.reshape(8, 8 * 128).astype(ml_dtypes.bfloat16), "c_Sesel").ap()

    P = Prog()
    ctx = ExitStack()
    en = ctx.enter_context
    sb = nc.sbuf_tensor
    psn = nc.psum_tensor

    identf = en(sb("identf", [128, 128], F32))
    zer_sb = en(sb("zer_sb", [128, 128], BF16))
    oner0b = en(sb("oner0b", [128, 128], BF16))
    oner0f = en(sb("oner0f", [128, 128], F32))
    consts = en(sb("consts", [128, 4], F32))
    rs_s = en(sb("rs_s", [128, 1], F32))
    rsb = en(sb("rsb", [128, 1], F32))
    xnT_s = en(sb("xnT_s", [128, 8 * HT], BF16))
    stat = en(sb("stat", [128, 20], F32))
    g_scr = en(sb("g_scr", [128, 32], F32))

    sDg = Sem(en(nc.semaphore("sDg")))
    sDs = Sem(en(nc.semaphore("sDs")))
    sT = Sem(en(nc.semaphore("sT")))
    sS = Sem(en(nc.semaphore("sS")))
    sV = Sem(en(nc.semaphore("sV")))
    sOut = Sem(en(nc.semaphore("sOut")))

    def bar():
        return [(sT, sT.v), (sS, sS.v), (sV, sV.v), (sDg, sDg.v), (sDs, sDs.v)]

    def dmag(dst, src):
        return lambda g: g.dma_start(dst, src)

    _UPTO = int(os.environ.get("KUPTO", "9"))
    _SKIP = set(x for x in os.environ.get("KSKIP", "").split(",") if x)
    _DBG = os.environ.get("KDBG", "") == "1"

    def finish():
        with nc.Block() as block:
            @block.gpsimd
            def _(g):
                P.emit("gpsimd", g)

            @block.sync
            def _(sy):
                P.emit("sync", sy)

            @block.tensor
            def _(te):
                P.emit("tensor", te)

            @block.scalar
            def _(s):
                P.emit("scalar", s)

            @block.vector
            def _(v):
                P.emit("vector", v)
        return nc

    def stub_out(waits):
        P.op("gpsimd", lambda g: g.dma_start(out[:, :], xh[HALO:HALO + T, :]),
             waits=waits, inc=sOut, inc_n=16)
        P.op("gpsimd", lambda g: g.engine_nop(), waits=[(sOut, 16)])

    sbA = ExitStack()
    enA = sbA.enter_context
    Sesel = enA(sb("Sesel", [8, 8 * 128], BF16))
    b1_s = enA(sb("b1_s", [128, NE * 4], F32))
    bs1_s = enA(sb("bs1_s", [128, 4], F32))
    bs2_s = enA(sb("bs2_s", [128, 8], F32))
    B2_s = enA(sb("B2_s", [8, D], BF16))
    gateW_s = enA(sb("gateW_s", [128, 8 * NE], BF16))
    gT_s = enA(sb("gT_s", [8, T], BF16))
    x_sb = [enA(sb(f"x_sb{i}", [128, D], F32)) for i in range(2)]
    xn_sb = [enA(sb(f"xn_sb{i}", [128, D], BF16)) for i in range(2)]
    Ws1_s = enA(sb("Ws1_s", [128, 8 * HF], BF16))
    Ws2_s = enA(sb("Ws2_s", [128, 4 * D], BF16))
    W1a = enA(sb("W1a", [128, NE * 4096], BF16))
    W2a = enA(sb("W2a", [128, NE * 4096], BF16))
    hg_sb = [enA(sb(f"hg_sb{i}", [128, FT], BF16)) for i in range(2)]
    hsT_sb = enA(sb("hsT_sb", [128, 4 * FT], BF16))
    hgg_sb2 = [hsT_sb, enA(sb("hgg_b", [128, 4 * FT], BF16))]
    y2sb = enA(sb("y2sb", [128, 4 * FT], BF16))
    zsc_sb = enA(sb("zsc_sb", [8, 128], F32))
    psA = ExitStack()
    enAp = psA.enter_context
    sm_ps = enAp(psn("sm_ps", [128, 512], F32))
    pA = [enAp(psn(f"pA{i}", [128, 512], F32)) for i in range(3)]
    y2p = enAp(psn("y2p", [128, 2048], F32))

    for dst, srcap in [
        (identf[:, :], identf_d),
        (oner0b[:, :], oner0b_d), (oner0f[:, :], oner0f_d),
        (Sesel[:, :], Sesel_d),
        (b1_s[:, :], b1t), (bs1_s[:, :], bs1t), (bs2_s[:, :], bs2t),
        (B2_s[:, :], B2), (rs_s[:, :], rs),
        (gateW_s[:, :].rearrange("p (kc j) -> p kc j", kc=8),
         gW.rearrange("(kc p) j -> p kc j", p=128)),
        (Ws1_s[:, :].rearrange("p (kc f) -> p kc f", kc=8),
         Ws1.rearrange("(kc p) f -> p kc f", p=128)),
        (Ws2_s[:, :].rearrange("p (kc f) -> p kc f", kc=4),
         Ws2.rearrange("(kc p) f -> p kc f", p=128)),
    ]:
        P.op("gpsimd", dmag(dst, srcap), inc=sDg, inc_n=16)
    c_small = (sDg, sDg.v)

    def f_consts(v):
        v.memset(consts[:, 0:1], LN_EPS)
        v.memset(consts[:, 1:2], ATT_EPS)
        v.memset(zer_sb[:, :], 0.0)
        return v.memset(consts[:, 2:3], 0.0)
    P.op("vector", f_consts, inc=sV)
    c_consts = (sV, sV.v)

    P.op("tensor", lambda t: t.matmul(sm_ps[:, 0:1], oner0f[:, :], rs_s[:, :],
                                      start=True, stop=True),
         waits=[c_consts, c_small], inc=sT)
    c_rsmm = (sT, sT.v)
    P.op("vector", lambda v: v.tensor_copy(rsb[:, :], sm_ps[:, 0:1]),
         waits=[c_rsmm], inc=sV)
    c_rsb = (sV, sV.v)

    slot_store = {0: None, 1: None}
    slot_free = {0: None, 1: None}
    prev_var = None
    c_xn_tile = {}
    for t in range(NHT):
        sl = t % 2
        P.op("sync", lambda sy, sl=sl, t=t: sy.dma_start(
            x_sb[sl][:, :], xh[t * 128:(t + 1) * 128, :]),
             waits=[slot_free[sl]], inc=sDs, inc_n=16)
        c_xload = (sDs, sDs.v)
        def f_bn(v, sl=sl):
            v.bn_stats(stat[:, 6:12], x_sb[sl][:, 0:512])
            return v.bn_stats(stat[:, 12:18], x_sb[sl][:, 512:1024])
        P.op("vector", f_bn, waits=[c_xload, prev_var], inc=sV)
        c_bn = (sV, sV.v)
        P.op("vector", lambda v: v.bn_aggr(stat[:, 2:4], stat[:, 6:18]),
             waits=[c_bn], inc=sV)
        c_var = (sV, sV.v)
        prev_var = c_var
        P.op("scalar", lambda s: s.activation(stat[:, 5:6], stat[:, 3:4], AF.Ln,
                                              bias=consts[:, 0:1]),
             waits=[c_var, c_consts], inc=sS)
        c_ln = (sS, sS.v)
        P.op("scalar", lambda s: s.activation(stat[:, 5:6], stat[:, 5:6], AF.Exp,
                                              scale=-0.5), waits=[c_ln], inc=sS)
        c_rstd = (sS, sS.v)
        P.op("vector", lambda v, sl=sl: v.tensor_scalar(
            xn_sb[sl][:, :], x_sb[sl][:, :], stat[:, 2:3], stat[:, 5:6],
            ALU.subtract, ALU.mult),
             waits=[c_rstd, slot_store[sl]], inc=sV)
        c_xn = (sV, sV.v)
        slot_free[sl] = c_xn
        P.op("gpsimd", dmag(xn_dram[t * 128:(t + 1) * 128, :], xn_sb[sl][:, :]),
             waits=[c_xn], inc=sDg, inc_n=16)
        slot_store[sl] = (sDg, sDg.v)
        c_xn_tile[t] = (sDg, sDg.v)
    c_xn_all = (sDg, sDg.v)

    W1r = W1.rearrange("e (kc p) f -> p e kc f", p=128)
    W2r = W2.rearrange("e (kc p) f -> p e kc f", p=128)
    for e in range(NE):
        P.op("gpsimd", dmag(
            W1a[:, e * 4096:(e + 1) * 4096].rearrange("p (kc f) -> p kc f", kc=8),
            W1r[:, e:e + 1, :, :].rearrange("p one kc f -> p (one kc) f")),
             inc=sDg, inc_n=16)
        P.op("gpsimd", dmag(
            W2a[:, e * 4096:(e + 1) * 4096].rearrange("p (kc f) -> p kc f", kc=4),
            W2r[:, e:e + 1, :, :].rearrange("p one kc f -> p (one kc) f")),
             inc=sDg, inc_n=16)
    c_expw = (sDg, sDg.v)

    for wbf, w in [(wq_bf, Wq), (wk_bf, Wk), (wv_bf, Wv), (wo_bf, Wo)]:
        P.op("gpsimd", dmag(wbf, w), inc=sDg, inc_n=16)
    c_wcast = (sDg, sDg.v)


    bands = [576, 1152, HT]
    c_band = []
    b_prev = 0
    for bk in bands:
        mark = c_xn_tile[(bk + 127) // 128 - 1]
        for c in range(8):
            P.op("sync", lambda sy, c=c, b0=b_prev, b1=bk: sy.dma_start(
                xnT_s[:, c * HT + b0:c * HT + b1],
                xn_dram[b0:b1, c * 128:(c + 1) * 128],
                transpose=True), waits=[mark], inc=sDs, inc_n=16)
        c_band.append((sDs, sDs.v))
        b_prev = bk
    c_xnT = c_band[-1]

    def xnT_ready(last_col):
        for bk, cb in zip(bands, c_band):
            if last_col <= bk:
                return cb
        return c_xnT

    if _UPTO <= 0:
        stub_out(bar())
        return finish()

    gst = {"c_gdone": None, "c_ez": None}

    def emit_gate(t):
        c_gdone = gst["c_gdone"]
        c_ez = gst["c_ez"]
        hb = HALO + t * 128
    
        def mm_gate(te, hb=hb):
            for kc in range(8):
                ins = te.matmul(sm_ps[0:8, 0:128],
                                gateW_s[:, kc * 8:(kc + 1) * 8],
                                xnT_s[:, kc * HT + hb:kc * HT + hb + 128],
                                start=(kc == 0), stop=(kc == 7))
            return ins
        P.op("tensor", mm_gate,
             waits=[xnT_ready(hb + 128), c_small, c_gdone], inc=sT)
        c_zmm = (sT, sT.v)
        P.op("vector", lambda v: v.tensor_copy(zsc_sb[:, :], sm_ps[0:8, 0:128]),
             waits=[c_zmm], inc=sV)
        c_zev = (sV, sV.v)
        P.op("tensor", lambda te: te.transpose(sm_ps[:, 128:136], zsc_sb[:, :],
                                               identf[0:8, 0:8]),
             waits=[c_zev, c_ez], inc=sT)
        c_z = (sT, sT.v)
        P.op("scalar", lambda s: s.activation(g_scr[:, 8:16], sm_ps[:, 128:136],
                                              AF.Exp, scale=-1.0),
             waits=[c_z, c_gdone], inc=sS)
        c_ez = (sS, sS.v)
        prev = c_ez
        for fn in [
            lambda v: v.tensor_scalar(g_scr[:, 8:16], g_scr[:, 8:16], 1.0, None,
                                      ALU.add),
            lambda v: v.reciprocal(g_scr[:, 0:8], g_scr[:, 8:16]),
            lambda v: v.tensor_scalar(g_scr[:, 0:8], g_scr[:, 0:8], rsb[:, 0:1],
                                      None, ALU.mult),
            lambda v: v.max(out=g_scr[:, 16:24], in_=g_scr[:, 0:8]),
            lambda v: v.tensor_add(g_scr[:, 24:25], g_scr[:, 16:17], g_scr[:, 17:18]),
            lambda v: v.reciprocal(g_scr[:, 25:26], g_scr[:, 24:25]),
            lambda v: v.tensor_scalar(g_scr[:, 8:16], g_scr[:, 0:8], g_scr[:, 17:18],
                                      None, ALU.is_ge),
            lambda v: v.scalar_tensor_tensor(g_scr[:, 0:8], g_scr[:, 0:8],
                                             g_scr[:, 25:26], g_scr[:, 8:16],
                                             ALU.mult, ALU.mult),
        ]:
            P.op("vector", fn, waits=[prev, c_rsb], inc=sV)
            prev = (sV, sV.v)
        c_g = prev
        P.op("tensor", lambda te: te.transpose(
            sm_ps[0:8, 0:128], g_scr[:, 0:8], identf[:, :]),
             waits=[c_g, c_rsb, c_gdone], inc=sT)
        c_gT = (sT, sT.v)
        P.op("vector", lambda v, t=t: v.tensor_copy(
            gT_s[:, t * 128:(t + 1) * 128], sm_ps[0:8, 0:128]),
             waits=[c_gT], inc=sV)
        c_gdone = (sV, sV.v)
        gst["c_gdone"] = c_gdone
        gst["c_ez"] = c_ez

    c_y2_store = None
    c_prev_evac = None
    c_pA_free = [None, None, None]
    c_hsT_used = None
    c_hg_used = [None, None]
    c_hgg_used = None
    c_hgg_used2 = [None, None]
    c_G_used = [None, None]
    emit_gate(0)
    emit_gate(1)
    g_ready = {0: gst["c_gdone"]}
    for ft in range(NFT):
        fb = HALO + ft * FT
        if ft + 1 < NFT:
            emit_gate(2 * (ft + 1))
            emit_gate(2 * (ft + 1) + 1)
            g_ready[ft + 1] = gst["c_gdone"]
        c_gdone = g_ready[ft]
        c_hs_evac = [None] * 4
        for mc in range(4):
            def mm_hs(te, mc=mc, fb=fb):
                for kc in range(8):
                    ins = te.matmul(pA[mc % 3][:, 0:FT],
                                    Ws1_s[:, kc * HF + mc * 128:kc * HF + (mc + 1) * 128],
                                    xnT_s[:, kc * HT + fb:kc * HT + fb + FT],
                                    start=(kc == 0), stop=(kc == 7))
                return ins
            P.op("tensor", mm_hs,
                 waits=[xnT_ready(fb + FT), c_small, c_pA_free[mc % 3]], inc=sT)
            c_mm = (sT, sT.v)
            P.op("scalar", lambda s, mc=mc: s.activation(
                hsT_sb[:, mc * FT:(mc + 1) * FT], pA[mc % 3][:, 0:FT], AF.Gelu,
                bias=bs1_s[:, mc:mc + 1]),
                 waits=[c_mm, c_hsT_used, c_hgg_used2[0]], inc=sS)
            c_hs_evac[mc] = (sS, sS.v)
            c_pA_free[mc % 3] = c_hs_evac[mc]

        def mm_y2z(te):
            for b in range(4):
                ins = te.matmul(y2p[:, b * 512:(b + 1) * 512], zer_sb[:, :],
                                Ws1_s[:, 0:512], start=True, stop=False,
                                skip_group_check=True)
            return ins
        P.op("tensor", mm_y2z, waits=[c_prev_evac, c_consts, c_small], inc=sT)
        c_y2z = (sT, sT.v)

        def mm_ws2(te):
            ins = None
            for c in range(8):
                for mc in range(4):
                    ins = te.matmul(y2p[:, c * FT:(c + 1) * FT],
                                    Ws2_s[:, mc * D + c * 128:mc * D + (c + 1) * 128],
                                    hsT_sb[:, mc * FT:(mc + 1) * FT],
                                    start=False, stop=False,
                                    skip_group_check=True)
            return ins
        P.op("tensor", mm_ws2, waits=[c_hs_evac[3], c_y2z], inc=sT)
        c_hsT_used = (sT, sT.v)

        def mm_b2(te, ft=ft):
            ins = None
            for c in range(8):
                ins = te.matmul(y2p[:, c * FT:(c + 1) * FT],
                                B2_s[:, c * 128:(c + 1) * 128],
                                gT_s[:, ft * FT:(ft + 1) * FT],
                                start=False, stop=False, skip_group_check=True)
            return ins
        P.op("tensor", mm_b2, waits=[c_gdone], inc=sT)

        for e in range(NE):
            P.op("tensor", lambda te, e=e, ft=ft: te.matmul(
                sm_ps[:, 256:256 + FT],
                Sesel[:, e * 128:(e + 1) * 128],
                gT_s[:, ft * FT:(ft + 1) * FT], start=True, stop=True),
                 waits=[c_gdone, c_G_used[0]], inc=sT)
            c_G = (sT, sT.v)
            for mc in range(4):
                psl = (4 * e + mc) % 3
                def mm_h(te, e=e, mc=mc, fb=fb, psl=psl):
                    for kc in range(8):
                        ins = te.matmul(
                            pA[psl][:, 0:FT],
                            W1a[:, e * 4096 + kc * HF + mc * 128:
                                 e * 4096 + kc * HF + (mc + 1) * 128],
                            xnT_s[:, kc * HT + fb:kc * HT + fb + FT],
                            start=(kc == 0), stop=(kc == 7))
                    return ins
                P.op("tensor", mm_h,
                     waits=[xnT_ready(fb + FT), c_expw, c_pA_free[psl]], inc=sT)
                c_hmm = (sT, sT.v)
                P.op("scalar", lambda s, e=e, mc=mc, psl=psl: s.activation(
                    hg_sb[mc % 2][:, :], pA[psl][:, 0:FT], AF.Gelu,
                    bias=b1_s[:, e * 4 + mc:e * 4 + mc + 1]),
                     waits=[c_hmm, c_hg_used[mc % 2]], inc=sS)
                c_hg = (sS, sS.v)
                c_pA_free[psl] = c_hg
                P.op("vector", lambda v, mc=mc, e=e: v.tensor_mul(
                    hgg_sb2[e % 2][:, mc * FT:(mc + 1) * FT], hg_sb[mc % 2][:, :],
                    sm_ps[:, 256:256 + FT]),
                     waits=[c_hg, c_G, c_hgg_used2[e % 2], c_hsT_used], inc=sV)
                c_hg_used[mc % 2] = (sV, sV.v)
            c_hgg = (sV, sV.v)
            c_G_used[0] = c_hgg

            def mm_w2(te, e=e):
                ins = None
                for c in range(8):
                    for mc in range(4):
                        ins = te.matmul(
                            y2p[:, c * FT:(c + 1) * FT],
                            W2a[:, e * 4096 + mc * D + c * 128:
                                 e * 4096 + mc * D + (c + 1) * 128],
                            hgg_sb2[e % 2][:, mc * FT:(mc + 1) * FT],
                            start=False, stop=(e == NE - 1 and c == 7 and mc == 3),
                            skip_group_check=True)
                return ins
            P.op("tensor", mm_w2, waits=[c_hgg], inc=sT)
            c_hgg_used2[e % 2] = (sT, sT.v)
            c_hgg_used = (sT, sT.v)
        c_y2mm = (sT, sT.v)

        for half in range(2):
            def ev_y2(s, half=half):
                for c in range(4 * half, 4 * half + 4):
                    ins = s.activation(y2sb[:, (c - 4 * half) * FT:(c - 4 * half + 1) * FT],
                                       y2p[:, c * FT:(c + 1) * FT], AF.Identity,
                                       bias=bs2_s[:, c:c + 1])
                return ins
            P.op("scalar", ev_y2, waits=[c_y2mm, c_y2_store], inc=sS)
            c_prev_evac = (sS, sS.v)

            def st_y2(g, ft=ft, half=half):
                dst = y2_dram.rearrange("(c p) t -> p c t", p=128)[
                    :, 4 * half:4 * half + 4, ft * FT:(ft + 1) * FT]
                srcap = y2sb[:, :].rearrange("p (c f) -> p c f", c=4)
                return g.dma_start(dst, srcap)
            P.op("gpsimd", st_y2, waits=[c_prev_evac], inc=sDg, inc_n=16)
            c_y2_store = (sDg, sDg.v)
    c_y2_all = c_y2_store
    if _DBG:
        d_gT = nc.dram_tensor("d_gT", [8, T], BF16, kind="ExternalOutput").ap()
        P.op("gpsimd", lambda g: g.dma_start(d_gT, gT_s[:, :]),
             waits=[gst["c_gdone"]], inc=sDg, inc_n=16)
    c_p2_bar = bar()
    if _UPTO <= 2:
        stub_out(c_p2_bar)
        return finish()

    sbA.close()
    psA.close()
    sbB = ExitStack()
    enB = sbB.enter_context
    WoT_s = enB(sb("WoT_s", [128, 8 * D], BF16))
    knT_s = enB(sb("knT_s", [128, 8 * HT], BF16))
    qT_s = enB(sb("qT_s", [128, 8 * T], BF16))
    rqT_s = enB(sb("rqT_s", [128, NTI * NH], F32))
    masks = enB(sb("masks", [128, 3 * WIN], BF16))
    identb = enB(sb("identb", [128, 128], BF16))
    bor = enB(sb("bor", [128, D], BF16))
    sbB1 = ExitStack()
    enB1 = sbB1.enter_context
    WqT_s = enB1(sb("WqT_s", [128, 8 * D], BF16))
    WkT_s = enB1(sb("WkT_s", [128, 8 * D], BF16))
    WvT_s = enB1(sb("WvT_s", [128, 8 * D], BF16))
    kraw_s = enB1(sb("kraw_s", [128, 8 * 512], BF16))
    sq_scr = [enB1(sb(f"sq_scr{i}", [128, D], BF16)) for i in range(2)]
    rq_f = enB1(sb("rq_f", [16, 512], F32))
    rk_f = enB1(sb("rk_f", [16, 512], F32))
    Ssel = enB1(sb("Ssel", [16, 8 * 128], F32))
    Esel = enB1(sb("Esel", [128, 128], BF16))
    bv_s = enB1(sb("bv_s", [128, 8], F32))
    bq_s = enB1(sb("bq_s", [128, 8], F32))
    bk_s = enB1(sb("bk_s", [128, 8], F32))
    P.op("vector", lambda v: v.memset(bor[:, :], 0.0), waits=c_p2_bar, inc=sV)
    c_bz = (sV, sV.v)
    for dst, srcap in [
        (masks[:, 0:WIN], mfirst), (masks[:, WIN:2 * WIN], mint),
        (masks[:, 2 * WIN:3 * WIN], mlast),
        (identb[:, :], identb_d), (bor[0:1, :], bo_row),
        (Ssel[:, :], Ssel_d), (Esel[:, :], Esel_d), (bv_s[:, :], bvt),
        (bq_s[:, :], bqt), (bk_s[:, :], bkt),
    ]:
        P.op("gpsimd", dmag(dst, srcap), waits=c_p2_bar + [c_bz], inc=sDg, inc_n=16)
    c_bsmall = (sDg, sDg.v)
    psB = ExitStack()
    enBp = psB.enter_context
    sm2_ps = enBp(psn("sm2_ps", [128, 128], F32))
    pB2 = [enBp(psn(f"pB2{i}", [128, 512], F32)) for i in range(4)]
    ss_ps = enBp(psn("ss_ps", [16, 512], F32))
    rr2_ps = [enBp(psn(f"rr2_ps{i}", [128, 512], F32)) for i in range(2)]

    wt_marks = {}
    for wbf, wt, nm in [(wk_bf, WkT_s, "k"), (wv_bf, WvT_s, "v"),
                        (wq_bf, WqT_s, "q"), (wo_bf, WoT_s, "o")]:
        for c in range(8):
            P.op("sync", lambda sy, wbf=wbf, wt=wt, c=c: sy.dma_start(
                wt[:, c * D:(c + 1) * D], wbf[:, c * 128:(c + 1) * 128],
                transpose=True), waits=[c_wcast] + c_p2_bar, inc=sDs, inc_n=16)
        wt_marks[nm] = (sDs, sDs.v)
    c_WkT, c_WvT, c_WqT, c_WoT = (wt_marks["k"], wt_marks["v"],
                                  wt_marks["q"], wt_marks["o"])

    c_pB2_free = [None, None, None, None]
    c_kraw_used = None
    c_rr2_used = [None, None]
    c_sq_used = [None, None]
    c_ss_free = None

    k_macs = [(0, 512), (512, 512), (1024, 512), (1536, 512), (2048, 128)]
    st1 = {"ss_free": None, "kraw_used": None, "smT_used": None}

    def emit_proj_k(i):
        off, w = k_macs[i]
        prev_tt = st1.get(("tt", i - 1))
        for cch in range(8):
            sl = cch % 4

            def mm_k(te, cch=cch, off=off, w=w, sl=sl):
                for kc in range(8):
                    ins = te.matmul(pB2[sl][:, 0:w],
                                    WkT_s[:, kc * D + cch * 128:kc * D + (cch + 1) * 128],
                                    xnT_s[:, kc * HT + off:kc * HT + off + w],
                                    start=(kc == 0), stop=(kc == 7))
                return ins
            P.op("tensor", mm_k, waits=[c_WkT, c_xnT, c_pB2_free[sl]], inc=sT)
            c_kmm = (sT, sT.v)
            P.op("scalar", lambda s, cch=cch, w=w, sl=sl: s.activation(
                kraw_s[:, cch * 512:cch * 512 + w], pB2[sl][:, 0:w], AF.Identity,
                bias=bk_s[:, cch:cch + 1]),
                 waits=[c_kmm, c_bsmall, prev_tt[cch] if prev_tt else None], inc=sS)
            c_kev = (sS, sS.v)
            c_pB2_free[sl] = c_kev
            P.op("vector", lambda v, cch=cch, w=w, sq=cch % 2: v.tensor_mul(
                sq_scr[sq][:, 0:w], kraw_s[:, cch * 512:cch * 512 + w],
                kraw_s[:, cch * 512:cch * 512 + w]),
                 waits=[c_kev, c_sq_used[cch % 2]], inc=sV)
            c_sq = (sV, sV.v)
            P.op("tensor", lambda te, cch=cch, w=w, sq=cch % 2: te.matmul(
                ss_ps[:, 0:w], Esel[:, cch * 16:(cch + 1) * 16], sq_scr[sq][:, 0:w],
                start=(cch == 0), stop=(cch == 7)),
                 waits=[c_sq, c_bsmall, st1["ss_free"] if cch == 0 else None], inc=sT)
            c_sq_used[cch % 2] = (sT, sT.v)
        st1[("ssmm", "k", i)] = (sT, sT.v)

    def emit_chain_k(i):
        off, w = k_macs[i]
        prevk = st1[("ssmm", "k", i)]
        for j, fn in enumerate([
            lambda s, w=w: s.activation(rk_f[0:16, 0:w], ss_ps[0:16, 0:w], AF.Ln),
            lambda s, w=w: s.activation(rk_f[0:16, 0:w], rk_f[0:16, 0:w], AF.Exp,
                                        scale=0.5),
            lambda s, w=w: s.activation(rk_f[0:16, 0:w], rk_f[0:16, 0:w], AF.Ln,
                                        bias=consts[0:16, 1:2]),
            lambda s, w=w: s.activation(rk_f[0:16, 0:w], rk_f[0:16, 0:w], AF.Exp,
                                        scale=-1.0),
        ]):
            P.op("scalar", fn, waits=[prevk], inc=sS)
            prevk = (sS, sS.v)
            if j == 0:
                st1["ss_free"] = prevk
        st1[("rk", i)] = prevk

    def emit_Rtt_k(i):
        off, w = k_macs[i]
        c_rk = st1[("rk", i)]
        tts = [None] * 8
        for cch in range(8):
            sl = cch % 2
            P.op("tensor", lambda te, cch=cch, w=w, sl=sl: te.matmul(
                rr2_ps[sl][:, 0:w], Ssel[:, cch * 128:(cch + 1) * 128],
                rk_f[0:16, 0:w], start=True, stop=True),
                 waits=[c_rk, c_bsmall, c_rr2_used[sl]], inc=sT)
            c_R = (sT, sT.v)
            P.op("vector", lambda v, cch=cch, off=off, w=w, sl=sl: v.tensor_mul(
                knT_s[:, cch * HT + off:cch * HT + off + w],
                kraw_s[:, cch * 512:cch * 512 + w], rr2_ps[sl][:, 0:w]),
                 waits=[c_R], inc=sV)
            c_rr2_used[sl] = (sV, sV.v)
            tts[cch] = (sV, sV.v)
        st1[("tt", i)] = tts
        st1["kraw_used"] = (sV, sV.v)

    def emit_proj_q(m):
        off = HALO + m * 512
        for cch in range(8):
            sl = cch % 4

            def mm_q(te, cch=cch, off=off, sl=sl):
                for kc in range(8):
                    ins = te.matmul(pB2[sl][:, 0:512],
                                    WqT_s[:, kc * D + cch * 128:kc * D + (cch + 1) * 128],
                                    xnT_s[:, kc * HT + off:kc * HT + off + 512],
                                    start=(kc == 0), stop=(kc == 7))
                return ins
            P.op("tensor", mm_q, waits=[c_WqT, c_pB2_free[sl]], inc=sT)
            c_qmm = (sT, sT.v)
            P.op("scalar", lambda s, cch=cch, m=m, sl=sl: s.activation(
                qT_s[:, cch * T + m * 512:cch * T + (m + 1) * 512], pB2[sl][:, 0:512],
                AF.Identity, bias=bq_s[:, cch:cch + 1]), waits=[c_qmm, c_bsmall], inc=sS)
            c_qev = (sS, sS.v)
            c_pB2_free[sl] = c_qev
            P.op("vector", lambda v, cch=cch, m=m, sq=cch % 2: v.tensor_mul(
                sq_scr[sq][:, 0:512],
                qT_s[:, cch * T + m * 512:cch * T + (m + 1) * 512],
                qT_s[:, cch * T + m * 512:cch * T + (m + 1) * 512]),
                 waits=[c_qev, c_sq_used[cch % 2]], inc=sV)
            c_sq = (sV, sV.v)
            P.op("tensor", lambda te, cch=cch, sq=cch % 2: te.matmul(
                ss_ps[:, 0:512], Esel[:, cch * 16:(cch + 1) * 16],
                sq_scr[sq][:, 0:512], start=(cch == 0), stop=(cch == 7)),
                 waits=[c_sq, c_bsmall, st1["ss_free"] if cch == 0 else None], inc=sT)
            c_sq_used[cch % 2] = (sT, sT.v)
        st1[("ssmm", "q", m)] = (sT, sT.v)

    def emit_chain_q(m):
        prevq = st1[("ssmm", "q", m)]
        for j, fn in enumerate([
            lambda s: s.activation(rq_f[0:16, 0:512], ss_ps[0:16, 0:512], AF.Ln),
            lambda s: s.activation(rq_f[0:16, 0:512], rq_f[0:16, 0:512], AF.Exp,
                                   scale=0.5),
            lambda s: s.activation(rq_f[0:16, 0:512], rq_f[0:16, 0:512], AF.Ln,
                                   bias=consts[0:16, 1:2]),
            lambda s: s.activation(rq_f[0:16, 0:512], rq_f[0:16, 0:512], AF.Exp,
                                   scale=-1.0),
        ]):
            P.op("scalar", fn, waits=[prevq], inc=sS)
            prevq = (sS, sS.v)
            if j == 0:
                st1["ss_free"] = prevq
        st1[("rq", m)] = prevq

    def emit_tr_q(m):
        c_rq = st1[("rq", m)]
        for tt in range(4):
            P.op("tensor", lambda te, tt=tt: te.transpose(
                sm2_ps[0:128, 0:16], rq_f[0:16, tt * 128:(tt + 1) * 128],
                identf[0:16, 0:16]), waits=[c_rq, st1["smT_used"]], inc=sT)
            c_trq = (sT, sT.v)
            P.op("vector", lambda v, m=m, tt=tt: v.tensor_copy(
                rqT_s[:, (m * 4 + tt) * 16:(m * 4 + tt) * 16 + 16],
                sm2_ps[0:128, 0:16]), waits=[c_trq], inc=sV)
            st1["smT_used"] = (sV, sV.v)

    pend_tr = None
    for i in range(5):
        emit_proj_k(i)
        if pend_tr is not None:
            emit_tr_q(pend_tr)
        emit_chain_k(i)
        if i < 4:
            emit_proj_q(i)
        emit_Rtt_k(i)
        if i < 4:
            emit_chain_q(i)
            pend_tr = i
    emit_tr_q(pend_tr)
    c_kraw_used = st1["kraw_used"]

    c_pB2v_free = list(c_pB2_free)
    prev_st = list(st1[("tt", 4)])
    for (off, w) in k_macs:
        cur_st = [None] * 8
        for mc in range(8):
            sl = mc % 4

            def mm_v(te, mc=mc, off=off, w=w, sl=sl):
                for kc in range(8):
                    ins = te.matmul(
                        pB2[sl][:, 0:w],
                        WvT_s[:, kc * D + mc * 128:kc * D + (mc + 1) * 128],
                        xnT_s[:, kc * HT + off:kc * HT + off + w],
                        start=(kc == 0), stop=(kc == 7))
                return ins
            P.op("tensor", mm_v, waits=[c_WvT, c_pB2v_free[sl]], inc=sT)
            c_vmm = (sT, sT.v)
            P.op("scalar", lambda s, mc=mc, w=w, sl=sl: s.activation(
                kraw_s[:, mc * 512:mc * 512 + w], pB2[sl][:, 0:w], AF.Identity,
                bias=bv_s[:, mc:mc + 1]),
                 waits=[c_vmm, c_bsmall, prev_st[mc]], inc=sS)
            c_vev = (sS, sS.v)
            c_pB2v_free[sl] = c_vev
            P.op("gpsimd", lambda g, mc=mc, off=off, w=w: g.dma_start(
                v_dram[mc * 128:(mc + 1) * 128, off:off + w],
                kraw_s[:, mc * 512:mc * 512 + w]),
                 waits=[c_vev], inc=sDg, inc_n=16)
            cur_st[mc] = (sDg, sDg.v)
        prev_st = cur_st
    c_p1a_bar = bar()

    if _DBG:
        d_knT = nc.dram_tensor("d_knT", [128, 8 * HT], BF16, kind="ExternalOutput").ap()
        d_qT = nc.dram_tensor("d_qT", [128, 8 * T], BF16, kind="ExternalOutput").ap()
        d_rqT = nc.dram_tensor("d_rqT", [128, NTI * NH], F32, kind="ExternalOutput").ap()
        for dst, srct in [(d_knT, knT_s), (d_qT, qT_s), (d_rqT, rqT_s)]:
            P.op("gpsimd", lambda g, dst=dst, srct=srct: g.dma_start(dst, srct[:, :]),
                 waits=c_p1a_bar, inc=sDg, inc_n=16)

    if _UPTO <= 3:
        stub_out(c_p1a_bar)
        return finish()

    psB.close()
    sbB1.close()
    xL_sb = [enB(sb(f"xL_sb{i}", [128, D], F32)) for i in range(2)]
    vw_a = [enB(sb(f"vw_a{i}", [128, D], BF16)) for i in range(2)]
    vw_b = [enB(sb(f"vw_b{i}", [128, D], BF16)) for i in range(2)]
    vw_c = [enB(sb(f"vw_c{i}", [128, D], BF16)) for i in range(2)]
    e_sb = [enB(sb(f"e_sb{i}", [128, 4 * WIN], BF16)) for i in range(2)]
    pn_sb = [enB(sb(f"pn_sb{i}", [128, 4 * WIN], BF16)) for i in range(4)]
    sums = enB(sb("sums", [128, NH], F32))
    rsum = enB(sb("rsum", [128, NH], F32))
    pT_sb = [enB(sb(f"pT_sb{i}", [128, 1024], BF16)) for i in range(2)]
    outT_sb = [enB(sb(f"outT_sb{i}", [128, 128], BF16)) for i in range(2)]
    out2_sb = [enB(sb(f"out2_sb{i}", [128, 128], BF16)) for i in range(2)]
    out_sb = [enB(sb(f"out_sb{i}", [128, D], F32)) for i in range(2)]
    y1_sb = [enB(sb(f"y1_sb{i}", [128, D], F32)) for i in range(2)]
    yb_sb = [enB(sb(f"yb_sb{i}", [128, D], BF16)) for i in range(2)]
    c_zer = c_consts
    psC = ExitStack()
    enCp = psC.enter_context
    y1_ps = enCp(psn("y1_ps", [128, 1024], F32))
    scg_ps = enCp(psn("scg_ps", [128, 4 * WIN], F32))
    pT_ps = [enCp(psn(f"pT_ps{i}", [128, 1024], BF16)) for i in range(2)]
    av_ps = [enCp(psn(f"av_ps{i}", [128, 512], F32)) for i in range(2)]

    c_vw_used = [None, None]
    c_x_free = [None, None]
    c_y1sb_free = [None, None]
    c_yb_free = [None, None]
    c_out_store = [None, None]
    c_y1_free = None
    c_scg_free = None
    c_exp_by = [None, None]
    c_stt_prev = None
    c_stt_by = [None, None]
    c_pn_by = [None, None]
    c_tr_by = [None, None]
    c_pTev_by = [None, None]
    c_av_by = [None, None]
    c_avev_by = [None, None]
    c_wo_by = [None, None]
    c_wo_last = None
    c_pn_key = {}

    def emit_A_hg(t, hg, first_bar):
        wb = t * 128 + 32
        moff = 0 if t == 0 else (2 * WIN if t == NTI - 1 else WIN)
        if True:
            par = hg % 2

            def mm_scg(te, t=t, hg=hg, wb=wb):
                for i in range(2):
                    c = 2 * hg + i
                    te.matmul(scg_ps[:, 2 * i * WIN:(2 * i + 1) * WIN],
                              qT_s[0:64, c * T + t * 128:c * T + (t + 1) * 128],
                              knT_s[0:64, c * HT + wb:c * HT + wb + WIN],
                              start=True, stop=True)
                te.matmul(y1_ps[:, 0:128], zer_sb[:, :], zer_sb[:, :],
                          start=False, stop=False, skip_group_check=True)
                for i in range(2):
                    c = 2 * hg + i
                    te.matmul(scg_ps[:, (2 * i + 1) * WIN:(2 * i + 2) * WIN],
                              qT_s[64:128, c * T + t * 128:c * T + (t + 1) * 128],
                              knT_s[64:128, c * HT + wb:c * HT + wb + WIN],
                              start=True, stop=True)
                return te.matmul(y1_ps[:, 0:128], zer_sb[:, :], zer_sb[:, :],
                                 start=False, stop=False, skip_group_check=True)
            P.op("tensor", mm_scg, waits=[st["scg_free"], c_zer] + first_bar,
                 inc=sT)
            c_sc = (sT, sT.v)

            def f_exp(s, t=t, hg=hg, par=par):
                for i in range(4):
                    h = 4 * hg + i
                    ins = s.activation(e_sb[par][:, i * WIN:(i + 1) * WIN],
                                       scg_ps[:, i * WIN:(i + 1) * WIN], AF.Exp,
                                       scale=rqT_s[:, t * NH + h:t * NH + h + 1])
                return ins
            P.op("scalar", f_exp,
                 waits=[c_sc, st["stt_by"][par]] + first_bar, inc=sS)
            c_exp = (sS, sS.v)
            st["scg_free"] = c_exp
            st["exp_by"][par] = c_exp

            def f_stt(v, t=t, hg=hg, par=par, moff=moff):
                for i in range(4):
                    h = 4 * hg + i
                    ins = v.scalar_tensor_tensor(
                        pn_sb[hg][:, i * WIN:(i + 1) * WIN],
                        e_sb[par][:, i * WIN:(i + 1) * WIN], 1.0,
                        masks[:, moff:moff + WIN], ALU.mult, ALU.mult,
                        accum_out=sums[:, 4 * hg + i:4 * hg + i + 1])
                return ins
            P.op("vector", f_stt,
                 waits=[c_exp, c_bsmall, st["tr_by"][hg]] + first_bar, inc=sV)
            st["stt_by"][par] = (sV, sV.v)
            P.op("vector", lambda v, hg=hg: v.reciprocal(
                rsum[:, 4 * hg:4 * hg + 4], sums[:, 4 * hg:4 * hg + 4]),
                 waits=[st["stt_by"][par]], inc=sV)
            c_rs2 = (sV, sV.v)

            def f_pn(v, hg=hg):
                for i in range(4):
                    h = 4 * hg + i
                    ins = v.tensor_scalar(pn_sb[hg][:, i * WIN:(i + 1) * WIN],
                                          pn_sb[hg][:, i * WIN:(i + 1) * WIN],
                                          rsum[:, h:h + 1], None, ALU.mult)
                return ins
            P.op("vector", f_pn, waits=[c_rs2], inc=sV)
            c_pn_key[(t, hg)] = (sV, sV.v)

    ld = {}

    def emit_loads(t):
        sl2 = t % 2
        hb = HALO + t * 128
        P.op("sync", lambda sy, t=t, sl2=sl2: sy.dma_start(
            vw_a[sl2][:, :], v_dram[:, t * 128 + 32:t * 128 + 160], transpose=True),
             waits=[c_vw_used[sl2], c_p1a_bar[3]], inc=sDs, inc_n=16)
        P.op("sync", lambda sy, t=t, sl2=sl2: sy.dma_start(
            vw_b[sl2][:, :], v_dram[:, t * 128 + 160:t * 128 + 288], transpose=True),
             inc=sDs, inc_n=16)
        P.op("sync", lambda sy, t=t, sl2=sl2: sy.dma_start(
            vw_c[sl2][:, :], v_dram[:, t * 128 + 96:t * 128 + 224], transpose=True),
             inc=sDs, inc_n=16)
        c_vw = (sDs, sDs.v)
        P.op("gpsimd", lambda g, hb=hb, sl2=sl2: g.dma_start(
            xL_sb[sl2][:, :], xh[hb:hb + 128, :]),
             waits=[c_x_free[sl2]], inc=sDg, inc_n=16)
        c_xl = (sDg, sDg.v)
        P.op("sync", lambda sy, t=t, sl2=sl2: sy.dma_start(
            yb_sb[sl2][:, :], y2_dram[:, t * 128:(t + 1) * 128], transpose=True),
             waits=[c_y2_all, c_yb_free[sl2]], inc=sDs, inc_n=16)
        c_yb = (sDs, sDs.v)
        ld[t] = (c_vw, c_xl, c_yb)

    def emit_B_pre(t, first_bar):
        sl2 = t % 2
        c_vw, c_xl, c_yb = ld[t]

        def mm_bo(te):
            te.matmul(y1_ps[:, 0:512], oner0b[:, :], bor[:, 0:512],
                      start=True, stop=False, skip_group_check=True)
            return te.matmul(y1_ps[:, 512:1024], oner0b[:, :], bor[:, 512:1024],
                             start=True, stop=False, skip_group_check=True)
        P.op("tensor", mm_bo, waits=[c_bsmall, st["y1_free"]] + first_bar, inc=sT)
        return c_vw, c_xl, c_yb

    def emit_B_sb(t, sb2, c_vw, first_bar):
        sl2 = t % 2
        if True:
            pairs = (2 * sb2, 2 * sb2 + 1)
            par = sb2 % 2
            slb = sb2 % 2

            def f_tr(te, t=t, sb2=sb2, slb=slb):
                for j in range(2):
                    i0 = 2 * j * WIN
                    o0 = 512 * j
                    te.transpose(pT_ps[slb][:, o0:o0 + 128],
                                 pn_sb[sb2][:, i0:i0 + 128], identb[:, :])
                    te.transpose(pT_ps[slb][0:64, o0 + 128:o0 + 256],
                                 pn_sb[sb2][:, i0 + 128:i0 + WIN], identb[:, :])
                    te.transpose(pT_ps[slb][0:64, o0 + 256:o0 + 384],
                                 pn_sb[sb2][:, i0 + WIN:i0 + WIN + 64], identb[:, :])
                    ins = te.transpose(pT_ps[slb][:, o0 + 384:o0 + 512],
                                       pn_sb[sb2][:, i0 + WIN + 64:i0 + 2 * WIN],
                                       identb[:, :])
                return ins
            P.op("tensor", f_tr,
                 waits=[c_pn_key[(t, sb2)], st["pTev_by"][slb]] + first_bar, inc=sT)
            st["tr_by"][sb2] = (sT, sT.v)
            c_tr = st["tr_by"][sb2]

            if sb2 % 2 == 0:
                P.op("scalar", lambda s, slb=slb: s.copy(pT_sb[slb][:, :],
                                                         pT_ps[slb][:, :]),
                     waits=[c_tr, st["av_by"][slb]], inc=sS)
                st["pTev_by"][slb] = (sS, sS.v)
            else:
                P.op("vector", lambda v, slb=slb: v.tensor_copy(pT_sb[slb][:, :],
                                                                pT_ps[slb][:, :]),
                     waits=[c_tr, st["av_by"][slb]], inc=sV)
                st["pTev_by"][slb] = (sV, sV.v)

            def f_av(te, pairs=pairs, sl2=sl2, slb=slb):
                for j, p in enumerate(pairs):
                    A, Bh = 2 * p, 2 * p + 1
                    co = j * 256
                    po = j * 512
                    te.matmul(av_ps[slb][0:64, co:co + 128],
                              vw_a[sl2][:, A * 64:(A + 1) * 64],
                              pT_sb[slb][:, po:po + 128], start=True, stop=False,
                              skip_group_check=True)
                    te.matmul(av_ps[slb][0:64, co:co + 128],
                              vw_b[sl2][0:64, A * 64:(A + 1) * 64],
                              pT_sb[slb][0:64, po + 128:po + 256], start=False,
                              stop=True, skip_group_check=True)
                    te.matmul(av_ps[slb][0:64, co + 128:co + 256],
                              vw_a[sl2][0:64, Bh * 64:(Bh + 1) * 64],
                              pT_sb[slb][0:64, po + 256:po + 384], start=True,
                              stop=False, skip_group_check=True)
                    ins = te.matmul(av_ps[slb][0:64, co + 128:co + 256],
                                    vw_c[sl2][:, Bh * 64:(Bh + 1) * 64],
                                    pT_sb[slb][:, po + 384:po + 512], start=False,
                                    stop=True, skip_group_check=True)
                return ins
            P.op("tensor", f_av,
                 waits=[st["pTev_by"][slb], c_vw, st["avev_by"][slb]] + first_bar,
                 inc=sT)
            st["av_by"][slb] = (sT, sT.v)

            if sb2 % 2 == 0:
                def f_ev(v, slb=slb):
                    v.tensor_copy(outT_sb[slb][0:64, :], av_ps[slb][0:64, 0:128])
                    v.tensor_copy(outT_sb[slb][64:128, :], av_ps[slb][0:64, 128:256])
                    v.tensor_copy(out2_sb[slb][0:64, :], av_ps[slb][0:64, 256:384])
                    return v.tensor_copy(out2_sb[slb][64:128, :],
                                         av_ps[slb][0:64, 384:512])
                P.op("vector", f_ev, waits=[st["av_by"][slb], st["wo_by"][slb]],
                     inc=sV)
                st["avev_by"][slb] = (sV, sV.v)
            else:
                def f_ev(s, slb=slb):
                    s.copy(outT_sb[slb][0:64, :], av_ps[slb][0:64, 0:128])
                    s.copy(outT_sb[slb][64:128, :], av_ps[slb][0:64, 128:256])
                    s.copy(out2_sb[slb][0:64, :], av_ps[slb][0:64, 256:384])
                    return s.copy(out2_sb[slb][64:128, :], av_ps[slb][0:64, 384:512])
                P.op("scalar", f_ev, waits=[st["av_by"][slb], st["wo_by"][slb]],
                     inc=sS)
                st["avev_by"][slb] = (sS, sS.v)

            def f_wo(te, pairs=pairs, slb=slb):
                for j, p in enumerate(pairs):
                    src_t = outT_sb[slb] if j == 0 else out2_sb[slb]
                    te.matmul(y1_ps[:, 0:512], src_t[:, :],
                              WoT_s[:, p * D:p * D + 512], start=False, stop=False,
                              skip_group_check=True)
                    ins = te.matmul(y1_ps[:, 512:1024], src_t[:, :],
                                    WoT_s[:, p * D + 512:(p + 1) * D], start=False,
                                    stop=(p == 7), skip_group_check=True)
                return ins
            P.op("tensor", f_wo, waits=[st["avev_by"][slb], c_WoT], inc=sT)
            st["wo_by"][slb] = (sT, sT.v)

    def emit_B_post(t, c_xl, c_yb):
        sl2 = t % 2
        c_wo_last = st["wo_by"][1]

        P.op("scalar", lambda s, sl2=sl2: s.copy(y1_sb[sl2][:, :], y1_ps[:, :]),
             waits=[c_wo_last, c_y1sb_free[sl2]], inc=sS)
        st["y1_free"] = (sS, sS.v)
        P.op("vector", lambda v, sl2=sl2: v.scalar_tensor_tensor(
            out_sb[sl2][:, :], y1_sb[sl2][:, :], 1.0, xL_sb[sl2][:, :],
            ALU.mult, ALU.add),
             waits=[st["y1_free"], c_xl, c_out_store[sl2]], inc=sV)
        P.op("vector", lambda v, sl2=sl2: v.tensor_add(
            out_sb[sl2][:, :], out_sb[sl2][:, :], yb_sb[sl2][:, :]),
             waits=[c_yb, (sV, sV.v)], inc=sV)
        c_fin = (sV, sV.v)
        c_y1sb_free[sl2] = c_fin
        c_yb_free[sl2] = c_fin
        c_x_free[sl2] = c_fin
        c_vw_used[sl2] = st["av_by"][1]
        P.op("gpsimd", lambda g, t=t, sl2=sl2: g.dma_start(
            out[t * 128:(t + 1) * 128, :], out_sb[sl2][:, :]),
             waits=[c_fin], inc=sOut, inc_n=16)
        c_out_store[sl2] = (sOut, sOut.v)

    st = {"scg_free": None, "exp_by": [None, None], "stt_by": [None, None],
          "tr_by": [None, None, None, None], "pTev_by": [None, None],
          "av_by": [None, None], "avev_by": [None, None], "wo_by": [None, None],
          "y1_free": None}

    for hg in range(4):
        emit_A_hg(0, hg, c_p1a_bar)
    for t in range(NTI):
        fb2 = c_p1a_bar if t == 0 else []
        emit_loads(t)
        c_vw, c_xl, c_yb = emit_B_pre(t, fb2)
        for i in range(4):
            emit_B_sb(t, i, c_vw, fb2)
            if t + 1 < NTI:
                emit_A_hg(t + 1, i, fb2)
        emit_B_post(t, c_xl, c_yb)

    P.op("gpsimd", lambda g: g.engine_nop(), waits=[(sOut, 16 * NTI)])

    return finish()


def _masks_for(q0):
    def m(t):
        qi = np.arange(128)[:, None]
        kk = np.arange(WIN)[None, :]
        qg = q0 + t * 128 + qi
        kg = q0 + t * 128 - 32 + kk
        allowed = ((np.abs(kg - qg) <= 32) & ((kg - qg) % 4 == 0)
                   & (kg >= 0) & (kg < 8192))
        return np.ascontiguousarray(allowed.astype(np.float32))
    return m(0), m(7), m(NTI - 1)


_NC_CACHE = None


def kernel(**inputs):
    global _NC_CACHE
    f = lambda k: np.ascontiguousarray(np.asarray(inputs[k], dtype=np.float32))
    x = f("L0_attn_x")
    Wq, bq = f("L0_attn_Wq"), f("L0_attn_bq")
    Wk, bk = f("L0_attn_Wk"), f("L0_attn_bk")
    Wv, bv = f("L0_attn_Wv"), f("L0_attn_bv")
    Wo, bo = f("L0_attn_Wo"), f("L0_attn_bo")
    gW = f("L0_ffn_gate_W")
    W1, b1 = f("L0_ffn_W1"), f("L0_ffn_b1")
    W2, b2 = f("L0_ffn_W2"), f("L0_ffn_b2")
    Ws1, bs1 = f("L0_ffn_Ws1"), f("L0_ffn_bs1")
    Ws2, bs2 = f("L0_ffn_Ws2"), f("L0_ffn_bs2")
    rsv = f("L0_ffn_route_scale")

    shared = {
        "Wq": Wq, "Wk": Wk, "Wv": Wv, "Wo": Wo,
        "bqt": np.ascontiguousarray(bq.reshape(8, 128).T),
        "bkt": np.ascontiguousarray(bk.reshape(8, 128).T),
        "bvt": np.ascontiguousarray(bv.reshape(8, 128).T),
        "bo_row": bo.reshape(1, D),
        "gW": gW, "W1": W1, "W2": W2,
        "b1t": np.ascontiguousarray(b1.reshape(8, 4, 128).transpose(2, 0, 1)
                                    .reshape(128, 32)),
        "B2": b2,
        "Ws1": Ws1, "Ws2": Ws2,
        "bs1t": np.ascontiguousarray(bs1.reshape(4, 128).T),
        "bs2t": np.ascontiguousarray(bs2.reshape(8, 128).T),
        "rs": np.concatenate([rsv.reshape(1, 1),
                              np.zeros((127, 1), np.float32)], axis=0),
    }

    in_maps = []
    for core in range(8):
        b, qq = core // 4, core % 4
        q0 = qq * T
        xh = np.zeros((HT, D), np.float32)
        lo = max(0, q0 - HALO)
        hi = min(8192, q0 + T + HALO)
        xh[lo - (q0 - HALO):hi - (q0 - HALO)] = x[b, lo:hi]
        mf, mi, ml = _masks_for(q0)
        m = dict(shared)
        m.update({"xh": xh, "mfirst": mf, "mint": mi, "mlast": ml})
        in_maps.append(m)

    if _NC_CACHE is None:
        _NC_CACHE = build_nc()
    nc = _NC_CACHE
    res = run_bass_kernel_spmd(nc, in_maps, core_ids=list(range(8)))

    outp = np.empty((2, 8192, D), np.float32)
    for core in range(8):
        b, qq = core // 4, core % 4
        outp[b, qq * T:(qq + 1) * T] = np.asarray(res.results[core]["out"],
                                                  dtype=np.float32)
    return outp


if __name__ == "__main__":
    import sys
    sys.path.insert(0, "/root/problem")
    from reference import setup_inputs, reference
    ins = setup_inputs(0)
    ref = np.asarray(reference(**ins))
    got = kernel(**{k: np.asarray(v) for k, v in ins.items()})
    err = np.mean((got - ref) ** 2) / np.mean(ref ** 2)
    print("residual variance ratio:", err)


# version 24
# speedup vs baseline: 1.0405x; 1.0055x over previous
import os
import numpy as np
import ml_dtypes
from contextlib import ExitStack

import concourse.bass as bass
import concourse.mybir as mybir
from concourse.bass_utils import run_bass_kernel_spmd

F32 = mybir.dt.float32
BF16 = mybir.dt.bfloat16
AF = mybir.ActivationFunctionType
ALU = mybir.AluOpType
AX = mybir.AxisListType

D = 1024
NH = 16
NE = 8
HF = 512
T = 2048
HALO = 64
HT = T + 2 * HALO
NTI = T // 128
NHT = HT // 128
WIN = 192
FT = 256
NFT = T // FT
LN_EPS = 1e-5
ATT_EPS = 1e-6


class Sem:
    def __init__(self, handle):
        self.h = handle
        self.v = 0


class Ev:
    __slots__ = ("eng", "fn", "waits", "incs")

    def __init__(self, eng, fn, waits, incs):
        self.eng, self.fn, self.waits, self.incs = eng, fn, waits, incs


class Prog:
    def __init__(self):
        self.events = []

    def op(self, eng, fn, waits=(), inc=None, inc_n=1):
        incs = []
        if inc is not None:
            inc.v += inc_n
            incs.append((inc, inc_n))
        self.events.append(Ev(eng, fn, [w for w in waits if w is not None], incs))
        return (inc, inc.v) if inc is not None else None

    def emit(self, engine_name, eng):
        for ev in self.events:
            if ev.eng != engine_name:
                continue
            for (sem, val) in ev.waits:
                eng.wait_ge(sem.h, val)
            ins = ev.fn(eng)
            for (sem, n) in ev.incs:
                ins.then_inc(sem.h, n)


def build_nc():
    nc = bass.Bass("TRN2", target_bir_lowering=False)

    xh = nc.dram_tensor("xh", [HT, D], F32, kind="ExternalInput").ap()
    Wq = nc.dram_tensor("Wq", [D, D], F32, kind="ExternalInput").ap()
    Wk = nc.dram_tensor("Wk", [D, D], F32, kind="ExternalInput").ap()
    Wv = nc.dram_tensor("Wv", [D, D], F32, kind="ExternalInput").ap()
    Wo = nc.dram_tensor("Wo", [D, D], F32, kind="ExternalInput").ap()
    bqt = nc.dram_tensor("bqt", [128, 8], F32, kind="ExternalInput").ap()
    bkt = nc.dram_tensor("bkt", [128, 8], F32, kind="ExternalInput").ap()
    bvt = nc.dram_tensor("bvt", [128, 8], F32, kind="ExternalInput").ap()
    bo_row = nc.dram_tensor("bo_row", [1, D], F32, kind="ExternalInput").ap()
    gW = nc.dram_tensor("gW", [D, NE], F32, kind="ExternalInput").ap()
    W1 = nc.dram_tensor("W1", [NE, D, HF], F32, kind="ExternalInput").ap()
    b1t = nc.dram_tensor("b1t", [128, NE * 4], F32, kind="ExternalInput").ap()
    W2 = nc.dram_tensor("W2", [NE, HF, D], F32, kind="ExternalInput").ap()
    B2 = nc.dram_tensor("B2", [NE, D], F32, kind="ExternalInput").ap()
    Ws1 = nc.dram_tensor("Ws1", [D, HF], F32, kind="ExternalInput").ap()
    bs1t = nc.dram_tensor("bs1t", [128, 4], F32, kind="ExternalInput").ap()
    Ws2 = nc.dram_tensor("Ws2", [HF, D], F32, kind="ExternalInput").ap()
    bs2t = nc.dram_tensor("bs2t", [128, 8], F32, kind="ExternalInput").ap()
    rs = nc.dram_tensor("rs", [128, 1], F32, kind="ExternalInput").ap()
    mfirst = nc.dram_tensor("mfirst", [128, WIN], F32, kind="ExternalInput").ap()
    mint = nc.dram_tensor("mint", [128, WIN], F32, kind="ExternalInput").ap()
    mlast = nc.dram_tensor("mlast", [128, WIN], F32, kind="ExternalInput").ap()

    out = nc.dram_tensor("out", [T, D], F32, kind="ExternalOutput").ap()

    wq_bf = nc.dram_tensor("wq_bf", [D, D], BF16, kind="Internal").ap()
    wk_bf = nc.dram_tensor("wk_bf", [D, D], BF16, kind="Internal").ap()
    wv_bf = nc.dram_tensor("wv_bf", [D, D], BF16, kind="Internal").ap()
    wo_bf = nc.dram_tensor("wo_bf", [D, D], BF16, kind="Internal").ap()
    _dbgkind = "Internal" if os.environ.get("KDBG", "") != "1" else "ExternalOutput"
    xn_dram = nc.dram_tensor("xn_dram", [HT, D], BF16, kind=_dbgkind).ap()
    v_dram = nc.dram_tensor("v_dram", [D, HT + 128], BF16, kind=_dbgkind).ap()
    y2_dram = nc.dram_tensor("y2_dram", [D, T], BF16, kind=_dbgkind).ap()

    identb_d = nc.inline_tensor(np.eye(128, dtype=ml_dtypes.bfloat16), "c_identb").ap()
    identf_d = nc.inline_tensor(np.eye(128, dtype=np.float32), "c_identf").ap()
    _r0 = np.zeros((128, 128), np.float32); _r0[0, :] = 1.0
    oner0b_d = nc.inline_tensor(_r0.astype(ml_dtypes.bfloat16), "c_oner0b").ap()
    oner0f_d = nc.inline_tensor(_r0, "c_oner0f").ap()
    E_np = np.zeros((128, 8, 16), np.float32)
    for c in range(8):
        E_np[0:64, c, 2 * c] = 1.0
        E_np[64:128, c, 2 * c + 1] = 1.0
    Esel_d = nc.inline_tensor(
        E_np.reshape(128, 128).astype(ml_dtypes.bfloat16), "c_Esel").ap()
    S_np = np.zeros((16, 8, 128), np.float32)
    for c in range(8):
        S_np[2 * c, c, 0:64] = 1.0
        S_np[2 * c + 1, c, 64:128] = 1.0
    Ssel_d = nc.inline_tensor(S_np.reshape(16, 8 * 128), "c_Ssel").ap()
    Se_np = np.zeros((8, 8, 128), np.float32)
    for e in range(NE):
        Se_np[e, e, :] = 1.0
    Sesel_d = nc.inline_tensor(Se_np.reshape(8, 8 * 128).astype(ml_dtypes.bfloat16), "c_Sesel").ap()

    P = Prog()
    ctx = ExitStack()
    en = ctx.enter_context
    sb = nc.sbuf_tensor
    psn = nc.psum_tensor

    identf = en(sb("identf", [128, 128], F32))
    zer_sb = en(sb("zer_sb", [128, 128], BF16))
    oner0b = en(sb("oner0b", [128, 128], BF16))
    oner0f = en(sb("oner0f", [128, 128], F32))
    consts = en(sb("consts", [128, 4], F32))
    rs_s = en(sb("rs_s", [128, 1], F32))
    rsb = en(sb("rsb", [128, 1], F32))
    xnT_s = en(sb("xnT_s", [128, 8 * HT], BF16))
    stat = en(sb("stat", [128, 20], F32))
    g_scr = en(sb("g_scr", [128, 32], F32))

    sDg = Sem(en(nc.semaphore("sDg")))
    sDs = Sem(en(nc.semaphore("sDs")))
    sT = Sem(en(nc.semaphore("sT")))
    sS = Sem(en(nc.semaphore("sS")))
    sV = Sem(en(nc.semaphore("sV")))
    sOut = Sem(en(nc.semaphore("sOut")))

    def bar():
        return [(sT, sT.v), (sS, sS.v), (sV, sV.v), (sDg, sDg.v), (sDs, sDs.v)]

    def dmag(dst, src):
        return lambda g: g.dma_start(dst, src)

    _UPTO = int(os.environ.get("KUPTO", "9"))
    _SKIP = set(x for x in os.environ.get("KSKIP", "").split(",") if x)
    _DBG = os.environ.get("KDBG", "") == "1"

    def finish():
        with nc.Block() as block:
            @block.gpsimd
            def _(g):
                P.emit("gpsimd", g)

            @block.sync
            def _(sy):
                P.emit("sync", sy)

            @block.tensor
            def _(te):
                P.emit("tensor", te)

            @block.scalar
            def _(s):
                P.emit("scalar", s)

            @block.vector
            def _(v):
                P.emit("vector", v)
        return nc

    def stub_out(waits):
        P.op("gpsimd", lambda g: g.dma_start(out[:, :], xh[HALO:HALO + T, :]),
             waits=waits, inc=sOut, inc_n=16)
        P.op("gpsimd", lambda g: g.engine_nop(), waits=[(sOut, 16)])

    sbA = ExitStack()
    enA = sbA.enter_context
    Sesel = enA(sb("Sesel", [8, 8 * 128], BF16))
    b1_s = enA(sb("b1_s", [128, NE * 4], F32))
    bs1_s = enA(sb("bs1_s", [128, 4], F32))
    bs2_s = enA(sb("bs2_s", [128, 8], F32))
    B2_s = enA(sb("B2_s", [8, D], BF16))
    gateW_s = enA(sb("gateW_s", [128, 8 * NE], BF16))
    gT_s = enA(sb("gT_s", [8, T], BF16))
    x_sb = [enA(sb(f"x_sb{i}", [128, D], F32)) for i in range(2)]
    xn_sb = [enA(sb(f"xn_sb{i}", [128, D], BF16)) for i in range(2)]
    Ws1_s = enA(sb("Ws1_s", [128, 8 * HF], BF16))
    Ws2_s = enA(sb("Ws2_s", [128, 4 * D], BF16))
    W1a = enA(sb("W1a", [128, NE * 4096], BF16))
    W2a = enA(sb("W2a", [128, NE * 4096], BF16))
    hg_sb = [enA(sb(f"hg_sb{i}", [128, FT], BF16)) for i in range(2)]
    hsT_sb = enA(sb("hsT_sb", [128, 4 * FT], BF16))
    hgg_sb2 = [hsT_sb, enA(sb("hgg_b", [128, 4 * FT], BF16))]
    y2sb = enA(sb("y2sb", [128, 4 * FT], BF16))
    zsc_sb = enA(sb("zsc_sb", [8, 128], F32))
    psA = ExitStack()
    enAp = psA.enter_context
    sm_ps = enAp(psn("sm_ps", [128, 512], F32))
    pA = [enAp(psn(f"pA{i}", [128, 512], F32)) for i in range(3)]
    y2p = enAp(psn("y2p", [128, 2048], F32))

    for dst, srcap in [
        (identf[:, :], identf_d),
        (oner0b[:, :], oner0b_d), (oner0f[:, :], oner0f_d),
        (Sesel[:, :], Sesel_d),
        (b1_s[:, :], b1t), (bs1_s[:, :], bs1t), (bs2_s[:, :], bs2t),
        (B2_s[:, :], B2), (rs_s[:, :], rs),
        (gateW_s[:, :].rearrange("p (kc j) -> p kc j", kc=8),
         gW.rearrange("(kc p) j -> p kc j", p=128)),
        (Ws1_s[:, :].rearrange("p (kc f) -> p kc f", kc=8),
         Ws1.rearrange("(kc p) f -> p kc f", p=128)),
        (Ws2_s[:, :].rearrange("p (kc f) -> p kc f", kc=4),
         Ws2.rearrange("(kc p) f -> p kc f", p=128)),
    ]:
        P.op("gpsimd", dmag(dst, srcap), inc=sDg, inc_n=16)
    c_small = (sDg, sDg.v)

    def f_consts(v):
        v.memset(consts[:, 0:1], LN_EPS)
        v.memset(consts[:, 1:2], ATT_EPS)
        v.memset(zer_sb[:, :], 0.0)
        return v.memset(consts[:, 2:3], 0.0)
    P.op("vector", f_consts, inc=sV)
    c_consts = (sV, sV.v)

    P.op("tensor", lambda t: t.matmul(sm_ps[:, 0:1], oner0f[:, :], rs_s[:, :],
                                      start=True, stop=True),
         waits=[c_consts, c_small], inc=sT)
    c_rsmm = (sT, sT.v)
    P.op("vector", lambda v: v.tensor_copy(rsb[:, :], sm_ps[:, 0:1]),
         waits=[c_rsmm], inc=sV)
    c_rsb = (sV, sV.v)

    slot_store = {0: None, 1: None}
    slot_free = {0: None, 1: None}
    prev_var = None
    c_xn_tile = {}
    for t in range(NHT):
        sl = t % 2
        P.op("sync", lambda sy, sl=sl, t=t: sy.dma_start(
            x_sb[sl][:, :], xh[t * 128:(t + 1) * 128, :]),
             waits=[slot_free[sl]], inc=sDs, inc_n=16)
        c_xload = (sDs, sDs.v)
        def f_bn(v, sl=sl):
            v.bn_stats(stat[:, 6:12], x_sb[sl][:, 0:512])
            return v.bn_stats(stat[:, 12:18], x_sb[sl][:, 512:1024])
        P.op("vector", f_bn, waits=[c_xload, prev_var], inc=sV)
        c_bn = (sV, sV.v)
        P.op("vector", lambda v: v.bn_aggr(stat[:, 2:4], stat[:, 6:18]),
             waits=[c_bn], inc=sV)
        c_var = (sV, sV.v)
        prev_var = c_var
        P.op("scalar", lambda s: s.activation(stat[:, 5:6], stat[:, 3:4], AF.Ln,
                                              bias=consts[:, 0:1]),
             waits=[c_var, c_consts], inc=sS)
        c_ln = (sS, sS.v)
        P.op("scalar", lambda s: s.activation(stat[:, 5:6], stat[:, 5:6], AF.Exp,
                                              scale=-0.5), waits=[c_ln], inc=sS)
        c_rstd = (sS, sS.v)
        P.op("vector", lambda v, sl=sl: v.tensor_scalar(
            xn_sb[sl][:, :], x_sb[sl][:, :], stat[:, 2:3], stat[:, 5:6],
            ALU.subtract, ALU.mult),
             waits=[c_rstd, slot_store[sl]], inc=sV)
        c_xn = (sV, sV.v)
        slot_free[sl] = c_xn
        P.op("gpsimd", dmag(xn_dram[t * 128:(t + 1) * 128, :], xn_sb[sl][:, :]),
             waits=[c_xn], inc=sDg, inc_n=16)
        slot_store[sl] = (sDg, sDg.v)
        c_xn_tile[t] = (sDg, sDg.v)
    c_xn_all = (sDg, sDg.v)

    W1r = W1.rearrange("e (kc p) f -> p e kc f", p=128)
    W2r = W2.rearrange("e (kc p) f -> p e kc f", p=128)
    for e in range(NE):
        P.op("gpsimd", dmag(
            W1a[:, e * 4096:(e + 1) * 4096].rearrange("p (kc f) -> p kc f", kc=8),
            W1r[:, e:e + 1, :, :].rearrange("p one kc f -> p (one kc) f")),
             inc=sDg, inc_n=16)
        P.op("gpsimd", dmag(
            W2a[:, e * 4096:(e + 1) * 4096].rearrange("p (kc f) -> p kc f", kc=4),
            W2r[:, e:e + 1, :, :].rearrange("p one kc f -> p (one kc) f")),
             inc=sDg, inc_n=16)
    c_expw = (sDg, sDg.v)

    for wbf, w in [(wq_bf, Wq), (wk_bf, Wk), (wv_bf, Wv), (wo_bf, Wo)]:
        P.op("gpsimd", dmag(wbf, w), inc=sDg, inc_n=16)
    c_wcast = (sDg, sDg.v)


    bands = [576, 1152, HT]
    c_band = []
    b_prev = 0
    for bk in bands:
        mark = c_xn_tile[(bk + 127) // 128 - 1]
        for c in range(8):
            P.op("sync", lambda sy, c=c, b0=b_prev, b1=bk: sy.dma_start(
                xnT_s[:, c * HT + b0:c * HT + b1],
                xn_dram[b0:b1, c * 128:(c + 1) * 128],
                transpose=True), waits=[mark], inc=sDs, inc_n=16)
        c_band.append((sDs, sDs.v))
        b_prev = bk
    c_xnT = c_band[-1]

    def xnT_ready(last_col):
        for bk, cb in zip(bands, c_band):
            if last_col <= bk:
                return cb
        return c_xnT

    if _UPTO <= 0:
        stub_out(bar())
        return finish()

    gst = {"c_gdone": None, "c_ez": None, "c_zev": None, "c_g": None}

    def gate_stepA(t):
        hb = HALO + t * 128

        def mm_gate(te, hb=hb):
            for kc in range(8):
                ins = te.matmul(sm_ps[0:8, 0:128],
                                gateW_s[:, kc * 8:(kc + 1) * 8],
                                xnT_s[:, kc * HT + hb:kc * HT + hb + 128],
                                start=(kc == 0), stop=(kc == 7))
            return ins
        P.op("tensor", mm_gate,
             waits=[xnT_ready(hb + 128), c_small, gst["c_gdone"]], inc=sT)
        c_zmm = (sT, sT.v)
        P.op("vector", lambda v: v.tensor_copy(zsc_sb[:, :], sm_ps[0:8, 0:128]),
             waits=[c_zmm], inc=sV)
        gst["c_zev"] = (sV, sV.v)

    def gate_stepB(t):
        P.op("tensor", lambda te: te.transpose(sm_ps[:, 128:136], zsc_sb[:, :],
                                               identf[0:8, 0:8]),
             waits=[gst["c_zev"], gst["c_ez"]], inc=sT)
        c_z = (sT, sT.v)
        P.op("scalar", lambda s: s.activation(g_scr[:, 8:16], sm_ps[:, 128:136],
                                              AF.Exp, scale=-1.0),
             waits=[c_z, gst["c_gdone"]], inc=sS)
        gst["c_ez"] = (sS, sS.v)

    def gate_stepC(t):
        prev = gst["c_ez"]
        for fn in [
            lambda v: v.tensor_scalar(g_scr[:, 8:16], g_scr[:, 8:16], 1.0, None,
                                      ALU.add),
            lambda v: v.reciprocal(g_scr[:, 0:8], g_scr[:, 8:16]),
            lambda v: v.tensor_scalar(g_scr[:, 0:8], g_scr[:, 0:8], rsb[:, 0:1],
                                      None, ALU.mult),
            lambda v: v.max(out=g_scr[:, 16:24], in_=g_scr[:, 0:8]),
            lambda v: v.tensor_add(g_scr[:, 24:25], g_scr[:, 16:17], g_scr[:, 17:18]),
            lambda v: v.reciprocal(g_scr[:, 25:26], g_scr[:, 24:25]),
            lambda v: v.tensor_scalar(g_scr[:, 8:16], g_scr[:, 0:8], g_scr[:, 17:18],
                                      None, ALU.is_ge),
            lambda v: v.scalar_tensor_tensor(g_scr[:, 0:8], g_scr[:, 0:8],
                                             g_scr[:, 25:26], g_scr[:, 8:16],
                                             ALU.mult, ALU.mult),
        ]:
            P.op("vector", fn, waits=[prev, c_rsb], inc=sV)
            prev = (sV, sV.v)
        gst["c_g"] = prev

    def gate_stepD(t):
        P.op("tensor", lambda te: te.transpose(
            sm_ps[0:8, 0:128], g_scr[:, 0:8], identf[:, :]),
             waits=[gst["c_g"], c_rsb, gst["c_gdone"]], inc=sT)
        c_gT = (sT, sT.v)
        P.op("vector", lambda v, t=t: v.tensor_copy(
            gT_s[:, t * 128:(t + 1) * 128], sm_ps[0:8, 0:128]),
             waits=[c_gT], inc=sV)
        gst["c_gdone"] = (sV, sV.v)

    def emit_gate(t):
        gate_stepA(t)
        gate_stepB(t)
        gate_stepC(t)
        gate_stepD(t)

    c_y2_store = None
    c_prev_evac = None
    c_pA_free = [None, None, None]
    c_hsT_used = None
    c_hg_used = [None, None]
    c_hgg_used = None
    c_hgg_used2 = [None, None]
    c_G_used = [None, None]
    emit_gate(0)
    emit_gate(1)
    g_ready = {0: gst["c_gdone"]}
    for ft in range(NFT):
        fb = HALO + ft * FT
        if ft + 1 < NFT:
            t0 = 2 * (ft + 1)
            gslot = {0: [(gate_stepA, t0)], 1: [(gate_stepB, t0)],
                     2: [(gate_stepC, t0)],
                     3: [(gate_stepD, t0), (gate_stepA, t0 + 1)],
                     4: [(gate_stepB, t0 + 1), (gate_stepC, t0 + 1),
                         (gate_stepD, t0 + 1)]}
        else:
            gslot = {}
        c_gdone = g_ready[ft]
        c_hs_evac = [None] * 4
        for mc in range(4):
            for fn, gt in gslot.get(mc, []):
                fn(gt)
            def mm_hs(te, mc=mc, fb=fb):
                for kc in range(8):
                    ins = te.matmul(pA[mc % 3][:, 0:FT],
                                    Ws1_s[:, kc * HF + mc * 128:kc * HF + (mc + 1) * 128],
                                    xnT_s[:, kc * HT + fb:kc * HT + fb + FT],
                                    start=(kc == 0), stop=(kc == 7))
                return ins
            P.op("tensor", mm_hs,
                 waits=[xnT_ready(fb + FT), c_small, c_pA_free[mc % 3]], inc=sT)
            c_mm = (sT, sT.v)
            P.op("scalar", lambda s, mc=mc: s.activation(
                hsT_sb[:, mc * FT:(mc + 1) * FT], pA[mc % 3][:, 0:FT], AF.Gelu,
                bias=bs1_s[:, mc:mc + 1]),
                 waits=[c_mm, c_hsT_used, c_hgg_used2[0]], inc=sS)
            c_hs_evac[mc] = (sS, sS.v)
            c_pA_free[mc % 3] = c_hs_evac[mc]

        for fn, gt in gslot.get(4, []):
            fn(gt)
        if gslot:
            g_ready[ft + 1] = gst["c_gdone"]

        def mm_y2z(te):
            for b in range(4):
                ins = te.matmul(y2p[:, b * 512:(b + 1) * 512], zer_sb[:, :],
                                Ws1_s[:, 0:512], start=True, stop=False,
                                skip_group_check=True)
            return ins
        P.op("tensor", mm_y2z, waits=[c_prev_evac, c_consts, c_small], inc=sT)
        c_y2z = (sT, sT.v)

        def mm_ws2(te):
            ins = None
            for c in range(8):
                for mc in range(4):
                    ins = te.matmul(y2p[:, c * FT:(c + 1) * FT],
                                    Ws2_s[:, mc * D + c * 128:mc * D + (c + 1) * 128],
                                    hsT_sb[:, mc * FT:(mc + 1) * FT],
                                    start=False, stop=False,
                                    skip_group_check=True)
            return ins
        P.op("tensor", mm_ws2, waits=[c_hs_evac[3], c_y2z], inc=sT)
        c_hsT_used = (sT, sT.v)

        def mm_b2(te, ft=ft):
            ins = None
            for c in range(8):
                ins = te.matmul(y2p[:, c * FT:(c + 1) * FT],
                                B2_s[:, c * 128:(c + 1) * 128],
                                gT_s[:, ft * FT:(ft + 1) * FT],
                                start=False, stop=False, skip_group_check=True)
            return ins
        P.op("tensor", mm_b2, waits=[c_gdone], inc=sT)

        for e in range(NE):
            P.op("tensor", lambda te, e=e, ft=ft: te.matmul(
                sm_ps[:, 256:256 + FT],
                Sesel[:, e * 128:(e + 1) * 128],
                gT_s[:, ft * FT:(ft + 1) * FT], start=True, stop=True),
                 waits=[c_gdone, c_G_used[0]], inc=sT)
            c_G = (sT, sT.v)
            for mc in range(4):
                psl = (4 * e + mc) % 3
                def mm_h(te, e=e, mc=mc, fb=fb, psl=psl):
                    for kc in range(8):
                        ins = te.matmul(
                            pA[psl][:, 0:FT],
                            W1a[:, e * 4096 + kc * HF + mc * 128:
                                 e * 4096 + kc * HF + (mc + 1) * 128],
                            xnT_s[:, kc * HT + fb:kc * HT + fb + FT],
                            start=(kc == 0), stop=(kc == 7))
                    return ins
                P.op("tensor", mm_h,
                     waits=[xnT_ready(fb + FT), c_expw, c_pA_free[psl]], inc=sT)
                c_hmm = (sT, sT.v)
                P.op("scalar", lambda s, e=e, mc=mc, psl=psl: s.activation(
                    hg_sb[mc % 2][:, :], pA[psl][:, 0:FT], AF.Gelu,
                    bias=b1_s[:, e * 4 + mc:e * 4 + mc + 1]),
                     waits=[c_hmm, c_hg_used[mc % 2]], inc=sS)
                c_hg = (sS, sS.v)
                c_pA_free[psl] = c_hg
                P.op("vector", lambda v, mc=mc, e=e: v.tensor_mul(
                    hgg_sb2[e % 2][:, mc * FT:(mc + 1) * FT], hg_sb[mc % 2][:, :],
                    sm_ps[:, 256:256 + FT]),
                     waits=[c_hg, c_G, c_hgg_used2[e % 2], c_hsT_used], inc=sV)
                c_hg_used[mc % 2] = (sV, sV.v)
            c_hgg = (sV, sV.v)
            c_G_used[0] = c_hgg

            def mm_w2(te, e=e):
                ins = None
                for c in range(8):
                    for mc in range(4):
                        ins = te.matmul(
                            y2p[:, c * FT:(c + 1) * FT],
                            W2a[:, e * 4096 + mc * D + c * 128:
                                 e * 4096 + mc * D + (c + 1) * 128],
                            hgg_sb2[e % 2][:, mc * FT:(mc + 1) * FT],
                            start=False, stop=(e == NE - 1 and c == 7 and mc == 3),
                            skip_group_check=True)
                return ins
            P.op("tensor", mm_w2, waits=[c_hgg], inc=sT)
            c_hgg_used2[e % 2] = (sT, sT.v)
            c_hgg_used = (sT, sT.v)
        c_y2mm = (sT, sT.v)

        for half in range(2):
            def ev_y2(s, half=half):
                for c in range(4 * half, 4 * half + 4):
                    ins = s.activation(y2sb[:, (c - 4 * half) * FT:(c - 4 * half + 1) * FT],
                                       y2p[:, c * FT:(c + 1) * FT], AF.Identity,
                                       bias=bs2_s[:, c:c + 1])
                return ins
            P.op("scalar", ev_y2, waits=[c_y2mm, c_y2_store], inc=sS)
            c_prev_evac = (sS, sS.v)

            def st_y2(g, ft=ft, half=half):
                dst = y2_dram.rearrange("(c p) t -> p c t", p=128)[
                    :, 4 * half:4 * half + 4, ft * FT:(ft + 1) * FT]
                srcap = y2sb[:, :].rearrange("p (c f) -> p c f", c=4)
                return g.dma_start(dst, srcap)
            P.op("gpsimd", st_y2, waits=[c_prev_evac], inc=sDg, inc_n=16)
            c_y2_store = (sDg, sDg.v)
    c_y2_all = c_y2_store
    if _DBG:
        d_gT = nc.dram_tensor("d_gT", [8, T], BF16, kind="ExternalOutput").ap()
        P.op("gpsimd", lambda g: g.dma_start(d_gT, gT_s[:, :]),
             waits=[gst["c_gdone"]], inc=sDg, inc_n=16)
    c_p2_bar = bar()
    if _UPTO <= 2:
        stub_out(c_p2_bar)
        return finish()

    sbA.close()
    psA.close()
    sbB = ExitStack()
    enB = sbB.enter_context
    WoT_s = enB(sb("WoT_s", [128, 8 * D], BF16))
    knT_s = enB(sb("knT_s", [128, 8 * HT], BF16))
    qT_s = enB(sb("qT_s", [128, 8 * T], BF16))
    rqT_s = enB(sb("rqT_s", [128, NTI * NH], F32))
    masks = enB(sb("masks", [128, 3 * WIN], BF16))
    identb = enB(sb("identb", [128, 128], BF16))
    bor = enB(sb("bor", [128, D], BF16))
    sbB1 = ExitStack()
    enB1 = sbB1.enter_context
    WqT_s = enB1(sb("WqT_s", [128, 8 * D], BF16))
    WkT_s = enB1(sb("WkT_s", [128, 8 * D], BF16))
    WvT_s = enB1(sb("WvT_s", [128, 8 * D], BF16))
    kraw_s = enB1(sb("kraw_s", [128, 8 * 512], BF16))
    sq_scr = [enB1(sb(f"sq_scr{i}", [128, D], BF16)) for i in range(2)]
    rq_f = enB1(sb("rq_f", [16, 512], F32))
    rk_f = enB1(sb("rk_f", [16, 512], F32))
    Ssel = enB1(sb("Ssel", [16, 8 * 128], F32))
    Esel = enB1(sb("Esel", [128, 128], BF16))
    bv_s = enB1(sb("bv_s", [128, 8], F32))
    bq_s = enB1(sb("bq_s", [128, 8], F32))
    bk_s = enB1(sb("bk_s", [128, 8], F32))
    P.op("vector", lambda v: v.memset(bor[:, :], 0.0), waits=c_p2_bar, inc=sV)
    c_bz = (sV, sV.v)
    for dst, srcap in [
        (masks[:, 0:WIN], mfirst), (masks[:, WIN:2 * WIN], mint),
        (masks[:, 2 * WIN:3 * WIN], mlast),
        (identb[:, :], identb_d), (bor[0:1, :], bo_row),
        (Ssel[:, :], Ssel_d), (Esel[:, :], Esel_d), (bv_s[:, :], bvt),
        (bq_s[:, :], bqt), (bk_s[:, :], bkt),
    ]:
        P.op("gpsimd", dmag(dst, srcap), waits=c_p2_bar + [c_bz], inc=sDg, inc_n=16)
    c_bsmall = (sDg, sDg.v)
    psB = ExitStack()
    enBp = psB.enter_context
    sm2_ps = enBp(psn("sm2_ps", [128, 128], F32))
    pB2 = [enBp(psn(f"pB2{i}", [128, 512], F32)) for i in range(4)]
    ss_ps = enBp(psn("ss_ps", [16, 512], F32))
    rr2_ps = [enBp(psn(f"rr2_ps{i}", [128, 512], F32)) for i in range(2)]

    wt_marks = {}
    for wbf, wt, nm in [(wk_bf, WkT_s, "k"), (wv_bf, WvT_s, "v"),
                        (wq_bf, WqT_s, "q"), (wo_bf, WoT_s, "o")]:
        for c in range(8):
            P.op("sync", lambda sy, wbf=wbf, wt=wt, c=c: sy.dma_start(
                wt[:, c * D:(c + 1) * D], wbf[:, c * 128:(c + 1) * 128],
                transpose=True), waits=[c_wcast] + c_p2_bar, inc=sDs, inc_n=16)
        wt_marks[nm] = (sDs, sDs.v)
    c_WkT, c_WvT, c_WqT, c_WoT = (wt_marks["k"], wt_marks["v"],
                                  wt_marks["q"], wt_marks["o"])

    c_pB2_free = [None, None, None, None]
    c_kraw_used = None
    c_rr2_used = [None, None]
    c_sq_used = [None, None]
    c_ss_free = None

    k_macs = [(0, 512), (512, 512), (1024, 512), (1536, 512), (2048, 128)]
    st1 = {"ss_free": None, "kraw_used": None, "smT_used": None}

    def emit_proj_k(i):
        off, w = k_macs[i]
        prev_tt = st1.get(("tt", i - 1))
        for cch in range(8):
            sl = cch % 4

            def mm_k(te, cch=cch, off=off, w=w, sl=sl):
                for kc in range(8):
                    ins = te.matmul(pB2[sl][:, 0:w],
                                    WkT_s[:, kc * D + cch * 128:kc * D + (cch + 1) * 128],
                                    xnT_s[:, kc * HT + off:kc * HT + off + w],
                                    start=(kc == 0), stop=(kc == 7))
                return ins
            P.op("tensor", mm_k, waits=[c_WkT, c_xnT, c_pB2_free[sl]], inc=sT)
            c_kmm = (sT, sT.v)
            P.op("scalar", lambda s, cch=cch, w=w, sl=sl: s.activation(
                kraw_s[:, cch * 512:cch * 512 + w], pB2[sl][:, 0:w], AF.Identity,
                bias=bk_s[:, cch:cch + 1]),
                 waits=[c_kmm, c_bsmall, prev_tt[cch] if prev_tt else None], inc=sS)
            c_kev = (sS, sS.v)
            c_pB2_free[sl] = c_kev
            P.op("vector", lambda v, cch=cch, w=w, sq=cch % 2: v.tensor_mul(
                sq_scr[sq][:, 0:w], kraw_s[:, cch * 512:cch * 512 + w],
                kraw_s[:, cch * 512:cch * 512 + w]),
                 waits=[c_kev, c_sq_used[cch % 2]], inc=sV)
            c_sq = (sV, sV.v)
            P.op("tensor", lambda te, cch=cch, w=w, sq=cch % 2: te.matmul(
                ss_ps[:, 0:w], Esel[:, cch * 16:(cch + 1) * 16], sq_scr[sq][:, 0:w],
                start=(cch == 0), stop=(cch == 7)),
                 waits=[c_sq, c_bsmall, st1["ss_free"] if cch == 0 else None], inc=sT)
            c_sq_used[cch % 2] = (sT, sT.v)
        st1[("ssmm", "k", i)] = (sT, sT.v)

    def emit_chain_k(i):
        off, w = k_macs[i]
        prevk = st1[("ssmm", "k", i)]
        for j, fn in enumerate([
            lambda s, w=w: s.activation(rk_f[0:16, 0:w], ss_ps[0:16, 0:w], AF.Ln),
            lambda s, w=w: s.activation(rk_f[0:16, 0:w], rk_f[0:16, 0:w], AF.Exp,
                                        scale=0.5),
            lambda s, w=w: s.activation(rk_f[0:16, 0:w], rk_f[0:16, 0:w], AF.Ln,
                                        bias=consts[0:16, 1:2]),
            lambda s, w=w: s.activation(rk_f[0:16, 0:w], rk_f[0:16, 0:w], AF.Exp,
                                        scale=-1.0),
        ]):
            P.op("scalar", fn, waits=[prevk], inc=sS)
            prevk = (sS, sS.v)
            if j == 0:
                st1["ss_free"] = prevk
        st1[("rk", i)] = prevk

    def emit_Rtt_k(i):
        off, w = k_macs[i]
        c_rk = st1[("rk", i)]
        tts = [None] * 8
        for cch in range(8):
            sl = cch % 2
            P.op("tensor", lambda te, cch=cch, w=w, sl=sl: te.matmul(
                rr2_ps[sl][:, 0:w], Ssel[:, cch * 128:(cch + 1) * 128],
                rk_f[0:16, 0:w], start=True, stop=True),
                 waits=[c_rk, c_bsmall, c_rr2_used[sl]], inc=sT)
            c_R = (sT, sT.v)
            P.op("vector", lambda v, cch=cch, off=off, w=w, sl=sl: v.tensor_mul(
                knT_s[:, cch * HT + off:cch * HT + off + w],
                kraw_s[:, cch * 512:cch * 512 + w], rr2_ps[sl][:, 0:w]),
                 waits=[c_R], inc=sV)
            c_rr2_used[sl] = (sV, sV.v)
            tts[cch] = (sV, sV.v)
        st1[("tt", i)] = tts
        st1["kraw_used"] = (sV, sV.v)

    def emit_proj_q(m):
        off = HALO + m * 512
        for cch in range(8):
            sl = cch % 4

            def mm_q(te, cch=cch, off=off, sl=sl):
                for kc in range(8):
                    ins = te.matmul(pB2[sl][:, 0:512],
                                    WqT_s[:, kc * D + cch * 128:kc * D + (cch + 1) * 128],
                                    xnT_s[:, kc * HT + off:kc * HT + off + 512],
                                    start=(kc == 0), stop=(kc == 7))
                return ins
            P.op("tensor", mm_q, waits=[c_WqT, c_pB2_free[sl]], inc=sT)
            c_qmm = (sT, sT.v)
            P.op("scalar", lambda s, cch=cch, m=m, sl=sl: s.activation(
                qT_s[:, cch * T + m * 512:cch * T + (m + 1) * 512], pB2[sl][:, 0:512],
                AF.Identity, bias=bq_s[:, cch:cch + 1]), waits=[c_qmm, c_bsmall], inc=sS)
            c_qev = (sS, sS.v)
            c_pB2_free[sl] = c_qev
            P.op("vector", lambda v, cch=cch, m=m, sq=cch % 2: v.tensor_mul(
                sq_scr[sq][:, 0:512],
                qT_s[:, cch * T + m * 512:cch * T + (m + 1) * 512],
                qT_s[:, cch * T + m * 512:cch * T + (m + 1) * 512]),
                 waits=[c_qev, c_sq_used[cch % 2]], inc=sV)
            c_sq = (sV, sV.v)
            P.op("tensor", lambda te, cch=cch, sq=cch % 2: te.matmul(
                ss_ps[:, 0:512], Esel[:, cch * 16:(cch + 1) * 16],
                sq_scr[sq][:, 0:512], start=(cch == 0), stop=(cch == 7)),
                 waits=[c_sq, c_bsmall, st1["ss_free"] if cch == 0 else None], inc=sT)
            c_sq_used[cch % 2] = (sT, sT.v)
        st1[("ssmm", "q", m)] = (sT, sT.v)

    def emit_chain_q(m):
        prevq = st1[("ssmm", "q", m)]
        for j, fn in enumerate([
            lambda s: s.activation(rq_f[0:16, 0:512], ss_ps[0:16, 0:512], AF.Ln),
            lambda s: s.activation(rq_f[0:16, 0:512], rq_f[0:16, 0:512], AF.Exp,
                                   scale=0.5),
            lambda s: s.activation(rq_f[0:16, 0:512], rq_f[0:16, 0:512], AF.Ln,
                                   bias=consts[0:16, 1:2]),
            lambda s: s.activation(rq_f[0:16, 0:512], rq_f[0:16, 0:512], AF.Exp,
                                   scale=-1.0),
        ]):
            P.op("scalar", fn, waits=[prevq], inc=sS)
            prevq = (sS, sS.v)
            if j == 0:
                st1["ss_free"] = prevq
        st1[("rq", m)] = prevq

    def emit_tr_q(m):
        c_rq = st1[("rq", m)]
        for tt in range(4):
            P.op("tensor", lambda te, tt=tt: te.transpose(
                sm2_ps[0:128, 0:16], rq_f[0:16, tt * 128:(tt + 1) * 128],
                identf[0:16, 0:16]), waits=[c_rq, st1["smT_used"]], inc=sT)
            c_trq = (sT, sT.v)
            P.op("vector", lambda v, m=m, tt=tt: v.tensor_copy(
                rqT_s[:, (m * 4 + tt) * 16:(m * 4 + tt) * 16 + 16],
                sm2_ps[0:128, 0:16]), waits=[c_trq], inc=sV)
            st1["smT_used"] = (sV, sV.v)

    pend_tr = None
    for i in range(5):
        emit_proj_k(i)
        if pend_tr is not None:
            emit_tr_q(pend_tr)
        emit_chain_k(i)
        if i < 4:
            emit_proj_q(i)
        emit_Rtt_k(i)
        if i < 4:
            emit_chain_q(i)
            pend_tr = i
    emit_tr_q(pend_tr)
    c_kraw_used = st1["kraw_used"]

    c_pB2v_free = list(c_pB2_free)
    prev_st = list(st1[("tt", 4)])
    for (off, w) in k_macs:
        cur_st = [None] * 8
        for mc in range(8):
            sl = mc % 4

            def mm_v(te, mc=mc, off=off, w=w, sl=sl):
                for kc in range(8):
                    ins = te.matmul(
                        pB2[sl][:, 0:w],
                        WvT_s[:, kc * D + mc * 128:kc * D + (mc + 1) * 128],
                        xnT_s[:, kc * HT + off:kc * HT + off + w],
                        start=(kc == 0), stop=(kc == 7))
                return ins
            P.op("tensor", mm_v, waits=[c_WvT, c_pB2v_free[sl]], inc=sT)
            c_vmm = (sT, sT.v)
            P.op("scalar", lambda s, mc=mc, w=w, sl=sl: s.activation(
                kraw_s[:, mc * 512:mc * 512 + w], pB2[sl][:, 0:w], AF.Identity,
                bias=bv_s[:, mc:mc + 1]),
                 waits=[c_vmm, c_bsmall, prev_st[mc]], inc=sS)
            c_vev = (sS, sS.v)
            c_pB2v_free[sl] = c_vev
            P.op("gpsimd", lambda g, mc=mc, off=off, w=w: g.dma_start(
                v_dram[mc * 128:(mc + 1) * 128, off:off + w],
                kraw_s[:, mc * 512:mc * 512 + w]),
                 waits=[c_vev], inc=sDg, inc_n=16)
            cur_st[mc] = (sDg, sDg.v)
        prev_st = cur_st
    c_p1a_bar = bar()

    if _DBG:
        d_knT = nc.dram_tensor("d_knT", [128, 8 * HT], BF16, kind="ExternalOutput").ap()
        d_qT = nc.dram_tensor("d_qT", [128, 8 * T], BF16, kind="ExternalOutput").ap()
        d_rqT = nc.dram_tensor("d_rqT", [128, NTI * NH], F32, kind="ExternalOutput").ap()
        for dst, srct in [(d_knT, knT_s), (d_qT, qT_s), (d_rqT, rqT_s)]:
            P.op("gpsimd", lambda g, dst=dst, srct=srct: g.dma_start(dst, srct[:, :]),
                 waits=c_p1a_bar, inc=sDg, inc_n=16)

    if _UPTO <= 3:
        stub_out(c_p1a_bar)
        return finish()

    psB.close()
    sbB1.close()
    xL_sb = [enB(sb(f"xL_sb{i}", [128, D], F32)) for i in range(2)]
    vw_a = [enB(sb(f"vw_a{i}", [128, D], BF16)) for i in range(2)]
    vw_b = [enB(sb(f"vw_b{i}", [128, D], BF16)) for i in range(2)]
    vw_c = [enB(sb(f"vw_c{i}", [128, D], BF16)) for i in range(2)]
    e_sb = [enB(sb(f"e_sb{i}", [128, 4 * WIN], BF16)) for i in range(2)]
    pn_sb = [enB(sb(f"pn_sb{i}", [128, 4 * WIN], BF16)) for i in range(4)]
    sums = enB(sb("sums", [128, NH], F32))
    rsum = enB(sb("rsum", [128, NH], F32))
    pT_sb = [enB(sb(f"pT_sb{i}", [128, 1024], BF16)) for i in range(2)]
    outT_sb = [enB(sb(f"outT_sb{i}", [128, 128], BF16)) for i in range(2)]
    out2_sb = [enB(sb(f"out2_sb{i}", [128, 128], BF16)) for i in range(2)]
    out_sb = [enB(sb(f"out_sb{i}", [128, D], F32)) for i in range(2)]
    y1_sb = [enB(sb(f"y1_sb{i}", [128, D], F32)) for i in range(2)]
    yb_sb = [enB(sb(f"yb_sb{i}", [128, D], BF16)) for i in range(2)]
    c_zer = c_consts
    psC = ExitStack()
    enCp = psC.enter_context
    y1_ps = enCp(psn("y1_ps", [128, 1024], F32))
    scg_ps = enCp(psn("scg_ps", [128, 4 * WIN], F32))
    pT_ps = [enCp(psn(f"pT_ps{i}", [128, 1024], BF16)) for i in range(2)]
    av_ps = [enCp(psn(f"av_ps{i}", [128, 512], F32)) for i in range(2)]

    c_vw_used = [None, None]
    c_x_free = [None, None]
    c_y1sb_free = [None, None]
    c_yb_free = [None, None]
    c_out_store = [None, None]
    c_y1_free = None
    c_scg_free = None
    c_exp_by = [None, None]
    c_stt_prev = None
    c_stt_by = [None, None]
    c_pn_by = [None, None]
    c_tr_by = [None, None]
    c_pTev_by = [None, None]
    c_av_by = [None, None]
    c_avev_by = [None, None]
    c_wo_by = [None, None]
    c_wo_last = None
    c_pn_key = {}

    def emit_A_hg(t, hg, first_bar):
        wb = t * 128 + 32
        moff = 0 if t == 0 else (2 * WIN if t == NTI - 1 else WIN)
        if True:
            par = hg % 2

            def mm_scg(te, t=t, hg=hg, wb=wb):
                for i in range(2):
                    c = 2 * hg + i
                    te.matmul(scg_ps[:, 2 * i * WIN:(2 * i + 1) * WIN],
                              qT_s[0:64, c * T + t * 128:c * T + (t + 1) * 128],
                              knT_s[0:64, c * HT + wb:c * HT + wb + WIN],
                              start=True, stop=True)
                te.matmul(y1_ps[:, 0:128], zer_sb[:, :], zer_sb[:, :],
                          start=False, stop=False, skip_group_check=True)
                for i in range(2):
                    c = 2 * hg + i
                    te.matmul(scg_ps[:, (2 * i + 1) * WIN:(2 * i + 2) * WIN],
                              qT_s[64:128, c * T + t * 128:c * T + (t + 1) * 128],
                              knT_s[64:128, c * HT + wb:c * HT + wb + WIN],
                              start=True, stop=True)
                return te.matmul(y1_ps[:, 0:128], zer_sb[:, :], zer_sb[:, :],
                                 start=False, stop=False, skip_group_check=True)
            P.op("tensor", mm_scg, waits=[st["scg_free"], c_zer] + first_bar,
                 inc=sT)
            c_sc = (sT, sT.v)

            def f_exp(s, t=t, hg=hg, par=par):
                for i in range(4):
                    h = 4 * hg + i
                    ins = s.activation(e_sb[par][:, i * WIN:(i + 1) * WIN],
                                       scg_ps[:, i * WIN:(i + 1) * WIN], AF.Exp,
                                       scale=rqT_s[:, t * NH + h:t * NH + h + 1])
                return ins
            P.op("scalar", f_exp,
                 waits=[c_sc, st["stt_by"][par]] + first_bar, inc=sS)
            c_exp = (sS, sS.v)
            st["scg_free"] = c_exp
            st["exp_by"][par] = c_exp

            def f_stt(v, t=t, hg=hg, par=par, moff=moff):
                for i in range(4):
                    h = 4 * hg + i
                    ins = v.scalar_tensor_tensor(
                        pn_sb[hg][:, i * WIN:(i + 1) * WIN],
                        e_sb[par][:, i * WIN:(i + 1) * WIN], 1.0,
                        masks[:, moff:moff + WIN], ALU.mult, ALU.mult,
                        accum_out=sums[:, 4 * hg + i:4 * hg + i + 1])
                return ins
            P.op("vector", f_stt,
                 waits=[c_exp, c_bsmall, st["tr_by"][hg]] + first_bar, inc=sV)
            st["stt_by"][par] = (sV, sV.v)
            P.op("vector", lambda v, hg=hg: v.reciprocal(
                rsum[:, 4 * hg:4 * hg + 4], sums[:, 4 * hg:4 * hg + 4]),
                 waits=[st["stt_by"][par]], inc=sV)
            c_rs2 = (sV, sV.v)

            def f_pn(v, hg=hg):
                for i in range(4):
                    h = 4 * hg + i
                    ins = v.tensor_scalar(pn_sb[hg][:, i * WIN:(i + 1) * WIN],
                                          pn_sb[hg][:, i * WIN:(i + 1) * WIN],
                                          rsum[:, h:h + 1], None, ALU.mult)
                return ins
            P.op("vector", f_pn, waits=[c_rs2], inc=sV)
            c_pn_key[(t, hg)] = (sV, sV.v)

    ld = {}

    def emit_loads(t):
        sl2 = t % 2
        hb = HALO + t * 128
        P.op("sync", lambda sy, t=t, sl2=sl2: sy.dma_start(
            vw_a[sl2][:, :], v_dram[:, t * 128 + 32:t * 128 + 160], transpose=True),
             waits=[c_vw_used[sl2], c_p1a_bar[3]], inc=sDs, inc_n=16)
        P.op("sync", lambda sy, t=t, sl2=sl2: sy.dma_start(
            vw_b[sl2][:, :], v_dram[:, t * 128 + 160:t * 128 + 288], transpose=True),
             inc=sDs, inc_n=16)
        P.op("sync", lambda sy, t=t, sl2=sl2: sy.dma_start(
            vw_c[sl2][:, :], v_dram[:, t * 128 + 96:t * 128 + 224], transpose=True),
             inc=sDs, inc_n=16)
        c_vw = (sDs, sDs.v)
        P.op("gpsimd", lambda g, hb=hb, sl2=sl2: g.dma_start(
            xL_sb[sl2][:, :], xh[hb:hb + 128, :]),
             waits=[c_x_free[sl2]], inc=sDg, inc_n=16)
        c_xl = (sDg, sDg.v)
        P.op("sync", lambda sy, t=t, sl2=sl2: sy.dma_start(
            yb_sb[sl2][:, :], y2_dram[:, t * 128:(t + 1) * 128], transpose=True),
             waits=[c_y2_all, c_yb_free[sl2]], inc=sDs, inc_n=16)
        c_yb = (sDs, sDs.v)
        ld[t] = (c_vw, c_xl, c_yb)

    def emit_B_pre(t, first_bar):
        sl2 = t % 2
        c_vw, c_xl, c_yb = ld[t]

        def mm_bo(te):
            te.matmul(y1_ps[:, 0:512], oner0b[:, :], bor[:, 0:512],
                      start=True, stop=False, skip_group_check=True)
            return te.matmul(y1_ps[:, 512:1024], oner0b[:, :], bor[:, 512:1024],
                             start=True, stop=False, skip_group_check=True)
        P.op("tensor", mm_bo, waits=[c_bsmall, st["y1_free"]] + first_bar, inc=sT)
        return c_vw, c_xl, c_yb

    def emit_B_sb(t, sb2, c_vw, first_bar):
        sl2 = t % 2
        if True:
            pairs = (2 * sb2, 2 * sb2 + 1)
            par = sb2 % 2
            slb = sb2 % 2

            def f_tr(te, t=t, sb2=sb2, slb=slb):
                for j in range(2):
                    i0 = 2 * j * WIN
                    o0 = 512 * j
                    te.transpose(pT_ps[slb][:, o0:o0 + 128],
                                 pn_sb[sb2][:, i0:i0 + 128], identb[:, :])
                    te.transpose(pT_ps[slb][0:64, o0 + 128:o0 + 256],
                                 pn_sb[sb2][:, i0 + 128:i0 + WIN], identb[:, :])
                    te.transpose(pT_ps[slb][0:64, o0 + 256:o0 + 384],
                                 pn_sb[sb2][:, i0 + WIN:i0 + WIN + 64], identb[:, :])
                    ins = te.transpose(pT_ps[slb][:, o0 + 384:o0 + 512],
                                       pn_sb[sb2][:, i0 + WIN + 64:i0 + 2 * WIN],
                                       identb[:, :])
                return ins
            P.op("tensor", f_tr,
                 waits=[c_pn_key[(t, sb2)], st["pTev_by"][slb]] + first_bar, inc=sT)
            st["tr_by"][sb2] = (sT, sT.v)
            c_tr = st["tr_by"][sb2]

            if sb2 % 2 == 0:
                P.op("scalar", lambda s, slb=slb: s.copy(pT_sb[slb][:, :],
                                                         pT_ps[slb][:, :]),
                     waits=[c_tr, st["av_by"][slb]], inc=sS)
                st["pTev_by"][slb] = (sS, sS.v)
            else:
                P.op("vector", lambda v, slb=slb: v.tensor_copy(pT_sb[slb][:, :],
                                                                pT_ps[slb][:, :]),
                     waits=[c_tr, st["av_by"][slb]], inc=sV)
                st["pTev_by"][slb] = (sV, sV.v)

            def f_av(te, pairs=pairs, sl2=sl2, slb=slb):
                for j, p in enumerate(pairs):
                    A, Bh = 2 * p, 2 * p + 1
                    co = j * 256
                    po = j * 512
                    te.matmul(av_ps[slb][0:64, co:co + 128],
                              vw_a[sl2][:, A * 64:(A + 1) * 64],
                              pT_sb[slb][:, po:po + 128], start=True, stop=False,
                              skip_group_check=True)
                    te.matmul(av_ps[slb][0:64, co:co + 128],
                              vw_b[sl2][0:64, A * 64:(A + 1) * 64],
                              pT_sb[slb][0:64, po + 128:po + 256], start=False,
                              stop=True, skip_group_check=True)
                    te.matmul(av_ps[slb][0:64, co + 128:co + 256],
                              vw_a[sl2][0:64, Bh * 64:(Bh + 1) * 64],
                              pT_sb[slb][0:64, po + 256:po + 384], start=True,
                              stop=False, skip_group_check=True)
                    ins = te.matmul(av_ps[slb][0:64, co + 128:co + 256],
                                    vw_c[sl2][:, Bh * 64:(Bh + 1) * 64],
                                    pT_sb[slb][:, po + 384:po + 512], start=False,
                                    stop=True, skip_group_check=True)
                return ins
            P.op("tensor", f_av,
                 waits=[st["pTev_by"][slb], c_vw, st["avev_by"][slb]] + first_bar,
                 inc=sT)
            st["av_by"][slb] = (sT, sT.v)

            if sb2 % 2 == 0:
                def f_ev(v, slb=slb):
                    v.tensor_copy(outT_sb[slb][0:64, :], av_ps[slb][0:64, 0:128])
                    v.tensor_copy(outT_sb[slb][64:128, :], av_ps[slb][0:64, 128:256])
                    v.tensor_copy(out2_sb[slb][0:64, :], av_ps[slb][0:64, 256:384])
                    return v.tensor_copy(out2_sb[slb][64:128, :],
                                         av_ps[slb][0:64, 384:512])
                P.op("vector", f_ev, waits=[st["av_by"][slb], st["wo_by"][slb]],
                     inc=sV)
                st["avev_by"][slb] = (sV, sV.v)
            else:
                def f_ev(s, slb=slb):
                    s.copy(outT_sb[slb][0:64, :], av_ps[slb][0:64, 0:128])
                    s.copy(outT_sb[slb][64:128, :], av_ps[slb][0:64, 128:256])
                    s.copy(out2_sb[slb][0:64, :], av_ps[slb][0:64, 256:384])
                    return s.copy(out2_sb[slb][64:128, :], av_ps[slb][0:64, 384:512])
                P.op("scalar", f_ev, waits=[st["av_by"][slb], st["wo_by"][slb]],
                     inc=sS)
                st["avev_by"][slb] = (sS, sS.v)

            def f_wo(te, pairs=pairs, slb=slb):
                for j, p in enumerate(pairs):
                    src_t = outT_sb[slb] if j == 0 else out2_sb[slb]
                    te.matmul(y1_ps[:, 0:512], src_t[:, :],
                              WoT_s[:, p * D:p * D + 512], start=False, stop=False,
                              skip_group_check=True)
                    ins = te.matmul(y1_ps[:, 512:1024], src_t[:, :],
                                    WoT_s[:, p * D + 512:(p + 1) * D], start=False,
                                    stop=(p == 7), skip_group_check=True)
                return ins
            P.op("tensor", f_wo, waits=[st["avev_by"][slb], c_WoT], inc=sT)
            st["wo_by"][slb] = (sT, sT.v)

    def emit_B_post(t, c_xl, c_yb):
        sl2 = t % 2
        c_wo_last = st["wo_by"][1]

        P.op("scalar", lambda s, sl2=sl2: s.copy(y1_sb[sl2][:, :], y1_ps[:, :]),
             waits=[c_wo_last, c_y1sb_free[sl2]], inc=sS)
        st["y1_free"] = (sS, sS.v)
        P.op("vector", lambda v, sl2=sl2: v.scalar_tensor_tensor(
            out_sb[sl2][:, :], y1_sb[sl2][:, :], 1.0, xL_sb[sl2][:, :],
            ALU.mult, ALU.add),
             waits=[st["y1_free"], c_xl, c_out_store[sl2]], inc=sV)
        P.op("vector", lambda v, sl2=sl2: v.tensor_add(
            out_sb[sl2][:, :], out_sb[sl2][:, :], yb_sb[sl2][:, :]),
             waits=[c_yb, (sV, sV.v)], inc=sV)
        c_fin = (sV, sV.v)
        c_y1sb_free[sl2] = c_fin
        c_yb_free[sl2] = c_fin
        c_x_free[sl2] = c_fin
        c_vw_used[sl2] = st["av_by"][1]
        P.op("gpsimd", lambda g, t=t, sl2=sl2: g.dma_start(
            out[t * 128:(t + 1) * 128, :], out_sb[sl2][:, :]),
             waits=[c_fin], inc=sOut, inc_n=16)
        c_out_store[sl2] = (sOut, sOut.v)

    st = {"scg_free": None, "exp_by": [None, None], "stt_by": [None, None],
          "tr_by": [None, None, None, None], "pTev_by": [None, None],
          "av_by": [None, None], "avev_by": [None, None], "wo_by": [None, None],
          "y1_free": None}

    for hg in range(4):
        emit_A_hg(0, hg, c_p1a_bar)
    for t in range(NTI):
        fb2 = c_p1a_bar if t == 0 else []
        emit_loads(t)
        c_vw, c_xl, c_yb = emit_B_pre(t, fb2)
        for i in range(4):
            emit_B_sb(t, i, c_vw, fb2)
            if t + 1 < NTI:
                emit_A_hg(t + 1, i, fb2)
        emit_B_post(t, c_xl, c_yb)

    P.op("gpsimd", lambda g: g.engine_nop(), waits=[(sOut, 16 * NTI)])

    return finish()


def _masks_for(q0):
    def m(t):
        qi = np.arange(128)[:, None]
        kk = np.arange(WIN)[None, :]
        qg = q0 + t * 128 + qi
        kg = q0 + t * 128 - 32 + kk
        allowed = ((np.abs(kg - qg) <= 32) & ((kg - qg) % 4 == 0)
                   & (kg >= 0) & (kg < 8192))
        return np.ascontiguousarray(allowed.astype(np.float32))
    return m(0), m(7), m(NTI - 1)


_NC_CACHE = None


def kernel(**inputs):
    global _NC_CACHE
    f = lambda k: np.ascontiguousarray(np.asarray(inputs[k], dtype=np.float32))
    x = f("L0_attn_x")
    Wq, bq = f("L0_attn_Wq"), f("L0_attn_bq")
    Wk, bk = f("L0_attn_Wk"), f("L0_attn_bk")
    Wv, bv = f("L0_attn_Wv"), f("L0_attn_bv")
    Wo, bo = f("L0_attn_Wo"), f("L0_attn_bo")
    gW = f("L0_ffn_gate_W")
    W1, b1 = f("L0_ffn_W1"), f("L0_ffn_b1")
    W2, b2 = f("L0_ffn_W2"), f("L0_ffn_b2")
    Ws1, bs1 = f("L0_ffn_Ws1"), f("L0_ffn_bs1")
    Ws2, bs2 = f("L0_ffn_Ws2"), f("L0_ffn_bs2")
    rsv = f("L0_ffn_route_scale")

    shared = {
        "Wq": Wq, "Wk": Wk, "Wv": Wv, "Wo": Wo,
        "bqt": np.ascontiguousarray(bq.reshape(8, 128).T),
        "bkt": np.ascontiguousarray(bk.reshape(8, 128).T),
        "bvt": np.ascontiguousarray(bv.reshape(8, 128).T),
        "bo_row": bo.reshape(1, D),
        "gW": gW, "W1": W1, "W2": W2,
        "b1t": np.ascontiguousarray(b1.reshape(8, 4, 128).transpose(2, 0, 1)
                                    .reshape(128, 32)),
        "B2": b2,
        "Ws1": Ws1, "Ws2": Ws2,
        "bs1t": np.ascontiguousarray(bs1.reshape(4, 128).T),
        "bs2t": np.ascontiguousarray(bs2.reshape(8, 128).T),
        "rs": np.concatenate([rsv.reshape(1, 1),
                              np.zeros((127, 1), np.float32)], axis=0),
    }

    in_maps = []
    for core in range(8):
        b, qq = core // 4, core % 4
        q0 = qq * T
        xh = np.zeros((HT, D), np.float32)
        lo = max(0, q0 - HALO)
        hi = min(8192, q0 + T + HALO)
        xh[lo - (q0 - HALO):hi - (q0 - HALO)] = x[b, lo:hi]
        mf, mi, ml = _masks_for(q0)
        m = dict(shared)
        m.update({"xh": xh, "mfirst": mf, "mint": mi, "mlast": ml})
        in_maps.append(m)

    if _NC_CACHE is None:
        _NC_CACHE = build_nc()
    nc = _NC_CACHE
    res = run_bass_kernel_spmd(nc, in_maps, core_ids=list(range(8)))

    outp = np.empty((2, 8192, D), np.float32)
    for core in range(8):
        b, qq = core // 4, core % 4
        outp[b, qq * T:(qq + 1) * T] = np.asarray(res.results[core]["out"],
                                                  dtype=np.float32)
    return outp


if __name__ == "__main__":
    import sys
    sys.path.insert(0, "/root/problem")
    from reference import setup_inputs, reference
    ins = setup_inputs(0)
    ref = np.asarray(reference(**ins))
    got = kernel(**{k: np.asarray(v) for k, v in ins.items()})
    err = np.mean((got - ref) ** 2) / np.mean(ref ** 2)
    print("residual variance ratio:", err)


# version 25
# speedup vs baseline: 1.0896x; 1.0471x over previous
import os
import numpy as np
import ml_dtypes
from contextlib import ExitStack

import concourse.bass as bass
import concourse.mybir as mybir
from concourse.bass_utils import run_bass_kernel_spmd

F32 = mybir.dt.float32
BF16 = mybir.dt.bfloat16
AF = mybir.ActivationFunctionType
ALU = mybir.AluOpType
AX = mybir.AxisListType

D = 1024
NH = 16
NE = 8
HF = 512
T = 2048
HALO = 64
HT = T + 2 * HALO
NTI = T // 128
NHT = HT // 128
WIN = 192
FT = 256
NFT = T // FT
LN_EPS = 1e-5
ATT_EPS = 1e-6


class Sem:
    def __init__(self, handle):
        self.h = handle
        self.v = 0


class Ev:
    __slots__ = ("eng", "fn", "waits", "incs")

    def __init__(self, eng, fn, waits, incs):
        self.eng, self.fn, self.waits, self.incs = eng, fn, waits, incs


class Prog:
    def __init__(self):
        self.events = []

    def op(self, eng, fn, waits=(), inc=None, inc_n=1):
        incs = []
        if inc is not None:
            inc.v += inc_n
            incs.append((inc, inc_n))
        self.events.append(Ev(eng, fn, [w for w in waits if w is not None], incs))
        return (inc, inc.v) if inc is not None else None

    def emit(self, engine_name, eng):
        for ev in self.events:
            if ev.eng != engine_name:
                continue
            for (sem, val) in ev.waits:
                eng.wait_ge(sem.h, val)
            ins = ev.fn(eng)
            for (sem, n) in ev.incs:
                ins.then_inc(sem.h, n)


def build_nc():
    nc = bass.Bass("TRN2", target_bir_lowering=False)

    xh = nc.dram_tensor("xh", [HT, D], F32, kind="ExternalInput").ap()
    Wq = nc.dram_tensor("Wq", [D, D], F32, kind="ExternalInput").ap()
    Wk = nc.dram_tensor("Wk", [D, D], F32, kind="ExternalInput").ap()
    Wv = nc.dram_tensor("Wv", [D, D], F32, kind="ExternalInput").ap()
    Wo = nc.dram_tensor("Wo", [D, D], F32, kind="ExternalInput").ap()
    bqt = nc.dram_tensor("bqt", [128, 8], F32, kind="ExternalInput").ap()
    bkt = nc.dram_tensor("bkt", [128, 8], F32, kind="ExternalInput").ap()
    bvt = nc.dram_tensor("bvt", [128, 8], F32, kind="ExternalInput").ap()
    bo_row = nc.dram_tensor("bo_row", [1, D], F32, kind="ExternalInput").ap()
    gW = nc.dram_tensor("gW", [D, NE], F32, kind="ExternalInput").ap()
    W1 = nc.dram_tensor("W1", [NE, D, HF], F32, kind="ExternalInput").ap()
    b1t = nc.dram_tensor("b1t", [128, NE * 4], F32, kind="ExternalInput").ap()
    W2 = nc.dram_tensor("W2", [NE, HF, D], F32, kind="ExternalInput").ap()
    B2 = nc.dram_tensor("B2", [NE, D], F32, kind="ExternalInput").ap()
    Ws1 = nc.dram_tensor("Ws1", [D, HF], F32, kind="ExternalInput").ap()
    bs1t = nc.dram_tensor("bs1t", [128, 4], F32, kind="ExternalInput").ap()
    Ws2 = nc.dram_tensor("Ws2", [HF, D], F32, kind="ExternalInput").ap()
    bs2t = nc.dram_tensor("bs2t", [128, 8], F32, kind="ExternalInput").ap()
    rs = nc.dram_tensor("rs", [128, 1], F32, kind="ExternalInput").ap()
    mfirst = nc.dram_tensor("mfirst", [128, WIN], F32, kind="ExternalInput").ap()
    mint = nc.dram_tensor("mint", [128, WIN], F32, kind="ExternalInput").ap()
    mlast = nc.dram_tensor("mlast", [128, WIN], F32, kind="ExternalInput").ap()

    out = nc.dram_tensor("out", [T, D], F32, kind="ExternalOutput").ap()

    wq_bf = nc.dram_tensor("wq_bf", [D, D], BF16, kind="Internal").ap()
    wk_bf = nc.dram_tensor("wk_bf", [D, D], BF16, kind="Internal").ap()
    wv_bf = nc.dram_tensor("wv_bf", [D, D], BF16, kind="Internal").ap()
    wo_bf = nc.dram_tensor("wo_bf", [D, D], BF16, kind="Internal").ap()
    _dbgkind = "Internal" if os.environ.get("KDBG", "") != "1" else "ExternalOutput"
    xn_dram = nc.dram_tensor("xn_dram", [HT, D], BF16, kind=_dbgkind).ap()
    v_dram = nc.dram_tensor("v_dram", [D, HT + 128], BF16, kind=_dbgkind).ap()
    y2_dram = nc.dram_tensor("y2_dram", [D, T], BF16, kind=_dbgkind).ap()

    identb_d = nc.inline_tensor(np.eye(128, dtype=ml_dtypes.bfloat16), "c_identb").ap()
    identf_d = nc.inline_tensor(np.eye(128, dtype=np.float32), "c_identf").ap()
    _r0 = np.zeros((128, 128), np.float32); _r0[0, :] = 1.0
    oner0b_d = nc.inline_tensor(_r0.astype(ml_dtypes.bfloat16), "c_oner0b").ap()
    oner0f_d = nc.inline_tensor(_r0, "c_oner0f").ap()
    E_np = np.zeros((128, 8, 16), np.float32)
    for c in range(8):
        E_np[0:64, c, 2 * c] = 1.0
        E_np[64:128, c, 2 * c + 1] = 1.0
    Esel_d = nc.inline_tensor(
        E_np.reshape(128, 128).astype(ml_dtypes.bfloat16), "c_Esel").ap()
    S_np = np.zeros((16, 8, 128), np.float32)
    for c in range(8):
        S_np[2 * c, c, 0:64] = 1.0
        S_np[2 * c + 1, c, 64:128] = 1.0
    Ssel_d = nc.inline_tensor(S_np.reshape(16, 8 * 128), "c_Ssel").ap()
    Se_np = np.zeros((8, 8, 128), np.float32)
    for e in range(NE):
        Se_np[e, e, :] = 1.0
    Sesel_d = nc.inline_tensor(Se_np.reshape(8, 8 * 128).astype(ml_dtypes.bfloat16), "c_Sesel").ap()

    P = Prog()
    ctx = ExitStack()
    en = ctx.enter_context
    sb = nc.sbuf_tensor
    psn = nc.psum_tensor

    identf = en(sb("identf", [128, 128], F32))
    zer_sb = en(sb("zer_sb", [128, 128], BF16))
    oner0b = en(sb("oner0b", [128, 128], BF16))
    oner0f = en(sb("oner0f", [128, 128], F32))
    consts = en(sb("consts", [128, 4], F32))
    rs_s = en(sb("rs_s", [128, 1], F32))
    rsb = en(sb("rsb", [128, 1], F32))
    xnT_s = en(sb("xnT_s", [128, 8 * HT], BF16))
    stat = en(sb("stat", [128, 20], F32))
    g_scr = en(sb("g_scr", [128, 32], F32))

    sDg = Sem(en(nc.semaphore("sDg")))
    sDs = Sem(en(nc.semaphore("sDs")))
    sT = Sem(en(nc.semaphore("sT")))
    sS = Sem(en(nc.semaphore("sS")))
    sV = Sem(en(nc.semaphore("sV")))
    sOut = Sem(en(nc.semaphore("sOut")))

    def bar():
        return [(sT, sT.v), (sS, sS.v), (sV, sV.v), (sDg, sDg.v), (sDs, sDs.v)]

    def dmag(dst, src):
        return lambda g: g.dma_start(dst, src)

    _UPTO = int(os.environ.get("KUPTO", "9"))
    _SKIP = set(x for x in os.environ.get("KSKIP", "").split(",") if x)
    _DBG = os.environ.get("KDBG", "") == "1"

    def finish():
        with nc.Block() as block:
            @block.gpsimd
            def _(g):
                P.emit("gpsimd", g)

            @block.sync
            def _(sy):
                P.emit("sync", sy)

            @block.tensor
            def _(te):
                P.emit("tensor", te)

            @block.scalar
            def _(s):
                P.emit("scalar", s)

            @block.vector
            def _(v):
                P.emit("vector", v)
        return nc

    def stub_out(waits):
        P.op("gpsimd", lambda g: g.dma_start(out[:, :], xh[HALO:HALO + T, :]),
             waits=waits, inc=sOut, inc_n=16)
        P.op("gpsimd", lambda g: g.engine_nop(), waits=[(sOut, 16)])

    sbA = ExitStack()
    enA = sbA.enter_context
    Sesel = enA(sb("Sesel", [8, 8 * 128], BF16))
    b1_s = enA(sb("b1_s", [128, NE * 4], F32))
    bs1_s = enA(sb("bs1_s", [128, 4], F32))
    bs2_s = enA(sb("bs2_s", [128, 8], F32))
    B2_s = enA(sb("B2_s", [8, D], BF16))
    gateW_s = enA(sb("gateW_s", [128, 8 * NE], BF16))
    gT_s = enA(sb("gT_s", [8, T], BF16))
    x_sb = [enA(sb(f"x_sb{i}", [128, D], F32)) for i in range(2)]
    xn_sb = [enA(sb(f"xn_sb{i}", [128, D], BF16)) for i in range(2)]
    Ws1_s = enA(sb("Ws1_s", [128, 8 * HF], BF16))
    Ws2_s = enA(sb("Ws2_s", [128, 4 * D], BF16))
    W1a = enA(sb("W1a", [128, NE * 4096], BF16))
    W2a = enA(sb("W2a", [128, NE * 4096], BF16))
    hg_sb = [enA(sb(f"hg_sb{i}", [128, FT], BF16)) for i in range(2)]
    hsT_sb = enA(sb("hsT_sb", [128, 4 * FT], BF16))
    hgg_sb2 = [hsT_sb, enA(sb("hgg_b", [128, 4 * FT], BF16))]
    y2sb = enA(sb("y2sb", [128, 4 * FT], BF16))
    zsc_sb = enA(sb("zsc_sb", [8, 128], F32))
    psA = ExitStack()
    enAp = psA.enter_context
    sm_ps = enAp(psn("sm_ps", [128, 512], F32))
    pA = [enAp(psn(f"pA{i}", [128, 512], F32)) for i in range(3)]
    y2p = enAp(psn("y2p", [128, 2048], F32))

    for dst, srcap in [
        (identf[:, :], identf_d),
        (oner0b[:, :], oner0b_d), (oner0f[:, :], oner0f_d),
        (Sesel[:, :], Sesel_d),
        (b1_s[:, :], b1t), (bs1_s[:, :], bs1t), (bs2_s[:, :], bs2t),
        (B2_s[:, :], B2), (rs_s[:, :], rs),
        (gateW_s[:, :].rearrange("p (kc j) -> p kc j", kc=8),
         gW.rearrange("(kc p) j -> p kc j", p=128)),
        (Ws1_s[:, :].rearrange("p (kc f) -> p kc f", kc=8),
         Ws1.rearrange("(kc p) f -> p kc f", p=128)),
        (Ws2_s[:, :].rearrange("p (kc f) -> p kc f", kc=4),
         Ws2.rearrange("(kc p) f -> p kc f", p=128)),
    ]:
        P.op("gpsimd", dmag(dst, srcap), inc=sDg, inc_n=16)
    c_small = (sDg, sDg.v)

    def f_consts(v):
        v.memset(consts[:, 0:1], LN_EPS)
        v.memset(consts[:, 1:2], ATT_EPS)
        v.memset(zer_sb[:, :], 0.0)
        return v.memset(consts[:, 2:3], 0.0)
    P.op("vector", f_consts, inc=sV)
    c_consts = (sV, sV.v)

    P.op("tensor", lambda t: t.matmul(sm_ps[:, 0:1], oner0f[:, :], rs_s[:, :],
                                      start=True, stop=True),
         waits=[c_consts, c_small], inc=sT)
    c_rsmm = (sT, sT.v)
    P.op("vector", lambda v: v.tensor_copy(rsb[:, :], sm_ps[:, 0:1]),
         waits=[c_rsmm], inc=sV)
    c_rsb = (sV, sV.v)

    slot_store = {0: None, 1: None}
    slot_free = {0: None, 1: None}
    prev_var = None
    c_xn_tile = {}
    for t in range(NHT):
        sl = t % 2
        P.op("sync", lambda sy, sl=sl, t=t: sy.dma_start(
            x_sb[sl][:, :], xh[t * 128:(t + 1) * 128, :]),
             waits=[slot_free[sl]], inc=sDs, inc_n=16)
        c_xload = (sDs, sDs.v)
        def f_bn(v, sl=sl):
            v.bn_stats(stat[:, 6:12], x_sb[sl][:, 0:512])
            return v.bn_stats(stat[:, 12:18], x_sb[sl][:, 512:1024])
        P.op("vector", f_bn, waits=[c_xload, prev_var], inc=sV)
        c_bn = (sV, sV.v)
        P.op("vector", lambda v: v.bn_aggr(stat[:, 2:4], stat[:, 6:18]),
             waits=[c_bn], inc=sV)
        c_var = (sV, sV.v)
        prev_var = c_var
        P.op("scalar", lambda s: s.activation(stat[:, 5:6], stat[:, 3:4], AF.Ln,
                                              bias=consts[:, 0:1]),
             waits=[c_var, c_consts], inc=sS)
        c_ln = (sS, sS.v)
        P.op("scalar", lambda s: s.activation(stat[:, 5:6], stat[:, 5:6], AF.Exp,
                                              scale=-0.5), waits=[c_ln], inc=sS)
        c_rstd = (sS, sS.v)
        P.op("vector", lambda v, sl=sl: v.tensor_scalar(
            xn_sb[sl][:, :], x_sb[sl][:, :], stat[:, 2:3], stat[:, 5:6],
            ALU.subtract, ALU.mult),
             waits=[c_rstd, slot_store[sl]], inc=sV)
        c_xn = (sV, sV.v)
        slot_free[sl] = c_xn
        P.op("gpsimd", dmag(xn_dram[t * 128:(t + 1) * 128, :], xn_sb[sl][:, :]),
             waits=[c_xn], inc=sDg, inc_n=16)
        slot_store[sl] = (sDg, sDg.v)
        c_xn_tile[t] = (sDg, sDg.v)
    c_xn_all = (sDg, sDg.v)

    W1r = W1.rearrange("e (kc p) f -> p e kc f", p=128)
    W2r = W2.rearrange("e (kc p) f -> p e kc f", p=128)
    for e in range(NE):
        P.op("gpsimd", dmag(
            W1a[:, e * 4096:(e + 1) * 4096].rearrange("p (kc f) -> p kc f", kc=8),
            W1r[:, e:e + 1, :, :].rearrange("p one kc f -> p (one kc) f")),
             inc=sDg, inc_n=16)
        P.op("gpsimd", dmag(
            W2a[:, e * 4096:(e + 1) * 4096].rearrange("p (kc f) -> p kc f", kc=4),
            W2r[:, e:e + 1, :, :].rearrange("p one kc f -> p (one kc) f")),
             inc=sDg, inc_n=16)
    c_expw = (sDg, sDg.v)

    for wbf, w in [(wq_bf, Wq), (wk_bf, Wk), (wv_bf, Wv), (wo_bf, Wo)]:
        P.op("gpsimd", dmag(wbf, w), inc=sDg, inc_n=16)
    c_wcast = (sDg, sDg.v)


    bands = [576, 1152, HT]
    c_band = []
    b_prev = 0
    for bk in bands:
        mark = c_xn_tile[(bk + 127) // 128 - 1]
        for c in range(8):
            P.op("sync", lambda sy, c=c, b0=b_prev, b1=bk: sy.dma_start(
                xnT_s[:, c * HT + b0:c * HT + b1],
                xn_dram[b0:b1, c * 128:(c + 1) * 128],
                transpose=True), waits=[mark], inc=sDs, inc_n=16)
        c_band.append((sDs, sDs.v))
        b_prev = bk
    c_xnT = c_band[-1]

    def xnT_ready(last_col):
        for bk, cb in zip(bands, c_band):
            if last_col <= bk:
                return cb
        return c_xnT

    if _UPTO <= 0:
        stub_out(bar())
        return finish()

    gst = {"c_gdone": None, "c_ez": None, "c_zev": None, "c_g": None}

    def gate_stepA(t):
        hb = HALO + t * 128

        def mm_gate(te, hb=hb):
            for kc in range(8):
                ins = te.matmul(sm_ps[0:8, 0:128],
                                gateW_s[:, kc * 8:(kc + 1) * 8],
                                xnT_s[:, kc * HT + hb:kc * HT + hb + 128],
                                start=(kc == 0), stop=(kc == 7))
            return ins
        P.op("tensor", mm_gate,
             waits=[xnT_ready(hb + 128), c_small, gst["c_gdone"]], inc=sT)
        c_zmm = (sT, sT.v)
        P.op("vector", lambda v: v.tensor_copy(zsc_sb[:, :], sm_ps[0:8, 0:128]),
             waits=[c_zmm], inc=sV)
        gst["c_zev"] = (sV, sV.v)

    def gate_stepB(t):
        P.op("tensor", lambda te: te.transpose(sm_ps[:, 128:136], zsc_sb[:, :],
                                               identf[0:8, 0:8]),
             waits=[gst["c_zev"], gst["c_ez"]], inc=sT)
        c_z = (sT, sT.v)
        P.op("scalar", lambda s: s.activation(g_scr[:, 8:16], sm_ps[:, 128:136],
                                              AF.Exp, scale=-1.0),
             waits=[c_z, gst["c_gdone"]], inc=sS)
        gst["c_ez"] = (sS, sS.v)

    def gate_stepC(t):
        prev = gst["c_ez"]
        for fn in [
            lambda v: v.tensor_scalar(g_scr[:, 8:16], g_scr[:, 8:16], 1.0, None,
                                      ALU.add),
            lambda v: v.reciprocal(g_scr[:, 0:8], g_scr[:, 8:16]),
            lambda v: v.tensor_scalar(g_scr[:, 0:8], g_scr[:, 0:8], rsb[:, 0:1],
                                      None, ALU.mult),
            lambda v: v.max(out=g_scr[:, 16:24], in_=g_scr[:, 0:8]),
            lambda v: v.tensor_add(g_scr[:, 24:25], g_scr[:, 16:17], g_scr[:, 17:18]),
            lambda v: v.reciprocal(g_scr[:, 25:26], g_scr[:, 24:25]),
            lambda v: v.tensor_scalar(g_scr[:, 8:16], g_scr[:, 0:8], g_scr[:, 17:18],
                                      None, ALU.is_ge),
            lambda v: v.scalar_tensor_tensor(g_scr[:, 0:8], g_scr[:, 0:8],
                                             g_scr[:, 25:26], g_scr[:, 8:16],
                                             ALU.mult, ALU.mult),
        ]:
            P.op("vector", fn, waits=[prev, c_rsb], inc=sV)
            prev = (sV, sV.v)
        gst["c_g"] = prev

    def gate_stepD(t):
        P.op("tensor", lambda te: te.transpose(
            sm_ps[0:8, 0:128], g_scr[:, 0:8], identf[:, :]),
             waits=[gst["c_g"], c_rsb, gst["c_gdone"]], inc=sT)
        c_gT = (sT, sT.v)
        P.op("vector", lambda v, t=t: v.tensor_copy(
            gT_s[:, t * 128:(t + 1) * 128], sm_ps[0:8, 0:128]),
             waits=[c_gT], inc=sV)
        gst["c_gdone"] = (sV, sV.v)

    def emit_gate(t):
        gate_stepA(t)
        gate_stepB(t)
        gate_stepC(t)
        gate_stepD(t)

    c_y2_store = None
    c_prev_evac = None
    c_pA_free = [None, None, None]
    c_hsT_used = None
    c_hg_used = [None, None]
    c_hgg_used = None
    c_hgg_used2 = [None, None]
    c_G_used = [None, None]
    emit_gate(0)
    emit_gate(1)
    g_ready = {0: gst["c_gdone"]}
    for ft in range(NFT):
        fb = HALO + ft * FT
        if ft + 1 < NFT:
            t0 = 2 * (ft + 1)
            gslot = {0: [(gate_stepA, t0)], 1: [(gate_stepB, t0)],
                     2: [(gate_stepC, t0)],
                     3: [(gate_stepD, t0), (gate_stepA, t0 + 1)],
                     4: [(gate_stepB, t0 + 1), (gate_stepC, t0 + 1),
                         (gate_stepD, t0 + 1)]}
        else:
            gslot = {}
        c_gdone = g_ready[ft]
        c_hs_evac = [None] * 4
        for mc in range(4):
            for fn, gt in gslot.get(mc, []):
                fn(gt)
            def mm_hs(te, mc=mc, fb=fb):
                for kc in range(8):
                    ins = te.matmul(pA[mc % 3][:, 0:FT],
                                    Ws1_s[:, kc * HF + mc * 128:kc * HF + (mc + 1) * 128],
                                    xnT_s[:, kc * HT + fb:kc * HT + fb + FT],
                                    start=(kc == 0), stop=(kc == 7))
                return ins
            P.op("tensor", mm_hs,
                 waits=[xnT_ready(fb + FT), c_small, c_pA_free[mc % 3]], inc=sT)
            c_mm = (sT, sT.v)
            P.op("scalar", lambda s, mc=mc: s.activation(
                hsT_sb[:, mc * FT:(mc + 1) * FT], pA[mc % 3][:, 0:FT], AF.Gelu,
                bias=bs1_s[:, mc:mc + 1]),
                 waits=[c_mm, c_hsT_used, c_hgg_used2[0]], inc=sS)
            c_hs_evac[mc] = (sS, sS.v)
            c_pA_free[mc % 3] = c_hs_evac[mc]

        for fn, gt in gslot.get(4, []):
            fn(gt)
        if gslot:
            g_ready[ft + 1] = gst["c_gdone"]

        def mm_y2z(te):
            for b in range(4):
                ins = te.matmul(y2p[:, b * 512:(b + 1) * 512], zer_sb[:, :],
                                Ws1_s[:, 0:512], start=True, stop=False,
                                skip_group_check=True)
            return ins
        P.op("tensor", mm_y2z, waits=[c_prev_evac, c_consts, c_small], inc=sT)
        c_y2z = (sT, sT.v)

        def mm_ws2(te):
            ins = None
            for c in range(8):
                for mc in range(4):
                    ins = te.matmul(y2p[:, c * FT:(c + 1) * FT],
                                    Ws2_s[:, mc * D + c * 128:mc * D + (c + 1) * 128],
                                    hsT_sb[:, mc * FT:(mc + 1) * FT],
                                    start=False, stop=False,
                                    skip_group_check=True)
            return ins
        P.op("tensor", mm_ws2, waits=[c_hs_evac[3], c_y2z], inc=sT)
        c_hsT_used = (sT, sT.v)

        def mm_b2(te, ft=ft):
            ins = None
            for c in range(8):
                ins = te.matmul(y2p[:, c * FT:(c + 1) * FT],
                                B2_s[:, c * 128:(c + 1) * 128],
                                gT_s[:, ft * FT:(ft + 1) * FT],
                                start=False, stop=False, skip_group_check=True)
            return ins
        P.op("tensor", mm_b2, waits=[c_gdone], inc=sT)

        for e in range(NE):
            P.op("tensor", lambda te, e=e, ft=ft: te.matmul(
                sm_ps[:, 256:256 + FT],
                Sesel[:, e * 128:(e + 1) * 128],
                gT_s[:, ft * FT:(ft + 1) * FT], start=True, stop=True),
                 waits=[c_gdone, c_G_used[0]], inc=sT)
            c_G = (sT, sT.v)
            tts = []
            for mc in range(4):
                psl = (4 * e + mc) % 3
                def mm_h(te, e=e, mc=mc, fb=fb, psl=psl):
                    for kc in range(8):
                        ins = te.matmul(
                            pA[psl][:, 0:FT],
                            W1a[:, e * 4096 + kc * HF + mc * 128:
                                 e * 4096 + kc * HF + (mc + 1) * 128],
                            xnT_s[:, kc * HT + fb:kc * HT + fb + FT],
                            start=(kc == 0), stop=(kc == 7))
                    return ins
                P.op("tensor", mm_h,
                     waits=[xnT_ready(fb + FT), c_expw, c_pA_free[psl]], inc=sT)
                c_hmm = (sT, sT.v)
                P.op("scalar", lambda s, e=e, mc=mc, psl=psl: s.activation(
                    hg_sb[mc % 2][:, :], pA[psl][:, 0:FT], AF.Gelu,
                    bias=b1_s[:, e * 4 + mc:e * 4 + mc + 1]),
                     waits=[c_hmm, c_hg_used[mc % 2]], inc=sS)
                c_hg = (sS, sS.v)
                c_pA_free[psl] = c_hg
                P.op("vector", lambda v, mc=mc, e=e: v.tensor_mul(
                    hgg_sb2[e % 2][:, mc * FT:(mc + 1) * FT], hg_sb[mc % 2][:, :],
                    sm_ps[:, 256:256 + FT]),
                     waits=[c_hg, c_G, c_hgg_used2[e % 2], c_hsT_used], inc=sV)
                c_hg_used[mc % 2] = (sV, sV.v)
                tts.append((sV, sV.v))
            c_hgg = (sV, sV.v)
            c_G_used[0] = c_hgg

            for mc in range(4):
                def mm_w2(te, e=e, mc=mc):
                    ins = None
                    for c in range(8):
                        ins = te.matmul(
                            y2p[:, c * FT:(c + 1) * FT],
                            W2a[:, e * 4096 + mc * D + c * 128:
                                 e * 4096 + mc * D + (c + 1) * 128],
                            hgg_sb2[e % 2][:, mc * FT:(mc + 1) * FT],
                            start=False, stop=(e == NE - 1 and mc == 3 and c == 7),
                            skip_group_check=True)
                    return ins
                P.op("tensor", mm_w2, waits=[tts[mc]], inc=sT)
            c_hgg_used2[e % 2] = (sT, sT.v)
            c_hgg_used = (sT, sT.v)
        c_y2mm = (sT, sT.v)

        for half in range(2):
            def ev_y2(s, half=half):
                for c in range(4 * half, 4 * half + 4):
                    ins = s.activation(y2sb[:, (c - 4 * half) * FT:(c - 4 * half + 1) * FT],
                                       y2p[:, c * FT:(c + 1) * FT], AF.Identity,
                                       bias=bs2_s[:, c:c + 1])
                return ins
            P.op("scalar", ev_y2, waits=[c_y2mm, c_y2_store], inc=sS)
            c_prev_evac = (sS, sS.v)

            def st_y2(g, ft=ft, half=half):
                dst = y2_dram.rearrange("(c p) t -> p c t", p=128)[
                    :, 4 * half:4 * half + 4, ft * FT:(ft + 1) * FT]
                srcap = y2sb[:, :].rearrange("p (c f) -> p c f", c=4)
                return g.dma_start(dst, srcap)
            P.op("gpsimd", st_y2, waits=[c_prev_evac], inc=sDg, inc_n=16)
            c_y2_store = (sDg, sDg.v)
    c_y2_all = c_y2_store
    if _DBG:
        d_gT = nc.dram_tensor("d_gT", [8, T], BF16, kind="ExternalOutput").ap()
        P.op("gpsimd", lambda g: g.dma_start(d_gT, gT_s[:, :]),
             waits=[gst["c_gdone"]], inc=sDg, inc_n=16)
    c_p2_bar = bar()
    if _UPTO <= 2:
        stub_out(c_p2_bar)
        return finish()

    sbA.close()
    psA.close()
    sbB = ExitStack()
    enB = sbB.enter_context
    WoT_s = enB(sb("WoT_s", [128, 8 * D], BF16))
    knT_s = enB(sb("knT_s", [128, 8 * HT], BF16))
    qT_s = enB(sb("qT_s", [128, 8 * T], BF16))
    rqT_s = enB(sb("rqT_s", [128, NTI * NH], F32))
    masks = enB(sb("masks", [128, 3 * WIN], BF16))
    identb = enB(sb("identb", [128, 128], BF16))
    bor = enB(sb("bor", [128, D], BF16))
    sbB1 = ExitStack()
    enB1 = sbB1.enter_context
    WqT_s = enB1(sb("WqT_s", [128, 8 * D], BF16))
    WkT_s = enB1(sb("WkT_s", [128, 8 * D], BF16))
    WvT_s = enB1(sb("WvT_s", [128, 8 * D], BF16))
    kraw_s = enB1(sb("kraw_s", [128, 8 * 512], BF16))
    sq_scr = [enB1(sb(f"sq_scr{i}", [128, D], BF16)) for i in range(2)]
    rq_f = enB1(sb("rq_f", [16, 512], F32))
    rk_f = enB1(sb("rk_f", [16, 512], F32))
    Ssel = enB1(sb("Ssel", [16, 8 * 128], F32))
    Esel = enB1(sb("Esel", [128, 128], BF16))
    bv_s = enB1(sb("bv_s", [128, 8], F32))
    bq_s = enB1(sb("bq_s", [128, 8], F32))
    bk_s = enB1(sb("bk_s", [128, 8], F32))
    P.op("vector", lambda v: v.memset(bor[:, :], 0.0), waits=c_p2_bar, inc=sV)
    c_bz = (sV, sV.v)
    for dst, srcap in [
        (masks[:, 0:WIN], mfirst), (masks[:, WIN:2 * WIN], mint),
        (masks[:, 2 * WIN:3 * WIN], mlast),
        (identb[:, :], identb_d), (bor[0:1, :], bo_row),
        (Ssel[:, :], Ssel_d), (Esel[:, :], Esel_d), (bv_s[:, :], bvt),
        (bq_s[:, :], bqt), (bk_s[:, :], bkt),
    ]:
        P.op("gpsimd", dmag(dst, srcap), waits=c_p2_bar + [c_bz], inc=sDg, inc_n=16)
    c_bsmall = (sDg, sDg.v)
    psB = ExitStack()
    enBp = psB.enter_context
    sm2_ps = enBp(psn("sm2_ps", [128, 128], F32))
    pB2 = [enBp(psn(f"pB2{i}", [128, 512], F32)) for i in range(4)]
    ss_ps = enBp(psn("ss_ps", [16, 512], F32))
    rr2_ps = [enBp(psn(f"rr2_ps{i}", [128, 512], F32)) for i in range(2)]

    wt_marks = {}
    for wbf, wt, nm in [(wk_bf, WkT_s, "k"), (wv_bf, WvT_s, "v"),
                        (wq_bf, WqT_s, "q"), (wo_bf, WoT_s, "o")]:
        for c in range(8):
            P.op("sync", lambda sy, wbf=wbf, wt=wt, c=c: sy.dma_start(
                wt[:, c * D:(c + 1) * D], wbf[:, c * 128:(c + 1) * 128],
                transpose=True), waits=[c_wcast] + c_p2_bar, inc=sDs, inc_n=16)
        wt_marks[nm] = (sDs, sDs.v)
    c_WkT, c_WvT, c_WqT, c_WoT = (wt_marks["k"], wt_marks["v"],
                                  wt_marks["q"], wt_marks["o"])

    c_pB2_free = [None, None, None, None]
    c_kraw_used = None
    c_rr2_used = [None, None]
    c_sq_used = [None, None]
    c_ss_free = None

    k_macs = [(0, 512), (512, 512), (1024, 512), (1536, 512), (2048, 128)]
    st1 = {"ss_free": None, "kraw_used": None, "smT_used": None}

    def emit_proj_k(i):
        off, w = k_macs[i]
        prev_tt = st1.get(("tt", i - 1))
        for cch in range(8):
            sl = cch % 4

            def mm_k(te, cch=cch, off=off, w=w, sl=sl):
                for kc in range(8):
                    ins = te.matmul(pB2[sl][:, 0:w],
                                    WkT_s[:, kc * D + cch * 128:kc * D + (cch + 1) * 128],
                                    xnT_s[:, kc * HT + off:kc * HT + off + w],
                                    start=(kc == 0), stop=(kc == 7))
                return ins
            P.op("tensor", mm_k, waits=[c_WkT, c_xnT, c_pB2_free[sl]], inc=sT)
            c_kmm = (sT, sT.v)
            P.op("scalar", lambda s, cch=cch, w=w, sl=sl: s.activation(
                kraw_s[:, cch * 512:cch * 512 + w], pB2[sl][:, 0:w], AF.Identity,
                bias=bk_s[:, cch:cch + 1]),
                 waits=[c_kmm, c_bsmall, prev_tt[cch] if prev_tt else None], inc=sS)
            c_kev = (sS, sS.v)
            c_pB2_free[sl] = c_kev
            P.op("vector", lambda v, cch=cch, w=w, sq=cch % 2: v.tensor_mul(
                sq_scr[sq][:, 0:w], kraw_s[:, cch * 512:cch * 512 + w],
                kraw_s[:, cch * 512:cch * 512 + w]),
                 waits=[c_kev, c_sq_used[cch % 2]], inc=sV)
            c_sq = (sV, sV.v)
            P.op("tensor", lambda te, cch=cch, w=w, sq=cch % 2: te.matmul(
                ss_ps[:, 0:w], Esel[:, cch * 16:(cch + 1) * 16], sq_scr[sq][:, 0:w],
                start=(cch == 0), stop=(cch == 7)),
                 waits=[c_sq, c_bsmall, st1["ss_free"] if cch == 0 else None], inc=sT)
            c_sq_used[cch % 2] = (sT, sT.v)
        st1[("ssmm", "k", i)] = (sT, sT.v)

    def emit_chain_k(i):
        off, w = k_macs[i]
        prevk = st1[("ssmm", "k", i)]
        for j, fn in enumerate([
            lambda s, w=w: s.activation(rk_f[0:16, 0:w], ss_ps[0:16, 0:w], AF.Ln),
            lambda s, w=w: s.activation(rk_f[0:16, 0:w], rk_f[0:16, 0:w], AF.Exp,
                                        scale=0.5),
            lambda s, w=w: s.activation(rk_f[0:16, 0:w], rk_f[0:16, 0:w], AF.Ln,
                                        bias=consts[0:16, 1:2]),
            lambda s, w=w: s.activation(rk_f[0:16, 0:w], rk_f[0:16, 0:w], AF.Exp,
                                        scale=-1.0),
        ]):
            P.op("scalar", fn, waits=[prevk], inc=sS)
            prevk = (sS, sS.v)
            if j == 0:
                st1["ss_free"] = prevk
        st1[("rk", i)] = prevk

    def emit_Rtt_k(i):
        off, w = k_macs[i]
        c_rk = st1[("rk", i)]
        tts = [None] * 8
        for cch in range(8):
            sl = cch % 2
            P.op("tensor", lambda te, cch=cch, w=w, sl=sl: te.matmul(
                rr2_ps[sl][:, 0:w], Ssel[:, cch * 128:(cch + 1) * 128],
                rk_f[0:16, 0:w], start=True, stop=True),
                 waits=[c_rk, c_bsmall, c_rr2_used[sl]], inc=sT)
            c_R = (sT, sT.v)
            P.op("vector", lambda v, cch=cch, off=off, w=w, sl=sl: v.tensor_mul(
                knT_s[:, cch * HT + off:cch * HT + off + w],
                kraw_s[:, cch * 512:cch * 512 + w], rr2_ps[sl][:, 0:w]),
                 waits=[c_R], inc=sV)
            c_rr2_used[sl] = (sV, sV.v)
            tts[cch] = (sV, sV.v)
        st1[("tt", i)] = tts
        st1["kraw_used"] = (sV, sV.v)

    def emit_proj_q(m):
        off = HALO + m * 512
        for cch in range(8):
            sl = cch % 4

            def mm_q(te, cch=cch, off=off, sl=sl):
                for kc in range(8):
                    ins = te.matmul(pB2[sl][:, 0:512],
                                    WqT_s[:, kc * D + cch * 128:kc * D + (cch + 1) * 128],
                                    xnT_s[:, kc * HT + off:kc * HT + off + 512],
                                    start=(kc == 0), stop=(kc == 7))
                return ins
            P.op("tensor", mm_q, waits=[c_WqT, c_pB2_free[sl]], inc=sT)
            c_qmm = (sT, sT.v)
            P.op("scalar", lambda s, cch=cch, m=m, sl=sl: s.activation(
                qT_s[:, cch * T + m * 512:cch * T + (m + 1) * 512], pB2[sl][:, 0:512],
                AF.Identity, bias=bq_s[:, cch:cch + 1]), waits=[c_qmm, c_bsmall], inc=sS)
            c_qev = (sS, sS.v)
            c_pB2_free[sl] = c_qev
            P.op("vector", lambda v, cch=cch, m=m, sq=cch % 2: v.tensor_mul(
                sq_scr[sq][:, 0:512],
                qT_s[:, cch * T + m * 512:cch * T + (m + 1) * 512],
                qT_s[:, cch * T + m * 512:cch * T + (m + 1) * 512]),
                 waits=[c_qev, c_sq_used[cch % 2]], inc=sV)
            c_sq = (sV, sV.v)
            P.op("tensor", lambda te, cch=cch, sq=cch % 2: te.matmul(
                ss_ps[:, 0:512], Esel[:, cch * 16:(cch + 1) * 16],
                sq_scr[sq][:, 0:512], start=(cch == 0), stop=(cch == 7)),
                 waits=[c_sq, c_bsmall, st1["ss_free"] if cch == 0 else None], inc=sT)
            c_sq_used[cch % 2] = (sT, sT.v)
        st1[("ssmm", "q", m)] = (sT, sT.v)

    def emit_chain_q(m):
        prevq = st1[("ssmm", "q", m)]
        for j, fn in enumerate([
            lambda s: s.activation(rq_f[0:16, 0:512], ss_ps[0:16, 0:512], AF.Ln),
            lambda s: s.activation(rq_f[0:16, 0:512], rq_f[0:16, 0:512], AF.Exp,
                                   scale=0.5),
            lambda s: s.activation(rq_f[0:16, 0:512], rq_f[0:16, 0:512], AF.Ln,
                                   bias=consts[0:16, 1:2]),
            lambda s: s.activation(rq_f[0:16, 0:512], rq_f[0:16, 0:512], AF.Exp,
                                   scale=-1.0),
        ]):
            P.op("scalar", fn, waits=[prevq], inc=sS)
            prevq = (sS, sS.v)
            if j == 0:
                st1["ss_free"] = prevq
        st1[("rq", m)] = prevq

    def emit_tr_q(m):
        c_rq = st1[("rq", m)]
        for tt in range(4):
            P.op("tensor", lambda te, tt=tt: te.transpose(
                sm2_ps[0:128, 0:16], rq_f[0:16, tt * 128:(tt + 1) * 128],
                identf[0:16, 0:16]), waits=[c_rq, st1["smT_used"]], inc=sT)
            c_trq = (sT, sT.v)
            P.op("vector", lambda v, m=m, tt=tt: v.tensor_copy(
                rqT_s[:, (m * 4 + tt) * 16:(m * 4 + tt) * 16 + 16],
                sm2_ps[0:128, 0:16]), waits=[c_trq], inc=sV)
            st1["smT_used"] = (sV, sV.v)

    pend_tr = None
    for i in range(5):
        emit_proj_k(i)
        if pend_tr is not None:
            emit_tr_q(pend_tr)
        emit_chain_k(i)
        if i < 4:
            emit_proj_q(i)
        emit_Rtt_k(i)
        if i < 4:
            emit_chain_q(i)
            pend_tr = i
    emit_tr_q(pend_tr)
    c_kraw_used = st1["kraw_used"]

    c_pB2v_free = list(c_pB2_free)
    prev_st = list(st1[("tt", 4)])
    for (off, w) in k_macs:
        cur_st = [None] * 8
        for mc in range(8):
            sl = mc % 4

            def mm_v(te, mc=mc, off=off, w=w, sl=sl):
                for kc in range(8):
                    ins = te.matmul(
                        pB2[sl][:, 0:w],
                        WvT_s[:, kc * D + mc * 128:kc * D + (mc + 1) * 128],
                        xnT_s[:, kc * HT + off:kc * HT + off + w],
                        start=(kc == 0), stop=(kc == 7))
                return ins
            P.op("tensor", mm_v, waits=[c_WvT, c_pB2v_free[sl]], inc=sT)
            c_vmm = (sT, sT.v)
            P.op("scalar", lambda s, mc=mc, w=w, sl=sl: s.activation(
                kraw_s[:, mc * 512:mc * 512 + w], pB2[sl][:, 0:w], AF.Identity,
                bias=bv_s[:, mc:mc + 1]),
                 waits=[c_vmm, c_bsmall, prev_st[mc]], inc=sS)
            c_vev = (sS, sS.v)
            c_pB2v_free[sl] = c_vev
            P.op("gpsimd", lambda g, mc=mc, off=off, w=w: g.dma_start(
                v_dram[mc * 128:(mc + 1) * 128, off:off + w],
                kraw_s[:, mc * 512:mc * 512 + w]),
                 waits=[c_vev], inc=sDg, inc_n=16)
            cur_st[mc] = (sDg, sDg.v)
        prev_st = cur_st
    c_p1a_bar = bar()

    if _DBG:
        d_knT = nc.dram_tensor("d_knT", [128, 8 * HT], BF16, kind="ExternalOutput").ap()
        d_qT = nc.dram_tensor("d_qT", [128, 8 * T], BF16, kind="ExternalOutput").ap()
        d_rqT = nc.dram_tensor("d_rqT", [128, NTI * NH], F32, kind="ExternalOutput").ap()
        for dst, srct in [(d_knT, knT_s), (d_qT, qT_s), (d_rqT, rqT_s)]:
            P.op("gpsimd", lambda g, dst=dst, srct=srct: g.dma_start(dst, srct[:, :]),
                 waits=c_p1a_bar, inc=sDg, inc_n=16)

    if _UPTO <= 3:
        stub_out(c_p1a_bar)
        return finish()

    psB.close()
    sbB1.close()
    xL_sb = [enB(sb(f"xL_sb{i}", [128, D], F32)) for i in range(2)]
    vw_a = [enB(sb(f"vw_a{i}", [128, D], BF16)) for i in range(2)]
    vw_b = [enB(sb(f"vw_b{i}", [128, D], BF16)) for i in range(2)]
    vw_c = [enB(sb(f"vw_c{i}", [128, D], BF16)) for i in range(2)]
    e_sb = [enB(sb(f"e_sb{i}", [128, 4 * WIN], BF16)) for i in range(2)]
    pn_sb = [enB(sb(f"pn_sb{i}", [128, 4 * WIN], BF16)) for i in range(4)]
    sums = enB(sb("sums", [128, NH], F32))
    rsum = enB(sb("rsum", [128, NH], F32))
    pT_sb = [enB(sb(f"pT_sb{i}", [128, 1024], BF16)) for i in range(2)]
    outT_sb = [enB(sb(f"outT_sb{i}", [128, 128], BF16)) for i in range(2)]
    out2_sb = [enB(sb(f"out2_sb{i}", [128, 128], BF16)) for i in range(2)]
    out_sb = [enB(sb(f"out_sb{i}", [128, D], F32)) for i in range(2)]
    y1_sb = [enB(sb(f"y1_sb{i}", [128, D], F32)) for i in range(2)]
    yb_sb = [enB(sb(f"yb_sb{i}", [128, D], BF16)) for i in range(2)]
    c_zer = c_consts
    psC = ExitStack()
    enCp = psC.enter_context
    y1_ps = enCp(psn("y1_ps", [128, 1024], F32))
    scg_ps = enCp(psn("scg_ps", [128, 4 * WIN], F32))
    pT_ps = [enCp(psn(f"pT_ps{i}", [128, 1024], BF16)) for i in range(2)]
    av_ps = [enCp(psn(f"av_ps{i}", [128, 512], F32)) for i in range(2)]

    c_vw_used = [None, None]
    c_x_free = [None, None]
    c_y1sb_free = [None, None]
    c_yb_free = [None, None]
    c_out_store = [None, None]
    c_y1_free = None
    c_scg_free = None
    c_exp_by = [None, None]
    c_stt_prev = None
    c_stt_by = [None, None]
    c_pn_by = [None, None]
    c_tr_by = [None, None]
    c_pTev_by = [None, None]
    c_av_by = [None, None]
    c_avev_by = [None, None]
    c_wo_by = [None, None]
    c_wo_last = None
    c_pn_key = {}

    def emit_A_hg(t, hg, first_bar):
        wb = t * 128 + 32
        moff = 0 if t == 0 else (2 * WIN if t == NTI - 1 else WIN)
        if True:
            par = hg % 2

            def mm_scg(te, t=t, hg=hg, wb=wb):
                for i in range(2):
                    c = 2 * hg + i
                    te.matmul(scg_ps[:, 2 * i * WIN:(2 * i + 1) * WIN],
                              qT_s[0:64, c * T + t * 128:c * T + (t + 1) * 128],
                              knT_s[0:64, c * HT + wb:c * HT + wb + WIN],
                              start=True, stop=True)
                te.matmul(y1_ps[:, 0:128], zer_sb[:, :], zer_sb[:, :],
                          start=False, stop=False, skip_group_check=True)
                for i in range(2):
                    c = 2 * hg + i
                    te.matmul(scg_ps[:, (2 * i + 1) * WIN:(2 * i + 2) * WIN],
                              qT_s[64:128, c * T + t * 128:c * T + (t + 1) * 128],
                              knT_s[64:128, c * HT + wb:c * HT + wb + WIN],
                              start=True, stop=True)
                return te.matmul(y1_ps[:, 0:128], zer_sb[:, :], zer_sb[:, :],
                                 start=False, stop=False, skip_group_check=True)
            P.op("tensor", mm_scg, waits=[st["scg_free"], c_zer] + first_bar,
                 inc=sT)
            c_sc = (sT, sT.v)

            def f_exp(s, t=t, hg=hg, par=par):
                for i in range(4):
                    h = 4 * hg + i
                    ins = s.activation(e_sb[par][:, i * WIN:(i + 1) * WIN],
                                       scg_ps[:, i * WIN:(i + 1) * WIN], AF.Exp,
                                       scale=rqT_s[:, t * NH + h:t * NH + h + 1])
                return ins
            P.op("scalar", f_exp,
                 waits=[c_sc, st["stt_by"][par]] + first_bar, inc=sS)
            c_exp = (sS, sS.v)
            st["scg_free"] = c_exp
            st["exp_by"][par] = c_exp

            def f_stt(v, t=t, hg=hg, par=par, moff=moff):
                for i in range(4):
                    h = 4 * hg + i
                    ins = v.scalar_tensor_tensor(
                        pn_sb[hg][:, i * WIN:(i + 1) * WIN],
                        e_sb[par][:, i * WIN:(i + 1) * WIN], 1.0,
                        masks[:, moff:moff + WIN], ALU.mult, ALU.mult,
                        accum_out=sums[:, 4 * hg + i:4 * hg + i + 1])
                return ins
            P.op("vector", f_stt,
                 waits=[c_exp, c_bsmall, st["tr_by"][hg]] + first_bar, inc=sV)
            st["stt_by"][par] = (sV, sV.v)
            P.op("vector", lambda v, hg=hg: v.reciprocal(
                rsum[:, 4 * hg:4 * hg + 4], sums[:, 4 * hg:4 * hg + 4]),
                 waits=[st["stt_by"][par]], inc=sV)
            c_rs2 = (sV, sV.v)

            def f_pn(v, hg=hg):
                for i in range(4):
                    h = 4 * hg + i
                    ins = v.tensor_scalar(pn_sb[hg][:, i * WIN:(i + 1) * WIN],
                                          pn_sb[hg][:, i * WIN:(i + 1) * WIN],
                                          rsum[:, h:h + 1], None, ALU.mult)
                return ins
            P.op("vector", f_pn, waits=[c_rs2], inc=sV)
            c_pn_key[(t, hg)] = (sV, sV.v)

    ld = {}

    def emit_loads(t):
        sl2 = t % 2
        hb = HALO + t * 128
        P.op("sync", lambda sy, t=t, sl2=sl2: sy.dma_start(
            vw_a[sl2][:, :], v_dram[:, t * 128 + 32:t * 128 + 160], transpose=True),
             waits=[c_vw_used[sl2], c_p1a_bar[3]], inc=sDs, inc_n=16)
        P.op("sync", lambda sy, t=t, sl2=sl2: sy.dma_start(
            vw_b[sl2][:, :], v_dram[:, t * 128 + 160:t * 128 + 288], transpose=True),
             inc=sDs, inc_n=16)
        P.op("sync", lambda sy, t=t, sl2=sl2: sy.dma_start(
            vw_c[sl2][:, :], v_dram[:, t * 128 + 96:t * 128 + 224], transpose=True),
             inc=sDs, inc_n=16)
        c_vw = (sDs, sDs.v)
        P.op("gpsimd", lambda g, hb=hb, sl2=sl2: g.dma_start(
            xL_sb[sl2][:, :], xh[hb:hb + 128, :]),
             waits=[c_x_free[sl2]], inc=sDg, inc_n=16)
        c_xl = (sDg, sDg.v)
        P.op("sync", lambda sy, t=t, sl2=sl2: sy.dma_start(
            yb_sb[sl2][:, :], y2_dram[:, t * 128:(t + 1) * 128], transpose=True),
             waits=[c_y2_all, c_yb_free[sl2]], inc=sDs, inc_n=16)
        c_yb = (sDs, sDs.v)
        ld[t] = (c_vw, c_xl, c_yb)

    def emit_B_pre(t, first_bar):
        sl2 = t % 2
        c_vw, c_xl, c_yb = ld[t]

        def mm_bo(te):
            te.matmul(y1_ps[:, 0:512], oner0b[:, :], bor[:, 0:512],
                      start=True, stop=False, skip_group_check=True)
            return te.matmul(y1_ps[:, 512:1024], oner0b[:, :], bor[:, 512:1024],
                             start=True, stop=False, skip_group_check=True)
        P.op("tensor", mm_bo, waits=[c_bsmall, st["y1_free"]] + first_bar, inc=sT)
        return c_vw, c_xl, c_yb

    def emit_B_sb(t, sb2, c_vw, first_bar):
        sl2 = t % 2
        if True:
            pairs = (2 * sb2, 2 * sb2 + 1)
            par = sb2 % 2
            slb = sb2 % 2

            def f_tr(te, t=t, sb2=sb2, slb=slb):
                for j in range(2):
                    i0 = 2 * j * WIN
                    o0 = 512 * j
                    te.transpose(pT_ps[slb][:, o0:o0 + 128],
                                 pn_sb[sb2][:, i0:i0 + 128], identb[:, :])
                    te.transpose(pT_ps[slb][0:64, o0 + 128:o0 + 256],
                                 pn_sb[sb2][:, i0 + 128:i0 + WIN], identb[:, :])
                    te.transpose(pT_ps[slb][0:64, o0 + 256:o0 + 384],
                                 pn_sb[sb2][:, i0 + WIN:i0 + WIN + 64], identb[:, :])
                    ins = te.transpose(pT_ps[slb][:, o0 + 384:o0 + 512],
                                       pn_sb[sb2][:, i0 + WIN + 64:i0 + 2 * WIN],
                                       identb[:, :])
                return ins
            P.op("tensor", f_tr,
                 waits=[c_pn_key[(t, sb2)], st["pTev_by"][slb]] + first_bar, inc=sT)
            st["tr_by"][sb2] = (sT, sT.v)
            c_tr = st["tr_by"][sb2]

            if sb2 % 2 == 0:
                P.op("scalar", lambda s, slb=slb: s.copy(pT_sb[slb][:, :],
                                                         pT_ps[slb][:, :]),
                     waits=[c_tr, st["av_by"][slb]], inc=sS)
                st["pTev_by"][slb] = (sS, sS.v)
            else:
                P.op("vector", lambda v, slb=slb: v.tensor_copy(pT_sb[slb][:, :],
                                                                pT_ps[slb][:, :]),
                     waits=[c_tr, st["av_by"][slb]], inc=sV)
                st["pTev_by"][slb] = (sV, sV.v)

            def f_av(te, pairs=pairs, sl2=sl2, slb=slb):
                for j, p in enumerate(pairs):
                    A, Bh = 2 * p, 2 * p + 1
                    co = j * 256
                    po = j * 512
                    te.matmul(av_ps[slb][0:64, co:co + 128],
                              vw_a[sl2][:, A * 64:(A + 1) * 64],
                              pT_sb[slb][:, po:po + 128], start=True, stop=False,
                              skip_group_check=True)
                    te.matmul(av_ps[slb][0:64, co:co + 128],
                              vw_b[sl2][0:64, A * 64:(A + 1) * 64],
                              pT_sb[slb][0:64, po + 128:po + 256], start=False,
                              stop=True, skip_group_check=True)
                    te.matmul(av_ps[slb][0:64, co + 128:co + 256],
                              vw_a[sl2][0:64, Bh * 64:(Bh + 1) * 64],
                              pT_sb[slb][0:64, po + 256:po + 384], start=True,
                              stop=False, skip_group_check=True)
                    ins = te.matmul(av_ps[slb][0:64, co + 128:co + 256],
                                    vw_c[sl2][:, Bh * 64:(Bh + 1) * 64],
                                    pT_sb[slb][:, po + 384:po + 512], start=False,
                                    stop=True, skip_group_check=True)
                return ins
            P.op("tensor", f_av,
                 waits=[st["pTev_by"][slb], c_vw, st["avev_by"][slb]] + first_bar,
                 inc=sT)
            st["av_by"][slb] = (sT, sT.v)

            if sb2 % 2 == 0:
                def f_ev(v, slb=slb):
                    v.tensor_copy(outT_sb[slb][0:64, :], av_ps[slb][0:64, 0:128])
                    v.tensor_copy(outT_sb[slb][64:128, :], av_ps[slb][0:64, 128:256])
                    v.tensor_copy(out2_sb[slb][0:64, :], av_ps[slb][0:64, 256:384])
                    return v.tensor_copy(out2_sb[slb][64:128, :],
                                         av_ps[slb][0:64, 384:512])
                P.op("vector", f_ev, waits=[st["av_by"][slb], st["wo_by"][slb]],
                     inc=sV)
                st["avev_by"][slb] = (sV, sV.v)
            else:
                def f_ev(s, slb=slb):
                    s.copy(outT_sb[slb][0:64, :], av_ps[slb][0:64, 0:128])
                    s.copy(outT_sb[slb][64:128, :], av_ps[slb][0:64, 128:256])
                    s.copy(out2_sb[slb][0:64, :], av_ps[slb][0:64, 256:384])
                    return s.copy(out2_sb[slb][64:128, :], av_ps[slb][0:64, 384:512])
                P.op("scalar", f_ev, waits=[st["av_by"][slb], st["wo_by"][slb]],
                     inc=sS)
                st["avev_by"][slb] = (sS, sS.v)

            def f_wo(te, pairs=pairs, slb=slb):
                for j, p in enumerate(pairs):
                    src_t = outT_sb[slb] if j == 0 else out2_sb[slb]
                    te.matmul(y1_ps[:, 0:512], src_t[:, :],
                              WoT_s[:, p * D:p * D + 512], start=False, stop=False,
                              skip_group_check=True)
                    ins = te.matmul(y1_ps[:, 512:1024], src_t[:, :],
                                    WoT_s[:, p * D + 512:(p + 1) * D], start=False,
                                    stop=(p == 7), skip_group_check=True)
                return ins
            P.op("tensor", f_wo, waits=[st["avev_by"][slb], c_WoT], inc=sT)
            st["wo_by"][slb] = (sT, sT.v)

    def emit_B_post(t, c_xl, c_yb):
        sl2 = t % 2
        c_wo_last = st["wo_by"][1]

        P.op("scalar", lambda s, sl2=sl2: s.copy(y1_sb[sl2][:, :], y1_ps[:, :]),
             waits=[c_wo_last, c_y1sb_free[sl2]], inc=sS)
        st["y1_free"] = (sS, sS.v)
        P.op("vector", lambda v, sl2=sl2: v.scalar_tensor_tensor(
            out_sb[sl2][:, :], y1_sb[sl2][:, :], 1.0, xL_sb[sl2][:, :],
            ALU.mult, ALU.add),
             waits=[st["y1_free"], c_xl, c_out_store[sl2]], inc=sV)
        P.op("vector", lambda v, sl2=sl2: v.tensor_add(
            out_sb[sl2][:, :], out_sb[sl2][:, :], yb_sb[sl2][:, :]),
             waits=[c_yb, (sV, sV.v)], inc=sV)
        c_fin = (sV, sV.v)
        c_y1sb_free[sl2] = c_fin
        c_yb_free[sl2] = c_fin
        c_x_free[sl2] = c_fin
        c_vw_used[sl2] = st["av_by"][1]
        P.op("gpsimd", lambda g, t=t, sl2=sl2: g.dma_start(
            out[t * 128:(t + 1) * 128, :], out_sb[sl2][:, :]),
             waits=[c_fin], inc=sOut, inc_n=16)
        c_out_store[sl2] = (sOut, sOut.v)

    st = {"scg_free": None, "exp_by": [None, None], "stt_by": [None, None],
          "tr_by": [None, None, None, None], "pTev_by": [None, None],
          "av_by": [None, None], "avev_by": [None, None], "wo_by": [None, None],
          "y1_free": None}

    for hg in range(4):
        emit_A_hg(0, hg, c_p1a_bar)
    for t in range(NTI):
        fb2 = c_p1a_bar if t == 0 else []
        emit_loads(t)
        c_vw, c_xl, c_yb = emit_B_pre(t, fb2)
        for i in range(4):
            emit_B_sb(t, i, c_vw, fb2)
            if t + 1 < NTI:
                emit_A_hg(t + 1, i, fb2)
        emit_B_post(t, c_xl, c_yb)

    P.op("gpsimd", lambda g: g.engine_nop(), waits=[(sOut, 16 * NTI)])

    return finish()


def _masks_for(q0):
    def m(t):
        qi = np.arange(128)[:, None]
        kk = np.arange(WIN)[None, :]
        qg = q0 + t * 128 + qi
        kg = q0 + t * 128 - 32 + kk
        allowed = ((np.abs(kg - qg) <= 32) & ((kg - qg) % 4 == 0)
                   & (kg >= 0) & (kg < 8192))
        return np.ascontiguousarray(allowed.astype(np.float32))
    return m(0), m(7), m(NTI - 1)


_NC_CACHE = None


def kernel(**inputs):
    global _NC_CACHE
    f = lambda k: np.ascontiguousarray(np.asarray(inputs[k], dtype=np.float32))
    x = f("L0_attn_x")
    Wq, bq = f("L0_attn_Wq"), f("L0_attn_bq")
    Wk, bk = f("L0_attn_Wk"), f("L0_attn_bk")
    Wv, bv = f("L0_attn_Wv"), f("L0_attn_bv")
    Wo, bo = f("L0_attn_Wo"), f("L0_attn_bo")
    gW = f("L0_ffn_gate_W")
    W1, b1 = f("L0_ffn_W1"), f("L0_ffn_b1")
    W2, b2 = f("L0_ffn_W2"), f("L0_ffn_b2")
    Ws1, bs1 = f("L0_ffn_Ws1"), f("L0_ffn_bs1")
    Ws2, bs2 = f("L0_ffn_Ws2"), f("L0_ffn_bs2")
    rsv = f("L0_ffn_route_scale")

    shared = {
        "Wq": Wq, "Wk": Wk, "Wv": Wv, "Wo": Wo,
        "bqt": np.ascontiguousarray(bq.reshape(8, 128).T),
        "bkt": np.ascontiguousarray(bk.reshape(8, 128).T),
        "bvt": np.ascontiguousarray(bv.reshape(8, 128).T),
        "bo_row": bo.reshape(1, D),
        "gW": gW, "W1": W1, "W2": W2,
        "b1t": np.ascontiguousarray(b1.reshape(8, 4, 128).transpose(2, 0, 1)
                                    .reshape(128, 32)),
        "B2": b2,
        "Ws1": Ws1, "Ws2": Ws2,
        "bs1t": np.ascontiguousarray(bs1.reshape(4, 128).T),
        "bs2t": np.ascontiguousarray(bs2.reshape(8, 128).T),
        "rs": np.concatenate([rsv.reshape(1, 1),
                              np.zeros((127, 1), np.float32)], axis=0),
    }

    in_maps = []
    for core in range(8):
        b, qq = core // 4, core % 4
        q0 = qq * T
        xh = np.zeros((HT, D), np.float32)
        lo = max(0, q0 - HALO)
        hi = min(8192, q0 + T + HALO)
        xh[lo - (q0 - HALO):hi - (q0 - HALO)] = x[b, lo:hi]
        mf, mi, ml = _masks_for(q0)
        m = dict(shared)
        m.update({"xh": xh, "mfirst": mf, "mint": mi, "mlast": ml})
        in_maps.append(m)

    if _NC_CACHE is None:
        _NC_CACHE = build_nc()
    nc = _NC_CACHE
    res = run_bass_kernel_spmd(nc, in_maps, core_ids=list(range(8)))

    outp = np.empty((2, 8192, D), np.float32)
    for core in range(8):
        b, qq = core // 4, core % 4
        outp[b, qq * T:(qq + 1) * T] = np.asarray(res.results[core]["out"],
                                                  dtype=np.float32)
    return outp


if __name__ == "__main__":
    import sys
    sys.path.insert(0, "/root/problem")
    from reference import setup_inputs, reference
    ins = setup_inputs(0)
    ref = np.asarray(reference(**ins))
    got = kernel(**{k: np.asarray(v) for k, v in ins.items()})
    err = np.mean((got - ref) ** 2) / np.mean(ref ** 2)
    print("residual variance ratio:", err)


# version 26 (final)
# speedup vs baseline: 1.0994x; 1.0090x over previous
import os
import numpy as np
import ml_dtypes
from contextlib import ExitStack

import concourse.bass as bass
import concourse.mybir as mybir
from concourse.bass_utils import run_bass_kernel_spmd

F32 = mybir.dt.float32
BF16 = mybir.dt.bfloat16
AF = mybir.ActivationFunctionType
ALU = mybir.AluOpType
AX = mybir.AxisListType

D = 1024
NH = 16
NE = 8
HF = 512
T = 2048
HALO = 64
HT = T + 2 * HALO
NTI = T // 128
NHT = HT // 128
WIN = 192
FT = 256
NFT = T // FT
LN_EPS = 1e-5
ATT_EPS = 1e-6


class Sem:
    def __init__(self, handle):
        self.h = handle
        self.v = 0


class Ev:
    __slots__ = ("eng", "fn", "waits", "incs")

    def __init__(self, eng, fn, waits, incs):
        self.eng, self.fn, self.waits, self.incs = eng, fn, waits, incs


class Prog:
    def __init__(self):
        self.events = []

    def op(self, eng, fn, waits=(), inc=None, inc_n=1):
        incs = []
        if inc is not None:
            inc.v += inc_n
            incs.append((inc, inc_n))
        self.events.append(Ev(eng, fn, [w for w in waits if w is not None], incs))
        return (inc, inc.v) if inc is not None else None

    def emit(self, engine_name, eng):
        for ev in self.events:
            if ev.eng != engine_name:
                continue
            for (sem, val) in ev.waits:
                eng.wait_ge(sem.h, val)
            ins = ev.fn(eng)
            for (sem, n) in ev.incs:
                ins.then_inc(sem.h, n)


def build_nc():
    nc = bass.Bass("TRN2", target_bir_lowering=False)

    xh = nc.dram_tensor("xh", [HT, D], F32, kind="ExternalInput").ap()
    Wq = nc.dram_tensor("Wq", [D, D], F32, kind="ExternalInput").ap()
    Wk = nc.dram_tensor("Wk", [D, D], F32, kind="ExternalInput").ap()
    Wv = nc.dram_tensor("Wv", [D, D], F32, kind="ExternalInput").ap()
    Wo = nc.dram_tensor("Wo", [D, D], F32, kind="ExternalInput").ap()
    bqt = nc.dram_tensor("bqt", [128, 8], F32, kind="ExternalInput").ap()
    bkt = nc.dram_tensor("bkt", [128, 8], F32, kind="ExternalInput").ap()
    bvt = nc.dram_tensor("bvt", [128, 8], F32, kind="ExternalInput").ap()
    bo_row = nc.dram_tensor("bo_row", [1, D], F32, kind="ExternalInput").ap()
    gW = nc.dram_tensor("gW", [D, NE], F32, kind="ExternalInput").ap()
    W1 = nc.dram_tensor("W1", [NE, D, HF], F32, kind="ExternalInput").ap()
    b1t = nc.dram_tensor("b1t", [128, NE * 4], F32, kind="ExternalInput").ap()
    W2 = nc.dram_tensor("W2", [NE, HF, D], F32, kind="ExternalInput").ap()
    B2 = nc.dram_tensor("B2", [NE, D], F32, kind="ExternalInput").ap()
    Ws1 = nc.dram_tensor("Ws1", [D, HF], F32, kind="ExternalInput").ap()
    bs1t = nc.dram_tensor("bs1t", [128, 4], F32, kind="ExternalInput").ap()
    Ws2 = nc.dram_tensor("Ws2", [HF, D], F32, kind="ExternalInput").ap()
    bs2t = nc.dram_tensor("bs2t", [128, 8], F32, kind="ExternalInput").ap()
    rs = nc.dram_tensor("rs", [128, 1], F32, kind="ExternalInput").ap()
    mfirst = nc.dram_tensor("mfirst", [128, WIN], F32, kind="ExternalInput").ap()
    mint = nc.dram_tensor("mint", [128, WIN], F32, kind="ExternalInput").ap()
    mlast = nc.dram_tensor("mlast", [128, WIN], F32, kind="ExternalInput").ap()

    out = nc.dram_tensor("out", [T, D], F32, kind="ExternalOutput").ap()

    wq_bf = nc.dram_tensor("wq_bf", [D, D], BF16, kind="Internal").ap()
    wk_bf = nc.dram_tensor("wk_bf", [D, D], BF16, kind="Internal").ap()
    wv_bf = nc.dram_tensor("wv_bf", [D, D], BF16, kind="Internal").ap()
    wo_bf = nc.dram_tensor("wo_bf", [D, D], BF16, kind="Internal").ap()
    _dbgkind = "Internal" if os.environ.get("KDBG", "") != "1" else "ExternalOutput"
    xn_dram = nc.dram_tensor("xn_dram", [HT, D], BF16, kind=_dbgkind).ap()
    v_dram = nc.dram_tensor("v_dram", [D, HT + 128], BF16, kind=_dbgkind).ap()
    y2_dram = nc.dram_tensor("y2_dram", [D, T], BF16, kind=_dbgkind).ap()

    identb_d = nc.inline_tensor(np.eye(128, dtype=ml_dtypes.bfloat16), "c_identb").ap()
    identf_d = nc.inline_tensor(np.eye(128, dtype=np.float32), "c_identf").ap()
    _r0 = np.zeros((128, 128), np.float32); _r0[0, :] = 1.0
    oner0b_d = nc.inline_tensor(_r0.astype(ml_dtypes.bfloat16), "c_oner0b").ap()
    oner0f_d = nc.inline_tensor(_r0, "c_oner0f").ap()
    E_np = np.zeros((128, 8, 16), np.float32)
    for c in range(8):
        E_np[0:64, c, 2 * c] = 1.0
        E_np[64:128, c, 2 * c + 1] = 1.0
    Esel_d = nc.inline_tensor(
        E_np.reshape(128, 128).astype(ml_dtypes.bfloat16), "c_Esel").ap()
    S_np = np.zeros((16, 8, 128), np.float32)
    for c in range(8):
        S_np[2 * c, c, 0:64] = 1.0
        S_np[2 * c + 1, c, 64:128] = 1.0
    Ssel_d = nc.inline_tensor(S_np.reshape(16, 8 * 128), "c_Ssel").ap()
    Se_np = np.zeros((8, 8, 128), np.float32)
    for e in range(NE):
        Se_np[e, e, :] = 1.0
    Sesel_d = nc.inline_tensor(Se_np.reshape(8, 8 * 128).astype(ml_dtypes.bfloat16), "c_Sesel").ap()

    P = Prog()
    ctx = ExitStack()
    en = ctx.enter_context
    sb = nc.sbuf_tensor
    psn = nc.psum_tensor

    identf = en(sb("identf", [128, 128], F32))
    zer_sb = en(sb("zer_sb", [128, 128], BF16))
    oner0b = en(sb("oner0b", [128, 128], BF16))
    oner0f = en(sb("oner0f", [128, 128], F32))
    consts = en(sb("consts", [128, 4], F32))
    rs_s = en(sb("rs_s", [128, 1], F32))
    rsb = en(sb("rsb", [128, 1], F32))
    xnT_s = en(sb("xnT_s", [128, 8 * HT], BF16))
    stat = en(sb("stat", [128, 20], F32))
    g_scr = en(sb("g_scr", [128, 32], F32))

    sDg = Sem(en(nc.semaphore("sDg")))
    sDs = Sem(en(nc.semaphore("sDs")))
    sT = Sem(en(nc.semaphore("sT")))
    sS = Sem(en(nc.semaphore("sS")))
    sV = Sem(en(nc.semaphore("sV")))
    sOut = Sem(en(nc.semaphore("sOut")))

    def bar():
        return [(sT, sT.v), (sS, sS.v), (sV, sV.v), (sDg, sDg.v), (sDs, sDs.v)]

    def dmag(dst, src):
        return lambda g: g.dma_start(dst, src)

    _UPTO = int(os.environ.get("KUPTO", "9"))
    _SKIP = set(x for x in os.environ.get("KSKIP", "").split(",") if x)
    _DBG = os.environ.get("KDBG", "") == "1"

    def finish():
        with nc.Block() as block:
            @block.gpsimd
            def _(g):
                P.emit("gpsimd", g)

            @block.sync
            def _(sy):
                P.emit("sync", sy)

            @block.tensor
            def _(te):
                P.emit("tensor", te)

            @block.scalar
            def _(s):
                P.emit("scalar", s)

            @block.vector
            def _(v):
                P.emit("vector", v)
        return nc

    def stub_out(waits):
        P.op("gpsimd", lambda g: g.dma_start(out[:, :], xh[HALO:HALO + T, :]),
             waits=waits, inc=sOut, inc_n=16)
        P.op("gpsimd", lambda g: g.engine_nop(), waits=[(sOut, 16)])

    sbA = ExitStack()
    enA = sbA.enter_context
    Sesel = enA(sb("Sesel", [8, 8 * 128], BF16))
    b1_s = enA(sb("b1_s", [128, NE * 4], F32))
    bs1_s = enA(sb("bs1_s", [128, 4], F32))
    bs2_s = enA(sb("bs2_s", [128, 8], F32))
    B2_s = enA(sb("B2_s", [8, D], BF16))
    gateW_s = enA(sb("gateW_s", [128, 8 * NE], BF16))
    gT_s = enA(sb("gT_s", [8, T], BF16))
    x_sb = [enA(sb(f"x_sb{i}", [128, D], F32)) for i in range(2)]
    xn_sb = [enA(sb(f"xn_sb{i}", [128, D], BF16)) for i in range(2)]
    Ws1_s = enA(sb("Ws1_s", [128, 8 * HF], BF16))
    Ws2_s = enA(sb("Ws2_s", [128, 4 * D], BF16))
    W1a = enA(sb("W1a", [128, NE * 4096], BF16))
    W2a = enA(sb("W2a", [128, NE * 4096], BF16))
    hg_sb = [enA(sb(f"hg_sb{i}", [128, FT], BF16)) for i in range(2)]
    hsT_sb = enA(sb("hsT_sb", [128, 4 * FT], BF16))
    hgg_sb2 = [hsT_sb, enA(sb("hgg_b", [128, 4 * FT], BF16))]
    y2sb = enA(sb("y2sb", [128, 4 * FT], BF16))
    zsc_sb = enA(sb("zsc_sb", [8, 128], F32))
    psA = ExitStack()
    enAp = psA.enter_context
    sm_ps = enAp(psn("sm_ps", [128, 512], F32))
    pA = [enAp(psn(f"pA{i}", [128, 512], F32)) for i in range(3)]
    y2p = enAp(psn("y2p", [128, 2048], F32))

    for dst, srcap in [
        (identf[:, :], identf_d),
        (oner0b[:, :], oner0b_d), (oner0f[:, :], oner0f_d),
        (Sesel[:, :], Sesel_d),
        (b1_s[:, :], b1t), (bs1_s[:, :], bs1t), (bs2_s[:, :], bs2t),
        (B2_s[:, :], B2), (rs_s[:, :], rs),
        (gateW_s[:, :].rearrange("p (kc j) -> p kc j", kc=8),
         gW.rearrange("(kc p) j -> p kc j", p=128)),
        (Ws1_s[:, :].rearrange("p (kc f) -> p kc f", kc=8),
         Ws1.rearrange("(kc p) f -> p kc f", p=128)),
        (Ws2_s[:, :].rearrange("p (kc f) -> p kc f", kc=4),
         Ws2.rearrange("(kc p) f -> p kc f", p=128)),
    ]:
        P.op("gpsimd", dmag(dst, srcap), inc=sDg, inc_n=16)
    c_small = (sDg, sDg.v)

    def f_consts(v):
        v.memset(consts[:, 0:1], LN_EPS)
        v.memset(consts[:, 1:2], ATT_EPS)
        v.memset(zer_sb[:, :], 0.0)
        return v.memset(consts[:, 2:3], 0.0)
    P.op("vector", f_consts, inc=sV)
    c_consts = (sV, sV.v)

    P.op("tensor", lambda t: t.matmul(sm_ps[:, 0:1], oner0f[:, :], rs_s[:, :],
                                      start=True, stop=True),
         waits=[c_consts, c_small], inc=sT)
    c_rsmm = (sT, sT.v)
    P.op("vector", lambda v: v.tensor_copy(rsb[:, :], sm_ps[:, 0:1]),
         waits=[c_rsmm], inc=sV)
    c_rsb = (sV, sV.v)

    slot_store = {0: None, 1: None}
    slot_free = {0: None, 1: None}
    prev_var = None
    c_xn_tile = {}
    for t in range(NHT):
        sl = t % 2
        P.op("sync", lambda sy, sl=sl, t=t: sy.dma_start(
            x_sb[sl][:, :], xh[t * 128:(t + 1) * 128, :]),
             waits=[slot_free[sl]], inc=sDs, inc_n=16)
        c_xload = (sDs, sDs.v)
        def f_bn(v, sl=sl):
            v.bn_stats(stat[:, 6:12], x_sb[sl][:, 0:512])
            return v.bn_stats(stat[:, 12:18], x_sb[sl][:, 512:1024])
        P.op("vector", f_bn, waits=[c_xload, prev_var], inc=sV)
        c_bn = (sV, sV.v)
        P.op("vector", lambda v: v.bn_aggr(stat[:, 2:4], stat[:, 6:18]),
             waits=[c_bn], inc=sV)
        c_var = (sV, sV.v)
        prev_var = c_var
        P.op("scalar", lambda s: s.activation(stat[:, 5:6], stat[:, 3:4], AF.Ln,
                                              bias=consts[:, 0:1]),
             waits=[c_var, c_consts], inc=sS)
        c_ln = (sS, sS.v)
        P.op("scalar", lambda s: s.activation(stat[:, 5:6], stat[:, 5:6], AF.Exp,
                                              scale=-0.5), waits=[c_ln], inc=sS)
        c_rstd = (sS, sS.v)
        P.op("vector", lambda v, sl=sl: v.tensor_scalar(
            xn_sb[sl][:, :], x_sb[sl][:, :], stat[:, 2:3], stat[:, 5:6],
            ALU.subtract, ALU.mult),
             waits=[c_rstd, slot_store[sl]], inc=sV)
        c_xn = (sV, sV.v)
        slot_free[sl] = c_xn
        P.op("gpsimd", dmag(xn_dram[t * 128:(t + 1) * 128, :], xn_sb[sl][:, :]),
             waits=[c_xn], inc=sDg, inc_n=16)
        slot_store[sl] = (sDg, sDg.v)
        c_xn_tile[t] = (sDg, sDg.v)
    c_xn_all = (sDg, sDg.v)

    W1r = W1.rearrange("e (kc p) f -> p e kc f", p=128)
    W2r = W2.rearrange("e (kc p) f -> p e kc f", p=128)
    for e in range(NE):
        P.op("gpsimd", dmag(
            W1a[:, e * 4096:(e + 1) * 4096].rearrange("p (kc f) -> p kc f", kc=8),
            W1r[:, e:e + 1, :, :].rearrange("p one kc f -> p (one kc) f")),
             inc=sDg, inc_n=16)
        P.op("gpsimd", dmag(
            W2a[:, e * 4096:(e + 1) * 4096].rearrange("p (kc f) -> p kc f", kc=4),
            W2r[:, e:e + 1, :, :].rearrange("p one kc f -> p (one kc) f")),
             inc=sDg, inc_n=16)
    c_expw = (sDg, sDg.v)

    for wbf, w in [(wq_bf, Wq), (wk_bf, Wk), (wv_bf, Wv), (wo_bf, Wo)]:
        P.op("gpsimd", dmag(wbf, w), inc=sDg, inc_n=16)
    c_wcast = (sDg, sDg.v)


    bands = [576, 1152, HT]
    c_band = []
    b_prev = 0
    for bk in bands:
        mark = c_xn_tile[(bk + 127) // 128 - 1]
        for c in range(8):
            P.op("sync", lambda sy, c=c, b0=b_prev, b1=bk: sy.dma_start(
                xnT_s[:, c * HT + b0:c * HT + b1],
                xn_dram[b0:b1, c * 128:(c + 1) * 128],
                transpose=True), waits=[mark], inc=sDs, inc_n=16)
        c_band.append((sDs, sDs.v))
        b_prev = bk
    c_xnT = c_band[-1]

    def xnT_ready(last_col):
        for bk, cb in zip(bands, c_band):
            if last_col <= bk:
                return cb
        return c_xnT

    if _UPTO <= 0:
        stub_out(bar())
        return finish()

    gst = {"c_gdone": None, "c_ez": None, "c_zev": None, "c_g": None}

    def gate_stepA(t):
        hb = HALO + t * 128

        def mm_gate(te, hb=hb):
            for kc in range(8):
                ins = te.matmul(sm_ps[0:8, 0:128],
                                gateW_s[:, kc * 8:(kc + 1) * 8],
                                xnT_s[:, kc * HT + hb:kc * HT + hb + 128],
                                start=(kc == 0), stop=(kc == 7))
            return ins
        P.op("tensor", mm_gate,
             waits=[xnT_ready(hb + 128), c_small, gst["c_gdone"]], inc=sT)
        c_zmm = (sT, sT.v)
        P.op("vector", lambda v: v.tensor_copy(zsc_sb[:, :], sm_ps[0:8, 0:128]),
             waits=[c_zmm], inc=sV)
        gst["c_zev"] = (sV, sV.v)

    def gate_stepB(t):
        P.op("tensor", lambda te: te.transpose(sm_ps[:, 128:136], zsc_sb[:, :],
                                               identf[0:8, 0:8]),
             waits=[gst["c_zev"], gst["c_ez"]], inc=sT)
        c_z = (sT, sT.v)
        P.op("scalar", lambda s: s.activation(g_scr[:, 8:16], sm_ps[:, 128:136],
                                              AF.Exp, scale=-1.0),
             waits=[c_z, gst["c_gdone"]], inc=sS)
        gst["c_ez"] = (sS, sS.v)

    def gate_stepC(t):
        prev = gst["c_ez"]
        for fn in [
            lambda v: v.tensor_scalar(g_scr[:, 8:16], g_scr[:, 8:16], 1.0, None,
                                      ALU.add),
            lambda v: v.reciprocal(g_scr[:, 0:8], g_scr[:, 8:16]),
            lambda v: v.tensor_scalar(g_scr[:, 0:8], g_scr[:, 0:8], rsb[:, 0:1],
                                      None, ALU.mult),
            lambda v: v.max(out=g_scr[:, 16:24], in_=g_scr[:, 0:8]),
            lambda v: v.tensor_add(g_scr[:, 24:25], g_scr[:, 16:17], g_scr[:, 17:18]),
            lambda v: v.reciprocal(g_scr[:, 25:26], g_scr[:, 24:25]),
            lambda v: v.tensor_scalar(g_scr[:, 8:16], g_scr[:, 0:8], g_scr[:, 17:18],
                                      None, ALU.is_ge),
            lambda v: v.scalar_tensor_tensor(g_scr[:, 0:8], g_scr[:, 0:8],
                                             g_scr[:, 25:26], g_scr[:, 8:16],
                                             ALU.mult, ALU.mult),
        ]:
            P.op("vector", fn, waits=[prev, c_rsb], inc=sV)
            prev = (sV, sV.v)
        gst["c_g"] = prev

    def gate_stepD(t):
        P.op("tensor", lambda te: te.transpose(
            sm_ps[0:8, 0:128], g_scr[:, 0:8], identf[:, :]),
             waits=[gst["c_g"], c_rsb, gst["c_gdone"]], inc=sT)
        c_gT = (sT, sT.v)
        P.op("vector", lambda v, t=t: v.tensor_copy(
            gT_s[:, t * 128:(t + 1) * 128], sm_ps[0:8, 0:128]),
             waits=[c_gT], inc=sV)
        gst["c_gdone"] = (sV, sV.v)

    def emit_gate(t):
        gate_stepA(t)
        gate_stepB(t)
        gate_stepC(t)
        gate_stepD(t)

    c_y2_store = None
    c_prev_evac = None
    c_pA_free = [None, None, None]
    c_hsT_used = None
    c_hg_used = [None, None]
    c_hgg_used = None
    c_hgg_used2 = [None, None]
    c_G_used = [None, None]
    emit_gate(0)
    emit_gate(1)
    g_ready = {0: gst["c_gdone"]}
    for ft in range(NFT):
        fb = HALO + ft * FT
        if ft + 1 < NFT:
            t0 = 2 * (ft + 1)
            gslot = {0: [(gate_stepA, t0)], 1: [(gate_stepB, t0)],
                     2: [(gate_stepC, t0)],
                     3: [(gate_stepD, t0), (gate_stepA, t0 + 1)],
                     4: [(gate_stepB, t0 + 1), (gate_stepC, t0 + 1),
                         (gate_stepD, t0 + 1)]}
        else:
            gslot = {}
        c_gdone = g_ready[ft]
        c_hs_evac = [None] * 4
        for mc in range(4):
            for fn, gt in gslot.get(mc, []):
                fn(gt)
            def mm_hs(te, mc=mc, fb=fb):
                for kc in range(8):
                    ins = te.matmul(pA[mc % 3][:, 0:FT],
                                    Ws1_s[:, kc * HF + mc * 128:kc * HF + (mc + 1) * 128],
                                    xnT_s[:, kc * HT + fb:kc * HT + fb + FT],
                                    start=(kc == 0), stop=(kc == 7))
                return ins
            P.op("tensor", mm_hs,
                 waits=[xnT_ready(fb + FT), c_small, c_pA_free[mc % 3]], inc=sT)
            c_mm = (sT, sT.v)
            P.op("scalar", lambda s, mc=mc: s.activation(
                hsT_sb[:, mc * FT:(mc + 1) * FT], pA[mc % 3][:, 0:FT], AF.Gelu,
                bias=bs1_s[:, mc:mc + 1]),
                 waits=[c_mm, c_hsT_used, c_hgg_used2[0]], inc=sS)
            c_hs_evac[mc] = (sS, sS.v)
            c_pA_free[mc % 3] = c_hs_evac[mc]

        for fn, gt in gslot.get(4, []):
            fn(gt)
        if gslot:
            g_ready[ft + 1] = gst["c_gdone"]

        def mm_y2z(te):
            for b in range(4):
                ins = te.matmul(y2p[:, b * 512:(b + 1) * 512], zer_sb[:, :],
                                Ws1_s[:, 0:512], start=True, stop=False,
                                skip_group_check=True)
            return ins
        P.op("tensor", mm_y2z, waits=[c_prev_evac, c_consts, c_small], inc=sT)
        c_y2z = (sT, sT.v)

        for mc in range(4):
            def mm_ws2(te, mc=mc):
                ins = None
                for c in range(8):
                    ins = te.matmul(y2p[:, c * FT:(c + 1) * FT],
                                    Ws2_s[:, mc * D + c * 128:mc * D + (c + 1) * 128],
                                    hsT_sb[:, mc * FT:(mc + 1) * FT],
                                    start=False, stop=False,
                                    skip_group_check=True)
                return ins
            P.op("tensor", mm_ws2, waits=[c_hs_evac[mc], c_y2z], inc=sT)
        c_hsT_used = (sT, sT.v)

        def mm_b2(te, ft=ft):
            ins = None
            for c in range(8):
                ins = te.matmul(y2p[:, c * FT:(c + 1) * FT],
                                B2_s[:, c * 128:(c + 1) * 128],
                                gT_s[:, ft * FT:(ft + 1) * FT],
                                start=False, stop=False, skip_group_check=True)
            return ins
        P.op("tensor", mm_b2, waits=[c_gdone], inc=sT)

        for e in range(NE):
            P.op("tensor", lambda te, e=e, ft=ft: te.matmul(
                sm_ps[:, 256:256 + FT],
                Sesel[:, e * 128:(e + 1) * 128],
                gT_s[:, ft * FT:(ft + 1) * FT], start=True, stop=True),
                 waits=[c_gdone, c_G_used[0]], inc=sT)
            c_G = (sT, sT.v)
            tts = []
            for mc in range(4):
                psl = (4 * e + mc) % 3
                def mm_h(te, e=e, mc=mc, fb=fb, psl=psl):
                    for kc in range(8):
                        ins = te.matmul(
                            pA[psl][:, 0:FT],
                            W1a[:, e * 4096 + kc * HF + mc * 128:
                                 e * 4096 + kc * HF + (mc + 1) * 128],
                            xnT_s[:, kc * HT + fb:kc * HT + fb + FT],
                            start=(kc == 0), stop=(kc == 7))
                    return ins
                P.op("tensor", mm_h,
                     waits=[xnT_ready(fb + FT), c_expw, c_pA_free[psl]], inc=sT)
                c_hmm = (sT, sT.v)
                P.op("scalar", lambda s, e=e, mc=mc, psl=psl: s.activation(
                    hg_sb[mc % 2][:, :], pA[psl][:, 0:FT], AF.Gelu,
                    bias=b1_s[:, e * 4 + mc:e * 4 + mc + 1]),
                     waits=[c_hmm, c_hg_used[mc % 2]], inc=sS)
                c_hg = (sS, sS.v)
                c_pA_free[psl] = c_hg
                P.op("vector", lambda v, mc=mc, e=e: v.tensor_mul(
                    hgg_sb2[e % 2][:, mc * FT:(mc + 1) * FT], hg_sb[mc % 2][:, :],
                    sm_ps[:, 256:256 + FT]),
                     waits=[c_hg, c_G, c_hgg_used2[e % 2], c_hsT_used], inc=sV)
                c_hg_used[mc % 2] = (sV, sV.v)
                tts.append((sV, sV.v))
            c_hgg = (sV, sV.v)
            c_G_used[0] = c_hgg

            for mc in range(4):
                def mm_w2(te, e=e, mc=mc):
                    ins = None
                    for c in range(8):
                        ins = te.matmul(
                            y2p[:, c * FT:(c + 1) * FT],
                            W2a[:, e * 4096 + mc * D + c * 128:
                                 e * 4096 + mc * D + (c + 1) * 128],
                            hgg_sb2[e % 2][:, mc * FT:(mc + 1) * FT],
                            start=False, stop=(e == NE - 1 and mc == 3 and c == 7),
                            skip_group_check=True)
                    return ins
                P.op("tensor", mm_w2, waits=[tts[mc]], inc=sT)
            c_hgg_used2[e % 2] = (sT, sT.v)
            c_hgg_used = (sT, sT.v)
        c_y2mm = (sT, sT.v)

        for half in range(2):
            def ev_y2(s, half=half):
                for c in range(4 * half, 4 * half + 4):
                    ins = s.activation(y2sb[:, (c - 4 * half) * FT:(c - 4 * half + 1) * FT],
                                       y2p[:, c * FT:(c + 1) * FT], AF.Identity,
                                       bias=bs2_s[:, c:c + 1])
                return ins
            P.op("scalar", ev_y2, waits=[c_y2mm, c_y2_store], inc=sS)
            c_prev_evac = (sS, sS.v)

            def st_y2(g, ft=ft, half=half):
                dst = y2_dram.rearrange("(c p) t -> p c t", p=128)[
                    :, 4 * half:4 * half + 4, ft * FT:(ft + 1) * FT]
                srcap = y2sb[:, :].rearrange("p (c f) -> p c f", c=4)
                return g.dma_start(dst, srcap)
            P.op("gpsimd", st_y2, waits=[c_prev_evac], inc=sDg, inc_n=16)
            c_y2_store = (sDg, sDg.v)
    c_y2_all = c_y2_store
    if _DBG:
        d_gT = nc.dram_tensor("d_gT", [8, T], BF16, kind="ExternalOutput").ap()
        P.op("gpsimd", lambda g: g.dma_start(d_gT, gT_s[:, :]),
             waits=[gst["c_gdone"]], inc=sDg, inc_n=16)
    c_p2_bar = bar()
    if _UPTO <= 2:
        stub_out(c_p2_bar)
        return finish()

    sbA.close()
    psA.close()
    sbB = ExitStack()
    enB = sbB.enter_context
    WoT_s = enB(sb("WoT_s", [128, 8 * D], BF16))
    knT_s = enB(sb("knT_s", [128, 8 * HT], BF16))
    qT_s = enB(sb("qT_s", [128, 8 * T], BF16))
    rqT_s = enB(sb("rqT_s", [128, NTI * NH], F32))
    masks = enB(sb("masks", [128, 3 * WIN], BF16))
    identb = enB(sb("identb", [128, 128], BF16))
    bor = enB(sb("bor", [128, D], BF16))
    sbB1 = ExitStack()
    enB1 = sbB1.enter_context
    WqT_s = enB1(sb("WqT_s", [128, 8 * D], BF16))
    WkT_s = enB1(sb("WkT_s", [128, 8 * D], BF16))
    WvT_s = enB1(sb("WvT_s", [128, 8 * D], BF16))
    kraw_s = enB1(sb("kraw_s", [128, 8 * 512], BF16))
    sq_scr = [enB1(sb(f"sq_scr{i}", [128, D], BF16)) for i in range(2)]
    rq_f = enB1(sb("rq_f", [16, 512], F32))
    rk_f = enB1(sb("rk_f", [16, 512], F32))
    Ssel = enB1(sb("Ssel", [16, 8 * 128], F32))
    Esel = enB1(sb("Esel", [128, 128], BF16))
    bv_s = enB1(sb("bv_s", [128, 8], F32))
    bq_s = enB1(sb("bq_s", [128, 8], F32))
    bk_s = enB1(sb("bk_s", [128, 8], F32))
    P.op("vector", lambda v: v.memset(bor[:, :], 0.0), waits=c_p2_bar, inc=sV)
    c_bz = (sV, sV.v)
    for dst, srcap in [
        (masks[:, 0:WIN], mfirst), (masks[:, WIN:2 * WIN], mint),
        (masks[:, 2 * WIN:3 * WIN], mlast),
        (identb[:, :], identb_d), (bor[0:1, :], bo_row),
        (Ssel[:, :], Ssel_d), (Esel[:, :], Esel_d), (bv_s[:, :], bvt),
        (bq_s[:, :], bqt), (bk_s[:, :], bkt),
    ]:
        P.op("gpsimd", dmag(dst, srcap), waits=c_p2_bar + [c_bz], inc=sDg, inc_n=16)
    c_bsmall = (sDg, sDg.v)
    psB = ExitStack()
    enBp = psB.enter_context
    sm2_ps = enBp(psn("sm2_ps", [128, 128], F32))
    pB2 = [enBp(psn(f"pB2{i}", [128, 512], F32)) for i in range(4)]
    ss_ps = enBp(psn("ss_ps", [16, 512], F32))
    rr2_ps = [enBp(psn(f"rr2_ps{i}", [128, 512], F32)) for i in range(2)]

    wt_marks = {}
    for wbf, wt, nm in [(wk_bf, WkT_s, "k"), (wv_bf, WvT_s, "v"),
                        (wq_bf, WqT_s, "q"), (wo_bf, WoT_s, "o")]:
        for c in range(8):
            P.op("sync", lambda sy, wbf=wbf, wt=wt, c=c: sy.dma_start(
                wt[:, c * D:(c + 1) * D], wbf[:, c * 128:(c + 1) * 128],
                transpose=True), waits=[c_wcast] + c_p2_bar, inc=sDs, inc_n=16)
        wt_marks[nm] = (sDs, sDs.v)
    c_WkT, c_WvT, c_WqT, c_WoT = (wt_marks["k"], wt_marks["v"],
                                  wt_marks["q"], wt_marks["o"])

    c_pB2_free = [None, None, None, None]
    c_kraw_used = None
    c_rr2_used = [None, None]
    c_sq_used = [None, None]
    c_ss_free = None

    k_macs = [(0, 512), (512, 512), (1024, 512), (1536, 512), (2048, 128)]
    st1 = {"ss_free": None, "kraw_used": None, "smT_used": None}

    def emit_proj_k(i):
        off, w = k_macs[i]
        prev_tt = st1.get(("tt", i - 1))
        for cch in range(8):
            sl = cch % 4

            def mm_k(te, cch=cch, off=off, w=w, sl=sl):
                for kc in range(8):
                    ins = te.matmul(pB2[sl][:, 0:w],
                                    WkT_s[:, kc * D + cch * 128:kc * D + (cch + 1) * 128],
                                    xnT_s[:, kc * HT + off:kc * HT + off + w],
                                    start=(kc == 0), stop=(kc == 7))
                return ins
            P.op("tensor", mm_k, waits=[c_WkT, c_xnT, c_pB2_free[sl]], inc=sT)
            c_kmm = (sT, sT.v)
            P.op("scalar", lambda s, cch=cch, w=w, sl=sl: s.activation(
                kraw_s[:, cch * 512:cch * 512 + w], pB2[sl][:, 0:w], AF.Identity,
                bias=bk_s[:, cch:cch + 1]),
                 waits=[c_kmm, c_bsmall, prev_tt[cch] if prev_tt else None], inc=sS)
            c_kev = (sS, sS.v)
            c_pB2_free[sl] = c_kev
            P.op("vector", lambda v, cch=cch, w=w, sq=cch % 2: v.tensor_mul(
                sq_scr[sq][:, 0:w], kraw_s[:, cch * 512:cch * 512 + w],
                kraw_s[:, cch * 512:cch * 512 + w]),
                 waits=[c_kev, c_sq_used[cch % 2]], inc=sV)
            c_sq = (sV, sV.v)
            P.op("tensor", lambda te, cch=cch, w=w, sq=cch % 2: te.matmul(
                ss_ps[:, 0:w], Esel[:, cch * 16:(cch + 1) * 16], sq_scr[sq][:, 0:w],
                start=(cch == 0), stop=(cch == 7)),
                 waits=[c_sq, c_bsmall, st1["ss_free"] if cch == 0 else None], inc=sT)
            c_sq_used[cch % 2] = (sT, sT.v)
        st1[("ssmm", "k", i)] = (sT, sT.v)

    def emit_chain_k(i):
        off, w = k_macs[i]
        prevk = st1[("ssmm", "k", i)]
        for j, fn in enumerate([
            lambda s, w=w: s.activation(rk_f[0:16, 0:w], ss_ps[0:16, 0:w], AF.Ln),
            lambda s, w=w: s.activation(rk_f[0:16, 0:w], rk_f[0:16, 0:w], AF.Exp,
                                        scale=0.5),
            lambda s, w=w: s.activation(rk_f[0:16, 0:w], rk_f[0:16, 0:w], AF.Ln,
                                        bias=consts[0:16, 1:2]),
            lambda s, w=w: s.activation(rk_f[0:16, 0:w], rk_f[0:16, 0:w], AF.Exp,
                                        scale=-1.0),
        ]):
            P.op("scalar", fn, waits=[prevk], inc=sS)
            prevk = (sS, sS.v)
            if j == 0:
                st1["ss_free"] = prevk
        st1[("rk", i)] = prevk

    def emit_Rtt_k(i):
        off, w = k_macs[i]
        c_rk = st1[("rk", i)]
        tts = [None] * 8
        for cch in range(8):
            sl = cch % 2
            P.op("tensor", lambda te, cch=cch, w=w, sl=sl: te.matmul(
                rr2_ps[sl][:, 0:w], Ssel[:, cch * 128:(cch + 1) * 128],
                rk_f[0:16, 0:w], start=True, stop=True),
                 waits=[c_rk, c_bsmall, c_rr2_used[sl]], inc=sT)
            c_R = (sT, sT.v)
            P.op("vector", lambda v, cch=cch, off=off, w=w, sl=sl: v.tensor_mul(
                knT_s[:, cch * HT + off:cch * HT + off + w],
                kraw_s[:, cch * 512:cch * 512 + w], rr2_ps[sl][:, 0:w]),
                 waits=[c_R], inc=sV)
            c_rr2_used[sl] = (sV, sV.v)
            tts[cch] = (sV, sV.v)
        st1[("tt", i)] = tts
        st1["kraw_used"] = (sV, sV.v)

    def emit_proj_q(m):
        off = HALO + m * 512
        for cch in range(8):
            sl = cch % 4

            def mm_q(te, cch=cch, off=off, sl=sl):
                for kc in range(8):
                    ins = te.matmul(pB2[sl][:, 0:512],
                                    WqT_s[:, kc * D + cch * 128:kc * D + (cch + 1) * 128],
                                    xnT_s[:, kc * HT + off:kc * HT + off + 512],
                                    start=(kc == 0), stop=(kc == 7))
                return ins
            P.op("tensor", mm_q, waits=[c_WqT, c_pB2_free[sl]], inc=sT)
            c_qmm = (sT, sT.v)
            P.op("scalar", lambda s, cch=cch, m=m, sl=sl: s.activation(
                qT_s[:, cch * T + m * 512:cch * T + (m + 1) * 512], pB2[sl][:, 0:512],
                AF.Identity, bias=bq_s[:, cch:cch + 1]), waits=[c_qmm, c_bsmall], inc=sS)
            c_qev = (sS, sS.v)
            c_pB2_free[sl] = c_qev
            P.op("vector", lambda v, cch=cch, m=m, sq=cch % 2: v.tensor_mul(
                sq_scr[sq][:, 0:512],
                qT_s[:, cch * T + m * 512:cch * T + (m + 1) * 512],
                qT_s[:, cch * T + m * 512:cch * T + (m + 1) * 512]),
                 waits=[c_qev, c_sq_used[cch % 2]], inc=sV)
            c_sq = (sV, sV.v)
            P.op("tensor", lambda te, cch=cch, sq=cch % 2: te.matmul(
                ss_ps[:, 0:512], Esel[:, cch * 16:(cch + 1) * 16],
                sq_scr[sq][:, 0:512], start=(cch == 0), stop=(cch == 7)),
                 waits=[c_sq, c_bsmall, st1["ss_free"] if cch == 0 else None], inc=sT)
            c_sq_used[cch % 2] = (sT, sT.v)
        st1[("ssmm", "q", m)] = (sT, sT.v)

    def emit_chain_q(m):
        prevq = st1[("ssmm", "q", m)]
        for j, fn in enumerate([
            lambda s: s.activation(rq_f[0:16, 0:512], ss_ps[0:16, 0:512], AF.Ln),
            lambda s: s.activation(rq_f[0:16, 0:512], rq_f[0:16, 0:512], AF.Exp,
                                   scale=0.5),
            lambda s: s.activation(rq_f[0:16, 0:512], rq_f[0:16, 0:512], AF.Ln,
                                   bias=consts[0:16, 1:2]),
            lambda s: s.activation(rq_f[0:16, 0:512], rq_f[0:16, 0:512], AF.Exp,
                                   scale=-1.0),
        ]):
            P.op("scalar", fn, waits=[prevq], inc=sS)
            prevq = (sS, sS.v)
            if j == 0:
                st1["ss_free"] = prevq
        st1[("rq", m)] = prevq

    def emit_tr_q(m):
        c_rq = st1[("rq", m)]
        for tt in range(4):
            P.op("tensor", lambda te, tt=tt: te.transpose(
                sm2_ps[0:128, 0:16], rq_f[0:16, tt * 128:(tt + 1) * 128],
                identf[0:16, 0:16]), waits=[c_rq, st1["smT_used"]], inc=sT)
            c_trq = (sT, sT.v)
            P.op("vector", lambda v, m=m, tt=tt: v.tensor_copy(
                rqT_s[:, (m * 4 + tt) * 16:(m * 4 + tt) * 16 + 16],
                sm2_ps[0:128, 0:16]), waits=[c_trq], inc=sV)
            st1["smT_used"] = (sV, sV.v)

    pend_tr = None
    for i in range(5):
        emit_proj_k(i)
        if pend_tr is not None:
            emit_tr_q(pend_tr)
        emit_chain_k(i)
        if i < 4:
            emit_proj_q(i)
        emit_Rtt_k(i)
        if i < 4:
            emit_chain_q(i)
            pend_tr = i
    emit_tr_q(pend_tr)
    c_kraw_used = st1["kraw_used"]

    c_pB2v_free = list(c_pB2_free)
    prev_st = list(st1[("tt", 4)])
    for (off, w) in k_macs:
        cur_st = [None] * 8
        for mc in range(8):
            sl = mc % 4

            def mm_v(te, mc=mc, off=off, w=w, sl=sl):
                for kc in range(8):
                    ins = te.matmul(
                        pB2[sl][:, 0:w],
                        WvT_s[:, kc * D + mc * 128:kc * D + (mc + 1) * 128],
                        xnT_s[:, kc * HT + off:kc * HT + off + w],
                        start=(kc == 0), stop=(kc == 7))
                return ins
            P.op("tensor", mm_v, waits=[c_WvT, c_pB2v_free[sl]], inc=sT)
            c_vmm = (sT, sT.v)
            P.op("scalar", lambda s, mc=mc, w=w, sl=sl: s.activation(
                kraw_s[:, mc * 512:mc * 512 + w], pB2[sl][:, 0:w], AF.Identity,
                bias=bv_s[:, mc:mc + 1]),
                 waits=[c_vmm, c_bsmall, prev_st[mc]], inc=sS)
            c_vev = (sS, sS.v)
            c_pB2v_free[sl] = c_vev
            P.op("gpsimd", lambda g, mc=mc, off=off, w=w: g.dma_start(
                v_dram[mc * 128:(mc + 1) * 128, off:off + w],
                kraw_s[:, mc * 512:mc * 512 + w]),
                 waits=[c_vev], inc=sDg, inc_n=16)
            cur_st[mc] = (sDg, sDg.v)
        prev_st = cur_st
    c_p1a_bar = bar()

    if _DBG:
        d_knT = nc.dram_tensor("d_knT", [128, 8 * HT], BF16, kind="ExternalOutput").ap()
        d_qT = nc.dram_tensor("d_qT", [128, 8 * T], BF16, kind="ExternalOutput").ap()
        d_rqT = nc.dram_tensor("d_rqT", [128, NTI * NH], F32, kind="ExternalOutput").ap()
        for dst, srct in [(d_knT, knT_s), (d_qT, qT_s), (d_rqT, rqT_s)]:
            P.op("gpsimd", lambda g, dst=dst, srct=srct: g.dma_start(dst, srct[:, :]),
                 waits=c_p1a_bar, inc=sDg, inc_n=16)

    if _UPTO <= 3:
        stub_out(c_p1a_bar)
        return finish()

    psB.close()
    sbB1.close()
    xL_sb = [enB(sb(f"xL_sb{i}", [128, D], F32)) for i in range(2)]
    vw_a = [enB(sb(f"vw_a{i}", [128, D], BF16)) for i in range(2)]
    vw_b = [enB(sb(f"vw_b{i}", [128, D], BF16)) for i in range(2)]
    vw_c = [enB(sb(f"vw_c{i}", [128, D], BF16)) for i in range(2)]
    e_sb = [enB(sb(f"e_sb{i}", [128, 4 * WIN], BF16)) for i in range(2)]
    pn_sb = [enB(sb(f"pn_sb{i}", [128, 4 * WIN], BF16)) for i in range(4)]
    sums = enB(sb("sums", [128, NH], F32))
    rsum = enB(sb("rsum", [128, NH], F32))
    pT_sb = [enB(sb(f"pT_sb{i}", [128, 1024], BF16)) for i in range(2)]
    outT_sb = [enB(sb(f"outT_sb{i}", [128, 128], BF16)) for i in range(2)]
    out2_sb = [enB(sb(f"out2_sb{i}", [128, 128], BF16)) for i in range(2)]
    out_sb = [enB(sb(f"out_sb{i}", [128, D], F32)) for i in range(2)]
    y1_sb = [enB(sb(f"y1_sb{i}", [128, D], F32)) for i in range(2)]
    yb_sb = [enB(sb(f"yb_sb{i}", [128, D], BF16)) for i in range(2)]
    c_zer = c_consts
    psC = ExitStack()
    enCp = psC.enter_context
    y1_ps = enCp(psn("y1_ps", [128, 1024], F32))
    scg_ps = enCp(psn("scg_ps", [128, 4 * WIN], F32))
    pT_ps = [enCp(psn(f"pT_ps{i}", [128, 1024], BF16)) for i in range(2)]
    av_ps = [enCp(psn(f"av_ps{i}", [128, 512], F32)) for i in range(2)]

    c_vw_used = [None, None]
    c_x_free = [None, None]
    c_y1sb_free = [None, None]
    c_yb_free = [None, None]
    c_out_store = [None, None]
    c_y1_free = None
    c_scg_free = None
    c_exp_by = [None, None]
    c_stt_prev = None
    c_stt_by = [None, None]
    c_pn_by = [None, None]
    c_tr_by = [None, None]
    c_pTev_by = [None, None]
    c_av_by = [None, None]
    c_avev_by = [None, None]
    c_wo_by = [None, None]
    c_wo_last = None
    c_pn_key = {}

    def emit_A_hg(t, hg, first_bar):
        wb = t * 128 + 32
        moff = 0 if t == 0 else (2 * WIN if t == NTI - 1 else WIN)
        if True:
            par = hg % 2

            def mm_scg(te, t=t, hg=hg, wb=wb):
                for i in range(2):
                    c = 2 * hg + i
                    te.matmul(scg_ps[:, 2 * i * WIN:(2 * i + 1) * WIN],
                              qT_s[0:64, c * T + t * 128:c * T + (t + 1) * 128],
                              knT_s[0:64, c * HT + wb:c * HT + wb + WIN],
                              start=True, stop=True)
                te.matmul(y1_ps[:, 0:128], zer_sb[:, :], zer_sb[:, :],
                          start=False, stop=False, skip_group_check=True)
                for i in range(2):
                    c = 2 * hg + i
                    te.matmul(scg_ps[:, (2 * i + 1) * WIN:(2 * i + 2) * WIN],
                              qT_s[64:128, c * T + t * 128:c * T + (t + 1) * 128],
                              knT_s[64:128, c * HT + wb:c * HT + wb + WIN],
                              start=True, stop=True)
                return te.matmul(y1_ps[:, 0:128], zer_sb[:, :], zer_sb[:, :],
                                 start=False, stop=False, skip_group_check=True)
            P.op("tensor", mm_scg, waits=[st["scg_free"], c_zer] + first_bar,
                 inc=sT)
            c_sc = (sT, sT.v)

            def f_exp(s, t=t, hg=hg, par=par):
                for i in range(4):
                    h = 4 * hg + i
                    ins = s.activation(e_sb[par][:, i * WIN:(i + 1) * WIN],
                                       scg_ps[:, i * WIN:(i + 1) * WIN], AF.Exp,
                                       scale=rqT_s[:, t * NH + h:t * NH + h + 1])
                return ins
            P.op("scalar", f_exp,
                 waits=[c_sc, st["stt_by"][par]] + first_bar, inc=sS)
            c_exp = (sS, sS.v)
            st["scg_free"] = c_exp
            st["exp_by"][par] = c_exp

            def f_stt(v, t=t, hg=hg, par=par, moff=moff):
                for i in range(4):
                    h = 4 * hg + i
                    ins = v.scalar_tensor_tensor(
                        pn_sb[hg][:, i * WIN:(i + 1) * WIN],
                        e_sb[par][:, i * WIN:(i + 1) * WIN], 1.0,
                        masks[:, moff:moff + WIN], ALU.mult, ALU.mult,
                        accum_out=sums[:, 4 * hg + i:4 * hg + i + 1])
                return ins
            P.op("vector", f_stt,
                 waits=[c_exp, c_bsmall, st["tr_by"][hg]] + first_bar, inc=sV)
            st["stt_by"][par] = (sV, sV.v)
            P.op("vector", lambda v, hg=hg: v.reciprocal(
                rsum[:, 4 * hg:4 * hg + 4], sums[:, 4 * hg:4 * hg + 4]),
                 waits=[st["stt_by"][par]], inc=sV)
            c_rs2 = (sV, sV.v)

            def f_pn(v, hg=hg):
                for i in range(4):
                    h = 4 * hg + i
                    ins = v.tensor_scalar(pn_sb[hg][:, i * WIN:(i + 1) * WIN],
                                          pn_sb[hg][:, i * WIN:(i + 1) * WIN],
                                          rsum[:, h:h + 1], None, ALU.mult)
                return ins
            P.op("vector", f_pn, waits=[c_rs2], inc=sV)
            c_pn_key[(t, hg)] = (sV, sV.v)

    ld = {}

    def emit_loads(t):
        sl2 = t % 2
        hb = HALO + t * 128
        P.op("sync", lambda sy, t=t, sl2=sl2: sy.dma_start(
            vw_a[sl2][:, :], v_dram[:, t * 128 + 32:t * 128 + 160], transpose=True),
             waits=[c_vw_used[sl2], c_p1a_bar[3]], inc=sDs, inc_n=16)
        P.op("sync", lambda sy, t=t, sl2=sl2: sy.dma_start(
            vw_b[sl2][:, :], v_dram[:, t * 128 + 160:t * 128 + 288], transpose=True),
             inc=sDs, inc_n=16)
        P.op("sync", lambda sy, t=t, sl2=sl2: sy.dma_start(
            vw_c[sl2][:, :], v_dram[:, t * 128 + 96:t * 128 + 224], transpose=True),
             inc=sDs, inc_n=16)
        c_vw = (sDs, sDs.v)
        P.op("gpsimd", lambda g, hb=hb, sl2=sl2: g.dma_start(
            xL_sb[sl2][:, :], xh[hb:hb + 128, :]),
             waits=[c_x_free[sl2]], inc=sDg, inc_n=16)
        c_xl = (sDg, sDg.v)
        P.op("sync", lambda sy, t=t, sl2=sl2: sy.dma_start(
            yb_sb[sl2][:, :], y2_dram[:, t * 128:(t + 1) * 128], transpose=True),
             waits=[c_y2_all, c_yb_free[sl2]], inc=sDs, inc_n=16)
        c_yb = (sDs, sDs.v)
        ld[t] = (c_vw, c_xl, c_yb)

    def emit_B_pre(t, first_bar):
        sl2 = t % 2
        c_vw, c_xl, c_yb = ld[t]

        def mm_bo(te):
            te.matmul(y1_ps[:, 0:512], oner0b[:, :], bor[:, 0:512],
                      start=True, stop=False, skip_group_check=True)
            return te.matmul(y1_ps[:, 512:1024], oner0b[:, :], bor[:, 512:1024],
                             start=True, stop=False, skip_group_check=True)
        P.op("tensor", mm_bo, waits=[c_bsmall, st["y1_free"]] + first_bar, inc=sT)
        return c_vw, c_xl, c_yb

    def emit_B_sb(t, sb2, c_vw, first_bar):
        sl2 = t % 2
        if True:
            pairs = (2 * sb2, 2 * sb2 + 1)
            par = sb2 % 2
            slb = sb2 % 2

            def f_tr(te, t=t, sb2=sb2, slb=slb):
                for j in range(2):
                    i0 = 2 * j * WIN
                    o0 = 512 * j
                    te.transpose(pT_ps[slb][:, o0:o0 + 128],
                                 pn_sb[sb2][:, i0:i0 + 128], identb[:, :])
                    te.transpose(pT_ps[slb][0:64, o0 + 128:o0 + 256],
                                 pn_sb[sb2][:, i0 + 128:i0 + WIN], identb[:, :])
                    te.transpose(pT_ps[slb][0:64, o0 + 256:o0 + 384],
                                 pn_sb[sb2][:, i0 + WIN:i0 + WIN + 64], identb[:, :])
                    ins = te.transpose(pT_ps[slb][:, o0 + 384:o0 + 512],
                                       pn_sb[sb2][:, i0 + WIN + 64:i0 + 2 * WIN],
                                       identb[:, :])
                return ins
            P.op("tensor", f_tr,
                 waits=[c_pn_key[(t, sb2)], st["pTev_by"][slb]] + first_bar, inc=sT)
            st["tr_by"][sb2] = (sT, sT.v)
            c_tr = st["tr_by"][sb2]

            if sb2 % 2 == 0:
                P.op("scalar", lambda s, slb=slb: s.copy(pT_sb[slb][:, :],
                                                         pT_ps[slb][:, :]),
                     waits=[c_tr, st["av_by"][slb]], inc=sS)
                st["pTev_by"][slb] = (sS, sS.v)
            else:
                P.op("vector", lambda v, slb=slb: v.tensor_copy(pT_sb[slb][:, :],
                                                                pT_ps[slb][:, :]),
                     waits=[c_tr, st["av_by"][slb]], inc=sV)
                st["pTev_by"][slb] = (sV, sV.v)

            def f_av(te, pairs=pairs, sl2=sl2, slb=slb):
                for j, p in enumerate(pairs):
                    A, Bh = 2 * p, 2 * p + 1
                    co = j * 256
                    po = j * 512
                    te.matmul(av_ps[slb][0:64, co:co + 128],
                              vw_a[sl2][:, A * 64:(A + 1) * 64],
                              pT_sb[slb][:, po:po + 128], start=True, stop=False,
                              skip_group_check=True)
                    te.matmul(av_ps[slb][0:64, co:co + 128],
                              vw_b[sl2][0:64, A * 64:(A + 1) * 64],
                              pT_sb[slb][0:64, po + 128:po + 256], start=False,
                              stop=True, skip_group_check=True)
                    te.matmul(av_ps[slb][0:64, co + 128:co + 256],
                              vw_a[sl2][0:64, Bh * 64:(Bh + 1) * 64],
                              pT_sb[slb][0:64, po + 256:po + 384], start=True,
                              stop=False, skip_group_check=True)
                    ins = te.matmul(av_ps[slb][0:64, co + 128:co + 256],
                                    vw_c[sl2][:, Bh * 64:(Bh + 1) * 64],
                                    pT_sb[slb][:, po + 384:po + 512], start=False,
                                    stop=True, skip_group_check=True)
                return ins
            P.op("tensor", f_av,
                 waits=[st["pTev_by"][slb], c_vw, st["avev_by"][slb]] + first_bar,
                 inc=sT)
            st["av_by"][slb] = (sT, sT.v)

            if sb2 % 2 == 0:
                def f_ev(v, slb=slb):
                    v.tensor_copy(outT_sb[slb][0:64, :], av_ps[slb][0:64, 0:128])
                    v.tensor_copy(outT_sb[slb][64:128, :], av_ps[slb][0:64, 128:256])
                    v.tensor_copy(out2_sb[slb][0:64, :], av_ps[slb][0:64, 256:384])
                    return v.tensor_copy(out2_sb[slb][64:128, :],
                                         av_ps[slb][0:64, 384:512])
                P.op("vector", f_ev, waits=[st["av_by"][slb], st["wo_by"][slb]],
                     inc=sV)
                st["avev_by"][slb] = (sV, sV.v)
            else:
                def f_ev(s, slb=slb):
                    s.copy(outT_sb[slb][0:64, :], av_ps[slb][0:64, 0:128])
                    s.copy(outT_sb[slb][64:128, :], av_ps[slb][0:64, 128:256])
                    s.copy(out2_sb[slb][0:64, :], av_ps[slb][0:64, 256:384])
                    return s.copy(out2_sb[slb][64:128, :], av_ps[slb][0:64, 384:512])
                P.op("scalar", f_ev, waits=[st["av_by"][slb], st["wo_by"][slb]],
                     inc=sS)
                st["avev_by"][slb] = (sS, sS.v)

            def f_wo(te, pairs=pairs, slb=slb):
                for j, p in enumerate(pairs):
                    src_t = outT_sb[slb] if j == 0 else out2_sb[slb]
                    te.matmul(y1_ps[:, 0:512], src_t[:, :],
                              WoT_s[:, p * D:p * D + 512], start=False, stop=False,
                              skip_group_check=True)
                    ins = te.matmul(y1_ps[:, 512:1024], src_t[:, :],
                                    WoT_s[:, p * D + 512:(p + 1) * D], start=False,
                                    stop=(p == 7), skip_group_check=True)
                return ins
            P.op("tensor", f_wo, waits=[st["avev_by"][slb], c_WoT], inc=sT)
            st["wo_by"][slb] = (sT, sT.v)

    def emit_B_post(t, c_xl, c_yb):
        sl2 = t % 2
        c_wo_last = st["wo_by"][1]

        P.op("scalar", lambda s, sl2=sl2: s.copy(y1_sb[sl2][:, :], y1_ps[:, :]),
             waits=[c_wo_last, c_y1sb_free[sl2]], inc=sS)
        st["y1_free"] = (sS, sS.v)
        P.op("vector", lambda v, sl2=sl2: v.scalar_tensor_tensor(
            out_sb[sl2][:, :], y1_sb[sl2][:, :], 1.0, xL_sb[sl2][:, :],
            ALU.mult, ALU.add),
             waits=[st["y1_free"], c_xl, c_out_store[sl2]], inc=sV)
        P.op("vector", lambda v, sl2=sl2: v.tensor_add(
            out_sb[sl2][:, :], out_sb[sl2][:, :], yb_sb[sl2][:, :]),
             waits=[c_yb, (sV, sV.v)], inc=sV)
        c_fin = (sV, sV.v)
        c_y1sb_free[sl2] = c_fin
        c_yb_free[sl2] = c_fin
        c_x_free[sl2] = c_fin
        c_vw_used[sl2] = st["av_by"][1]
        P.op("gpsimd", lambda g, t=t, sl2=sl2: g.dma_start(
            out[t * 128:(t + 1) * 128, :], out_sb[sl2][:, :]),
             waits=[c_fin], inc=sOut, inc_n=16)
        c_out_store[sl2] = (sOut, sOut.v)

    st = {"scg_free": None, "exp_by": [None, None], "stt_by": [None, None],
          "tr_by": [None, None, None, None], "pTev_by": [None, None],
          "av_by": [None, None], "avev_by": [None, None], "wo_by": [None, None],
          "y1_free": None}

    for hg in range(4):
        emit_A_hg(0, hg, c_p1a_bar)
    for t in range(NTI):
        fb2 = c_p1a_bar if t == 0 else []
        emit_loads(t)
        c_vw, c_xl, c_yb = emit_B_pre(t, fb2)
        for i in range(4):
            emit_B_sb(t, i, c_vw, fb2)
            if t + 1 < NTI:
                emit_A_hg(t + 1, i, fb2)
        emit_B_post(t, c_xl, c_yb)

    P.op("gpsimd", lambda g: g.engine_nop(), waits=[(sOut, 16 * NTI)])

    return finish()


def _masks_for(q0):
    def m(t):
        qi = np.arange(128)[:, None]
        kk = np.arange(WIN)[None, :]
        qg = q0 + t * 128 + qi
        kg = q0 + t * 128 - 32 + kk
        allowed = ((np.abs(kg - qg) <= 32) & ((kg - qg) % 4 == 0)
                   & (kg >= 0) & (kg < 8192))
        return np.ascontiguousarray(allowed.astype(np.float32))
    return m(0), m(7), m(NTI - 1)


_NC_CACHE = None


def kernel(**inputs):
    global _NC_CACHE
    f = lambda k: np.ascontiguousarray(np.asarray(inputs[k], dtype=np.float32))
    x = f("L0_attn_x")
    Wq, bq = f("L0_attn_Wq"), f("L0_attn_bq")
    Wk, bk = f("L0_attn_Wk"), f("L0_attn_bk")
    Wv, bv = f("L0_attn_Wv"), f("L0_attn_bv")
    Wo, bo = f("L0_attn_Wo"), f("L0_attn_bo")
    gW = f("L0_ffn_gate_W")
    W1, b1 = f("L0_ffn_W1"), f("L0_ffn_b1")
    W2, b2 = f("L0_ffn_W2"), f("L0_ffn_b2")
    Ws1, bs1 = f("L0_ffn_Ws1"), f("L0_ffn_bs1")
    Ws2, bs2 = f("L0_ffn_Ws2"), f("L0_ffn_bs2")
    rsv = f("L0_ffn_route_scale")

    shared = {
        "Wq": Wq, "Wk": Wk, "Wv": Wv, "Wo": Wo,
        "bqt": np.ascontiguousarray(bq.reshape(8, 128).T),
        "bkt": np.ascontiguousarray(bk.reshape(8, 128).T),
        "bvt": np.ascontiguousarray(bv.reshape(8, 128).T),
        "bo_row": bo.reshape(1, D),
        "gW": gW, "W1": W1, "W2": W2,
        "b1t": np.ascontiguousarray(b1.reshape(8, 4, 128).transpose(2, 0, 1)
                                    .reshape(128, 32)),
        "B2": b2,
        "Ws1": Ws1, "Ws2": Ws2,
        "bs1t": np.ascontiguousarray(bs1.reshape(4, 128).T),
        "bs2t": np.ascontiguousarray(bs2.reshape(8, 128).T),
        "rs": np.concatenate([rsv.reshape(1, 1),
                              np.zeros((127, 1), np.float32)], axis=0),
    }

    in_maps = []
    for core in range(8):
        b, qq = core // 4, core % 4
        q0 = qq * T
        xh = np.zeros((HT, D), np.float32)
        lo = max(0, q0 - HALO)
        hi = min(8192, q0 + T + HALO)
        xh[lo - (q0 - HALO):hi - (q0 - HALO)] = x[b, lo:hi]
        mf, mi, ml = _masks_for(q0)
        m = dict(shared)
        m.update({"xh": xh, "mfirst": mf, "mint": mi, "mlast": ml})
        in_maps.append(m)

    if _NC_CACHE is None:
        _NC_CACHE = build_nc()
    nc = _NC_CACHE
    res = run_bass_kernel_spmd(nc, in_maps, core_ids=list(range(8)))

    outp = np.empty((2, 8192, D), np.float32)
    for core in range(8):
        b, qq = core // 4, core % 4
        outp[b, qq * T:(qq + 1) * T] = np.asarray(res.results[core]["out"],
                                                  dtype=np.float32)
    return outp


if __name__ == "__main__":
    import sys
    sys.path.insert(0, "/root/problem")
    from reference import setup_inputs, reference
    ins = setup_inputs(0)
    ref = np.asarray(reference(**ins))
    got = kernel(**{k: np.asarray(v) for k, v in ins.items()})
    err = np.mean((got - ref) ** 2) / np.mean(ref ** 2)
    print("residual variance ratio:", err)
